# version 04
# baseline (speedup 1.0000x reference)
import math
from contextlib import ExitStack

import numpy as np
import ml_dtypes
import concourse.bass as bass
import concourse.mybir as mybir
from concourse.bass_utils import run_bass_kernel_spmd

F32 = mybir.dt.float32
BF16 = mybir.dt.bfloat16
AF = mybir.ActivationFunctionType
ALU = mybir.AluOpType
AX = mybir.AxisListType

S = 2048
D = 1024
NCH = 8
H = 16
DH = 64
FF = 4096
NE = 8
BF = ml_dtypes.bfloat16
LN_EPS = 1e-5
GATEK = 1e20


def _l0_bias(delta):
    d = np.asarray(delta, dtype=np.float64)
    return (np.exp(-0.5 * (d / 2.0) ** 2) + np.cos(2.0 * np.pi * d / 401.0)).astype(np.float32)


def make_l0_consts():
    tabs = []
    for t in range(16):
        B = np.full((128, 256), -2e9, dtype=np.float32)
        q = 128 * t + np.arange(128)
        c = np.arange(128)
        for s_i, base in ((0, 128 * t - 2), (1, 128 * t + 126)):
            key = base + c
            dlt = key[:, None] - q[None, :]
            v = (np.abs(dlt) <= 2) & (key[:, None] >= 0) & (key[:, None] < S) & (key[:, None] != 0)
            blk = np.full((128, 128), -2e9, dtype=np.float32)
            blk[v] = _l0_bias(dlt)[v]
            B[:, 128 * s_i:128 * (s_i + 1)] = blk
        tabs.append(B)
    Bfirst = np.ascontiguousarray(np.concatenate(tabs[0:8], axis=1))
    Blast = np.ascontiguousarray(np.concatenate(tabs[8:16], axis=1))
    b0 = _l0_bias(np.arange(S))
    B0T = np.ascontiguousarray(
        np.tile(b0.reshape(16, 128).T[:, :, None], (1, 1, 16))).astype(np.float32)
    key = 128 * np.arange(17)[None, :] - 2 + np.arange(128)[:, None]
    b0sh = np.where((key >= 0) & (key < S), _l0_bias(np.abs(key)), np.float32(-2e9)).astype(np.float32)
    B0sh = np.ascontiguousarray(np.tile(b0sh[:, :, None], (1, 1, 16))).astype(np.float32)
    return Bfirst, Blast, B0T, B0sh


def col_layout(v):
    return np.ascontiguousarray(np.asarray(v, np.float32).reshape(-1, 128).T)


def wchunks(W):
    W = np.asarray(W, np.float32)
    K, M = W.shape
    return np.ascontiguousarray(W.reshape(K // 128, 128, M))



_UN = [0]


def un(base):
    _UN[0] += 1
    return f"{base}_{_UN[0]}"

ENGS = ("sync", "tensor", "vector", "scalar", "gpsimd")


class Ctr:
    def __init__(self, h):
        self.h = h
        self.v = 0

    def bump(self, n=1):
        self.v += n
        return self.v


class Prog:
    def __init__(self, nc):
        self.nc = nc
        self.ctx = ExitStack()
        self.q = {e: [] for e in ENGS}
        self._n = 0
        self._sems = {}
        self.default_scope = None
        self.ec = {"vector": self.sem("dvc"), "scalar": self.sem("evc"),
                   "gpsimd": self.sem("gpc")}

    def sem(self, name=None):
        if name is not None and name in self._sems:
            return self._sems[name]
        self._n += 1
        c = Ctr(self.ctx.enter_context(self.nc.semaphore(name or f"s{self._n}")))
        if name is not None:
            self._sems[name] = c
        return c

    LEDGER = []

    def sb(self, name, shape, dtype, scope=None, side=None):
        import math as _m
        free = 1
        for d in shape[1:]:
            free *= d
        bts = free * (2 if dtype == BF16 else 4)
        try:
            h = (scope or self.default_scope or self.ctx).enter_context(
                self.nc.sbuf_tensor(name, shape, dtype, side=side))
            Prog.LEDGER.append((name, bts, side or "left", self.nc.sbuf_base, self.nc.sbuf_top))
            return h
        except BaseException:
            for rec in Prog.LEDGER:
                print("  alloc", rec)
            raise

    def on(self, eng, fn):
        self.q[eng].append(fn)

    def wait(self, eng, ctr, val=None):
        v = ctr.v if val is None else val
        if v > 0:
            self.on(eng, lambda e, v=v: e.wait_ge(ctr.h, v))

    def dma(self, eng, out, in_, ctr):
        self.on(eng, lambda e: e.dma_start(out=out, in_=in_).then_inc(ctr.h, 16))
        return ctr.bump(16)

    def mm(self, out, lhsT, rhs, start, stop, inc=None):
        if inc is None:
            self.on("tensor", lambda e: e.matmul(out, lhsT, rhs, start=start, stop=stop))
            return None
        v = inc.bump()
        self.on("tensor", lambda e: e.matmul(out, lhsT, rhs, start=start, stop=stop).then_inc(inc.h, 1))
        return v

    def t_transpose(self, out, in_, ident, inc=None):
        if inc is None:
            self.on("tensor", lambda e: e.transpose(out, in_, ident))
            return None
        v = inc.bump()
        self.on("tensor", lambda e: e.transpose(out, in_, ident).then_inc(inc.h, 1))
        return v

    def _chained(self, eng, fn):
        c = self.ec[eng]
        prev = c.v
        v = c.bump()
        def run(e, prev=prev, fn=fn, c=c):
            if prev > 0:
                e.wait_ge(c.h, prev)
            fn(e).then_inc(c.h, 1)
        self.on(eng, run)
        return v

    def act(self, out, in_, func, bias=0.0, scale=1.0, accum_out=None, inc=None):
        return self._chained("scalar",
                             lambda e: e.activation(out, in_, func, bias=bias,
                                                    scale=scale, accum_out=accum_out))

    def vec(self, meth, *args, inc=None, **kw):
        return self._chained("vector", lambda e: getattr(e, meth)(*args, **kw))

    def gp(self, meth, *args, inc=None, **kw):
        return self._chained("gpsimd", lambda e: getattr(e, meth)(*args, **kw))

    def run(self):
        with self.nc.Block() as block:
            for eng in ENGS:
                fns = self.q[eng]
                if not fns:
                    continue
                getattr(block, eng)(lambda e, fns=fns: [f(e) for f in fns])
        self.ctx.close()


def make_common(p):
    c = {}
    c["ones_colbf"] = p.sb(un("ones_colbf"), [128, 1], BF16)
    c["ones_rowf"] = p.sb(un("ones_rowf"), [1, 128], F32)
    c["ones_rowbf"] = p.sb(un("ones_rowbf"), [1, 128], BF16)
    c["one1bf"] = p.sb(un("one1bf"), [1, 1], BF16)
    c["onesP"] = p.sb(un("onesP"), [65, 128], F32)
    p.vec("memset", c["onesP"][:], 1.0)
    p.vec("memset", c["ones_colbf"][:], 1.0)
    p.vec("memset", c["ones_rowf"][:], 1.0)
    p.vec("memset", c["ones_rowbf"][:], 1.0)
    c["init_dv"] = p.vec("memset", c["one1bf"][:], 1.0)
    return c


def emit_ln_featmajor(p, xdram, lnq, common, lx, mm, ev, dv, raw_gate=None,
                      raw_gate2=None, keep=None):
    nc = p.nc
    sc = ExitStack()
    xc = p.sb(un("lnxc"), [128, 2, S], F32, scope=sc)
    xbf = p.sb(un("lnxbf"), [128, S], BF16, scope=sc)
    xsq = p.sb(un("lnxsq"), [128, S], BF16, scope=sc)
    rows_scope = keep if keep is not None else sc
    rows_side = "right" if keep is not None else None
    murow = p.sb(un("murow"), [1, S], F32, scope=rows_scope, side=rows_side)
    s2row = p.sb(un("s2row"), [1, S], F32, scope=sc)
    Abcs = p.sb(un("Abcs"), [128, S], F32, scope=rows_scope, side=rows_side)
    Bbcs = p.sb(un("Bbcs"), [128, S], F32, scope=sc)
    tf = p.sb(un("lntf"), [128, S], F32, scope=sc)
    tf2 = p.sb(un("lntf2"), [128, S], F32, scope=sc)

    Vc, Ec, Mc = [0] * NCH, [0] * NCH, [0] * NCH
    with nc.psum_tensor(un("lnps1"), [1, S], F32) as ps1, \
         nc.psum_tensor(un("lnps2"), [1, S], F32) as ps2:
        for c in range(NCH):
            slot = c % 2
            if c >= 2:
                p.wait("sync", dv, Vc[c - 2])
                p.wait("sync", mm, Mc[c - 2])
            ldv = p.dma("sync", xc[:, slot, :], xdram[:, c, :], lx[slot])
            p.wait("vector", lx[slot], ldv)
            if c >= 1:
                p.wait("vector", mm, Mc[c - 1])
            Vc[c] = p.vec("tensor_copy", xbf[:], xc[:, slot, :], inc=dv)
            p.wait("scalar", dv, Vc[c])
            if c >= 1:
                p.wait("scalar", mm, Mc[c - 1])
            Ec[c] = p.act(xsq[:], xbf[:], AF.Square, inc=ev)
            p.wait("tensor", ev, Ec[c])
            for n in range(4):
                p.mm(ps1[:, 512 * n:512 * (n + 1)], common["ones_colbf"][:],
                     xbf[:, 512 * n:512 * (n + 1)], start=(c == 0), stop=(c == NCH - 1))
                p.mm(ps2[:, 512 * n:512 * (n + 1)], common["ones_colbf"][:],
                     xsq[:, 512 * n:512 * (n + 1)], start=(c == 0), stop=(c == NCH - 1),
                     inc=mm if (n == 3 and raw_gate is None) else None)
            if raw_gate is not None:
                raw_gate(c, xc[:, slot, :], ldv)
            Mc[c] = mm.v
        p.wait("scalar", mm, Mc[NCH - 1])
        p.act(murow[:], ps1[:], AF.Copy, scale=1.0 / D)
        e2 = p.act(s2row[:], ps2[:], AF.Copy, scale=1.0 / D, inc=ev)
    with nc.psum_tensor(un("lnpbA"), [128, S], F32) as pbA, \
         nc.psum_tensor(un("lnpbB"), [128, S], F32) as pbB:
        p.wait("tensor", ev, e2)
        mmv = None
        for n in range(4):
            p.mm(pbA[:, 512 * n:512 * (n + 1)], common["ones_rowf"][:],
                 murow[:, 512 * n:512 * (n + 1)], start=True, stop=True)
            mmv = p.mm(pbB[:, 512 * n:512 * (n + 1)], common["ones_rowf"][:],
                       s2row[:, 512 * n:512 * (n + 1)], start=True, stop=True,
                       inc=mm if n == 3 else None)
        p.wait("scalar", mm, mmv)
        p.act(Abcs[:], pbA[:], AF.Copy)
        e2b = p.act(Bbcs[:], pbB[:], AF.Copy, inc=ev)
        p.wait("vector", ev, e2b)
        p.vec("tensor_mul", tf[:], Abcs[:], Abcs[:])
        p.vec("scalar_tensor_tensor", tf[:], tf[:], -1.0, Bbcs[:], ALU.mult, ALU.add)
        d1 = p.vec("tensor_scalar", tf[:], tf[:], LN_EPS, None, ALU.add, inc=dv)
        p.wait("scalar", dv, d1)
        p.act(tf[:], tf[:], AF.Ln)
        e3 = p.act(tf[:], tf[:], AF.Exp, scale=-0.5, inc=ev)
        p.wait("vector", ev, e3)
        p.vec("scalar_tensor_tensor", Bbcs[:], Abcs[:], -1.0, tf[:],
              ALU.mult, ALU.mult)
        ev2 = p.vec("tensor_copy", Abcs[:], tf[:], inc=dv)
    applied = []
    M2c = [0] * NCH
    for c in range(NCH):
        slot = c % 2
        if c < 2:
            p.wait("sync", dv, Vc[NCH - 2 + c])
            p.wait("sync", mm, Mc[NCH - 1])
        else:
            p.wait("sync", dv, applied[c - 2])
            if raw_gate2 is not None:
                p.wait("sync", mm, M2c[c - 2])
        ldv = p.dma("sync", xc[:, slot, :], xdram[:, c, :], lx[slot])
        p.wait("vector", lx[slot], ldv)
        p.vec("tensor_mul", tf2[:], xc[:, slot, :], Abcs[:])
        applied.append(p.vec("tensor_add", lnq[:, c, :], tf2[:], Bbcs[:], inc=dv))
        if raw_gate2 is not None:
            raw_gate2(c, xc[:, slot, :], ldv)
            M2c[c] = mm.v
    sc.close()
    return murow, Abcs


def emit_proj(p, wsb, wld_val, wld_ctr, xin, evac, mcs, mm, ev):
    nc = p.nc
    evs = []
    with nc.psum_tensor(un("prps"), [128, 2, S], F32) as prps:
        p.wait("tensor", wld_ctr, wld_val)
        p.wait("tensor", ev, ev.v)
        for m in range(mcs):
            g = m % 2
            if m >= 2:
                p.wait("tensor", ev, evs[m - 2])
            for c in range(NCH):
                for n in range(4):
                    p.mm(prps[:, g, 512 * n:512 * (n + 1)],
                         wsb[:, c, 128 * m:128 * (m + 1)],
                         xin[:, c, 512 * n:512 * (n + 1)],
                         start=(c == 0), stop=(c == NCH - 1),
                         inc=mm if (c == NCH - 1 and n == 3) else None)
            p.wait("scalar", mm, mm.v)
            evs.append(evac(m, prps[:, g, :]))
    return evs[-1]


def emit_postln_residual(p, ysb, ev_y, xdram, xodram, common, lx, mm, ev, dv, st):
    nc = p.nc
    sc = ExitStack()
    ybf = p.sb(un("plybf"), [128, S], BF16, scope=sc)
    ysq = p.sb(un("plysq"), [128, S], BF16, scope=sc)
    murow = p.sb(un("plmu"), [1, S], F32, scope=sc)
    s2row = p.sb(un("pls2"), [1, S], F32, scope=sc)
    Abcs = p.sb(un("plA"), [128, S], F32, scope=sc)
    Bbcs = p.sb(un("plB"), [128, S], F32, scope=sc)
    xc = p.sb(un("plxc"), [128, 2, S], F32, scope=sc)
    xoc = p.sb(un("plxoc"), [128, 2, S], F32, scope=sc)
    tf = p.sb(un("pltf"), [128, S], F32, scope=sc)

    p.wait("sync", mm, mm.v)
    p.wait("sync", dv, dv.v)
    p.wait("sync", ev, ev.v)
    with nc.psum_tensor(un("plps1"), [1, S], F32) as ps1, \
         nc.psum_tensor(un("plps2"), [1, S], F32) as ps2:
        p.wait("vector", ev, ev_y)
        Mc = [0] * NCH
        for c in range(NCH):
            if c >= 1:
                p.wait("vector", mm, Mc[c - 1])
            vc = p.vec("tensor_copy", ybf[:], ysb[:, c, :], inc=dv)
            p.wait("scalar", dv, vc)
            if c >= 1:
                p.wait("scalar", mm, Mc[c - 1])
            ec = p.act(ysq[:], ybf[:], AF.Square, inc=ev)
            p.wait("tensor", ev, ec)
            for n in range(4):
                p.mm(ps1[:, 512 * n:512 * (n + 1)], common["ones_colbf"][:],
                     ybf[:, 512 * n:512 * (n + 1)], start=(c == 0), stop=(c == NCH - 1))
                p.mm(ps2[:, 512 * n:512 * (n + 1)], common["ones_colbf"][:],
                     ysq[:, 512 * n:512 * (n + 1)], start=(c == 0), stop=(c == NCH - 1),
                     inc=mm if n == 3 else None)
            Mc[c] = mm.v
        p.wait("scalar", mm, Mc[NCH - 1])
        p.act(murow[:], ps1[:], AF.Copy, scale=1.0 / D)
        e2 = p.act(s2row[:], ps2[:], AF.Copy, scale=1.0 / D, inc=ev)
    with nc.psum_tensor(un("plpbA"), [128, S], F32) as pbA, \
         nc.psum_tensor(un("plpbB"), [128, S], F32) as pbB:
        p.wait("tensor", ev, e2)
        for n in range(4):
            p.mm(pbA[:, 512 * n:512 * (n + 1)], common["ones_rowf"][:],
                 murow[:, 512 * n:512 * (n + 1)], start=True, stop=True)
            mv = p.mm(pbB[:, 512 * n:512 * (n + 1)], common["ones_rowf"][:],
                      s2row[:, 512 * n:512 * (n + 1)], start=True, stop=True,
                      inc=mm if n == 3 else None)
        p.wait("scalar", mm, mv)
        p.act(Abcs[:], pbA[:], AF.Copy)
        e2b = p.act(Bbcs[:], pbB[:], AF.Copy, inc=ev)
        p.wait("vector", ev, e2b)
        p.vec("tensor_mul", tf[:], Abcs[:], Abcs[:])
        p.vec("scalar_tensor_tensor", tf[:], tf[:], -1.0, Bbcs[:], ALU.mult, ALU.add)
        d1 = p.vec("tensor_scalar", tf[:], tf[:], LN_EPS, None, ALU.add, inc=dv)
        p.wait("scalar", dv, d1)
        p.act(tf[:], tf[:], AF.Ln)
        e3 = p.act(tf[:], tf[:], AF.Exp, scale=-0.5, inc=ev)
        p.wait("vector", ev, e3)
        p.vec("scalar_tensor_tensor", Bbcs[:], Abcs[:], -1.0, tf[:],
              ALU.mult, ALU.mult)
        ev2 = p.vec("tensor_copy", Abcs[:], tf[:], inc=dv)
    outs = [0, 0]
    lxv = [lx[0].v, lx[1].v]
    for c in range(NCH):
        slot = c % 2
        if c >= 2:
            p.wait("sync", st[slot], outs[slot])
        ldv = p.dma("sync", xc[:, slot, :], xdram[:, c, :], lx[slot])
        p.wait("vector", lx[slot], ldv)
        p.vec("tensor_mul", tf[:], ysb[:, c, :], Abcs[:])
        p.vec("tensor_add", tf[:], tf[:], Bbcs[:])
        dvv = p.vec("tensor_add", xoc[:, slot, :], tf[:], xc[:, slot, :], inc=dv)
        p.wait("sync", dv, dvv)
        outs[slot] = p.dma("sync", xodram[:, c, :], xoc[:, slot, :], st[slot])
    sc.close()
    return None



def _finish_dummy(p, xodram, st):
    zb = p.sb(un("zdum"), [128, S], F32)
    p.wait("vector", st[0], st[0].v)
    p.wait("vector", st[1], st[1].v)
    p.wait("sync", p.ec["vector"], p.vec("memset", zb[:], 0.0))
    for c in range(NCH):
        p.dma("sync", xodram[:, c, :], zb[:], st[c % 2])
    p.wait("sync", st[0], st[0].v)
    p.wait("sync", st[1], st[1].v)

def build_l0_attn(nc, upto=7):
    p = Prog(nc)
    xdram = nc.dram_tensor("x", [128, NCH, S], F32, kind="ExternalInput").ap()
    xodram = nc.dram_tensor("xo", [128, NCH, S], F32, kind="ExternalOutput").ap()
    emit_l0_attn(p, xdram, xodram, upto)
    p.run()


def emit_l0_attn(p, xdram, xodram, upto=7):
    dbg = upto if upto in (29, 30, 31, 32, 33) else None
    nc = p.nc
    wq_d = nc.dram_tensor("wqc", [NCH, 128, D], F32, kind="ExternalInput").ap()
    wk_d = nc.dram_tensor("wkc", [NCH, 128, D], F32, kind="ExternalInput").ap()
    wv_d = nc.dram_tensor("wvc", [NCH, 128, D], F32, kind="ExternalInput").ap()
    wo_d = nc.dram_tensor("woc", [NCH, 128, D], F32, kind="ExternalInput").ap()
    bcols_d = nc.dram_tensor("bcols", [128, 4 * NCH], F32, kind="ExternalInput").ap()
    Bfirst_d = nc.dram_tensor("Bfirst", [128, 2048], F32, kind="ExternalInput").ap()
    Blast_d = nc.dram_tensor("Blast", [128, 2048], F32, kind="ExternalInput").ap()
    B0T_d = nc.dram_tensor("B0T", [128, 256], F32, kind="ExternalInput").ap()
    B0sh_d = nc.dram_tensor("B0sh", [128, 272], F32, kind="ExternalInput").ap()
    eye_d = nc.dram_tensor("eye", [128, 128], F32, kind="ExternalInput").ap()
    e0_dram = nc.dram_tensor("e0bounce", [16, S], BF16,
                             kind="ExternalOutput" if upto == 5 else "Internal").ap()
    oT_dram = nc.dram_tensor("oTd", [128, NCH, S], BF16,
                             kind="ExternalOutput" if upto == 5 else "Internal").ap()
    dbgz = nc.dram_tensor("dbgz", [16, 2, 1024], F32,
                          kind="ExternalOutput" if upto == 5 else "Internal").ap() if upto == 5 else None
    dbgv = nc.dram_tensor("dbgv", [128, NCH * 17 * 128], BF16,
                          kind="ExternalOutput").ap() if upto == 5 else None
    dbgr = nc.dram_tensor("dbgr", [2, 1024], BF16,
                          kind="ExternalOutput").ap() if upto == 5 else None
    v0_dram = nc.dram_tensor("v0bounce", [NCH, 128], BF16,
                             kind="ExternalOutput" if upto == 5 else "Internal").ap()
    vm_dram = nc.dram_tensor("vmbounce", [NCH, 128], F32,
                             kind="ExternalOutput" if upto == 5 else "Internal").ap()

    ldc = p.sem("ldc")
    ldg = p.sem("ldg")
    lx = [p.sem("lxA"), p.sem("lxB")]
    wl = [p.sem("wlA"), p.sem("wlB")]
    e0s = p.sem("e0s")
    g0 = p.sem("g0")
    oTs = p.sem("oTs")
    mm = p.sem("mm")
    ev = p.ec["scalar"]
    dv = p.ec["vector"]
    st = [p.sem("stA"), p.sem("stB")]

    common = make_common(p)
    eye = p.sb(un("eyebf"), [128, 128], BF16)
    ld_eye = p.dma("gpsimd", eye[:], eye_d, ldg)
    eyef = p.sb(un("eyef"), [128, 128], F32)
    p.dma("sync", eyef[:], eye_d, ldc)
    bcols = p.sb(un("bcolsb"), [128, 4 * NCH], F32)
    ld_const = p.dma("sync", bcols[:], bcols_d, ldc)

    kT = p.sb(un("kT"), [128, NCH, 2176], BF16)
    wscope = ExitStack()
    wbuf = p.sb(un("wbuf"), [128, NCH, D], BF16, scope=wscope, side="right")

    def load_w(wd):
        for c in range(NCH):
            v = p.dma("gpsimd", wbuf[:, c, :], wd[c], wl[0])
        return v

    wk_ld = load_w(wk_d)

    lnscope = ExitStack()
    lnq = p.sb(un("lnq"), [128, NCH, S], BF16, scope=lnscope, side="right")

    emit_ln_featmajor(p, xdram, lnq, common, lx, mm, ev, dv)
    if upto < 2:
        lnscope.close(); wscope.close()
        _finish_dummy(p, xodram, st)
        return p

    p.wait("vector", ev, ev.v)
    pad_dv = p.vec("memset", kT[:, :, 0:128], 0.0)

    bq = lambda m: bcols[:, 0 * NCH + m:0 * NCH + m + 1]
    bk = lambda m: bcols[:, 1 * NCH + m:1 * NCH + m + 1]
    bv = lambda m: bcols[:, 2 * NCH + m:2 * NCH + m + 1]
    bo = lambda m: bcols[:, 3 * NCH + m:3 * NCH + m + 1]
    p.wait("tensor", dv, dv.v)
    p.wait("tensor", ldc, ld_const)

    def ev_k(m, ps):
        return p.act(kT[:, m, 128:2176], ps, AF.Identity, bias=bk(m))

    emit_proj(p, wbuf, wk_ld, wl[0], lnq, ev_k, NCH, mm, ev)
    p.wait("gpsimd", mm, mm.v)
    wv_ld = load_w(wv_d)
    if upto < 3:
        lnscope.close(); wscope.close()
        _finish_dummy(p, xodram, st)
        return p

    p3p = ExitStack()
    vsh = p.sb(un("vsh"), [128, NCH, 17, 128], BF16, scope=p3p)
    vmcol = p.sb(un("vmcol"), [128, NCH], F32, scope=p3p)
    v0c = p.sb(un("v0c"), [128, NCH], BF16, scope=p3p)
    v0p = p.sb(un("v0p"), [1, 1024], BF16, scope=p3p)
    vmr = p.sb(un("vmr"), [1, 1024], BF16, scope=p3p)
    ov = p.sb(un("ov"), [64, 16], BF16, scope=p3p)
    p3v = ExitStack()
    vtmp = p.sb(un("vtmp"), [128, 2176], BF16, scope=p3v)
    vstage = p.sb(un("vstage"), [NCH, 128], BF16, scope=p3v)
    vmstage = p.sb(un("vmstage"), [NCH, 128], F32, scope=p3v)
    vmrraw = p.sb(un("vmrraw"), [1, 1024], F32, scope=p3v)
    p.wait("vector", ev, ev.v)
    p.vec("memset", vtmp[:, 0:128], 0.0)
    p.vec("memset", vsh[:, :, 16, :], 0.0)

    with nc.psum_tensor(un("vpps"), [128, S], F32) as vpps, \
         nc.psum_tensor(un("vtpsA"), [128, 128], BF16) as vtpsA, \
         nc.psum_tensor(un("vtpsB"), [128, 128], BF16) as vtpsB:
        vtps = [vtpsA, vtpsB]
        p.wait("tensor", wl[0], wv_ld)
        p.wait("tensor", ldg, ld_eye)
        p.wait("tensor", ev, ev.v)
        p.wait("tensor", dv, dv.v)
        trs = [0, 0]
        e_vt = 0
        dv_v0 = 0
        tr_mm = 0
        for hp in range(NCH):
            p.wait("tensor", ev, e_vt)
            for c in range(NCH):
                for n in range(4):
                    p.mm(vpps[:, 512 * n:512 * (n + 1)],
                         wbuf[:, c, 128 * hp:128 * (hp + 1)],
                         lnq[:, c, 512 * n:512 * (n + 1)],
                         start=(c == 0), stop=(c == NCH - 1),
                         inc=mm if (c == NCH - 1 and n == 3) else None)
            p.wait("scalar", mm, mm.v)
            if hp >= 1:
                p.wait("scalar", dv, dv_v0)
                p.wait("scalar", mm, tr_mm)
            e_vt = p.act(vtmp[:, 128:2176], vpps[:], AF.Identity, bias=bv(hp))
            if dbg == 29:
                dv_v0 = dv.v
                tr_mm = mm.v
                continue
            p.wait("vector", ev, e_vt)
            p.vec("tensor_reduce", vmcol[:, hp:hp + 1].unsqueeze(2),
                  vtmp[:, 128:2176].unsqueeze(1), axis=AX.X, op=ALU.add)
            dv_v0 = p.vec("tensor_copy", v0c[:, hp:hp + 1], vtmp[:, 128:129])
            if dbg == 30:
                tr_mm = mm.v
                continue
            p.wait("tensor", ev, e_vt)
            for j in range(17):
                g = j % 2
                p.wait("tensor", ev, trs[g])
                if j < 16:
                    tr_mm = p.t_transpose(vtps[g][:, :],
                                          vtmp[:, 126 + 128 * j:254 + 128 * j],
                                          eye[:], inc=mm)
                else:
                    tr_mm = p.t_transpose(vtps[g][0:2, :], vtmp[:, 2174:2176],
                                          eye[:], inc=mm)
                p.wait("scalar", mm, tr_mm)
                rows = 128 if j < 16 else 2
                p.act(vsh[0:rows, hp, j, 0:64], vtps[g][0:rows, 0:64], AF.Copy)
                trs[g] = p.act(vsh[0:rows, hp, j, 64:128], vtps[g][0:rows, 64:128],
                               AF.Copy)
        p.wait("tensor", dv, dv_v0)
        if dbg is None or dbg >= 32:
            p.wait("tensor", ev, trs[0])
            p.wait("tensor", ev, trs[1])
            t1 = p.t_transpose(vtps[0][0:NCH, :], v0c[:], eye[:], inc=mm)
            p.wait("scalar", mm, t1)
            e1 = p.act(vstage[:], vtps[0][0:NCH, :], AF.Copy, inc=ev)
            p.wait("gpsimd", ev, e1)
            b1 = p.dma("gpsimd", v0_dram, vstage[:], g0)
    if dbg is None or dbg >= 33:
        with nc.psum_tensor(un("vmtps"), [128, 128], F32) as vmtps:
            p.wait("tensor", ev, ev.v)
            t2 = p.mm(vmtps[0:NCH, :], vmcol[:], eyef[:], start=True, stop=True, inc=mm)
            p.wait("scalar", mm, t2)
            e2 = p.act(vmstage[:], vmtps[0:NCH, :], AF.Copy, inc=ev)
            p.wait("gpsimd", ev, e2)
            b2 = p.dma("gpsimd", vm_dram, vmstage[:], g0)
            p.wait("gpsimd", g0, g0.v)
            b3 = p.dma("gpsimd", v0p[:], v0_dram.rearrange("c w -> (c w)").unsqueeze(0), g0)
            b4 = p.dma("gpsimd", vmrraw[:], vm_dram.rearrange("c w -> (c w)").unsqueeze(0), g0)
            p.wait("scalar", g0, g0.v)
            e3 = p.act(vmr[:], vmrraw[:], AF.Copy, scale=1e-30 / S, inc=ev)
    if upto in (29, 30, 31, 32, 33):
        p3v.close(); p3p.close(); lnscope.close(); wscope.close()
        _finish_dummy(p, xodram, st)
        return p
    p3v.close()
    qkscope = ExitStack()
    qTe = p.sb(un("qTe"), [128, NCH, S], BF16, scope=qkscope)
    qTo = p.sb(un("qTo"), [128, NCH, S], BF16, scope=qkscope)
    p.wait("gpsimd", mm, mm.v)
    wq_ld = load_w(wq_d)
    p.wait("vector", ev, ev.v)
    p.vec("memset", qTe[64:128, :, :], 0.0)
    qz_dv = p.vec("memset", qTo[0:64, :, :], 0.0)
    p.wait("tensor", dv, qz_dv)

    def ev_q(m, ps):
        p.act(qTe[0:64, m, :], ps[0:64, :], AF.Identity, bias=bq(m)[0:64, :])
        return p.act(qTo[64:128, m, :], ps[64:128, :], AF.Identity, bias=bq(m)[64:128, :])

    emit_proj(p, wbuf, wq_ld, wl[0], lnq, ev_q, NCH, mm, ev)
    lnscope.close()
    wscope.close()
    if upto < 4:
        qkscope.close(); p3p.close()
        _finish_dummy(p, xodram, st)
        return p

    p3e = ExitStack()
    B0T = p.sb(un("B0Tb"), [128, 256], F32, scope=p3e)
    B0sh = p.sb(un("B0shb"), [128, 272], F32, scope=p3e)
    k0m = p.sb(un("k0m"), [128, NCH, 16], BF16, scope=p3e)
    q0m = p.sb(un("q0m"), [128, NCH, 16], BF16, scope=p3e)
    e0rows = p.sb(un("e0rows"), [16, S], BF16, scope=p3e)
    e0T = p.sb(un("e0T"), [128, 256], BF16, scope=p3e)
    er0 = p.sb(un("er0"), [128, 17 * 16], BF16, scope=p3e)
    c0sb = p.sb(un("c0sb"), [1, 1024], F32, scope=p3e)
    z0sb = p.sb(un("z0sb"), [1, 16], F32, scope=p3e)
    rz0 = p.sb(un("rz0"), [1, 16], F32, scope=p3e)
    e0sg = p.sb(un("e0sg"), [128, 272], F32, scope=p3e)
    e0sg2 = p.sb(un("e0sg2"), [128, 272], F32, scope=p3e)

    p.wait("sync", mm, mm.v)
    p.wait("sync", ev, ev.v)
    p.dma("sync", B0T[:], B0T_d, ldc)
    ld_tabs = p.dma("sync", B0sh[:], B0sh_d, ldc)
    p.wait("vector", ev, ev.v)
    p.vec("memset", k0m[:], 0.0)
    p.vec("memset", q0m[:], 0.0)
    for c in range(NCH):
        p.vec("tensor_copy", k0m[0:64, c, 2 * c:2 * c + 1], kT[0:64, c, 128:129])
        p.vec("tensor_copy", k0m[64:128, c, 2 * c + 1:2 * c + 2], kT[64:128, c, 128:129])
        p.vec("tensor_copy", q0m[0:64, c, 2 * c:2 * c + 1], qTe[0:64, c, 0:1])
        p.vec("tensor_copy", q0m[64:128, c, 2 * c + 1:2 * c + 2], qTo[64:128, c, 0:1])
    dv_pre = p.vec("memset", er0[:], 0.0)

    with nc.psum_tensor(un("s0ps"), [128, 256], F32) as s0ps, \
         nc.psum_tensor(un("r0ps"), [128, 272], F32) as r0ps, \
         nc.psum_tensor(un("tpsA"), [16, 128], BF16) as tpsA, \
         nc.psum_tensor(un("tpsB"), [16, 128], BF16) as tpsB:
        tps = [tpsA, tpsB]
        p.wait("tensor", dv, dv_pre)
        p.wait("tensor", ev, ev.v)
        p.wait("tensor", ldc, ld_tabs)
        for qt in range(16):
            for c in range(NCH):
                p.mm(s0ps[:, 16 * qt:16 * (qt + 1)], qTe[:, c, 128 * qt:128 * (qt + 1)],
                     k0m[:, c, :], start=(c == 0), stop=False)
                p.mm(s0ps[:, 16 * qt:16 * (qt + 1)], qTo[:, c, 128 * qt:128 * (qt + 1)],
                     k0m[:, c, :], start=False, stop=(c == NCH - 1),
                     inc=mm if (qt == 15 and c == NCH - 1) else None)
        for jt in range(17):
            off = 126 + 128 * jt
            w = 128 if jt < 16 else 2
            if w < 128:
                p.mm(r0ps[:, 16 * jt:16 * (jt + 1)], kT[:, 0, 0:128],
                     q0m[:, 0, :], start=True, stop=False)
            for c in range(NCH):
                p.mm(r0ps[0:w, 16 * jt:16 * (jt + 1)], kT[:, c, off:off + w],
                     q0m[:, c, :], start=(c == 0 and w == 128), stop=(c == NCH - 1 and w == 128),
                     inc=mm if (jt == 16 and c == NCH - 1 and w == 128) else None)
            if w < 128:
                p.mm(r0ps[:, 16 * jt:16 * (jt + 1)], kT[:, 0, 0:128],
                     q0m[:, 0, :], start=False, stop=True,
                     inc=mm if jt == 16 else None)
        p.wait("vector", mm, mm.v)
        p.wait("vector", ev, ev.v)
        p.vec("scalar_tensor_tensor", e0sg[:, 0:256], s0ps[:], 0.125, B0T[:],
              ALU.mult, ALU.add)
        d1 = p.vec("scalar_tensor_tensor", e0sg2[:, 0:256], e0sg[:, 0:256], GATEK,
                   e0sg[:, 0:256], ALU.mult, ALU.min)
        p.wait("scalar", dv, d1)
        e1 = p.act(e0T[:], e0sg2[:, 0:256], AF.Exp)
        p.wait("vector", ev, e1)
        p.vec("scalar_tensor_tensor", e0sg[:], r0ps[:], 0.125, B0sh[:],
              ALU.mult, ALU.add)
        d2 = p.vec("scalar_tensor_tensor", e0sg2[:], e0sg[:], GATEK,
                   e0sg[:], ALU.mult, ALU.min)
        p.wait("scalar", dv, d2)
        e2r = p.act(er0[:], e0sg2[:], AF.Exp)
        tevs = [0, 0]
        for qt in range(16):
            g = qt % 2
            p.wait("tensor", ev, max(e1, tevs[g]))
            tq = p.t_transpose(tps[g][:, :], e0T[:, 16 * qt:16 * (qt + 1)], eye[:], inc=mm)
            p.wait("scalar", mm, tq)
            tevs[g] = p.act(e0rows[:, 128 * qt:128 * (qt + 1)], tps[g][:, :], AF.Copy)
        p.wait("gpsimd", ev, max(tevs))
        g_e0 = p.dma("gpsimd", e0_dram, e0rows[:], g0)

    with nc.psum_tensor(un("c0ps"), [1, 1024], F32) as c0ps, \
         nc.psum_tensor(un("z0ps"), [1, 16], F32) as z0ps:
        p.wait("tensor", ev, ev.v)
        p.wait("tensor", dv, dv.v)
        for h in range(H):
            hp = h // 2
            o64 = 64 * (h % 2)
            for jt in range(17):
                p.mm(c0ps[:, 64 * h:64 * h + 64], er0[:, 16 * jt + h:16 * jt + h + 1],
                     vsh[:, hp, jt, o64:o64 + 64], start=(jt == 0), stop=False)
                p.mm(z0ps[:, h:h + 1], er0[:, 16 * jt + h:16 * jt + h + 1],
                     common["ones_colbf"][:], start=(jt == 0), stop=(jt == 16))
            mz = p.mm(c0ps[:, 64 * h:64 * h + 64], common["one1bf"][:],
                      vmr[:, 64 * h:64 * h + 64], start=False, stop=True,
                      inc=mm if h == H - 1 else None)
        p.wait("scalar", mm, mz)
        p.act(c0sb[:], c0ps[:], AF.Copy)
        ez0 = p.act(z0sb[:], z0ps[:], AF.Copy)
    p.wait("vector", ev, ez0)
    d3 = p.vec("tensor_scalar", rz0[:], z0sb[:], 1e-30, None, ALU.add)
    p.wait("scalar", dv, d3)
    p.act(z0sb[:], rz0[:], AF.Ln)
    erz = p.act(rz0[:], z0sb[:], AF.Exp, scale=-1.0)
    with nc.psum_tensor(un("ovps"), [64, 16], F32) as ovps:
        p.wait("tensor", ev, erz)
        for h in range(H):
            mv = p.mm(ovps[:, h:h + 1], c0sb[:, 64 * h:64 * h + 64], rz0[:, h:h + 1],
                      start=True, stop=True, inc=mm if h == H - 1 else None)
        p.wait("scalar", mm, mv)
        e_ov = p.act(ov[:], ovps[:], AF.Copy)
    p3e.close()
    if upto < 5:
        qkscope.close(); p3p.close()
        _finish_dummy(p, xodram, st)
        return p

    p3b = ExitStack()
    Bfirst = p.sb(un("Bfirstb"), [128, 2048], F32, scope=p3b)
    Blast = p.sb(un("Blastb"), [128, 2048], F32, scope=p3b)
    e0hb = p.sb(un("e0hb"), [1, S], BF16, scope=p3b)
    p.wait("sync", mm, mm.v)
    p.wait("sync", ev, ev.v)
    p.wait("sync", dv, dv.v)
    p.wait("gpsimd", mm, mm.v)
    p.wait("gpsimd", ev, ev.v)
    p.wait("gpsimd", g0, g0.v)
    p.dma("sync", Bfirst[:], Bfirst_d, ldc)
    ld_tabs2a = p.dma("sync", Blast[:], Blast_d, ldc)

    p4 = ExitStack()
    zrow = p.sb(un("zrow"), [1, 1024], F32, scope=p4)
    Rsb = p.sb(un("Rsb"), [64, 1024], F32, scope=p4)
    oTtmp = p.sb(un("oTtmp"), [64, S], BF16, scope=p4)
    Pb = p.sb(un("Pb"), [128, 2, 1024], BF16, scope=p4)
    sgb = p.sb(un("sgb"), [128, 512], F32, scope=p4)
    agb = p.sb(un("agb"), [128, 512], F32, scope=p4)

    with nc.psum_tensor(un("Sps"), [128, 1024], F32) as Sps, \
         nc.psum_tensor(un("ctxps"), [64, 1024], F32) as ctxps, \
         nc.psum_tensor(un("zps"), [1, 1024], F32) as zps, \
         nc.psum_tensor(un("Rps"), [64, 1024], F32) as Rps:
        p.wait("tensor", ev, e_ov)
        p.wait("tensor", dv, dv.v)
        p.wait("vector", ldc, ld_tabs2a)
        p.wait("gpsimd", g0, g_e0)
        e0ld = p.dma("gpsimd", e0hb[:], e0_dram[0:1, :], e0s)
        last_S_dv = 0
        last_exp = [0, 0]
        pv_done = 0
        norm_dv = 0
        rbc_ev = 0
        oT_dma = []
        for h in range(H):
            hp, o64 = h // 2, 64 * (h % 2)
            qT_h = qTe if h % 2 == 0 else qTo
            p.wait("tensor", e0s, e0ld)
            for half in range(2):
                for g2 in range(2):
                    g = 2 * half + g2
                    Btab = Bfirst if g < 2 else Blast
                    p.wait("tensor", dv, last_S_dv)
                    for ti in range(4):
                        t = 4 * g + ti
                        off = 126 + 128 * t
                        p.mm(Sps[:, 256 * ti:256 * ti + 128],
                             kT[:, hp, off:off + 128],
                             qT_h[:, hp, 128 * t:128 * (t + 1)],
                             start=True, stop=True)
                        if t < 15:
                            p.mm(Sps[:, 256 * ti + 128:256 * (ti + 1)],
                                 kT[:, hp, off + 128:off + 256],
                                 qT_h[:, hp, 128 * t:128 * (t + 1)],
                                 start=True, stop=True,
                                 inc=mm if ti == 3 else None)
                        else:
                            p.mm(Sps[:, 256 * ti + 128:256 * (ti + 1)],
                                 kT[:, hp, 0:128],
                                 qT_h[:, hp, 128 * t:128 * (t + 1)],
                                 start=True, stop=False)
                            p.mm(Sps[0:2, 256 * ti + 128:256 * (ti + 1)],
                                 kT[:, hp, off + 128:off + 130],
                                 qT_h[:, hp, 128 * t:128 * (t + 1)],
                                 start=False, stop=False)
                            p.mm(Sps[:, 256 * ti + 128:256 * (ti + 1)],
                                 kT[:, hp, 0:128],
                                 qT_h[:, hp, 128 * t:128 * (t + 1)],
                                 start=False, stop=True,
                                 inc=mm if ti == 3 else None)
                    Mg = mm.v
                    p.wait("vector", mm, Mg)
                    p.wait("vector", ev, max(last_exp))
                    p.wait("scalar", mm, pv_done)
                    for n2 in range(2):
                        Bsl = Btab[:, 256 * ((4 * g) % 8) + 512 * n2:
                                   256 * ((4 * g) % 8) + 512 * (n2 + 1)]
                        p.wait("vector", ev, max(last_exp))
                        p.vec("scalar_tensor_tensor", sgb[:], Sps[:, 512 * n2:512 * (n2 + 1)],
                              0.125, Bsl, ALU.mult, ALU.add)
                        dgg = p.vec("scalar_tensor_tensor", agb[:], sgb[:], GATEK, sgb[:],
                                    ALU.mult, ALU.min)
                        p.wait("scalar", dv, dgg)
                        last_exp[g % 2] = p.act(Pb[:, g % 2, 512 * n2:512 * (n2 + 1)],
                                                agb[:], AF.Exp)
                    last_S_dv = dgg
                p.wait("tensor", ev, last_exp[0])
                p.wait("tensor", ev, last_exp[1])
                if pv_done:
                    p.wait("tensor", dv, norm_dv)
                for g2 in range(2):
                    g = 2 * half + g2
                    for ti in range(4):
                        t = 4 * g + ti
                        cc = 128 * (t % 8)
                        p.mm(ctxps[:, cc:cc + 128], vsh[:, hp, t, o64:o64 + 64],
                             Pb[:, g % 2, 256 * ti:256 * ti + 128], start=True, stop=False)
                        p.mm(ctxps[:, cc:cc + 128], vsh[:, hp, t + 1, o64:o64 + 64],
                             Pb[:, g % 2, 256 * ti + 128:256 * (ti + 1)],
                             start=False, stop=False)
                        p.mm(ctxps[:, cc:cc + 128], v0p[:, 64 * h:64 * h + 64],
                             e0hb[:, 128 * t:128 * (t + 1)], start=False, stop=False)
                        p.mm(ctxps[:, cc:cc + 128], vmr[:, 64 * h:64 * h + 64],
                             common["ones_rowbf"][:], start=False, stop=True)
                        p.mm(zps[:, cc:cc + 128], common["ones_colbf"][:],
                             Pb[:, g % 2, 256 * ti:256 * ti + 128], start=True, stop=False)
                        p.mm(zps[:, cc:cc + 128], common["ones_colbf"][:],
                             Pb[:, g % 2, 256 * ti + 128:256 * (ti + 1)],
                             start=False, stop=False)
                        p.mm(zps[:, cc:cc + 128], common["one1bf"][:],
                             e0hb[:, 128 * t:128 * (t + 1)], start=False, stop=True,
                             inc=mm if (g2 == 1 and ti == 3) else None)
                pv_done = mm.v
                p.wait("scalar", mm, pv_done)
                ez = p.act(zrow[:], zps[:], AF.Identity)
                p.wait("vector", ev, ez)
                dz = p.vec("tensor_scalar", zrow[:], zrow[:], 1e-30, None, ALU.add)
                p.wait("scalar", dv, dz)
                p.act(zrow[:], zrow[:], AF.Ln)
                ez2 = p.act(zrow[:], zrow[:], AF.Exp, scale=-1.0)
                if dbgz is not None:
                    p.wait("sync", ev, ez2)
                    p.dma("sync", dbgz[h:h + 1, half, :].rearrange("a b -> a b") if False else dbgz[h, half, :].unsqueeze(0), zrow[:], st[0])
                p.wait("tensor", ev, ez2)
                p.wait("tensor", ev, rbc_ev)
                for n in range(2):
                    mv = p.mm(Rps[:, 512 * n:512 * (n + 1)], common["ones_rowf"][0:1, 0:64],
                              zrow[:, 512 * n:512 * (n + 1)], start=True, stop=True,
                              inc=mm if n == 1 else None)
                p.wait("scalar", mm, mv)
                rbc_ev = p.act(Rsb[:], Rps[:], AF.Copy)
                if oT_dma and half == 0:
                    p.wait("vector", oTs, oT_dma[-1])
                p.wait("vector", ev, rbc_ev)
                norm_dv = p.vec("tensor_mul", oTtmp[:, 1024 * half:1024 * (half + 1)],
                                ctxps[:], Rsb[:])
            p.wait("vector", ev, e_ov)
            norm_dv = p.vec("tensor_copy", oTtmp[:, 0:1], ov[:, h:h + 1])
            p.wait("gpsimd", dv, norm_dv)
            o64h = 64 * (h % 2)
            oT_dma.append(p.dma("gpsimd", oT_dram[o64h:o64h + 64, hp, :], oTtmp[:], oTs))
            if h + 1 < H:
                p.wait("gpsimd", mm, pv_done)
                e0ld = p.dma("gpsimd", e0hb[:], e0_dram[h + 1:h + 2, :], e0s)
    if dbgv is not None:
        p.wait("sync", dv, dv.v)
        p.dma("sync", dbgv, vsh.ap().rearrange("p a b c -> p (a b c)"), st[0])
        p.dma("sync", dbgr[0:1, :], v0p[:], st[0])
        p.dma("sync", dbgr[1:2, :], vmr[:], st[0])
    p4.close()
    p3b.close()
    qkscope.close()
    p3p.close()
    if upto < 6:
        _finish_dummy(p, xodram, st)
        return p

    p5 = ExitStack()
    wbuf2 = p.sb(un("wbuf2"), [128, NCH, D], BF16, scope=p5, side="right")
    y = p.sb(un("yattn"), [128, NCH, S], F32)
    ostr = p.sb(un("ostr"), [128, 2, S], BF16, scope=p5)
    osm = [p.sem("osA"), p.sem("osB")]
    p.wait("gpsimd", mm, mm.v)
    p.wait("gpsimd", dv, dv.v)
    p.wait("gpsimd", ev, ev.v)
    for c in range(NCH):
        wo_ld = p.dma("gpsimd", wbuf2[:, c, :], wo_d[c], wl[1])
    p.wait("sync", oTs, oTs.v)
    ev_yv = None
    with nc.psum_tensor(un("prps5"), [128, 2, S], F32) as prps:
        p.wait("tensor", wl[1], wo_ld)
        p.wait("tensor", ev, ev.v)
        seq = [(m, c) for m in range(NCH) for c in range(NCH)]
        loads, cons = {}, {}
        evs = []
        for i in range(2):
            loads[i] = p.dma("sync", ostr[:, i % 2, :], oT_dram[:, seq[i][1], :], osm[i % 2])
        for i, (m, c) in enumerate(seq):
            g = m % 2
            slot = i % 2
            if c == 0 and m >= 2:
                p.wait("tensor", ev, evs[m - 2])
            p.wait("tensor", osm[slot], loads[i])
            for n in range(4):
                p.mm(prps[:, g, 512 * n:512 * (n + 1)],
                     wbuf2[:, c, 128 * m:128 * (m + 1)],
                     ostr[:, slot, 512 * n:512 * (n + 1)],
                     start=(c == 0), stop=(c == NCH - 1),
                     inc=mm if n == 3 else None)
            cons[i] = mm.v
            j = i + 2
            if j < len(seq):
                p.wait("sync", mm, cons[i])
                loads[j] = p.dma("sync", ostr[:, j % 2, :], oT_dram[:, seq[j][1], :], osm[j % 2])
            if c == NCH - 1:
                p.wait("scalar", mm, mm.v)
                evs.append(p.act(y[:, m, :], prps[:, g, :], AF.Identity, bias=bo(m)))
        ev_yv = evs[-1]
    p5.close()
    if upto < 7:
        _finish_dummy(p, xodram, st)
        return p

    emit_postln_residual(p, y, ev_yv, xdram, xodram, common, lx, mm, ev, dv, st)
    p.wait("sync", st[0], st[0].v)
    p.wait("sync", st[1], st[1].v)
    return p


def build_l0_moe(nc, upto=7):
    p = Prog(nc)
    xdram = nc.dram_tensor("x", [128, NCH, S], F32, kind="ExternalInput").ap()
    xodram = nc.dram_tensor("xo", [128, NCH, S], F32, kind="ExternalOutput").ap()
    emit_l0_moe(p, xdram, xodram, upto)
    p.run()


def emit_l0_moe(p, xdram, xodram, upto=7):
    nc = p.nc
    gw_d = nc.dram_tensor("gwc", [NCH, 128, 8], F32, kind="ExternalInput").ap()
    W1_d = nc.dram_tensor("W1c", [9, NCH, 128, FF], F32, kind="ExternalInput").ap()
    W2_d = nc.dram_tensor("W2c", [9, 32, 128, D], F32, kind="ExternalInput").ap()
    b1_d = nc.dram_tensor("b1c", [128, 9, 32], F32, kind="ExternalInput").ap()
    bs2_d = nc.dram_tensor("bs2c", [128, 8], F32, kind="ExternalInput").ap()
    b2_d = nc.dram_tensor("b2r", [8, D], F32, kind="ExternalInput").ap()
    sel_d = nc.dram_tensor("selc", [8, 8 * 128], F32, kind="ExternalInput").ap()
    eye_d = nc.dram_tensor("eyeB", [128, 128], F32, kind="ExternalInput").ap()
    scd_t = nc.dram_tensor("scd", [8, S], F32, kind="Internal")
    gath_t = nc.dram_tensor("gath", [128, 1024], F32, kind="Internal")
    gdbg = nc.dram_tensor("gdbg", [8, S], F32,
                          kind="ExternalOutput").ap() if upto in (0, 1) else None
    ldbg = nc.dram_tensor("ldbg", [128, 128], F32,
                          kind="ExternalOutput").ap() if upto == 0 else None
    tdbg = nc.dram_tensor("tdbg", [1, 1], F32,
                          kind="ExternalOutput").ap() if upto in (0, 1) else None

    lx = [p.sem("lx0"), p.sem("lx1")]
    wl2 = [p.sem("wl20"), p.sem("wl21")]
    mm = p.sem("mm")
    st = [p.sem("st0"), p.sem("st1")]
    g0 = p.sem("g0")
    cc = p.sem("cc")
    ag = p.sem("ag")
    dv, ev = p.ec["vector"], p.ec["scalar"]

    common = make_common(p)
    ones_colf = p.sb(un("ones_colf"), [128, 1], F32)
    ones8cf = p.sb(un("ones8cf"), [8, 1], F32)
    ones_row8f = p.sb(un("ones_row8f"), [1, 8], F32)
    ones_rowW = p.sb(un("ones_rowW"), [1, 512], F32)
    p.vec("memset", ones_colf[:], 1.0)
    p.vec("memset", ones8cf[:], 1.0)
    p.vec("memset", ones_row8f[:], 1.0)
    p.vec("memset", ones_rowW[:], 1.0)

    sc_e = ExitStack()
    lnq = p.sb("lnq", [128, NCH, S], BF16, scope=sc_e)
    gwsb = p.sb("gwsb", [128, NCH, 8], F32, scope=sc_e)
    eyef = p.sb("eyef", [128, 128], F32, scope=sc_e)
    b1sb = p.sb("b1sb", [128, 9, 32], F32, scope=sc_e)
    bs2sb = p.sb("bs2sb", [128, 8], F32, scope=sc_e)
    b2sb = p.sb("b2sb", [8, D], BF16, scope=sc_e)
    selsb = p.sb("selsb", [8, 8 * 128], BF16, scope=sc_e)
    gatesT = p.sb("gatesT", [8, S], BF16, scope=sc_e)
    bs_lo = p.sb("bs_lo", [1, 1], F32, scope=sc_e)
    logits = p.sb("logitsb", [128, 128], F32, scope=sc_e)

    FB = 512
    NFB = FF // FB
    NFT = FB // 128
    steps = [(e, fb) for e in range(9) for fb in range(NFB)]
    NST = len(steps)
    sc_w = ExitStack()
    W1sb = p.sb("W1sb", [128, 2, NCH, FB], BF16, scope=sc_w, side="right")
    W2sb = p.sb("W2sb", [128, 2, NFT, 1024], BF16, scope=sc_w, side="right")

    def load_blk(slot, e, fb):
        for dc in range(NCH):
            p.dma("gpsimd", W1sb[:, slot, dc, :],
                  W1_d[e, dc, :, FB * fb:FB * (fb + 1)], wl2[slot])
        for fc in range(NFT):
            v = p.dma("gpsimd", W2sb[:, slot, fc, :],
                      W2_d[e, NFT * fb + fc], wl2[slot])
        return v

    wld = [0] * NST
    wld[0] = load_blk(0, 0, 0)
    for c in range(NCH):
        ld_gw = p.dma("sync", gwsb[:, c, :], gw_d[c], g0)
    p.dma("sync", eyef[:], eye_d, g0)
    p.dma("sync", b1sb[:], b1_d, g0)
    ld_small = p.dma("sync", bs2sb[:], bs2_d, g0)
    p.dma("gpsimd", b2sb[:], b2_d, wl2[0])
    ld_sel = p.dma("gpsimd", selsb[:], sel_d, wl2[0])
    wld[0] = ld_sel

    ps_a = ExitStack()
    LGref = [None, None, None]
    lg_evac = [0, None]

    def rg2(c, xslot, ldv):
        if LGref[0] is None:
            LGref[0] = ps_a.enter_context(nc.psum_tensor("LGps", [128, 128], F32))
            LGref[1] = ps_a.enter_context(nc.psum_tensor("RSps", [128, 128], F32))
            LGref[2] = ps_a.enter_context(nc.psum_tensor("G1ps", [1, 8], F32))
        LGps = LGref[0]
        p.wait("tensor", lx[c % 2], ldv)
        if c == 0:
            p.wait("tensor", g0, ld_small)
            p.wait("tensor", ev, ev.v)
            p.wait("tensor", dv, dv.v)
        else:
            p.wait("tensor", lg_evac[1], lg_evac[0])
        for ts in range(16):
            p.mm(LGps[:, 8 * ts:8 * (ts + 1)], xslot[:, 128 * ts:128 * (ts + 1)],
                 gwsb[:, c, :], start=True, stop=True,
                 inc=mm if ts == 15 else None)
        if c == 0:
            p.wait("scalar", mm, mm.v)
            lg_evac[0] = p.act(logits[:], LGps[:], AF.Copy)
            lg_evac[1] = ev
        else:
            p.wait("vector", mm, mm.v)
            lg_evac[0] = p.vec("tensor_add", logits[:], logits[:], LGps[:])
            lg_evac[1] = dv

    kp = ExitStack()
    murow, Abcs = emit_ln_featmajor(p, xdram, lnq, common, lx, mm, ev, dv,
                                    raw_gate2=rg2, keep=kp)
    v_lnq = dv.v
    yacc = p.sb("yacc", [128, NCH, S], F32, scope=sc_e)
    sc_g = ExitStack()
    esb = p.sb("esb", [128, 128], F32, scope=sc_g)
    scT = p.sb("scT", [8, S], F32, scope=sc_g)
    zrow = p.sb("zrow", [1, S], F32, scope=sc_g)
    taurow = p.sb("taurow", [1, S], F32, scope=sc_g)
    allsc = p.sb("allsc", [128, 1024], F32, scope=sc_g)
    ind8 = p.sb("ind8", [8, S], F32, scope=sc_g)
    shiftb = p.sb("shiftb", [128, 1024], F32, scope=sc_g)
    negmidcol = p.sb("negmidcol", [128, 1], F32, scope=sc_g)
    cntcol = p.sb("cntcol", [128, 1], F32, scope=sc_g)
    bs_hi = p.sb("bs_hi", [1, 1], F32, scope=sc_g)
    bs_mid = p.sb("bs_mid", [1, 1], F32, scope=sc_g)
    bs_nm = p.sb("bs_nm", [1, 1], F32, scope=sc_g)
    bs_cond = p.sb("bs_cond", [1, 1], F32, scope=sc_g)
    bs_d1 = p.sb("bs_d1", [1, 1], F32, scope=sc_g)
    bs_d2 = p.sb("bs_d2", [1, 1], F32, scope=sc_g)

    LGps, RSps, G1ps = LGref
    for c in range(NCH):
        p.mm(G1ps[:], ones_colf[:], gwsb[:, c, :], start=(c == 0), stop=(c == NCH - 1),
             inc=mm if c == NCH - 1 else None)
    p.wait("scalar", mm, mm.v)
    g1neg = p.sb("g1neg", [1, 8], F32, scope=sc_g)
    e_g1 = p.act(g1neg[:], G1ps[:], AF.Copy, scale=-1.0)
    p.wait("tensor", ev, e_g1)
    p.wait("tensor", lg_evac[1], lg_evac[0])
    for ts in range(16):
        p.mm(LGps[:, 8 * ts:8 * (ts + 1)], murow[:, 128 * ts:128 * (ts + 1)],
             g1neg[:], start=True, stop=True,
             inc=mm if ts == 15 else None)
        p.mm(RSps[:, 8 * ts:8 * (ts + 1)], Abcs[0:1, 128 * ts:128 * (ts + 1)],
             ones_row8f[:], start=True, stop=True,
             inc=mm if ts == 15 else None)
    p.wait("scalar", mm, mm.v)
    rstile = p.sb("rstile", [128, 128], F32, scope=sc_g)
    e_rs = p.act(rstile[:], RSps[:], AF.Copy)
    p.wait("vector", ev, e_rs)
    p.vec("tensor_add", logits[:], logits[:], LGps[:])
    v_lg = p.vec("tensor_mul", logits[:], logits[:], rstile[:])
    kp.close()

    p.wait("scalar", dv, v_lg)
    e_es = p.act(esb[:], logits[:], AF.Exp)
    ps_a.close()
    with nc.psum_tensor("TEpsA", [8, 128], F32) as TEpsA, \
         nc.psum_tensor("TEpsB", [8, 128], F32) as TEpsB, \
         nc.psum_tensor("Zps", [1, S], F32) as Zps:
        p.wait("tensor", ev, e_es)
        p.wait("tensor", g0, ld_small)
        te = [0, 0]
        for ts in range(16):
            tp = TEpsA if ts % 2 == 0 else TEpsB
            if te[ts % 2]:
                p.wait("tensor", ev, te[ts % 2])
            mv = p.t_transpose(tp[0:8, :], esb[:, 8 * ts:8 * (ts + 1)], eyef[:], inc=mm)
            p.wait("scalar", mm, mv)
            te[ts % 2] = p.act(scT[:, 128 * ts:128 * (ts + 1)], tp[0:8, :], AF.Copy)
        p.wait("tensor", ev, max(te))
        for n in range(4):
            mv = p.mm(Zps[:, 512 * n:512 * (n + 1)], ones8cf[:],
                      scT[:, 512 * n:512 * (n + 1)], start=True, stop=True,
                      inc=mm if n == 3 else None)
        p.wait("scalar", mm, mv)
        p.act(zrow[:], Zps[:], AF.Ln)
        e_rz = p.act(zrow[:], zrow[:], AF.Exp, scale=-1.0)
    with nc.psum_tensor("RZps", [8, S], F32) as RZps:
        p.wait("tensor", ev, e_rz)
        for n in range(4):
            mv = p.mm(RZps[:, 512 * n:512 * (n + 1)], ones_row8f[:],
                      zrow[:, 512 * n:512 * (n + 1)], start=True, stop=True,
                      inc=mm if n == 3 else None)
        p.wait("vector", mm, mv)
        v_sc = p.vec("tensor_mul", scT[:], scT[:], RZps[:])
    if upto == 0:
        p.vec("memset", bs_lo[:], 0.0)
        p.wait("sync", dv, v_sc)
        fsb = p.sb(un("gdbgf"), [8, S], F32, scope=sc_g)
        vv = p.vec("tensor_copy", fsb[:], scT[:])
        p.wait("sync", dv, vv)
        p.dma("sync", gdbg, fsb[:], st[0])
        p.dma("sync", tdbg, bs_lo[:], st[0])
        p.dma("sync", ldbg, logits[:], st[0])
        p.wait("sync", st[0], st[0].v)
        sc_g.close()
        sc_e.close()
        _finish_dummy(p, xodram, st)
        return
    p.wait("sync", dv, v_sc)
    v_out = p.dma("sync", scd_t.ap(), scT[:], ag)
    p.wait("gpsimd", ag, v_out)
    p.on("gpsimd", lambda e: e.collective_compute(
        "AllGather", ALU.bypass, replica_groups=[list(range(8))],
        ins=[scd_t.ap().opt()], outs=[gath_t.ap().opt()]).then_inc(cc.h, 1))
    cc.bump(1)
    p.wait("sync", cc, 1)
    v_all = p.dma("sync", allsc[:], gath_t.ap(), ag)

    KTOP = 32768
    p.vec("memset", bs_lo[:], 0.0)
    p.vec("memset", bs_hi[:], 1.1)
    p.vec("memset", bs_mid[:], 0.55)
    with nc.psum_tensor("NMps", [128, 1], F32) as NMps, \
         nc.psum_tensor("CTps", [1, 1], F32) as CTps:
        p.wait("vector", ag, v_all)
        e_evac = 0
        ct_read = 0
        for it in range(34):
            v_nm = p.vec("tensor_scalar", bs_nm[:], bs_mid[:], -1.0, None, ALU.mult)
            p.wait("tensor", dv, v_nm)
            if e_evac:
                p.wait("tensor", ev, e_evac)
            m1 = p.mm(NMps[:], common["ones_rowf"][:], bs_nm[:], start=True,
                      stop=True, inc=mm)
            p.wait("scalar", mm, m1)
            e_evac = p.act(negmidcol[:], NMps[:], AF.Copy)
            e_sh = p.act(shiftb[:], allsc[:], AF.Identity, bias=negmidcol[:])
            p.wait("vector", ev, e_sh)
            v_ct = p.vec("tensor_scalar", shiftb[:], shiftb[:], 0.0, None,
                         ALU.is_ge, ALU.add, accum_out=cntcol[:])
            p.wait("tensor", dv, v_ct)
            if ct_read:
                p.wait("tensor", dv, ct_read)
            m2 = p.mm(CTps[:], cntcol[:], ones_colf[:], start=True, stop=True, inc=mm)
            p.wait("vector", mm, m2)
            p.vec("tensor_scalar", bs_cond[:], CTps[:], KTOP - 0.5, None, ALU.is_ge)
            p.vec("tensor_sub", bs_d1[:], bs_mid[:], bs_lo[:])
            p.vec("tensor_mul", bs_d1[:], bs_d1[:], bs_cond[:])
            p.vec("tensor_add", bs_lo[:], bs_lo[:], bs_d1[:])
            p.vec("tensor_sub", bs_d2[:], bs_hi[:], bs_mid[:])
            p.vec("tensor_mul", bs_d2[:], bs_d2[:], bs_cond[:])
            p.vec("tensor_add", bs_hi[:], bs_mid[:], bs_d2[:])
            p.vec("tensor_add", bs_mid[:], bs_lo[:], bs_hi[:])
            ct_read = p.vec("tensor_scalar", bs_mid[:], bs_mid[:], 0.5, None, ALU.mult)
        p.wait("tensor", dv, ct_read)
    with nc.psum_tensor("TRps", [1, S], F32) as TRps:
        for n in range(4):
            mv = p.mm(TRps[:, 512 * n:512 * (n + 1)], bs_lo[:], ones_rowW[:],
                      start=True, stop=True, inc=mm if n == 3 else None)
        p.wait("scalar", mm, mv)
        e_tr = p.act(taurow[:], TRps[:], AF.Copy)
    with nc.psum_tensor("TPps", [8, S], F32) as TPps:
        p.wait("tensor", ev, e_tr)
        for n in range(4):
            mv = p.mm(TPps[:, 512 * n:512 * (n + 1)], ones_row8f[:],
                      taurow[:, 512 * n:512 * (n + 1)], start=True, stop=True,
                      inc=mm if n == 3 else None)
        p.wait("vector", mm, mv)
        p.vec("tensor_tensor", ind8[:], scT[:], TPps[:], ALU.is_ge)
        v_gates = p.vec("tensor_mul", gatesT[:], scT[:], ind8[:])

    if upto != 1:
        sc_g.close()
        sc_h = ExitStack()
        hg = p.sb("hgsb", [128, NFT, S], BF16, scope=sc_h)
        htmp = p.sb("htmp", [128, 2, 512], BF16, scope=sc_h)
        Gsb = p.sb("Gsb", [128, S], BF16, scope=sc_h)

    if upto == 1:
        p.wait("sync", dv, v_gates)
        fsb = p.sb(un("gdbgf"), [8, S], F32, scope=sc_g)
        vv = p.vec("tensor_copy", fsb[:], gatesT[:])
        p.wait("sync", dv, vv)
        p.dma("sync", gdbg, fsb[:], st[0])
        p.dma("sync", tdbg, bs_lo[:], st[0])
        p.wait("sync", st[0], st[0].v)
        sc_g.close()
        sc_e.close()
        _finish_dummy(p, xodram, st)
        return

    gpc = p.ec["gpsimd"]
    with nc.psum_tensor("GTps", [128, 512], F32) as GTps, \
         nc.psum_tensor("Hps", [128, 2, 512], F32) as Hps, \
         nc.psum_tensor("Yps", [128, 4, 512], F32) as Yps:
        mmdone = [0] * NST
        gt_ev = 0
        vhg_last = 0
        ehg_last = 0
        y_mm_last = 0
        hev = [0, 0]
        yev = [None] * 4
        for i, (e, fb) in enumerate(steps):
            slot = i % 2
            if i + 1 < NST:
                en, fbn = steps[i + 1]
                if i >= 1:
                    p.wait("gpsimd", mm, mmdone[i - 1])
                wld[i + 1] = load_blk(slot ^ 1, en, fbn)
            p.wait("tensor", wl2[slot], wld[i])
            if fb == 0 and e < 8:
                if e == 0:
                    p.wait("tensor", dv, v_gates)
                if vhg_last:
                    p.wait("scalar", gpc, vhg_last)
                for n in range(4):
                    if gt_ev:
                        p.wait("tensor", ev, gt_ev)
                    mv = p.mm(GTps[:], selsb[0:8, 128 * e:128 * (e + 1)],
                              gatesT[0:8, 512 * n:512 * (n + 1)], start=True,
                              stop=True, inc=mm)
                    p.wait("scalar", mm, mv)
                    gt_ev = p.act(Gsb[:, 512 * n:512 * (n + 1)], GTps[:], AF.Copy)
            if i == 0:
                p.wait("tensor", dv, v_lnq)
            first_hg = True
            for ft in range(NFT):
                for n in range(4):
                    g = (ft * 4 + n) % 2
                    if hev[g]:
                        p.wait("tensor", ev, hev[g])
                    for dc in range(NCH):
                        mv = p.mm(Hps[:, g, :],
                                  W1sb[:, slot, dc, 128 * ft:128 * (ft + 1)],
                                  lnq[:, dc, 512 * n:512 * (n + 1)],
                                  start=(dc == 0), stop=(dc == NCH - 1),
                                  inc=mm if dc == NCH - 1 else None)
                    p.wait("scalar", mm, mv)
                    if first_hg and y_mm_last:
                        if e == 8:
                            p.wait("scalar", mm, y_mm_last)
                        else:
                            p.wait("gpsimd", mm, y_mm_last)
                    bcol = b1sb[:, e, NFT * fb + ft:NFT * fb + ft + 1]
                    if e == 8:
                        hev[g] = p.act(hg[:, ft, 512 * n:512 * (n + 1)],
                                       Hps[:, g, :], AF.Gelu_apprx_tanh, bias=bcol)
                        ehg_last = hev[g]
                    else:
                        if vhg_last:
                            p.wait("scalar", gpc, vhg_last)
                        hev[g] = p.act(htmp[:, g, :], Hps[:, g, :],
                                       AF.Gelu_apprx_tanh, bias=bcol)
                        p.wait("gpsimd", ev, hev[g])
                        if first_hg:
                            p.wait("gpsimd", ev, gt_ev)
                        vhg_last = p.gp("tensor_mul",
                                        hg[:, ft, 512 * n:512 * (n + 1)],
                                        htmp[:, g, :], Gsb[:, 512 * n:512 * (n + 1)])
                    first_hg = False
            if e == 8:
                p.wait("tensor", ev, ehg_last)
            else:
                p.wait("tensor", gpc, vhg_last)
            for dt in range(NCH):
                for n in range(4):
                    g = (dt * 4 + n) % 4
                    if yev[g] is not None:
                        p.wait("tensor", yev[g][0], yev[g][1])
                    extra = (e == 0 and fb == 0)
                    for fc in range(NFT):
                        mv = p.mm(Yps[:, g, :],
                                  W2sb[:, slot, fc, 128 * dt:128 * (dt + 1)],
                                  hg[:, fc, 512 * n:512 * (n + 1)],
                                  start=(fc == 0),
                                  stop=(fc == NFT - 1 and not extra),
                                  inc=mm if (fc == NFT - 1 and not extra) else None)
                    if extra:
                        mv = p.mm(Yps[:, g, :], b2sb[0:8, 128 * dt:128 * (dt + 1)],
                                  gatesT[0:8, 512 * n:512 * (n + 1)],
                                  start=False, stop=True, inc=mm)
                    if e == 0 and fb == 0:
                        p.wait("scalar", mm, mv)
                        yev[g] = (ev, p.act(yacc[:, dt, 512 * n:512 * (n + 1)],
                                            Yps[:, g, :], AF.Identity,
                                            bias=bs2sb[:, dt:dt + 1]))
                    else:
                        p.wait("vector", mm, mv)
                        yev[g] = (dv, p.vec("tensor_add",
                                            yacc[:, dt, 512 * n:512 * (n + 1)],
                                            yacc[:, dt, 512 * n:512 * (n + 1)],
                                            Yps[:, g, :]))
                    y_mm_last = mv
            mmdone[i] = mm.v
    sc_h.close()
    sc_w.close()

    emit_postln_residual(p, yacc, 0, xdram, xodram, common, lx, mm, ev, dv, st)
    p.wait("sync", st[0], st[0].v)
    p.wait("sync", st[1], st[1].v)
    sc_e.close()


def build_l1_attn(nc):
    p = Prog(nc)
    xdram = nc.dram_tensor("x", [128, NCH, S], F32, kind="ExternalInput").ap()
    xodram = nc.dram_tensor("xo", [128, NCH, S], F32, kind="ExternalOutput").ap()
    emit_l1_attn(p, xdram, xodram)
    p.run()


def emit_l1_attn(p, xdram, xodram):
    nc = p.nc
    wq_d = nc.dram_tensor("wqcC", [NCH, 128, D], F32, kind="ExternalInput").ap()
    wk_d = nc.dram_tensor("wkcC", [NCH, 128, D], F32, kind="ExternalInput").ap()
    wv_d = nc.dram_tensor("wvcC", [NCH, 128, D], F32, kind="ExternalInput").ap()
    wo_d = nc.dram_tensor("wocC", [NCH, 128, D], F32, kind="ExternalInput").ap()
    bo_d = nc.dram_tensor("bocols", [128, 8], F32, kind="ExternalInput").ap()
    sc_d = nc.dram_tensor("scalec", [16, 1], F32, kind="ExternalInput").ap()
    E2_d = nc.dram_tensor("E2c", [NCH, 128, 16], F32, kind="ExternalInput").ap()
    E2T_d = nc.dram_tensor("E2Tc", [NCH, 16, 128], F32, kind="ExternalInput").ap()

    lx = [p.sem("lx0"), p.sem("lx1")]
    wl = p.sem("wl")
    mm = p.sem("mm")
    st = [p.sem("st0"), p.sem("st1")]
    g0 = p.sem("g0")
    sq = p.sem("sq")
    so = p.sem("so")
    dv, ev = p.ec["vector"], p.ec["scalar"]

    common = make_common(p)
    wbuf = p.sb("wbuf", [128, NCH, D], BF16)
    scalecol = p.sb("scalecol", [16, 1], F32)
    bocols = p.sb("bocols_sb", [128, 8], F32)
    sc_ln = ExitStack()
    lnq = p.sb("lnqC", [128, NCH, S], BF16, scope=sc_ln, side="right")

    for c in range(NCH):
        wq_ld = p.dma("gpsimd", wbuf[:, c, :], wq_d[c], wl)
    p.dma("sync", scalecol[:], sc_d, g0)
    ld_g0 = p.dma("sync", bocols[:], bo_d, g0)

    emit_ln_featmajor(p, xdram, lnq, common, lx, mm, ev, dv)

    sc_att = ExitStack()
    qT = p.sb("qT", [128, NCH, S], BF16, scope=sc_att)
    kT = p.sb("kT", [128, NCH, S], BF16, scope=sc_att)
    vtok = p.sb("vtok", [128, 16, 1024], BF16, scope=sc_att)
    E2sb = p.sb("E2sb", [128, NCH, 16], BF16, scope=sc_att)
    E2Tsb = p.sb("E2Tsb", [16, NCH, 128], BF16, scope=sc_att)
    ssq = p.sb("ssq", [16, S], F32, scope=sc_att)
    qfac = p.sb("qfac", [16, S], BF16, scope=sc_att)
    zr = p.sb("zr", [1, 512], F32, scope=sc_att)
    resb = p.sb("resb", [64, 512], BF16, scope=sc_att)
    e2l = p.sem("e2l")
    for c in range(NCH):
        p.dma("gpsimd", E2sb[:, c, :], E2_d[c], e2l)
        e2_ld = p.dma("gpsimd", E2Tsb[:, c, :], E2T_d[c], e2l)

    def mk_evac(dst):
        def evac(m, ps):
            return p.act(dst[:, m, :], ps, AF.Copy, inc=None)
        return evac
    p.wait("tensor", dv, dv.v)
    emit_proj(p, wbuf, wq_ld, wl, lnq, mk_evac(qT), NCH, mm, ev)
    p.wait("gpsimd", mm, mm.v)
    for c in range(NCH):
        wk_ld = p.dma("gpsimd", wbuf[:, c, :], wk_d[c], wl)
    emit_proj(p, wbuf, wk_ld, wl, lnq, mk_evac(kT), NCH, mm, ev)
    p.wait("gpsimd", mm, mm.v)
    for c in range(NCH):
        wv_ld = p.dma("gpsimd", wbuf[:, c, :], wv_d[c], wl)
    with nc.psum_tensor("Vps", [128, 2, 512], F32) as Vps:
        p.wait("tensor", wl, wv_ld)
        p.wait("tensor", ev, ev.v)
        vps_guard = [0, 0]
        for ts in range(16):
            for mh in range(2):
                g = (ts * 2 + mh) % 2
                if vps_guard[g]:
                    p.wait("tensor", ev, vps_guard[g])
                for dc in range(NCH):
                    mv = p.mm(Vps[:, g, :], lnq[:, dc, 128 * ts:128 * (ts + 1)],
                              wbuf[:, dc, 512 * mh:512 * (mh + 1)],
                              start=(dc == 0), stop=(dc == NCH - 1),
                              inc=mm if dc == NCH - 1 else None)
                p.wait("scalar", mm, mv)
                vps_guard[g] = p.act(vtok[:, ts, 512 * mh:512 * (mh + 1)],
                                     Vps[:, g, :], AF.Copy)
        p.wait("tensor", ev, max(vps_guard))
    sc_ln.close()

    sqpp = p.sb("sqpp", [128, 2, S], BF16, scope=sc_att)
    v_qn = v_kn = 0
    for which, tgt, bias in (("q", qT, scalecol[:]), ("k", kT, 0.0)):
        with nc.psum_tensor(un("SSps"), [16, S], F32) as SSps:
            p.wait("tensor", ev, ev.v)
            p.wait("tensor", dv, dv.v)
            sq_guard = [0, 0]
            for c in range(NCH):
                g = c % 2
                if sq_guard[g]:
                    p.wait("scalar", mm, sq_guard[g])
                e_sq = p.act(sqpp[:, g, :], tgt[:, c, :], AF.Square)
                p.wait("tensor", ev, e_sq)
                if c == 0 and which == "q":
                    p.wait("tensor", e2l, e2_ld)
                    p.wait("tensor", g0, ld_g0)
                for n in range(4):
                    mv = p.mm(SSps[:, 512 * n:512 * (n + 1)], E2sb[:, c, :],
                              sqpp[:, g, 512 * n:512 * (n + 1)],
                              start=(c == 0), stop=(c == NCH - 1),
                              inc=mm if n == 3 else None)
                sq_guard[g] = mv
            p.wait("vector", mm, mm.v)
            p.vec("tensor_scalar", ssq[:], SSps[:], 1e-24, None, ALU.add)
        d_ln = p.ec["vector"].v
        p.wait("scalar", dv, d_ln)
        p.act(ssq[:], ssq[:], AF.Ln)
        e_f = p.act(qfac[:], ssq[:], AF.Exp, scale=-0.5, bias=bias)
        with nc.psum_tensor(un("EXps"), [128, 2, 512], F32) as EXps:
            p.wait("tensor", ev, e_f)
            ex_guard = [0, 0]
            for c in range(NCH):
                for n in range(4):
                    g = (c * 4 + n) % 2
                    if ex_guard[g]:
                        p.wait("tensor", dv, ex_guard[g])
                    mv = p.mm(EXps[:, g, :], E2Tsb[0:16, c, :],
                              qfac[0:16, 512 * n:512 * (n + 1)],
                              start=True, stop=True, inc=mm)
                    p.wait("vector", mm, mv)
                    ex_guard[g] = p.vec("tensor_mul",
                                        tgt[:, c, 512 * n:512 * (n + 1)],
                                        tgt[:, c, 512 * n:512 * (n + 1)],
                                        EXps[:, g, :])
            if which == "q":
                v_qn = max(ex_guard)
            else:
                v_kn = max(ex_guard)

    qhb = p.sb("qhb", [64, S], BF16, scope=sc_att)
    khb = p.sb("khb", [64, S], BF16, scope=sc_att)
    ohb = p.sb("ohb", [64, S], BF16, scope=sc_att)
    oTpk = p.sb("oTpk", [128, NCH, S], BF16, side="right")
    Esb = p.sb("EsbC", [128, 16, 512], BF16, scope=sc_att)
    with nc.psum_tensor("EpsA", [128, 512], F32) as EpsA, \
         nc.psum_tensor("EpsB", [128, 512], F32) as EpsB, \
         nc.psum_tensor("Zps", [1, 512], F32) as Zps, \
         nc.psum_tensor("REps", [64, 512], F32) as REps, \
         nc.psum_tensor("Cps", [64, 512], F32) as Cps:
        p.wait("sync", dv, max(v_qn, v_kn))
        p.wait("tensor", dv, max(v_qn, v_kn))
        qh_mm = kh_mm = 0
        pv_last = 0
        eps_guard = [0, 0]
        z_ln_prev = 0
        re_copy_prev = 0
        v_o = 0
        so_val = 0
        for h in range(16):
            hc, half = h // 2, h % 2
            if half == 0:
                qh_ap, kh_ap = qT[0:64, hc, :], kT[0:64, hc, :]
            else:
                p.wait("sync", mm, max(qh_mm, kh_mm))
                rq = p.dma("sync", qhb[:], qT[64:128, hc, :], sq)
                rk = p.dma("sync", khb[:], kT[64:128, hc, :], sq)
                p.wait("tensor", sq, rk)
                qh_ap, kh_ap = qhb[:], khb[:]
            for qt in range(4):
                for kc in range(16):
                    g = kc % 2
                    Eps = EpsA if g == 0 else EpsB
                    if eps_guard[g]:
                        p.wait("tensor", ev, eps_guard[g])
                    mv = p.mm(Eps[:], kh_ap[:, 128 * kc:128 * (kc + 1)],
                              qh_ap[:, 512 * qt:512 * (qt + 1)],
                              start=True, stop=True, inc=mm)
                    if half == 1:
                        qh_mm = kh_mm = mv
                    p.wait("scalar", mm, mv)
                    if kc == 0 and pv_last:
                        p.wait("scalar", mm, pv_last)
                    eps_guard[g] = p.act(Esb[:, kc, :], Eps[:], AF.Exp)
                p.wait("tensor", ev, max(eps_guard))
                if z_ln_prev:
                    p.wait("tensor", ev, z_ln_prev)
                for kc in range(16):
                    mv = p.mm(Zps[:], common["ones_colbf"][:], Esb[:, kc, :],
                              start=(kc == 0), stop=(kc == 15),
                              inc=mm if kc == 15 else None)
                p.wait("scalar", mm, mv)
                p.act(zr[:], Zps[:], AF.Ln)
                z_ln_prev = p.act(zr[:], zr[:], AF.Exp, scale=-1.0)
                if re_copy_prev:
                    p.wait("tensor", ev, re_copy_prev)
                p.wait("tensor", ev, z_ln_prev)
                mv = p.mm(REps[:], common["ones_rowf"][0:1, 0:64], zr[:],
                          start=True, stop=True, inc=mm)
                p.wait("scalar", mm, mv)
                if v_o:
                    p.wait("scalar", dv, v_o)
                re_copy_prev = p.act(resb[:], REps[:], AF.Copy)
                if v_o:
                    p.wait("tensor", dv, v_o)
                for kc in range(16):
                    mv = p.mm(Cps[:], vtok[:, kc, 64 * h:64 * (h + 1)],
                              Esb[:, kc, :], start=(kc == 0), stop=(kc == 15),
                              inc=mm if kc == 15 else None)
                pv_last = mv
                p.wait("vector", mm, mv)
                p.wait("vector", ev, re_copy_prev)
                if half == 0:
                    dst = oTpk[0:64, hc, 512 * qt:512 * (qt + 1)]
                else:
                    if qt == 0 and so_val:
                        p.wait("vector", so, so_val)
                    dst = ohb[:, 512 * qt:512 * (qt + 1)]
                v_o = p.vec("tensor_mul", dst, Cps[:], resb[:])
            if half == 1:
                p.wait("sync", dv, v_o)
                so_val = p.dma("sync", oTpk[64:128, hc, :], ohb[:], so)

    sc_att.close()
    p.wait("gpsimd", mm, mm.v)
    for c in range(NCH):
        wo_ld = p.dma("gpsimd", wbuf[:, c, :], wo_d[c], wl)
    ysb = p.sb("ysbC", [128, NCH, S], F32)
    p.wait("tensor", dv, dv.v)
    p.wait("tensor", so, so_val)

    def evac_y(m, ps):
        return p.act(ysb[:, m, :], ps, AF.Identity, bias=bocols[:, m:m + 1])
    ev_y = emit_proj(p, wbuf, wo_ld, wl, oTpk, evac_y, NCH, mm, ev)
    emit_postln_residual(p, ysb, ev_y, xdram, xodram, common, lx, mm, ev, dv, st)
    p.wait("sync", st[0], st[0].v)
    p.wait("sync", st[1], st[1].v)


def prep_l1_attn_consts(w_q, w_kv, w_out, b_out, scale):
    w_kv = np.asarray(w_kv, np.float32)
    E2 = np.zeros((NCH, 128, 16), np.float32)
    for c in range(NCH):
        E2[c, 0:64, 2 * c] = 1.0
        E2[c, 64:128, 2 * c + 1] = 1.0
    return {
        "wqcC": wchunks(w_q), "wkcC": wchunks(w_kv[:, :D]),
        "wvcC": wchunks(w_kv[:, D:]), "wocC": wchunks(w_out),
        "bocols": col_layout(b_out),
        "scalec": np.asarray(scale, np.float32).reshape(16, 1),
        "E2c": E2, "E2Tc": np.ascontiguousarray(E2.transpose(0, 2, 1)),
    }


def build_l1_ffn(nc):
    p = Prog(nc)
    xdram = nc.dram_tensor("x", [128, NCH, S], F32, kind="ExternalInput").ap()
    xodram = nc.dram_tensor("xo", [128, NCH, S], F32, kind="ExternalOutput").ap()
    emit_l1_ffn(p, xdram, xodram)
    p.run()


def emit_l1_ffn(p, xdram, xodram):
    nc = p.nc
    SH = 512
    NBL = S // SH
    wi_d = nc.dram_tensor("wiTc", [NCH, 128, FF], F32, kind="ExternalInput").ap()
    wo_d = nc.dram_tensor("woTc", [32, 128, D], F32, kind="ExternalInput").ap()
    A_d = nc.dram_tensor("Ac", [NE, NCH, 128, 16], F32, kind="ExternalInput").ap()
    B_d = nc.dram_tensor("Bc", [NE, 16, FF], F32, kind="ExternalInput").ap()
    gw_d = nc.dram_tensor("gwD", [NCH, 128, 8], F32, kind="ExternalInput").ap()
    gb_d = nc.dram_tensor("gbrow", [1, 8], F32, kind="ExternalInput").ap()
    bi_d = nc.dram_tensor("bicols", [128, 32], F32, kind="ExternalInput").ap()
    bo_d = nc.dram_tensor("borow", [1, D], F32, kind="ExternalInput").ap()
    sel_d = nc.dram_tensor("selD", [8, 8 * 128], F32, kind="ExternalInput").ap()
    eye_d = nc.dram_tensor("eyeD", [128, 128], F32, kind="ExternalInput").ap()

    lx = [p.sem("lx0"), p.sem("lx1")]
    wl = [p.sem("wl0"), p.sem("wl1")]
    ol = [p.sem("ol0"), p.sem("ol1")]
    mm = p.sem("mm")
    st = [p.sem("st0"), p.sem("st1")]
    g0 = p.sem("g0")
    sl = p.sem("sl")
    dv, ev = p.ec["vector"], p.ec["scalar"]
    gp = p.ec["gpsimd"]

    common = make_common(p)
    ysb = p.sb("ysbD", [128, NCH, S], F32)
    sc_ln = ExitStack()
    lnq = p.sb("lnqD", [128, NCH, S], BF16, scope=sc_ln, side="right")

    emit_ln_featmajor(p, xdram, lnq, common, lx, mm, ev, dv)
    v_lnq = dv.v

    sc_f = ExitStack()
    gwsb = p.sb("gwsbD", [128, NCH, 8], BF16, scope=sc_f)
    gbsb = p.sb("gbsb", [1, 8], BF16, scope=sc_f)
    bisb = p.sb("bisb", [128, 32], F32, scope=sc_f)
    borow = p.sb("borowD", [1, D], BF16, scope=sc_f)
    selsb = p.sb("selD_sb", [8, 8 * 128], BF16, scope=sc_f)
    eyef = p.sb("eyefD", [128, 128], F32, scope=sc_f)
    eyebf = p.sb("eyebfD", [128, 128], BF16, scope=sc_f)
    Asb = p.sb("Asb", [128, NE, NCH, 16], BF16, scope=sc_f)
    Bst = p.sb("BstD", [16, 2, NE, 128], BF16, scope=sc_f)
    esb = p.sb("esbD", [128, 8], F32, scope=sc_f)
    scn = p.sb("scnD", [128, 8], F32, scope=sc_f)
    zcol = p.sb("zcolD", [128, 1], F32, scope=sc_f)
    rcpc = p.sb("rcpcD", [128, 1], F32, scope=sc_f)
    m1c = p.sb("m1c", [128, 1], F32, scope=sc_f)
    m2c = p.sb("m2c", [128, 1], F32, scope=sc_f)
    nmc = p.sb("nmc", [128, 1], F32, scope=sc_f)
    vsc = p.sb("vsc", [128, 1], F32, scope=sc_f)
    ind1 = p.sb("ind1", [128, 8], F32, scope=sc_f)
    ind2 = p.sb("ind2", [128, 8], F32, scope=sc_f)
    dsb = p.sb("dsbD", [128, 8], F32, scope=sc_f)
    gtok = p.sb("gtok", [128, 8], F32, scope=sc_f)
    gatesDb = p.sb("gatesDb", [8, S], BF16, scope=sc_f)
    vs2row = p.sb("vs2row", [1, S], BF16, scope=sc_f)
    tsb = p.sb("tsbD", [16, NE, SH], BF16, scope=sc_f)
    GD = p.sb("GDsb", [128, NE, SH], BF16, scope=sc_f)
    Hb = p.sb("HbD", [128, 32, SH], BF16, scope=sc_f)
    Hacc = p.sb("HaccD", [128, SH], F32, scope=sc_f)
    tmpD = p.sb("tmpD", [128, SH], F32, scope=sc_f)
    tmpDD = p.sb("tmpDD", [128, 2, SH], F32, scope=sc_f)
    btile = p.sb("btileD", [128, 2, SH], BF16, scope=sc_f)
    wist = p.sb("wistD", [128, 2, NCH, 128], BF16, scope=sc_f)
    wost = p.sb("wostD", [128, 2, 512], BF16, scope=sc_f)

    cl = p.sem("cl")
    p.wait("gpsimd", dv, v_lnq)
    p.wait("sync", dv, v_lnq)
    for c in range(NCH):
        p.dma("gpsimd", gwsb[:, c, :], gw_d[c], cl)
    p.dma("gpsimd", borow[:], bo_d, cl)
    p.dma("gpsimd", selsb[:], sel_d, cl)
    p.dma("gpsimd", eyebf[:], eye_d, cl)
    p.dma("gpsimd", gbsb[:], gb_d, cl)
    for e in range(NE):
        for dc in range(NCH):
            ld_wl0 = p.dma("gpsimd", Asb[:, e, dc, :], A_d[e, dc], cl)
    p.dma("sync", bisb[:], bi_d, g0)
    ld_g0 = p.dma("sync", eyef[:], eye_d, g0)

    with nc.psum_tensor("LGpsD", [128, 16, 8], F32) as LGps, \
         nc.psum_tensor("TGpsA", [8, 128], F32) as TGpsA, \
         nc.psum_tensor("TGpsB", [8, 128], F32) as TGpsB, \
         nc.psum_tensor("TVpsA", [1, 128], F32) as TVpsA, \
         nc.psum_tensor("TVpsB", [1, 128], F32) as TVpsB:
        p.wait("tensor", dv, v_lnq)
        p.wait("tensor", cl, ld_wl0)
        p.wait("tensor", g0, ld_g0)
        p.wait("tensor", ev, ev.v)
        tg_guard = [0, 0]
        tv_guard = [0, 0]
        lg_read = 0
        for ts in range(16):
            for c in range(NCH):
                p.mm(LGps[:, ts, :], lnq[:, c, 128 * ts:128 * (ts + 1)],
                     gwsb[:, c, :], start=(c == 0), stop=False)
            mv = p.mm(LGps[:, ts, :], common["ones_rowbf"][:],
                      gbsb[:], start=False, stop=True, inc=mm)
            p.wait("scalar", mm, mv)
            if lg_read:
                p.wait("scalar", dv, lg_read)
            e_e = p.act(esb[:], LGps[:, ts, :], AF.Exp, accum_out=zcol[:])
            p.act(rcpc[:], zcol[:], AF.Ln)
            e_r = p.act(rcpc[:], rcpc[:], AF.Exp, scale=-1.0)
            e_s = p.act(scn[:], esb[:], AF.Identity, scale=rcpc[:])
            p.wait("vector", ev, e_s)
            p.vec("tensor_scalar", dsb[:], scn[:], 0.0, None, ALU.add,
                  ALU.max, accum_out=m1c[:])
            v_n = p.vec("tensor_scalar", nmc[:], m1c[:], -1.0, None, ALU.mult)
            p.wait("scalar", dv, v_n)
            e_d = p.act(dsb[:], scn[:], AF.Identity, bias=nmc[:])
            p.wait("vector", ev, e_d)
            p.vec("tensor_scalar", ind1[:], dsb[:], 0.0, None, ALU.is_ge)
            p.vec("scalar_tensor_tensor", dsb[:], ind1[:], -1e9, scn[:],
                  ALU.mult, ALU.add)
            p.vec("tensor_scalar", tmpD[:, 0:8], dsb[:], 0.0, None, ALU.add,
                  ALU.max, accum_out=m2c[:])
            v_n2 = p.vec("tensor_scalar", nmc[:], m2c[:], -1.0, None, ALU.mult)
            p.wait("scalar", dv, v_n2)
            e_d2 = p.act(dsb[:], dsb[:], AF.Identity, bias=nmc[:])
            p.wait("vector", ev, e_d2)
            p.vec("tensor_scalar", ind2[:], dsb[:], 0.0, None, ALU.is_ge)
            p.vec("tensor_add", ind1[:], ind1[:], ind2[:])
            v_vs = p.vec("tensor_add", vsc[:], m1c[:], m2c[:])
            p.wait("scalar", dv, v_vs)
            e_g = p.act(gtok[:], ind1[:], AF.Identity, scale=vsc[:])
            g = ts % 2
            tg = TGpsA if g == 0 else TGpsB
            tv = TVpsA if g == 0 else TVpsB
            p.wait("tensor", ev, e_g)
            if tg_guard[g]:
                p.wait("tensor", ev, tg_guard[g])
            m_t = p.t_transpose(tg[0:8, :], gtok[:], eyef[:], inc=mm)
            if tv_guard[g]:
                p.wait("tensor", ev, tv_guard[g])
            m_t2 = p.t_transpose(tv[0:1, :], vsc[:], eyef[:], inc=mm)
            p.wait("scalar", mm, m_t2)
            tg_guard[g] = p.act(gatesDb[:, 128 * ts:128 * (ts + 1)], tg[0:8, :],
                                AF.Copy)
            tv_guard[g] = p.act(vs2row[:, 128 * ts:128 * (ts + 1)], tv[0:1, :],
                                AF.Copy, scale=2.0)
            lg_read = p.ec["vector"].v
        e_gall = max(max(tg_guard), max(tv_guard))

    def load_wi(slot, bl, fc):
        for dc in range(NCH):
            p.dma("gpsimd", wist[:, slot, dc, :],
                  wi_d[dc, :, 128 * fc:128 * (fc + 1)], wl[slot])
        for e in range(NE):
            v = p.dma("gpsimd", Bst[0:16, slot, e, :],
                      B_d[e, :, 128 * fc:128 * (fc + 1)], wl[slot])
        return v

    def load_wo(slot, fc, dth):
        return p.dma("gpsimd", wost[:, slot, :],
                     wo_d[fc, :, 512 * dth:512 * (dth + 1)], ol[slot])

    wld = {}
    wld[(0, 0)] = load_wi(0, 0, 0)
    if True:
        p.wait("tensor", ev, e_gall)
        gt_ev = 0
        t_ev = 0
        b_guard = [0, 0]
        h_guard = [0, 0, 0, 0]
        y_read = 0
        pool_last = 0
        pool_tmp = [0, 0]
        hb_pool = 0
        wo_mm = {}
        for bl in range(NBL):
            sb0 = SH * bl
            ph1 = ExitStack()
            Tps = ph1.enter_context(nc.psum_tensor(un("TpsD"), [16, SH], F32))
            GTps = ph1.enter_context(nc.psum_tensor(un("GTpsD"), [128, SH], F32))
            p.wait("tensor", ev, ev.v)
            p.wait("tensor", dv, dv.v)
            for e in range(NE):
                if t_ev:
                    p.wait("tensor", ev, t_ev)
                for dc in range(NCH):
                    mv = p.mm(Tps[:], Asb[:, e, dc, :],
                              lnq[:, dc, sb0:sb0 + SH],
                              start=(dc == 0), stop=(dc == NCH - 1),
                              inc=mm if dc == NCH - 1 else None)
                p.wait("scalar", mm, mv)
                t_ev = p.act(tsb[:, e, :], Tps[:], AF.Copy)
            for e in range(NE):
                if gt_ev:
                    p.wait("tensor", ev, gt_ev)
                mv = p.mm(GTps[:], selsb[0:8, 128 * e:128 * (e + 1)],
                          gatesDb[0:8, sb0:sb0 + SH], start=True, stop=True,
                          inc=mm)
                p.wait("scalar", mm, mv)
                gt_ev = p.act(GD[:, e, :], GTps[:], AF.Copy)
            ph1.close()
            ph2 = ExitStack()
            Bps = ph2.enter_context(nc.psum_tensor(un("BpsD"), [128, 2, SH], F32))
            Php = ph2.enter_context(nc.psum_tensor(un("PhD"), [128, 4, SH], F32))
            p.wait("tensor", ev, ev.v)
            p.wait("tensor", dv, dv.v)
            p.wait("gpsimd", mm, mm.v)
            for fc in range(32):
                nxt = (bl, fc + 1) if fc + 1 < 32 else ((bl + 1, 0) if bl + 1 < NBL else None)
                if nxt is not None:
                    slot_n = (fc + 1) % 2
                    p.wait("gpsimd", mm, mm.v)
                    wld[nxt] = load_wi(slot_n, *nxt)
                slot = fc % 2
                g = fc % 2
                p.wait("tensor", wl[slot], wld[(bl, fc)])
                if b_guard[g]:
                    p.wait("tensor", ev, b_guard[g])
                for dc in range(NCH):
                    mv = p.mm(Bps[:, g, :], wist[:, slot, dc, :],
                              lnq[:, dc, sb0:sb0 + SH],
                              start=(dc == 0), stop=(dc == NCH - 1),
                              inc=mm if dc == NCH - 1 else None)
                p.wait("scalar", mm, mv)
                b_guard[g] = p.act(btile[:, g, :], Bps[:, g, :], AF.Identity,
                                   bias=bisb[:, fc:fc + 1])
                for e in range(NE):
                    gph = e % 4
                    if h_guard[gph]:
                        p.wait("tensor", dv, h_guard[gph])
                    p.mm(Php[:, gph, :], Bst[0:16, slot, e, :],
                         tsb[:, e, :], start=True, stop=False)
                    if e == 0:
                        p.wait("tensor", ev, b_guard[g])
                    mv = p.mm(Php[:, gph, :], eyebf[:], btile[:, g, :],
                              start=False, stop=True, inc=mm)
                    p.wait("vector", mm, mv)
                    if e == 0:
                        if pool_last:
                            p.wait("vector", gp, pool_last)
                        h_guard[gph] = p.vec("scalar_tensor_tensor", Hacc[:],
                                             Php[:, gph, :], 0.0, GD[:, e, :],
                                             ALU.max, ALU.mult)
                    else:
                        par = e % 2
                        if pool_tmp[par]:
                            p.wait("vector", gp, pool_tmp[par])
                        h_guard[gph] = p.vec("scalar_tensor_tensor",
                                             tmpDD[:, par, :], Php[:, gph, :],
                                             0.0, GD[:, e, :], ALU.max, ALU.mult)
                        p.wait("gpsimd", dv, h_guard[gph])
                        if e < NE - 1:
                            pool_tmp[par] = p.gp("tensor_add", Hacc[:], Hacc[:],
                                                 tmpDD[:, par, :])
                        else:
                            pool_tmp[par] = p.gp("tensor_add", Hb[:, fc, :],
                                                 Hacc[:], tmpDD[:, par, :])
                pool_last = pool_tmp[1]
                hb_pool = pool_last
            ph2.close()
            ph3 = ExitStack()
            Yps = ph3.enter_context(nc.psum_tensor(un("YpsD"), [128, 4, SH], F32))
            p.wait("tensor", ev, ev.v)
            p.wait("tensor", dv, dv.v)
            p.wait("tensor", gp, hb_pool)
            for dth in range(2):
                wo_ld = {}
                p.wait("gpsimd", mm, mm.v)
                wo_ld[0] = load_wo(0, 0, dth)
                for fc in range(32):
                    slot = fc % 2
                    if fc + 1 < 32:
                        if fc >= 1:
                            p.wait("gpsimd", mm, wo_mm.get((fc - 1) % 2, 0))
                        wo_ld[fc + 1] = load_wo((fc + 1) % 2, fc + 1, dth)
                    p.wait("tensor", ol[slot], wo_ld[fc])
                    if fc == 0 and y_read:
                        p.wait("tensor", ev, y_read)
                    for dt in range(4):
                        mv = p.mm(Yps[:, dt, :], wost[:, slot, 128 * dt:128 * (dt + 1)],
                                  Hb[:, fc, :], start=(fc == 0), stop=False,
                                  inc=mm if dt == 3 else None)
                    wo_mm[slot] = mv
                for dt in range(4):
                    dtg = 4 * dth + dt
                    mv = p.mm(Yps[:, dt, :], borow[:, 128 * dtg:128 * (dtg + 1)],
                              vs2row[:, sb0:sb0 + SH], start=False, stop=True,
                              inc=mm)
                p.wait("scalar", mm, mv)
                for dt in range(4):
                    dtg = 4 * dth + dt
                    y_read = p.act(ysb[:, dtg, sb0:sb0 + SH], Yps[:, dt, :],
                                   AF.Copy)
            ph3.close()
    ev_y = y_read
    sc_f.close()
    sc_ln.close()

    emit_postln_residual(p, ysb, ev_y, xdram, xodram, common, lx, mm, ev, dv, st)
    p.wait("sync", st[0], st[0].v)
    p.wait("sync", st[1], st[1].v)


def prep_l1_ffn_consts(wi, bi, wo, bo, lora_A, lora_B, gate_w, gate_b):
    sel = np.zeros((8, 8 * 128), np.float32)
    for e in range(8):
        sel[e, 128 * e:128 * (e + 1)] = 1.0
    return {
        "wiTc": wchunks(np.asarray(wi, np.float32).T),
        "woTc": np.ascontiguousarray(
            np.asarray(wo, np.float32).T.reshape(32, 128, D)),
        "Ac": np.ascontiguousarray(
            np.asarray(lora_A, np.float32).transpose(0, 2, 1).reshape(NE, NCH, 128, 16)),
        "Bc": np.ascontiguousarray(np.asarray(lora_B, np.float32).transpose(0, 2, 1)),
        "gwD": wchunks(np.asarray(gate_w, np.float32).T),
        "gbrow": np.asarray(gate_b, np.float32).reshape(1, 8),
        "bicols": col_layout(bi),
        "borow": np.asarray(bo, np.float32).reshape(1, D),
        "selD": sel,
        "eyeD": np.eye(128, dtype=np.float32),
    }


class _NCProxy:
    def __init__(self, nc):
        object.__setattr__(self, "_nc", nc)

    def __getattr__(self, a):
        return getattr(self._nc, a)

    def psum_tensor(self, name, *a, **k):
        return self._nc.psum_tensor(un(name), *a, **k)

    def sbuf_tensor(self, name, *a, **k):
        return self._nc.sbuf_tensor(un(name), *a, **k)


def build_fused(nc):
    p = Prog(_NCProxy(nc))
    x0 = nc.dram_tensor("x", [128, NCH, S], F32, kind="ExternalInput").ap()
    x1 = nc.dram_tensor("x1d", [128, NCH, S], F32, kind="Internal").ap()
    x2 = nc.dram_tensor("x2d", [128, NCH, S], F32, kind="Internal").ap()
    x3 = nc.dram_tensor("x3d", [128, NCH, S], F32, kind="Internal").ap()
    xo = nc.dram_tensor("xo", [128, NCH, S], F32, kind="ExternalOutput").ap()
    phases = ((emit_l0_attn, x0, x1), (emit_l0_moe, x1, x2),
              (emit_l1_attn, x2, x3), (emit_l1_ffn, x3, xo))
    for emitter, xi, xod in phases:
        ph = ExitStack()
        p.default_scope = ph
        emitter(p, xi, xod)
        for eng in ("sync", "gpsimd"):
            p.wait(eng, p.sem("mm"), p.sem("mm").v)
            for ec in p.ec.values():
                p.wait(eng, ec, ec.v)
        ph.close()
    p.default_scope = None
    p.run()


def xcore_to_xT(x_core):
    return np.ascontiguousarray(np.asarray(x_core, np.float32).T.reshape(NCH, 128, S).transpose(1, 0, 2))


def xT_to_core(xo):
    return xo.transpose(1, 0, 2).reshape(D, S).T


def prep_l0_attn_inputs(x_core, wq, bq, wk, bk, wv, bv, wo, bo):
    Bfirst, Blast, B0T, B0sh = make_l0_consts()
    bcols = np.concatenate([col_layout(bq), col_layout(bk),
                            col_layout(bv), col_layout(bo)], axis=1)
    return {
        "x": xcore_to_xT(x_core),
        "wqc": wchunks(np.asarray(wq).T), "wkc": wchunks(np.asarray(wk).T),
        "wvc": wchunks(np.asarray(wv).T), "woc": wchunks(np.asarray(wo).T),
        "bcols": np.ascontiguousarray(bcols),
        "Bfirst": Bfirst, "Blast": Blast,
        "B0T": np.ascontiguousarray(B0T.reshape(128, 256)),
        "B0sh": np.ascontiguousarray(B0sh.reshape(128, 272)),
        "eye": np.eye(128, dtype=np.float32),
    }


def prep_l0_moe_consts(gate_weight, W1, b1, W2, b2, Ws1, bs1, Ws2, bs2):
    W1c = np.concatenate([np.asarray(W1, np.float32),
                          np.asarray(Ws1, np.float32)[None]], 0).reshape(9, 8, 128, FF)
    W2c = np.concatenate([np.asarray(W2, np.float32),
                          np.asarray(Ws2, np.float32)[None]], 0).reshape(9, 32, 128, D)
    b1c = np.stack([np.asarray(b, np.float32).reshape(32, 128).T
                    for b in list(b1) + [bs1]], 1)
    sel = np.zeros((8, 8 * 128), np.float32)
    for e in range(8):
        sel[e, 128 * e:128 * (e + 1)] = 1.0
    return {
        "gwc": wchunks(np.asarray(gate_weight, np.float32).T),
        "W1c": np.ascontiguousarray(W1c), "W2c": np.ascontiguousarray(W2c),
        "b1c": np.ascontiguousarray(b1c),
        "bs2c": col_layout(bs2),
        "b2r": np.ascontiguousarray(np.asarray(b2, np.float32)),
        "selc": sel,
        "eyeB": np.eye(128, dtype=np.float32),
    }


def kernel(**inputs):
    ii = {k: np.asarray(v) for k, v in inputs.items()}
    x = ii["L0_attn_x"].astype(np.float32)

    consts = prep_l0_moe_consts(
        ii["L0_ffn_gate_weight"], ii["L0_ffn_W1"], ii["L0_ffn_b1"],
        ii["L0_ffn_W2"], ii["L0_ffn_b2"], ii["L0_ffn_Ws1"], ii["L0_ffn_bs1"],
        ii["L0_ffn_Ws2"], ii["L0_ffn_bs2"])
    consts.update(prep_l1_attn_consts(
        ii["L1_attn_w_q"], ii["L1_attn_w_kv"], ii["L1_attn_w_out"],
        ii["L1_attn_b_out"], ii["L1_attn_scale"]))
    consts.update(prep_l1_ffn_consts(
        ii["L1_ffn_wi"], ii["L1_ffn_bi"], ii["L1_ffn_wo"], ii["L1_ffn_bo"],
        ii["L1_ffn_lora_A"], ii["L1_ffn_lora_B"], ii["L1_ffn_gate_w"],
        ii["L1_ffn_gate_b"]))

    nc = bass.Bass("TRN2", target_bir_lowering=False)
    build_fused(nc)
    in_maps = []
    for c in range(8):
        m = prep_l0_attn_inputs(
            x[c], ii["L0_attn_wq"], ii["L0_attn_bq"], ii["L0_attn_wk"],
            ii["L0_attn_bk"], ii["L0_attn_wv"], ii["L0_attn_bv"],
            ii["L0_attn_wo"], ii["L0_attn_bo"])
        m.update(consts)
        in_maps.append(m)
    res = run_bass_kernel_spmd(nc, in_maps, core_ids=list(range(8)))
    out = np.stack([xT_to_core(np.asarray(res.results[c]["xo"]))
                    for c in range(8)]).astype(np.float32)
    return out


# version 05
# speedup vs baseline: 1.2288x; 1.2288x over previous
import math
from contextlib import ExitStack

import numpy as np
import ml_dtypes
import concourse.bass as bass
import concourse.mybir as mybir
from concourse.bass_utils import run_bass_kernel_spmd

F32 = mybir.dt.float32
BF16 = mybir.dt.bfloat16
AF = mybir.ActivationFunctionType
ALU = mybir.AluOpType
AX = mybir.AxisListType

S = 2048
D = 1024
NCH = 8
H = 16
DH = 64
FF = 4096
NE = 8
BF = ml_dtypes.bfloat16
LN_EPS = 1e-5
GATEK = 1e20


def _l0_bias(delta):
    d = np.asarray(delta, dtype=np.float64)
    return (np.exp(-0.5 * (d / 2.0) ** 2) + np.cos(2.0 * np.pi * d / 401.0)).astype(np.float32)


def make_l0_consts():
    tabs = []
    for t in range(16):
        B = np.full((128, 256), -2e9, dtype=np.float32)
        q = 128 * t + np.arange(128)
        c = np.arange(128)
        for s_i, base in ((0, 128 * t - 2), (1, 128 * t + 126)):
            key = base + c
            dlt = key[:, None] - q[None, :]
            v = (np.abs(dlt) <= 2) & (key[:, None] >= 0) & (key[:, None] < S) & (key[:, None] != 0)
            blk = np.full((128, 128), -2e9, dtype=np.float32)
            blk[v] = _l0_bias(dlt)[v]
            B[:, 128 * s_i:128 * (s_i + 1)] = blk
        tabs.append(B)
    Bfirst = np.ascontiguousarray(np.concatenate(tabs[0:8], axis=1))
    Blast = np.ascontiguousarray(np.concatenate(tabs[8:16], axis=1))
    b0 = _l0_bias(np.arange(S))
    B0T = np.ascontiguousarray(
        np.tile(b0.reshape(16, 128).T[:, :, None], (1, 1, 16))).astype(np.float32)
    key = 128 * np.arange(17)[None, :] - 2 + np.arange(128)[:, None]
    b0sh = np.where((key >= 0) & (key < S), _l0_bias(np.abs(key)), np.float32(-2e9)).astype(np.float32)
    B0sh = np.ascontiguousarray(np.tile(b0sh[:, :, None], (1, 1, 16))).astype(np.float32)
    return Bfirst, Blast, B0T, B0sh


def col_layout(v):
    return np.ascontiguousarray(np.asarray(v, np.float32).reshape(-1, 128).T)


def wchunks(W):
    W = np.asarray(W, np.float32)
    K, M = W.shape
    return np.ascontiguousarray(W.reshape(K // 128, 128, M))



_UN = [0]


def un(base):
    _UN[0] += 1
    return f"{base}_{_UN[0]}"

ENGS = ("sync", "tensor", "vector", "scalar", "gpsimd")


class Ctr:
    def __init__(self, h):
        self.h = h
        self.v = 0

    def bump(self, n=1):
        self.v += n
        return self.v


class Prog:
    def __init__(self, nc):
        self.nc = nc
        self.ctx = ExitStack()
        self.q = {e: [] for e in ENGS}
        self._n = 0
        self._sems = {}
        self.default_scope = None
        self.ec = {"vector": self.sem("dvc"), "scalar": self.sem("evc"),
                   "gpsimd": self.sem("gpc")}

    def sem(self, name=None):
        if name is not None and name in self._sems:
            return self._sems[name]
        self._n += 1
        c = Ctr(self.ctx.enter_context(self.nc.semaphore(name or f"s{self._n}")))
        if name is not None:
            self._sems[name] = c
        return c

    LEDGER = []

    def sb(self, name, shape, dtype, scope=None, side=None):
        import math as _m
        free = 1
        for d in shape[1:]:
            free *= d
        bts = free * (2 if dtype == BF16 else 4)
        try:
            h = (scope or self.default_scope or self.ctx).enter_context(
                self.nc.sbuf_tensor(name, shape, dtype, side=side))
            Prog.LEDGER.append((name, bts, side or "left", self.nc.sbuf_base, self.nc.sbuf_top))
            return h
        except BaseException:
            for rec in Prog.LEDGER:
                print("  alloc", rec)
            raise

    def on(self, eng, fn):
        self.q[eng].append(fn)

    def wait(self, eng, ctr, val=None):
        v = ctr.v if val is None else val
        if v > 0:
            self.on(eng, lambda e, v=v: e.wait_ge(ctr.h, v))

    def dma(self, eng, out, in_, ctr):
        self.on(eng, lambda e: e.dma_start(out=out, in_=in_).then_inc(ctr.h, 16))
        return ctr.bump(16)

    def mm(self, out, lhsT, rhs, start, stop, inc=None):
        if inc is None:
            self.on("tensor", lambda e: e.matmul(out, lhsT, rhs, start=start, stop=stop))
            return None
        v = inc.bump()
        self.on("tensor", lambda e: e.matmul(out, lhsT, rhs, start=start, stop=stop).then_inc(inc.h, 1))
        return v

    def t_transpose(self, out, in_, ident, inc=None):
        if inc is None:
            self.on("tensor", lambda e: e.transpose(out, in_, ident))
            return None
        v = inc.bump()
        self.on("tensor", lambda e: e.transpose(out, in_, ident).then_inc(inc.h, 1))
        return v

    def _chained(self, eng, fn):
        c = self.ec[eng]
        prev = c.v
        v = c.bump()
        def run(e, prev=prev, fn=fn, c=c):
            if prev > 0:
                e.wait_ge(c.h, prev)
            fn(e).then_inc(c.h, 1)
        self.on(eng, run)
        return v

    def act(self, out, in_, func, bias=0.0, scale=1.0, accum_out=None, inc=None):
        return self._chained("scalar",
                             lambda e: e.activation(out, in_, func, bias=bias,
                                                    scale=scale, accum_out=accum_out))

    def vec(self, meth, *args, inc=None, **kw):
        return self._chained("vector", lambda e: getattr(e, meth)(*args, **kw))

    def gp(self, meth, *args, inc=None, **kw):
        return self._chained("gpsimd", lambda e: getattr(e, meth)(*args, **kw))

    def run(self):
        with self.nc.Block() as block:
            for eng in ENGS:
                fns = self.q[eng]
                if not fns:
                    continue
                getattr(block, eng)(lambda e, fns=fns: [f(e) for f in fns])
        self.ctx.close()


def make_common(p):
    c = {}
    c["ones_colbf"] = p.sb(un("ones_colbf"), [128, 1], BF16)
    c["ones_rowf"] = p.sb(un("ones_rowf"), [1, 128], F32)
    c["ones_rowbf"] = p.sb(un("ones_rowbf"), [1, 128], BF16)
    c["one1bf"] = p.sb(un("one1bf"), [1, 1], BF16)
    c["onesP"] = p.sb(un("onesP"), [65, 128], F32)
    p.vec("memset", c["onesP"][:], 1.0)
    p.vec("memset", c["ones_colbf"][:], 1.0)
    p.vec("memset", c["ones_rowf"][:], 1.0)
    p.vec("memset", c["ones_rowbf"][:], 1.0)
    c["init_dv"] = p.vec("memset", c["one1bf"][:], 1.0)
    return c


def emit_ln_featmajor(p, xdram, lnq, common, lx, mm, ev, dv, raw_gate=None,
                      raw_gate2=None, keep=None):
    nc = p.nc
    sc = ExitStack()
    xc = p.sb(un("lnxc"), [128, 2, S], F32, scope=sc)
    xbf = p.sb(un("lnxbf"), [128, S], BF16, scope=sc)
    xsq = p.sb(un("lnxsq"), [128, S], BF16, scope=sc)
    rows_scope = keep if keep is not None else sc
    rows_side = "right" if keep is not None else None
    murow = p.sb(un("murow"), [1, S], F32, scope=rows_scope, side=rows_side)
    s2row = p.sb(un("s2row"), [1, S], F32, scope=sc)
    Abcs = p.sb(un("Abcs"), [128, S], F32, scope=rows_scope, side=rows_side)
    Bbcs = p.sb(un("Bbcs"), [128, S], F32, scope=sc)
    tf = p.sb(un("lntf"), [128, S], F32, scope=sc)
    tf2 = p.sb(un("lntf2"), [128, S], F32, scope=sc)

    Vc, Ec, Mc = [0] * NCH, [0] * NCH, [0] * NCH
    with nc.psum_tensor(un("lnps1"), [1, S], F32) as ps1, \
         nc.psum_tensor(un("lnps2"), [1, S], F32) as ps2:
        for c in range(NCH):
            slot = c % 2
            if c >= 2:
                p.wait("sync", dv, Vc[c - 2])
                p.wait("sync", mm, Mc[c - 2])
            ldv = p.dma("sync", xc[:, slot, :], xdram[:, c, :], lx[slot])
            p.wait("vector", lx[slot], ldv)
            if c >= 1:
                p.wait("vector", mm, Mc[c - 1])
            Vc[c] = p.vec("tensor_copy", xbf[:], xc[:, slot, :], inc=dv)
            p.wait("scalar", dv, Vc[c])
            if c >= 1:
                p.wait("scalar", mm, Mc[c - 1])
            Ec[c] = p.act(xsq[:], xbf[:], AF.Square, inc=ev)
            p.wait("tensor", ev, Ec[c])
            for n in range(4):
                p.mm(ps1[:, 512 * n:512 * (n + 1)], common["ones_colbf"][:],
                     xbf[:, 512 * n:512 * (n + 1)], start=(c == 0), stop=(c == NCH - 1))
                p.mm(ps2[:, 512 * n:512 * (n + 1)], common["ones_colbf"][:],
                     xsq[:, 512 * n:512 * (n + 1)], start=(c == 0), stop=(c == NCH - 1),
                     inc=mm if (n == 3 and raw_gate is None) else None)
            if raw_gate is not None:
                raw_gate(c, xc[:, slot, :], ldv)
            Mc[c] = mm.v
        p.wait("scalar", mm, Mc[NCH - 1])
        p.act(murow[:], ps1[:], AF.Copy, scale=1.0 / D)
        e2 = p.act(s2row[:], ps2[:], AF.Copy, scale=1.0 / D, inc=ev)
    with nc.psum_tensor(un("lnpbA"), [128, S], F32) as pbA, \
         nc.psum_tensor(un("lnpbB"), [128, S], F32) as pbB:
        p.wait("tensor", ev, e2)
        mmv = None
        for n in range(4):
            p.mm(pbA[:, 512 * n:512 * (n + 1)], common["ones_rowf"][:],
                 murow[:, 512 * n:512 * (n + 1)], start=True, stop=True)
            mmv = p.mm(pbB[:, 512 * n:512 * (n + 1)], common["ones_rowf"][:],
                       s2row[:, 512 * n:512 * (n + 1)], start=True, stop=True,
                       inc=mm if n == 3 else None)
        p.wait("scalar", mm, mmv)
        p.act(Abcs[:], pbA[:], AF.Copy)
        e2b = p.act(Bbcs[:], pbB[:], AF.Copy, inc=ev)
        p.wait("vector", ev, e2b)
        p.vec("tensor_mul", tf[:], Abcs[:], Abcs[:])
        p.vec("scalar_tensor_tensor", tf[:], tf[:], -1.0, Bbcs[:], ALU.mult, ALU.add)
        d1 = p.vec("tensor_scalar", tf[:], tf[:], LN_EPS, None, ALU.add, inc=dv)
        p.wait("scalar", dv, d1)
        p.act(tf[:], tf[:], AF.Ln)
        e3 = p.act(tf[:], tf[:], AF.Exp, scale=-0.5, inc=ev)
        p.wait("vector", ev, e3)
        p.vec("scalar_tensor_tensor", Bbcs[:], Abcs[:], -1.0, tf[:],
              ALU.mult, ALU.mult)
        ev2 = p.vec("tensor_copy", Abcs[:], tf[:], inc=dv)
    applied = []
    M2c = [0] * NCH
    for c in range(NCH):
        slot = c % 2
        if c < 2:
            p.wait("sync", dv, Vc[NCH - 2 + c])
            p.wait("sync", mm, Mc[NCH - 1])
        else:
            p.wait("sync", dv, applied[c - 2])
            if raw_gate2 is not None:
                p.wait("sync", mm, M2c[c - 2])
        ldv = p.dma("sync", xc[:, slot, :], xdram[:, c, :], lx[slot])
        p.wait("vector", lx[slot], ldv)
        p.vec("tensor_mul", tf2[:], xc[:, slot, :], Abcs[:])
        applied.append(p.vec("tensor_add", lnq[:, c, :], tf2[:], Bbcs[:], inc=dv))
        if raw_gate2 is not None:
            raw_gate2(c, xc[:, slot, :], ldv)
            M2c[c] = mm.v
    sc.close()
    return murow, Abcs


def emit_proj(p, wsb, wld_val, wld_ctr, xin, evac, mcs, mm, ev):
    nc = p.nc
    evs = []
    with nc.psum_tensor(un("prps"), [128, 2, S], F32) as prps:
        p.wait("tensor", wld_ctr, wld_val)
        p.wait("tensor", ev, ev.v)
        for m in range(mcs):
            g = m % 2
            if m >= 2:
                p.wait("tensor", ev, evs[m - 2])
            for c in range(NCH):
                for n in range(4):
                    p.mm(prps[:, g, 512 * n:512 * (n + 1)],
                         wsb[:, c, 128 * m:128 * (m + 1)],
                         xin[:, c, 512 * n:512 * (n + 1)],
                         start=(c == 0), stop=(c == NCH - 1),
                         inc=mm if (c == NCH - 1 and n == 3) else None)
            p.wait("scalar", mm, mm.v)
            evs.append(evac(m, prps[:, g, :]))
    return evs[-1]


def emit_postln_residual(p, ysb, ev_y, xdram, xodram, common, lx, mm, ev, dv, st):
    nc = p.nc
    sc = ExitStack()
    ybf = p.sb(un("plybf"), [128, S], BF16, scope=sc)
    ysq = p.sb(un("plysq"), [128, S], BF16, scope=sc)
    murow = p.sb(un("plmu"), [1, S], F32, scope=sc)
    s2row = p.sb(un("pls2"), [1, S], F32, scope=sc)
    Abcs = p.sb(un("plA"), [128, S], F32, scope=sc)
    Bbcs = p.sb(un("plB"), [128, S], F32, scope=sc)
    xc = p.sb(un("plxc"), [128, 2, S], F32, scope=sc)
    xoc = p.sb(un("plxoc"), [128, 2, S], F32, scope=sc)
    tf = p.sb(un("pltf"), [128, S], F32, scope=sc)

    p.wait("sync", mm, mm.v)
    p.wait("sync", dv, dv.v)
    p.wait("sync", ev, ev.v)
    with nc.psum_tensor(un("plps1"), [1, S], F32) as ps1, \
         nc.psum_tensor(un("plps2"), [1, S], F32) as ps2:
        p.wait("vector", ev, ev_y)
        Mc = [0] * NCH
        for c in range(NCH):
            if c >= 1:
                p.wait("vector", mm, Mc[c - 1])
            vc = p.vec("tensor_copy", ybf[:], ysb[:, c, :], inc=dv)
            p.wait("scalar", dv, vc)
            if c >= 1:
                p.wait("scalar", mm, Mc[c - 1])
            ec = p.act(ysq[:], ybf[:], AF.Square, inc=ev)
            p.wait("tensor", ev, ec)
            for n in range(4):
                p.mm(ps1[:, 512 * n:512 * (n + 1)], common["ones_colbf"][:],
                     ybf[:, 512 * n:512 * (n + 1)], start=(c == 0), stop=(c == NCH - 1))
                p.mm(ps2[:, 512 * n:512 * (n + 1)], common["ones_colbf"][:],
                     ysq[:, 512 * n:512 * (n + 1)], start=(c == 0), stop=(c == NCH - 1),
                     inc=mm if n == 3 else None)
            Mc[c] = mm.v
        p.wait("scalar", mm, Mc[NCH - 1])
        p.act(murow[:], ps1[:], AF.Copy, scale=1.0 / D)
        e2 = p.act(s2row[:], ps2[:], AF.Copy, scale=1.0 / D, inc=ev)
    with nc.psum_tensor(un("plpbA"), [128, S], F32) as pbA, \
         nc.psum_tensor(un("plpbB"), [128, S], F32) as pbB:
        p.wait("tensor", ev, e2)
        for n in range(4):
            p.mm(pbA[:, 512 * n:512 * (n + 1)], common["ones_rowf"][:],
                 murow[:, 512 * n:512 * (n + 1)], start=True, stop=True)
            mv = p.mm(pbB[:, 512 * n:512 * (n + 1)], common["ones_rowf"][:],
                      s2row[:, 512 * n:512 * (n + 1)], start=True, stop=True,
                      inc=mm if n == 3 else None)
        p.wait("scalar", mm, mv)
        p.act(Abcs[:], pbA[:], AF.Copy)
        e2b = p.act(Bbcs[:], pbB[:], AF.Copy, inc=ev)
        p.wait("vector", ev, e2b)
        p.vec("tensor_mul", tf[:], Abcs[:], Abcs[:])
        p.vec("scalar_tensor_tensor", tf[:], tf[:], -1.0, Bbcs[:], ALU.mult, ALU.add)
        d1 = p.vec("tensor_scalar", tf[:], tf[:], LN_EPS, None, ALU.add, inc=dv)
        p.wait("scalar", dv, d1)
        p.act(tf[:], tf[:], AF.Ln)
        e3 = p.act(tf[:], tf[:], AF.Exp, scale=-0.5, inc=ev)
        p.wait("vector", ev, e3)
        p.vec("scalar_tensor_tensor", Bbcs[:], Abcs[:], -1.0, tf[:],
              ALU.mult, ALU.mult)
        ev2 = p.vec("tensor_copy", Abcs[:], tf[:], inc=dv)
    outs = [0, 0]
    lxv = [lx[0].v, lx[1].v]
    for c in range(NCH):
        slot = c % 2
        if c >= 2:
            p.wait("sync", st[slot], outs[slot])
        ldv = p.dma("sync", xc[:, slot, :], xdram[:, c, :], lx[slot])
        p.wait("vector", lx[slot], ldv)
        p.vec("tensor_mul", tf[:], ysb[:, c, :], Abcs[:])
        p.vec("tensor_add", tf[:], tf[:], Bbcs[:])
        dvv = p.vec("tensor_add", xoc[:, slot, :], tf[:], xc[:, slot, :], inc=dv)
        p.wait("sync", dv, dvv)
        outs[slot] = p.dma("sync", xodram[:, c, :], xoc[:, slot, :], st[slot])
    sc.close()
    return None



def _finish_dummy(p, xodram, st):
    zb = p.sb(un("zdum"), [128, S], F32)
    p.wait("vector", st[0], st[0].v)
    p.wait("vector", st[1], st[1].v)
    p.wait("sync", p.ec["vector"], p.vec("memset", zb[:], 0.0))
    for c in range(NCH):
        p.dma("sync", xodram[:, c, :], zb[:], st[c % 2])
    p.wait("sync", st[0], st[0].v)
    p.wait("sync", st[1], st[1].v)

def build_l0_attn(nc, upto=7):
    p = Prog(nc)
    xdram = nc.dram_tensor("x", [128, NCH, S], F32, kind="ExternalInput").ap()
    xodram = nc.dram_tensor("xo", [128, NCH, S], F32, kind="ExternalOutput").ap()
    emit_l0_attn(p, xdram, xodram, upto)
    p.run()


def emit_l0_attn(p, xdram, xodram, upto=7):
    dbg = upto if upto in (29, 30, 31, 32, 33) else None
    nc = p.nc
    wq_d = nc.dram_tensor("wqc", [NCH, 128, D], F32, kind="ExternalInput").ap()
    wk_d = nc.dram_tensor("wkc", [NCH, 128, D], F32, kind="ExternalInput").ap()
    wv_d = nc.dram_tensor("wvc", [NCH, 128, D], F32, kind="ExternalInput").ap()
    wo_d = nc.dram_tensor("woc", [NCH, 128, D], F32, kind="ExternalInput").ap()
    bcols_d = nc.dram_tensor("bcols", [128, 4 * NCH], F32, kind="ExternalInput").ap()
    Bfirst_d = nc.dram_tensor("Bfirst", [128, 2048], F32, kind="ExternalInput").ap()
    Blast_d = nc.dram_tensor("Blast", [128, 2048], F32, kind="ExternalInput").ap()
    B0T_d = nc.dram_tensor("B0T", [128, 256], F32, kind="ExternalInput").ap()
    B0sh_d = nc.dram_tensor("B0sh", [128, 272], F32, kind="ExternalInput").ap()
    eye_d = nc.dram_tensor("eye", [128, 128], F32, kind="ExternalInput").ap()
    e0_dram = nc.dram_tensor("e0bounce", [16, S], BF16,
                             kind="ExternalOutput" if upto == 5 else "Internal").ap()
    oT_dram = nc.dram_tensor("oTd", [128, NCH, S], BF16,
                             kind="ExternalOutput" if upto == 5 else "Internal").ap()
    dbgz = nc.dram_tensor("dbgz", [16, 2, 1024], F32,
                          kind="ExternalOutput" if upto == 5 else "Internal").ap() if upto == 5 else None
    dbgv = nc.dram_tensor("dbgv", [128, NCH * 17 * 128], BF16,
                          kind="ExternalOutput").ap() if upto == 5 else None
    dbgr = nc.dram_tensor("dbgr", [2, 1024], BF16,
                          kind="ExternalOutput").ap() if upto == 5 else None
    v0_dram = nc.dram_tensor("v0bounce", [NCH, 128], BF16,
                             kind="ExternalOutput" if upto == 5 else "Internal").ap()
    vm_dram = nc.dram_tensor("vmbounce", [NCH, 128], F32,
                             kind="ExternalOutput" if upto == 5 else "Internal").ap()

    ldc = p.sem("ldc")
    ldg = p.sem("ldg")
    lx = [p.sem("lxA"), p.sem("lxB")]
    wl = [p.sem("wlA"), p.sem("wlB")]
    e0s = p.sem("e0s")
    g0 = p.sem("g0")
    oTs = p.sem("oTs")
    mm = p.sem("mm")
    ev = p.ec["scalar"]
    dv = p.ec["vector"]
    st = [p.sem("stA"), p.sem("stB")]

    common = make_common(p)
    eye = p.sb(un("eyebf"), [128, 128], BF16)
    ld_eye = p.dma("gpsimd", eye[:], eye_d, ldg)
    eyef = p.sb(un("eyef"), [128, 128], F32)
    p.dma("sync", eyef[:], eye_d, ldc)
    bcols = p.sb(un("bcolsb"), [128, 4 * NCH], F32)
    ld_const = p.dma("sync", bcols[:], bcols_d, ldc)

    kT = p.sb(un("kT"), [128, NCH, 2176], BF16)
    wscope = ExitStack()
    wbuf = p.sb(un("wbuf"), [128, NCH, D], BF16, scope=wscope, side="right")

    def load_w(wd):
        for c in range(NCH):
            v = p.dma("gpsimd", wbuf[:, c, :], wd[c], wl[0])
        return v

    wk_ld = load_w(wk_d)

    lnscope = ExitStack()
    lnq = p.sb(un("lnq"), [128, NCH, S], BF16, scope=lnscope, side="right")

    emit_ln_featmajor(p, xdram, lnq, common, lx, mm, ev, dv)
    if upto < 2:
        lnscope.close(); wscope.close()
        _finish_dummy(p, xodram, st)
        return p

    p.wait("vector", ev, ev.v)
    pad_dv = p.vec("memset", kT[:, :, 0:128], 0.0)

    bq = lambda m: bcols[:, 0 * NCH + m:0 * NCH + m + 1]
    bk = lambda m: bcols[:, 1 * NCH + m:1 * NCH + m + 1]
    bv = lambda m: bcols[:, 2 * NCH + m:2 * NCH + m + 1]
    bo = lambda m: bcols[:, 3 * NCH + m:3 * NCH + m + 1]
    p.wait("tensor", dv, dv.v)
    p.wait("tensor", ldc, ld_const)

    def ev_k(m, ps):
        return p.act(kT[:, m, 128:2176], ps, AF.Identity, bias=bk(m))

    emit_proj(p, wbuf, wk_ld, wl[0], lnq, ev_k, NCH, mm, ev)
    p.wait("gpsimd", mm, mm.v)
    wv_ld = load_w(wv_d)
    if upto < 3:
        lnscope.close(); wscope.close()
        _finish_dummy(p, xodram, st)
        return p

    p3p = ExitStack()
    vsh = p.sb(un("vsh"), [128, NCH, 17, 128], BF16, scope=p3p)
    vmcol = p.sb(un("vmcol"), [128, NCH], F32, scope=p3p)
    v0c = p.sb(un("v0c"), [128, NCH], BF16, scope=p3p)
    v0p = p.sb(un("v0p"), [1, 1024], BF16, scope=p3p)
    vmr = p.sb(un("vmr"), [1, 1024], BF16, scope=p3p)
    ov = p.sb(un("ov"), [64, 16], BF16, scope=p3p)
    p3v = ExitStack()
    vtmp = p.sb(un("vtmp"), [128, 2176], BF16, scope=p3v)
    vstage = p.sb(un("vstage"), [NCH, 128], BF16, scope=p3v)
    vmstage = p.sb(un("vmstage"), [NCH, 128], F32, scope=p3v)
    vmrraw = p.sb(un("vmrraw"), [1, 1024], F32, scope=p3v)
    p.wait("vector", ev, ev.v)
    p.vec("memset", vtmp[:, 0:128], 0.0)
    p.vec("memset", vsh[:, :, 16, :], 0.0)

    with nc.psum_tensor(un("vpps"), [128, S], F32) as vpps, \
         nc.psum_tensor(un("vtpsA"), [128, 128], BF16) as vtpsA, \
         nc.psum_tensor(un("vtpsB"), [128, 128], BF16) as vtpsB:
        vtps = [vtpsA, vtpsB]
        p.wait("tensor", wl[0], wv_ld)
        p.wait("tensor", ldg, ld_eye)
        p.wait("tensor", ev, ev.v)
        p.wait("tensor", dv, dv.v)
        trs = [0, 0]
        e_vt = 0
        dv_v0 = 0
        tr_mm = 0
        for hp in range(NCH):
            p.wait("tensor", ev, e_vt)
            for c in range(NCH):
                for n in range(4):
                    p.mm(vpps[:, 512 * n:512 * (n + 1)],
                         wbuf[:, c, 128 * hp:128 * (hp + 1)],
                         lnq[:, c, 512 * n:512 * (n + 1)],
                         start=(c == 0), stop=(c == NCH - 1),
                         inc=mm if (c == NCH - 1 and n == 3) else None)
            p.wait("scalar", mm, mm.v)
            if hp >= 1:
                p.wait("scalar", dv, dv_v0)
                p.wait("scalar", mm, tr_mm)
            e_vt = p.act(vtmp[:, 128:2176], vpps[:], AF.Identity, bias=bv(hp))
            if dbg == 29:
                dv_v0 = dv.v
                tr_mm = mm.v
                continue
            p.wait("vector", ev, e_vt)
            p.vec("tensor_reduce", vmcol[:, hp:hp + 1].unsqueeze(2),
                  vtmp[:, 128:2176].unsqueeze(1), axis=AX.X, op=ALU.add)
            dv_v0 = p.vec("tensor_copy", v0c[:, hp:hp + 1], vtmp[:, 128:129])
            if dbg == 30:
                tr_mm = mm.v
                continue
            p.wait("tensor", ev, e_vt)
            for j in range(17):
                g = j % 2
                p.wait("tensor", ev, trs[g])
                if j < 16:
                    tr_mm = p.t_transpose(vtps[g][:, :],
                                          vtmp[:, 126 + 128 * j:254 + 128 * j],
                                          eye[:], inc=mm)
                else:
                    tr_mm = p.t_transpose(vtps[g][0:2, :], vtmp[:, 2174:2176],
                                          eye[:], inc=mm)
                p.wait("scalar", mm, tr_mm)
                rows = 128 if j < 16 else 2
                p.act(vsh[0:rows, hp, j, 0:64], vtps[g][0:rows, 0:64], AF.Copy)
                trs[g] = p.act(vsh[0:rows, hp, j, 64:128], vtps[g][0:rows, 64:128],
                               AF.Copy)
        p.wait("tensor", dv, dv_v0)
        if dbg is None or dbg >= 32:
            p.wait("tensor", ev, trs[0])
            p.wait("tensor", ev, trs[1])
            t1 = p.t_transpose(vtps[0][0:NCH, :], v0c[:], eye[:], inc=mm)
            p.wait("scalar", mm, t1)
            e1 = p.act(vstage[:], vtps[0][0:NCH, :], AF.Copy, inc=ev)
            p.wait("gpsimd", ev, e1)
            b1 = p.dma("gpsimd", v0_dram, vstage[:], g0)
    if dbg is None or dbg >= 33:
        with nc.psum_tensor(un("vmtps"), [128, 128], F32) as vmtps:
            p.wait("tensor", ev, ev.v)
            t2 = p.mm(vmtps[0:NCH, :], vmcol[:], eyef[:], start=True, stop=True, inc=mm)
            p.wait("scalar", mm, t2)
            e2 = p.act(vmstage[:], vmtps[0:NCH, :], AF.Copy, inc=ev)
            p.wait("gpsimd", ev, e2)
            b2 = p.dma("gpsimd", vm_dram, vmstage[:], g0)
            p.wait("gpsimd", g0, g0.v)
            b3 = p.dma("gpsimd", v0p[:], v0_dram.rearrange("c w -> (c w)").unsqueeze(0), g0)
            b4 = p.dma("gpsimd", vmrraw[:], vm_dram.rearrange("c w -> (c w)").unsqueeze(0), g0)
            p.wait("scalar", g0, g0.v)
            e3 = p.act(vmr[:], vmrraw[:], AF.Copy, scale=1e-30 / S, inc=ev)
    if upto in (29, 30, 31, 32, 33):
        p3v.close(); p3p.close(); lnscope.close(); wscope.close()
        _finish_dummy(p, xodram, st)
        return p
    p3v.close()
    qkscope = ExitStack()
    qTe = p.sb(un("qTe"), [128, NCH, S], BF16, scope=qkscope)
    qTo = p.sb(un("qTo"), [128, NCH, S], BF16, scope=qkscope)
    p.wait("gpsimd", mm, mm.v)
    wq_ld = load_w(wq_d)
    p.wait("vector", ev, ev.v)
    p.vec("memset", qTe[64:128, :, :], 0.0)
    qz_dv = p.vec("memset", qTo[0:64, :, :], 0.0)
    p.wait("tensor", dv, qz_dv)

    def ev_q(m, ps):
        p.act(qTe[0:64, m, :], ps[0:64, :], AF.Identity, bias=bq(m)[0:64, :])
        return p.act(qTo[64:128, m, :], ps[64:128, :], AF.Identity, bias=bq(m)[64:128, :])

    emit_proj(p, wbuf, wq_ld, wl[0], lnq, ev_q, NCH, mm, ev)
    lnscope.close()
    wscope.close()
    if upto < 4:
        qkscope.close(); p3p.close()
        _finish_dummy(p, xodram, st)
        return p

    p3e = ExitStack()
    B0T = p.sb(un("B0Tb"), [128, 256], F32, scope=p3e)
    B0sh = p.sb(un("B0shb"), [128, 272], F32, scope=p3e)
    k0m = p.sb(un("k0m"), [128, NCH, 16], BF16, scope=p3e)
    q0m = p.sb(un("q0m"), [128, NCH, 16], BF16, scope=p3e)
    e0rows = p.sb(un("e0rows"), [16, S], BF16, scope=p3e)
    e0T = p.sb(un("e0T"), [128, 256], BF16, scope=p3e)
    er0 = p.sb(un("er0"), [128, 17 * 16], BF16, scope=p3e)
    c0sb = p.sb(un("c0sb"), [1, 1024], F32, scope=p3e)
    z0sb = p.sb(un("z0sb"), [1, 16], F32, scope=p3e)
    rz0 = p.sb(un("rz0"), [1, 16], F32, scope=p3e)
    e0sg = p.sb(un("e0sg"), [128, 272], F32, scope=p3e)
    e0sg2 = p.sb(un("e0sg2"), [128, 272], F32, scope=p3e)

    p.wait("sync", mm, mm.v)
    p.wait("sync", ev, ev.v)
    p.dma("sync", B0T[:], B0T_d, ldc)
    ld_tabs = p.dma("sync", B0sh[:], B0sh_d, ldc)
    p.wait("vector", ev, ev.v)
    p.vec("memset", k0m[:], 0.0)
    p.vec("memset", q0m[:], 0.0)
    for c in range(NCH):
        p.vec("tensor_copy", k0m[0:64, c, 2 * c:2 * c + 1], kT[0:64, c, 128:129])
        p.vec("tensor_copy", k0m[64:128, c, 2 * c + 1:2 * c + 2], kT[64:128, c, 128:129])
        p.vec("tensor_copy", q0m[0:64, c, 2 * c:2 * c + 1], qTe[0:64, c, 0:1])
        p.vec("tensor_copy", q0m[64:128, c, 2 * c + 1:2 * c + 2], qTo[64:128, c, 0:1])
    dv_pre = p.vec("memset", er0[:], 0.0)

    with nc.psum_tensor(un("s0ps"), [128, 256], F32) as s0ps, \
         nc.psum_tensor(un("r0ps"), [128, 272], F32) as r0ps, \
         nc.psum_tensor(un("tpsA"), [16, 128], BF16) as tpsA, \
         nc.psum_tensor(un("tpsB"), [16, 128], BF16) as tpsB:
        tps = [tpsA, tpsB]
        p.wait("tensor", dv, dv_pre)
        p.wait("tensor", ev, ev.v)
        p.wait("tensor", ldc, ld_tabs)
        for qt in range(16):
            for c in range(NCH):
                p.mm(s0ps[:, 16 * qt:16 * (qt + 1)], qTe[:, c, 128 * qt:128 * (qt + 1)],
                     k0m[:, c, :], start=(c == 0), stop=False)
                p.mm(s0ps[:, 16 * qt:16 * (qt + 1)], qTo[:, c, 128 * qt:128 * (qt + 1)],
                     k0m[:, c, :], start=False, stop=(c == NCH - 1),
                     inc=mm if (qt == 15 and c == NCH - 1) else None)
        for jt in range(17):
            off = 126 + 128 * jt
            w = 128 if jt < 16 else 2
            if w < 128:
                p.mm(r0ps[:, 16 * jt:16 * (jt + 1)], kT[:, 0, 0:128],
                     q0m[:, 0, :], start=True, stop=False)
            for c in range(NCH):
                p.mm(r0ps[0:w, 16 * jt:16 * (jt + 1)], kT[:, c, off:off + w],
                     q0m[:, c, :], start=(c == 0 and w == 128), stop=(c == NCH - 1 and w == 128),
                     inc=mm if (jt == 16 and c == NCH - 1 and w == 128) else None)
            if w < 128:
                p.mm(r0ps[:, 16 * jt:16 * (jt + 1)], kT[:, 0, 0:128],
                     q0m[:, 0, :], start=False, stop=True,
                     inc=mm if jt == 16 else None)
        p.wait("vector", mm, mm.v)
        p.wait("vector", ev, ev.v)
        p.vec("scalar_tensor_tensor", e0sg[:, 0:256], s0ps[:], 0.125, B0T[:],
              ALU.mult, ALU.add)
        d1 = p.vec("scalar_tensor_tensor", e0sg2[:, 0:256], e0sg[:, 0:256], GATEK,
                   e0sg[:, 0:256], ALU.mult, ALU.min)
        p.wait("scalar", dv, d1)
        e1 = p.act(e0T[:], e0sg2[:, 0:256], AF.Exp)
        p.wait("vector", ev, e1)
        p.vec("scalar_tensor_tensor", e0sg[:], r0ps[:], 0.125, B0sh[:],
              ALU.mult, ALU.add)
        d2 = p.vec("scalar_tensor_tensor", e0sg2[:], e0sg[:], GATEK,
                   e0sg[:], ALU.mult, ALU.min)
        p.wait("scalar", dv, d2)
        e2r = p.act(er0[:], e0sg2[:], AF.Exp)
        tevs = [0, 0]
        for qt in range(16):
            g = qt % 2
            p.wait("tensor", ev, max(e1, tevs[g]))
            tq = p.t_transpose(tps[g][:, :], e0T[:, 16 * qt:16 * (qt + 1)], eye[:], inc=mm)
            p.wait("scalar", mm, tq)
            tevs[g] = p.act(e0rows[:, 128 * qt:128 * (qt + 1)], tps[g][:, :], AF.Copy)
        p.wait("gpsimd", ev, max(tevs))
        g_e0 = p.dma("gpsimd", e0_dram, e0rows[:], g0)

    with nc.psum_tensor(un("c0ps"), [1, 1024], F32) as c0ps, \
         nc.psum_tensor(un("z0ps"), [1, 16], F32) as z0ps:
        p.wait("tensor", ev, ev.v)
        p.wait("tensor", dv, dv.v)
        for h in range(H):
            hp = h // 2
            o64 = 64 * (h % 2)
            for jt in range(17):
                p.mm(c0ps[:, 64 * h:64 * h + 64], er0[:, 16 * jt + h:16 * jt + h + 1],
                     vsh[:, hp, jt, o64:o64 + 64], start=(jt == 0), stop=False)
                p.mm(z0ps[:, h:h + 1], er0[:, 16 * jt + h:16 * jt + h + 1],
                     common["ones_colbf"][:], start=(jt == 0), stop=(jt == 16))
            mz = p.mm(c0ps[:, 64 * h:64 * h + 64], common["one1bf"][:],
                      vmr[:, 64 * h:64 * h + 64], start=False, stop=True,
                      inc=mm if h == H - 1 else None)
        p.wait("scalar", mm, mz)
        p.act(c0sb[:], c0ps[:], AF.Copy)
        ez0 = p.act(z0sb[:], z0ps[:], AF.Copy)
    p.wait("vector", ev, ez0)
    d3 = p.vec("tensor_scalar", rz0[:], z0sb[:], 1e-30, None, ALU.add)
    p.wait("scalar", dv, d3)
    p.act(z0sb[:], rz0[:], AF.Ln)
    erz = p.act(rz0[:], z0sb[:], AF.Exp, scale=-1.0)
    with nc.psum_tensor(un("ovps"), [64, 16], F32) as ovps:
        p.wait("tensor", ev, erz)
        for h in range(H):
            mv = p.mm(ovps[:, h:h + 1], c0sb[:, 64 * h:64 * h + 64], rz0[:, h:h + 1],
                      start=True, stop=True, inc=mm if h == H - 1 else None)
        p.wait("scalar", mm, mv)
        e_ov = p.act(ov[:], ovps[:], AF.Copy)
    p3e.close()
    if upto < 5:
        qkscope.close(); p3p.close()
        _finish_dummy(p, xodram, st)
        return p

    p3b = ExitStack()
    Bfirst = p.sb(un("Bfirstb"), [128, 2048], F32, scope=p3b)
    Blast = p.sb(un("Blastb"), [128, 2048], F32, scope=p3b)
    e0hb = p.sb(un("e0hb"), [1, S], BF16, scope=p3b)
    p.wait("sync", mm, mm.v)
    p.wait("sync", ev, ev.v)
    p.wait("sync", dv, dv.v)
    p.wait("gpsimd", mm, mm.v)
    p.wait("gpsimd", ev, ev.v)
    p.wait("gpsimd", g0, g0.v)
    p.dma("sync", Bfirst[:], Bfirst_d, ldc)
    ld_tabs2a = p.dma("sync", Blast[:], Blast_d, ldc)

    p4 = ExitStack()
    zrow = p.sb(un("zrow"), [1, 1024], F32, scope=p4)
    Rsb = p.sb(un("Rsb"), [64, 1024], F32, scope=p4)
    oTtmp = p.sb(un("oTtmp"), [64, S], BF16, scope=p4)
    Pb = p.sb(un("Pb"), [128, 2, 1024], BF16, scope=p4)
    sgb = p.sb(un("sgb"), [128, 512], F32, scope=p4)
    agb = p.sb(un("agb"), [128, 512], F32, scope=p4)

    with nc.psum_tensor(un("Sps"), [128, 1024], F32) as Sps, \
         nc.psum_tensor(un("ctxps"), [64, 1024], F32) as ctxps, \
         nc.psum_tensor(un("zps"), [1, 1024], F32) as zps, \
         nc.psum_tensor(un("Rps"), [64, 1024], F32) as Rps:
        p.wait("tensor", ev, e_ov)
        p.wait("tensor", dv, dv.v)
        p.wait("vector", ldc, ld_tabs2a)
        p.wait("gpsimd", g0, g_e0)
        e0ld = p.dma("gpsimd", e0hb[:], e0_dram[0:1, :], e0s)
        last_S_dv = 0
        last_exp = [0, 0]
        pv_done = 0
        norm_dv = 0
        rbc_ev = 0
        oT_dma = []
        for h in range(H):
            hp, o64 = h // 2, 64 * (h % 2)
            qT_h = qTe if h % 2 == 0 else qTo
            p.wait("tensor", e0s, e0ld)
            for half in range(2):
                for g2 in range(2):
                    g = 2 * half + g2
                    Btab = Bfirst if g < 2 else Blast
                    p.wait("tensor", dv, last_S_dv)
                    for ti in range(4):
                        t = 4 * g + ti
                        off = 126 + 128 * t
                        p.mm(Sps[:, 256 * ti:256 * ti + 128],
                             kT[:, hp, off:off + 128],
                             qT_h[:, hp, 128 * t:128 * (t + 1)],
                             start=True, stop=True)
                        if t < 15:
                            p.mm(Sps[:, 256 * ti + 128:256 * (ti + 1)],
                                 kT[:, hp, off + 128:off + 256],
                                 qT_h[:, hp, 128 * t:128 * (t + 1)],
                                 start=True, stop=True,
                                 inc=mm if ti == 3 else None)
                        else:
                            p.mm(Sps[:, 256 * ti + 128:256 * (ti + 1)],
                                 kT[:, hp, 0:128],
                                 qT_h[:, hp, 128 * t:128 * (t + 1)],
                                 start=True, stop=False)
                            p.mm(Sps[0:2, 256 * ti + 128:256 * (ti + 1)],
                                 kT[:, hp, off + 128:off + 130],
                                 qT_h[:, hp, 128 * t:128 * (t + 1)],
                                 start=False, stop=False)
                            p.mm(Sps[:, 256 * ti + 128:256 * (ti + 1)],
                                 kT[:, hp, 0:128],
                                 qT_h[:, hp, 128 * t:128 * (t + 1)],
                                 start=False, stop=True,
                                 inc=mm if ti == 3 else None)
                    Mg = mm.v
                    p.wait("vector", mm, Mg)
                    p.wait("vector", ev, max(last_exp))
                    p.wait("scalar", mm, pv_done)
                    for n2 in range(2):
                        Bsl = Btab[:, 256 * ((4 * g) % 8) + 512 * n2:
                                   256 * ((4 * g) % 8) + 512 * (n2 + 1)]
                        p.wait("vector", ev, max(last_exp))
                        p.vec("scalar_tensor_tensor", sgb[:], Sps[:, 512 * n2:512 * (n2 + 1)],
                              0.125, Bsl, ALU.mult, ALU.add)
                        dgg = p.vec("scalar_tensor_tensor", agb[:], sgb[:], GATEK, sgb[:],
                                    ALU.mult, ALU.min)
                        p.wait("scalar", dv, dgg)
                        last_exp[g % 2] = p.act(Pb[:, g % 2, 512 * n2:512 * (n2 + 1)],
                                                agb[:], AF.Exp)
                    last_S_dv = dgg
                p.wait("tensor", ev, last_exp[0])
                p.wait("tensor", ev, last_exp[1])
                if pv_done:
                    p.wait("tensor", dv, norm_dv)
                for g2 in range(2):
                    g = 2 * half + g2
                    for ti in range(4):
                        t = 4 * g + ti
                        cc = 128 * (t % 8)
                        p.mm(ctxps[:, cc:cc + 128], vsh[:, hp, t, o64:o64 + 64],
                             Pb[:, g % 2, 256 * ti:256 * ti + 128], start=True, stop=False)
                        p.mm(ctxps[:, cc:cc + 128], vsh[:, hp, t + 1, o64:o64 + 64],
                             Pb[:, g % 2, 256 * ti + 128:256 * (ti + 1)],
                             start=False, stop=False)
                        p.mm(ctxps[:, cc:cc + 128], v0p[:, 64 * h:64 * h + 64],
                             e0hb[:, 128 * t:128 * (t + 1)], start=False, stop=False)
                        p.mm(ctxps[:, cc:cc + 128], vmr[:, 64 * h:64 * h + 64],
                             common["ones_rowbf"][:], start=False, stop=True)
                        p.mm(zps[:, cc:cc + 128], common["ones_colbf"][:],
                             Pb[:, g % 2, 256 * ti:256 * ti + 128], start=True, stop=False)
                        p.mm(zps[:, cc:cc + 128], common["ones_colbf"][:],
                             Pb[:, g % 2, 256 * ti + 128:256 * (ti + 1)],
                             start=False, stop=False)
                        p.mm(zps[:, cc:cc + 128], common["one1bf"][:],
                             e0hb[:, 128 * t:128 * (t + 1)], start=False, stop=True,
                             inc=mm if (g2 == 1 and ti == 3) else None)
                pv_done = mm.v
                p.wait("scalar", mm, pv_done)
                ez = p.act(zrow[:], zps[:], AF.Identity)
                p.wait("vector", ev, ez)
                dz = p.vec("tensor_scalar", zrow[:], zrow[:], 1e-30, None, ALU.add)
                p.wait("scalar", dv, dz)
                p.act(zrow[:], zrow[:], AF.Ln)
                ez2 = p.act(zrow[:], zrow[:], AF.Exp, scale=-1.0)
                if dbgz is not None:
                    p.wait("sync", ev, ez2)
                    p.dma("sync", dbgz[h:h + 1, half, :].rearrange("a b -> a b") if False else dbgz[h, half, :].unsqueeze(0), zrow[:], st[0])
                p.wait("tensor", ev, ez2)
                p.wait("tensor", ev, rbc_ev)
                for n in range(2):
                    mv = p.mm(Rps[:, 512 * n:512 * (n + 1)], common["ones_rowf"][0:1, 0:64],
                              zrow[:, 512 * n:512 * (n + 1)], start=True, stop=True,
                              inc=mm if n == 1 else None)
                p.wait("scalar", mm, mv)
                rbc_ev = p.act(Rsb[:], Rps[:], AF.Copy)
                if oT_dma and half == 0:
                    p.wait("vector", oTs, oT_dma[-1])
                p.wait("vector", ev, rbc_ev)
                norm_dv = p.vec("tensor_mul", oTtmp[:, 1024 * half:1024 * (half + 1)],
                                ctxps[:], Rsb[:])
            p.wait("vector", ev, e_ov)
            norm_dv = p.vec("tensor_copy", oTtmp[:, 0:1], ov[:, h:h + 1])
            p.wait("gpsimd", dv, norm_dv)
            o64h = 64 * (h % 2)
            oT_dma.append(p.dma("gpsimd", oT_dram[o64h:o64h + 64, hp, :], oTtmp[:], oTs))
            if h + 1 < H:
                p.wait("gpsimd", mm, pv_done)
                e0ld = p.dma("gpsimd", e0hb[:], e0_dram[h + 1:h + 2, :], e0s)
    if dbgv is not None:
        p.wait("sync", dv, dv.v)
        p.dma("sync", dbgv, vsh.ap().rearrange("p a b c -> p (a b c)"), st[0])
        p.dma("sync", dbgr[0:1, :], v0p[:], st[0])
        p.dma("sync", dbgr[1:2, :], vmr[:], st[0])
    p4.close()
    p3b.close()
    qkscope.close()
    p3p.close()
    if upto < 6:
        _finish_dummy(p, xodram, st)
        return p

    p5 = ExitStack()
    wbuf2 = p.sb(un("wbuf2"), [128, NCH, D], BF16, scope=p5, side="right")
    y = p.sb(un("yattn"), [128, NCH, S], F32)
    ostr = p.sb(un("ostr"), [128, 2, S], BF16, scope=p5)
    osm = [p.sem("osA"), p.sem("osB")]
    p.wait("gpsimd", mm, mm.v)
    p.wait("gpsimd", dv, dv.v)
    p.wait("gpsimd", ev, ev.v)
    for c in range(NCH):
        wo_ld = p.dma("gpsimd", wbuf2[:, c, :], wo_d[c], wl[1])
    p.wait("sync", oTs, oTs.v)
    ev_yv = None
    with nc.psum_tensor(un("prps5"), [128, 2, S], F32) as prps:
        p.wait("tensor", wl[1], wo_ld)
        p.wait("tensor", ev, ev.v)
        seq = [(m, c) for m in range(NCH) for c in range(NCH)]
        loads, cons = {}, {}
        evs = []
        for i in range(2):
            loads[i] = p.dma("sync", ostr[:, i % 2, :], oT_dram[:, seq[i][1], :], osm[i % 2])
        for i, (m, c) in enumerate(seq):
            g = m % 2
            slot = i % 2
            if c == 0 and m >= 2:
                p.wait("tensor", ev, evs[m - 2])
            p.wait("tensor", osm[slot], loads[i])
            for n in range(4):
                p.mm(prps[:, g, 512 * n:512 * (n + 1)],
                     wbuf2[:, c, 128 * m:128 * (m + 1)],
                     ostr[:, slot, 512 * n:512 * (n + 1)],
                     start=(c == 0), stop=(c == NCH - 1),
                     inc=mm if n == 3 else None)
            cons[i] = mm.v
            j = i + 2
            if j < len(seq):
                p.wait("sync", mm, cons[i])
                loads[j] = p.dma("sync", ostr[:, j % 2, :], oT_dram[:, seq[j][1], :], osm[j % 2])
            if c == NCH - 1:
                p.wait("scalar", mm, mm.v)
                evs.append(p.act(y[:, m, :], prps[:, g, :], AF.Identity, bias=bo(m)))
        ev_yv = evs[-1]
    p5.close()
    if upto < 7:
        _finish_dummy(p, xodram, st)
        return p

    emit_postln_residual(p, y, ev_yv, xdram, xodram, common, lx, mm, ev, dv, st)
    p.wait("sync", st[0], st[0].v)
    p.wait("sync", st[1], st[1].v)
    return p


def build_l0_moe(nc, upto=7):
    p = Prog(nc)
    xdram = nc.dram_tensor("x", [128, NCH, S], F32, kind="ExternalInput").ap()
    xodram = nc.dram_tensor("xo", [128, NCH, S], F32, kind="ExternalOutput").ap()
    emit_l0_moe(p, xdram, xodram, upto)
    p.run()


def emit_l0_moe(p, xdram, xodram, upto=7):
    nc = p.nc
    gw_d = nc.dram_tensor("gwc", [NCH, 128, 8], F32, kind="ExternalInput").ap()
    W1_d = nc.dram_tensor("W1c", [9, NCH, 128, FF], F32, kind="ExternalInput").ap()
    W2_d = nc.dram_tensor("W2c", [9, 32, 128, D], F32, kind="ExternalInput").ap()
    b1_d = nc.dram_tensor("b1c", [128, 9, 32], F32, kind="ExternalInput").ap()
    bs2_d = nc.dram_tensor("bs2c", [128, 8], F32, kind="ExternalInput").ap()
    b2_d = nc.dram_tensor("b2r", [8, D], F32, kind="ExternalInput").ap()
    sel_d = nc.dram_tensor("selc", [8, 8 * 128], F32, kind="ExternalInput").ap()
    eye_d = nc.dram_tensor("eyeB", [128, 128], F32, kind="ExternalInput").ap()
    scd_t = nc.dram_tensor("scd", [8, S], F32, kind="Internal")
    gath_t = nc.dram_tensor("gath", [128, 1024], F32, kind="Internal")
    gdbg = nc.dram_tensor("gdbg", [8, S], F32,
                          kind="ExternalOutput").ap() if upto in (0, 1) else None
    ldbg = nc.dram_tensor("ldbg", [128, 128], F32,
                          kind="ExternalOutput").ap() if upto == 0 else None
    tdbg = nc.dram_tensor("tdbg", [1, 1], F32,
                          kind="ExternalOutput").ap() if upto in (0, 1) else None

    lx = [p.sem("lx0"), p.sem("lx1")]
    wl2 = [p.sem("wl20"), p.sem("wl21")]
    mm = p.sem("mm")
    st = [p.sem("st0"), p.sem("st1")]
    g0 = p.sem("g0")
    cc = p.sem("cc")
    ag = p.sem("ag")
    dv, ev = p.ec["vector"], p.ec["scalar"]

    common = make_common(p)
    ones_colf = p.sb(un("ones_colf"), [128, 1], F32)
    ones8cf = p.sb(un("ones8cf"), [8, 1], F32)
    ones_row8f = p.sb(un("ones_row8f"), [1, 8], F32)
    ones_rowW = p.sb(un("ones_rowW"), [1, 512], F32)
    p.vec("memset", ones_colf[:], 1.0)
    p.vec("memset", ones8cf[:], 1.0)
    p.vec("memset", ones_row8f[:], 1.0)
    p.vec("memset", ones_rowW[:], 1.0)

    sc_e = ExitStack()
    lnq = p.sb("lnq", [128, NCH, S], BF16, scope=sc_e)
    gwsb = p.sb("gwsb", [128, NCH, 8], F32, scope=sc_e)
    eyef = p.sb("eyef", [128, 128], F32, scope=sc_e)
    b1sb = p.sb("b1sb", [128, 9, 32], F32, scope=sc_e)
    bs2sb = p.sb("bs2sb", [128, 8], F32, scope=sc_e)
    b2sb = p.sb("b2sb", [8, D], BF16, scope=sc_e)
    selsb = p.sb("selsb", [8, 8 * 128], BF16, scope=sc_e)
    gatesT = p.sb("gatesT", [8, S], BF16, scope=sc_e)
    bs_lo = p.sb("bs_lo", [1, 1], F32, scope=sc_e)
    logits = p.sb("logitsb", [128, 128], F32, scope=sc_e)

    FB = 512
    NFB = FF // FB
    NFT = FB // 128
    steps = [(e, fb) for e in range(9) for fb in range(NFB)]
    NST = len(steps)
    sc_w = ExitStack()
    W1sb = p.sb("W1sb", [128, 2, NCH, FB], BF16, scope=sc_w, side="right")
    W2sb = p.sb("W2sb", [128, 2, NFT, 1024], BF16, scope=sc_w, side="right")

    def load_blk(slot, e, fb):
        for dc in range(NCH):
            p.dma("gpsimd", W1sb[:, slot, dc, :],
                  W1_d[e, dc, :, FB * fb:FB * (fb + 1)], wl2[slot])
        for fc in range(NFT):
            v = p.dma("gpsimd", W2sb[:, slot, fc, :],
                      W2_d[e, NFT * fb + fc], wl2[slot])
        return v

    wld = [0] * NST
    wld[0] = load_blk(0, 0, 0)
    for c in range(NCH):
        ld_gw = p.dma("sync", gwsb[:, c, :], gw_d[c], g0)
    p.dma("sync", eyef[:], eye_d, g0)
    p.dma("sync", b1sb[:], b1_d, g0)
    ld_small = p.dma("sync", bs2sb[:], bs2_d, g0)
    p.dma("gpsimd", b2sb[:], b2_d, wl2[0])
    ld_sel = p.dma("gpsimd", selsb[:], sel_d, wl2[0])
    wld[0] = ld_sel

    ps_a = ExitStack()
    LGref = [None, None, None]
    lg_evac = [0, None]

    def rg2(c, xslot, ldv):
        if LGref[0] is None:
            LGref[0] = ps_a.enter_context(nc.psum_tensor("LGps", [128, 128], F32))
            LGref[1] = ps_a.enter_context(nc.psum_tensor("RSps", [128, 128], F32))
            LGref[2] = ps_a.enter_context(nc.psum_tensor("G1ps", [1, 8], F32))
        LGps = LGref[0]
        p.wait("tensor", lx[c % 2], ldv)
        if c == 0:
            p.wait("tensor", g0, ld_small)
            p.wait("tensor", ev, ev.v)
            p.wait("tensor", dv, dv.v)
        else:
            p.wait("tensor", lg_evac[1], lg_evac[0])
        for ts in range(16):
            p.mm(LGps[:, 8 * ts:8 * (ts + 1)], xslot[:, 128 * ts:128 * (ts + 1)],
                 gwsb[:, c, :], start=True, stop=True,
                 inc=mm if ts == 15 else None)
        if c == 0:
            p.wait("scalar", mm, mm.v)
            lg_evac[0] = p.act(logits[:], LGps[:], AF.Copy)
            lg_evac[1] = ev
        else:
            p.wait("vector", mm, mm.v)
            lg_evac[0] = p.vec("tensor_add", logits[:], logits[:], LGps[:])
            lg_evac[1] = dv

    kp = ExitStack()
    murow, Abcs = emit_ln_featmajor(p, xdram, lnq, common, lx, mm, ev, dv,
                                    raw_gate2=rg2, keep=kp)
    v_lnq = dv.v
    yacc = p.sb("yacc", [128, NCH, S], F32, scope=sc_e)
    sc_g = ExitStack()
    esb = p.sb("esb", [128, 128], F32, scope=sc_g)
    scT = p.sb("scT", [8, S], F32, scope=sc_g)
    zrow = p.sb("zrow", [1, S], F32, scope=sc_g)
    taurow = p.sb("taurow", [1, S], F32, scope=sc_g)
    allsc = p.sb("allsc", [128, 1024], F32, scope=sc_g)
    ind8 = p.sb("ind8", [8, S], F32, scope=sc_g)
    shiftb = p.sb("shiftb", [128, 1024], F32, scope=sc_g)
    negmidcol = p.sb("negmidcol", [128, 1], F32, scope=sc_g)
    cntcol = p.sb("cntcol", [128, 1], F32, scope=sc_g)
    bs_hi = p.sb("bs_hi", [1, 1], F32, scope=sc_g)
    bs_mid = p.sb("bs_mid", [1, 1], F32, scope=sc_g)
    bs_nm = p.sb("bs_nm", [1, 1], F32, scope=sc_g)
    bs_cond = p.sb("bs_cond", [1, 1], F32, scope=sc_g)
    bs_d1 = p.sb("bs_d1", [1, 1], F32, scope=sc_g)
    bs_d2 = p.sb("bs_d2", [1, 1], F32, scope=sc_g)

    LGps, RSps, G1ps = LGref
    for c in range(NCH):
        p.mm(G1ps[:], ones_colf[:], gwsb[:, c, :], start=(c == 0), stop=(c == NCH - 1),
             inc=mm if c == NCH - 1 else None)
    p.wait("scalar", mm, mm.v)
    g1neg = p.sb("g1neg", [1, 8], F32, scope=sc_g)
    e_g1 = p.act(g1neg[:], G1ps[:], AF.Copy, scale=-1.0)
    p.wait("tensor", ev, e_g1)
    p.wait("tensor", lg_evac[1], lg_evac[0])
    for ts in range(16):
        p.mm(LGps[:, 8 * ts:8 * (ts + 1)], murow[:, 128 * ts:128 * (ts + 1)],
             g1neg[:], start=True, stop=True,
             inc=mm if ts == 15 else None)
        p.mm(RSps[:, 8 * ts:8 * (ts + 1)], Abcs[0:1, 128 * ts:128 * (ts + 1)],
             ones_row8f[:], start=True, stop=True,
             inc=mm if ts == 15 else None)
    p.wait("scalar", mm, mm.v)
    rstile = p.sb("rstile", [128, 128], F32, scope=sc_g)
    e_rs = p.act(rstile[:], RSps[:], AF.Copy)
    p.wait("vector", ev, e_rs)
    p.vec("tensor_add", logits[:], logits[:], LGps[:])
    v_lg = p.vec("tensor_mul", logits[:], logits[:], rstile[:])
    kp.close()

    p.wait("scalar", dv, v_lg)
    e_es = p.act(esb[:], logits[:], AF.Exp)
    ps_a.close()
    with nc.psum_tensor("TEpsA", [8, 128], F32) as TEpsA, \
         nc.psum_tensor("TEpsB", [8, 128], F32) as TEpsB, \
         nc.psum_tensor("Zps", [1, S], F32) as Zps:
        p.wait("tensor", ev, e_es)
        p.wait("tensor", g0, ld_small)
        te = [0, 0]
        for ts in range(16):
            tp = TEpsA if ts % 2 == 0 else TEpsB
            if te[ts % 2]:
                p.wait("tensor", ev, te[ts % 2])
            mv = p.t_transpose(tp[0:8, :], esb[:, 8 * ts:8 * (ts + 1)], eyef[:], inc=mm)
            p.wait("scalar", mm, mv)
            te[ts % 2] = p.act(scT[:, 128 * ts:128 * (ts + 1)], tp[0:8, :], AF.Copy)
        p.wait("tensor", ev, max(te))
        for n in range(4):
            mv = p.mm(Zps[:, 512 * n:512 * (n + 1)], ones8cf[:],
                      scT[:, 512 * n:512 * (n + 1)], start=True, stop=True,
                      inc=mm if n == 3 else None)
        p.wait("scalar", mm, mv)
        p.act(zrow[:], Zps[:], AF.Ln)
        e_rz = p.act(zrow[:], zrow[:], AF.Exp, scale=-1.0)
    with nc.psum_tensor("RZps", [8, S], F32) as RZps:
        p.wait("tensor", ev, e_rz)
        for n in range(4):
            mv = p.mm(RZps[:, 512 * n:512 * (n + 1)], ones_row8f[:],
                      zrow[:, 512 * n:512 * (n + 1)], start=True, stop=True,
                      inc=mm if n == 3 else None)
        p.wait("vector", mm, mv)
        v_sc = p.vec("tensor_mul", scT[:], scT[:], RZps[:])
    if upto == 0:
        p.vec("memset", bs_lo[:], 0.0)
        p.wait("sync", dv, v_sc)
        fsb = p.sb(un("gdbgf"), [8, S], F32, scope=sc_g)
        vv = p.vec("tensor_copy", fsb[:], scT[:])
        p.wait("sync", dv, vv)
        p.dma("sync", gdbg, fsb[:], st[0])
        p.dma("sync", tdbg, bs_lo[:], st[0])
        p.dma("sync", ldbg, logits[:], st[0])
        p.wait("sync", st[0], st[0].v)
        sc_g.close()
        sc_e.close()
        _finish_dummy(p, xodram, st)
        return
    p.wait("sync", dv, v_sc)
    v_out = p.dma("sync", scd_t.ap(), scT[:], ag)
    p.wait("gpsimd", ag, v_out)
    p.on("gpsimd", lambda e: e.collective_compute(
        "AllGather", ALU.bypass, replica_groups=[list(range(8))],
        ins=[scd_t.ap().opt()], outs=[gath_t.ap().opt()]).then_inc(cc.h, 1))
    cc.bump(1)
    p.wait("sync", cc, 1)
    v_all = p.dma("sync", allsc[:], gath_t.ap(), ag)

    KTOP = 32768
    p.vec("memset", bs_lo[:], 0.0)
    p.vec("memset", bs_hi[:], 1.1)
    p.vec("memset", bs_mid[:], 0.55)
    with nc.psum_tensor("NMps", [128, 1], F32) as NMps, \
         nc.psum_tensor("CTps", [1, 1], F32) as CTps:
        p.wait("vector", ag, v_all)
        e_evac = 0
        ct_read = 0
        for it in range(34):
            v_nm = p.vec("tensor_scalar", bs_nm[:], bs_mid[:], -1.0, None, ALU.mult)
            p.wait("tensor", dv, v_nm)
            if e_evac:
                p.wait("tensor", ev, e_evac)
            m1 = p.mm(NMps[:], common["ones_rowf"][:], bs_nm[:], start=True,
                      stop=True, inc=mm)
            p.wait("scalar", mm, m1)
            e_evac = p.act(negmidcol[:], NMps[:], AF.Copy)
            e_sh = p.act(shiftb[:], allsc[:], AF.Identity, bias=negmidcol[:])
            p.wait("vector", ev, e_sh)
            v_ct = p.vec("tensor_scalar", shiftb[:], shiftb[:], 0.0, None,
                         ALU.is_ge, ALU.add, accum_out=cntcol[:])
            p.wait("tensor", dv, v_ct)
            if ct_read:
                p.wait("tensor", dv, ct_read)
            m2 = p.mm(CTps[:], cntcol[:], ones_colf[:], start=True, stop=True, inc=mm)
            p.wait("vector", mm, m2)
            p.vec("tensor_scalar", bs_cond[:], CTps[:], KTOP - 0.5, None, ALU.is_ge)
            p.vec("tensor_sub", bs_d1[:], bs_mid[:], bs_lo[:])
            p.vec("tensor_mul", bs_d1[:], bs_d1[:], bs_cond[:])
            p.vec("tensor_add", bs_lo[:], bs_lo[:], bs_d1[:])
            p.vec("tensor_sub", bs_d2[:], bs_hi[:], bs_mid[:])
            p.vec("tensor_mul", bs_d2[:], bs_d2[:], bs_cond[:])
            p.vec("tensor_add", bs_hi[:], bs_mid[:], bs_d2[:])
            p.vec("tensor_add", bs_mid[:], bs_lo[:], bs_hi[:])
            ct_read = p.vec("tensor_scalar", bs_mid[:], bs_mid[:], 0.5, None, ALU.mult)
        p.wait("tensor", dv, ct_read)
    with nc.psum_tensor("TRps", [1, S], F32) as TRps:
        for n in range(4):
            mv = p.mm(TRps[:, 512 * n:512 * (n + 1)], bs_lo[:], ones_rowW[:],
                      start=True, stop=True, inc=mm if n == 3 else None)
        p.wait("scalar", mm, mv)
        e_tr = p.act(taurow[:], TRps[:], AF.Copy)
    with nc.psum_tensor("TPps", [8, S], F32) as TPps:
        p.wait("tensor", ev, e_tr)
        for n in range(4):
            mv = p.mm(TPps[:, 512 * n:512 * (n + 1)], ones_row8f[:],
                      taurow[:, 512 * n:512 * (n + 1)], start=True, stop=True,
                      inc=mm if n == 3 else None)
        p.wait("vector", mm, mv)
        p.vec("tensor_tensor", ind8[:], scT[:], TPps[:], ALU.is_ge)
        v_gates = p.vec("tensor_mul", gatesT[:], scT[:], ind8[:])

    if upto != 1:
        sc_g.close()
        sc_h = ExitStack()
        hg = p.sb("hgsb", [128, NFT, S], BF16, scope=sc_h)
        htmp = p.sb("htmp", [128, 2, 512], BF16, scope=sc_h)
        Gsb = p.sb("Gsb", [128, S], BF16, scope=sc_h)

    if upto == 1:
        p.wait("sync", dv, v_gates)
        fsb = p.sb(un("gdbgf"), [8, S], F32, scope=sc_g)
        vv = p.vec("tensor_copy", fsb[:], gatesT[:])
        p.wait("sync", dv, vv)
        p.dma("sync", gdbg, fsb[:], st[0])
        p.dma("sync", tdbg, bs_lo[:], st[0])
        p.wait("sync", st[0], st[0].v)
        sc_g.close()
        sc_e.close()
        _finish_dummy(p, xodram, st)
        return

    gpc = p.ec["gpsimd"]
    with nc.psum_tensor("GTps", [128, 512], F32) as GTps, \
         nc.psum_tensor("Hps", [128, 2, 512], F32) as Hps, \
         nc.psum_tensor("Yps", [128, 4, 512], F32) as Yps:
        mmdone = [0] * NST
        gt_ev = 0
        vhg_last = 0
        ehg_last = 0
        y_mm_last = 0
        hev = [0, 0]
        yev = [None] * 4
        for i, (e, fb) in enumerate(steps):
            slot = i % 2
            if i + 1 < NST:
                en, fbn = steps[i + 1]
                if i >= 1:
                    p.wait("gpsimd", mm, mmdone[i - 1])
                wld[i + 1] = load_blk(slot ^ 1, en, fbn)
            p.wait("tensor", wl2[slot], wld[i])
            if fb == 0 and e < 8:
                if e == 0:
                    p.wait("tensor", dv, v_gates)
                if vhg_last:
                    p.wait("scalar", dv, vhg_last)
                for n in range(4):
                    if gt_ev:
                        p.wait("tensor", ev, gt_ev)
                    mv = p.mm(GTps[:], selsb[0:8, 128 * e:128 * (e + 1)],
                              gatesT[0:8, 512 * n:512 * (n + 1)], start=True,
                              stop=True, inc=mm)
                    p.wait("scalar", mm, mv)
                    gt_ev = p.act(Gsb[:, 512 * n:512 * (n + 1)], GTps[:], AF.Copy)
            if i == 0:
                p.wait("tensor", dv, v_lnq)
            first_hg = True
            for ft in range(NFT):
                for n in range(4):
                    g = (ft * 4 + n) % 2
                    if hev[g]:
                        p.wait("tensor", ev, hev[g])
                    for dc in range(NCH):
                        mv = p.mm(Hps[:, g, :],
                                  W1sb[:, slot, dc, 128 * ft:128 * (ft + 1)],
                                  lnq[:, dc, 512 * n:512 * (n + 1)],
                                  start=(dc == 0), stop=(dc == NCH - 1),
                                  inc=mm if dc == NCH - 1 else None)
                    p.wait("scalar", mm, mv)
                    if first_hg and y_mm_last:
                        if e == 8:
                            p.wait("scalar", mm, y_mm_last)
                        else:
                            p.wait("vector", mm, y_mm_last)
                    bcol = b1sb[:, e, NFT * fb + ft:NFT * fb + ft + 1]
                    if e == 8:
                        hev[g] = p.act(hg[:, ft, 512 * n:512 * (n + 1)],
                                       Hps[:, g, :], AF.Gelu_apprx_tanh, bias=bcol)
                        ehg_last = hev[g]
                    else:
                        if vhg_last:
                            p.wait("scalar", dv, vhg_last)
                        hev[g] = p.act(htmp[:, g, :], Hps[:, g, :],
                                       AF.Gelu_apprx_tanh, bias=bcol)
                        p.wait("vector", ev, hev[g])
                        if first_hg:
                            p.wait("vector", ev, gt_ev)
                        vhg_last = p.vec("tensor_mul",
                                         hg[:, ft, 512 * n:512 * (n + 1)],
                                         htmp[:, g, :], Gsb[:, 512 * n:512 * (n + 1)])
                    first_hg = False
            if e == 8:
                p.wait("tensor", ev, ehg_last)
            else:
                p.wait("tensor", dv, vhg_last)
            for dt in range(NCH):
                for n in range(4):
                    g = (dt * 4 + n) % 4
                    if yev[g] is not None:
                        p.wait("tensor", yev[g][0], yev[g][1])
                    extra = (e == 0 and fb == 0)
                    for fc in range(NFT):
                        mv = p.mm(Yps[:, g, :],
                                  W2sb[:, slot, fc, 128 * dt:128 * (dt + 1)],
                                  hg[:, fc, 512 * n:512 * (n + 1)],
                                  start=(fc == 0),
                                  stop=(fc == NFT - 1 and not extra),
                                  inc=mm if (fc == NFT - 1 and not extra) else None)
                    if extra:
                        mv = p.mm(Yps[:, g, :], b2sb[0:8, 128 * dt:128 * (dt + 1)],
                                  gatesT[0:8, 512 * n:512 * (n + 1)],
                                  start=False, stop=True, inc=mm)
                    if e == 0 and fb == 0:
                        p.wait("scalar", mm, mv)
                        yev[g] = (ev, p.act(yacc[:, dt, 512 * n:512 * (n + 1)],
                                            Yps[:, g, :], AF.Identity,
                                            bias=bs2sb[:, dt:dt + 1]))
                    else:
                        p.wait("vector", mm, mv)
                        yev[g] = (dv, p.vec("tensor_add",
                                            yacc[:, dt, 512 * n:512 * (n + 1)],
                                            yacc[:, dt, 512 * n:512 * (n + 1)],
                                            Yps[:, g, :]))
                    y_mm_last = mv
            mmdone[i] = mm.v
    sc_h.close()
    sc_w.close()

    emit_postln_residual(p, yacc, 0, xdram, xodram, common, lx, mm, ev, dv, st)
    p.wait("sync", st[0], st[0].v)
    p.wait("sync", st[1], st[1].v)
    sc_e.close()


def build_l1_attn(nc):
    p = Prog(nc)
    xdram = nc.dram_tensor("x", [128, NCH, S], F32, kind="ExternalInput").ap()
    xodram = nc.dram_tensor("xo", [128, NCH, S], F32, kind="ExternalOutput").ap()
    emit_l1_attn(p, xdram, xodram)
    p.run()


def emit_l1_attn(p, xdram, xodram):
    nc = p.nc
    wq_d = nc.dram_tensor("wqcC", [NCH, 128, D], F32, kind="ExternalInput").ap()
    wk_d = nc.dram_tensor("wkcC", [NCH, 128, D], F32, kind="ExternalInput").ap()
    wv_d = nc.dram_tensor("wvcC", [NCH, 128, D], F32, kind="ExternalInput").ap()
    wo_d = nc.dram_tensor("wocC", [NCH, 128, D], F32, kind="ExternalInput").ap()
    bo_d = nc.dram_tensor("bocols", [128, 8], F32, kind="ExternalInput").ap()
    sc_d = nc.dram_tensor("scalec", [16, 1], F32, kind="ExternalInput").ap()
    E2_d = nc.dram_tensor("E2c", [NCH, 128, 16], F32, kind="ExternalInput").ap()
    E2T_d = nc.dram_tensor("E2Tc", [NCH, 16, 128], F32, kind="ExternalInput").ap()

    lx = [p.sem("lx0"), p.sem("lx1")]
    wl = p.sem("wl")
    mm = p.sem("mm")
    st = [p.sem("st0"), p.sem("st1")]
    g0 = p.sem("g0")
    sq = p.sem("sq")
    so = p.sem("so")
    dv, ev = p.ec["vector"], p.ec["scalar"]

    common = make_common(p)
    wbuf = p.sb("wbuf", [128, NCH, D], BF16)
    scalecol = p.sb("scalecol", [16, 1], F32)
    bocols = p.sb("bocols_sb", [128, 8], F32)
    sc_ln = ExitStack()
    lnq = p.sb("lnqC", [128, NCH, S], BF16, scope=sc_ln, side="right")

    for c in range(NCH):
        wq_ld = p.dma("gpsimd", wbuf[:, c, :], wq_d[c], wl)
    p.dma("sync", scalecol[:], sc_d, g0)
    ld_g0 = p.dma("sync", bocols[:], bo_d, g0)

    emit_ln_featmajor(p, xdram, lnq, common, lx, mm, ev, dv)

    sc_att = ExitStack()
    qT = p.sb("qT", [128, NCH, S], BF16, scope=sc_att)
    kT = p.sb("kT", [128, NCH, S], BF16, scope=sc_att)
    vtok = p.sb("vtok", [128, 16, 1024], BF16, scope=sc_att)
    E2sb = p.sb("E2sb", [128, NCH, 16], BF16, scope=sc_att)
    E2Tsb = p.sb("E2Tsb", [16, NCH, 128], BF16, scope=sc_att)
    ssq = p.sb("ssq", [16, S], F32, scope=sc_att)
    qfac = p.sb("qfac", [16, S], BF16, scope=sc_att)
    zr = p.sb("zr", [1, 512], F32, scope=sc_att)
    resb = p.sb("resb", [64, 512], BF16, scope=sc_att)
    e2l = p.sem("e2l")
    for c in range(NCH):
        p.dma("gpsimd", E2sb[:, c, :], E2_d[c], e2l)
        e2_ld = p.dma("gpsimd", E2Tsb[:, c, :], E2T_d[c], e2l)

    def mk_evac(dst):
        def evac(m, ps):
            return p.act(dst[:, m, :], ps, AF.Copy, inc=None)
        return evac
    p.wait("tensor", dv, dv.v)
    emit_proj(p, wbuf, wq_ld, wl, lnq, mk_evac(qT), NCH, mm, ev)
    p.wait("gpsimd", mm, mm.v)
    for c in range(NCH):
        wk_ld = p.dma("gpsimd", wbuf[:, c, :], wk_d[c], wl)
    emit_proj(p, wbuf, wk_ld, wl, lnq, mk_evac(kT), NCH, mm, ev)
    p.wait("gpsimd", mm, mm.v)
    for c in range(NCH):
        wv_ld = p.dma("gpsimd", wbuf[:, c, :], wv_d[c], wl)
    with nc.psum_tensor("Vps", [128, 2, 512], F32) as Vps:
        p.wait("tensor", wl, wv_ld)
        p.wait("tensor", ev, ev.v)
        vps_guard = [0, 0]
        for ts in range(16):
            for mh in range(2):
                g = (ts * 2 + mh) % 2
                if vps_guard[g]:
                    p.wait("tensor", ev, vps_guard[g])
                for dc in range(NCH):
                    mv = p.mm(Vps[:, g, :], lnq[:, dc, 128 * ts:128 * (ts + 1)],
                              wbuf[:, dc, 512 * mh:512 * (mh + 1)],
                              start=(dc == 0), stop=(dc == NCH - 1),
                              inc=mm if dc == NCH - 1 else None)
                p.wait("scalar", mm, mv)
                vps_guard[g] = p.act(vtok[:, ts, 512 * mh:512 * (mh + 1)],
                                     Vps[:, g, :], AF.Copy)
        p.wait("tensor", ev, max(vps_guard))
    sc_ln.close()

    sqpp = p.sb("sqpp", [128, 2, S], BF16, scope=sc_att)
    v_qn = v_kn = 0
    for which, tgt, bias in (("q", qT, scalecol[:]), ("k", kT, 0.0)):
        with nc.psum_tensor(un("SSps"), [16, S], F32) as SSps:
            p.wait("tensor", ev, ev.v)
            p.wait("tensor", dv, dv.v)
            sq_guard = [0, 0]
            for c in range(NCH):
                g = c % 2
                if sq_guard[g]:
                    p.wait("scalar", mm, sq_guard[g])
                e_sq = p.act(sqpp[:, g, :], tgt[:, c, :], AF.Square)
                p.wait("tensor", ev, e_sq)
                if c == 0 and which == "q":
                    p.wait("tensor", e2l, e2_ld)
                    p.wait("tensor", g0, ld_g0)
                for n in range(4):
                    mv = p.mm(SSps[:, 512 * n:512 * (n + 1)], E2sb[:, c, :],
                              sqpp[:, g, 512 * n:512 * (n + 1)],
                              start=(c == 0), stop=(c == NCH - 1),
                              inc=mm if n == 3 else None)
                sq_guard[g] = mv
            p.wait("vector", mm, mm.v)
            p.vec("tensor_scalar", ssq[:], SSps[:], 1e-24, None, ALU.add)
        d_ln = p.ec["vector"].v
        p.wait("scalar", dv, d_ln)
        p.act(ssq[:], ssq[:], AF.Ln)
        e_f = p.act(qfac[:], ssq[:], AF.Exp, scale=-0.5, bias=bias)
        with nc.psum_tensor(un("EXps"), [128, 2, 512], F32) as EXps:
            p.wait("tensor", ev, e_f)
            ex_guard = [0, 0]
            for c in range(NCH):
                for n in range(4):
                    g = (c * 4 + n) % 2
                    if ex_guard[g]:
                        p.wait("tensor", dv, ex_guard[g])
                    mv = p.mm(EXps[:, g, :], E2Tsb[0:16, c, :],
                              qfac[0:16, 512 * n:512 * (n + 1)],
                              start=True, stop=True, inc=mm)
                    p.wait("vector", mm, mv)
                    ex_guard[g] = p.vec("tensor_mul",
                                        tgt[:, c, 512 * n:512 * (n + 1)],
                                        tgt[:, c, 512 * n:512 * (n + 1)],
                                        EXps[:, g, :])
            if which == "q":
                v_qn = max(ex_guard)
            else:
                v_kn = max(ex_guard)

    qhb = p.sb("qhb", [64, S], BF16, scope=sc_att)
    khb = p.sb("khb", [64, S], BF16, scope=sc_att)
    ohb = p.sb("ohb", [64, S], BF16, scope=sc_att)
    oTpk = p.sb("oTpk", [128, NCH, S], BF16, side="right")
    Esb = p.sb("EsbC", [128, 16, 512], BF16, scope=sc_att)
    with nc.psum_tensor("EpsA", [128, 512], F32) as EpsA, \
         nc.psum_tensor("EpsB", [128, 512], F32) as EpsB, \
         nc.psum_tensor("Zps", [1, 512], F32) as Zps, \
         nc.psum_tensor("REps", [64, 512], F32) as REps, \
         nc.psum_tensor("Cps", [64, 512], F32) as Cps:
        p.wait("sync", dv, max(v_qn, v_kn))
        p.wait("tensor", dv, max(v_qn, v_kn))
        qh_mm = kh_mm = 0
        pv_last = 0
        eps_guard = [0, 0]
        z_ln_prev = 0
        re_copy_prev = 0
        v_o = 0
        so_val = 0
        for h in range(16):
            hc, half = h // 2, h % 2
            if half == 0:
                qh_ap, kh_ap = qT[0:64, hc, :], kT[0:64, hc, :]
            else:
                p.wait("sync", mm, max(qh_mm, kh_mm))
                rq = p.dma("sync", qhb[:], qT[64:128, hc, :], sq)
                rk = p.dma("sync", khb[:], kT[64:128, hc, :], sq)
                p.wait("tensor", sq, rk)
                qh_ap, kh_ap = qhb[:], khb[:]
            for qt in range(4):
                for kc in range(16):
                    g = kc % 2
                    Eps = EpsA if g == 0 else EpsB
                    if eps_guard[g]:
                        p.wait("tensor", ev, eps_guard[g])
                    mv = p.mm(Eps[:], kh_ap[:, 128 * kc:128 * (kc + 1)],
                              qh_ap[:, 512 * qt:512 * (qt + 1)],
                              start=True, stop=True, inc=mm)
                    if half == 1:
                        qh_mm = kh_mm = mv
                    p.wait("scalar", mm, mv)
                    if kc == 0 and pv_last:
                        p.wait("scalar", mm, pv_last)
                    eps_guard[g] = p.act(Esb[:, kc, :], Eps[:], AF.Exp)
                p.wait("tensor", ev, max(eps_guard))
                if z_ln_prev:
                    p.wait("tensor", ev, z_ln_prev)
                for kc in range(16):
                    mv = p.mm(Zps[:], common["ones_colbf"][:], Esb[:, kc, :],
                              start=(kc == 0), stop=(kc == 15),
                              inc=mm if kc == 15 else None)
                p.wait("scalar", mm, mv)
                p.act(zr[:], Zps[:], AF.Ln)
                z_ln_prev = p.act(zr[:], zr[:], AF.Exp, scale=-1.0)
                if re_copy_prev:
                    p.wait("tensor", ev, re_copy_prev)
                p.wait("tensor", ev, z_ln_prev)
                mv = p.mm(REps[:], common["ones_rowf"][0:1, 0:64], zr[:],
                          start=True, stop=True, inc=mm)
                p.wait("scalar", mm, mv)
                if v_o:
                    p.wait("scalar", dv, v_o)
                re_copy_prev = p.act(resb[:], REps[:], AF.Copy)
                if v_o:
                    p.wait("tensor", dv, v_o)
                for kc in range(16):
                    mv = p.mm(Cps[:], vtok[:, kc, 64 * h:64 * (h + 1)],
                              Esb[:, kc, :], start=(kc == 0), stop=(kc == 15),
                              inc=mm if kc == 15 else None)
                pv_last = mv
                p.wait("vector", mm, mv)
                p.wait("vector", ev, re_copy_prev)
                if half == 0:
                    dst = oTpk[0:64, hc, 512 * qt:512 * (qt + 1)]
                else:
                    if qt == 0 and so_val:
                        p.wait("vector", so, so_val)
                    dst = ohb[:, 512 * qt:512 * (qt + 1)]
                v_o = p.vec("tensor_mul", dst, Cps[:], resb[:])
            if half == 1:
                p.wait("sync", dv, v_o)
                so_val = p.dma("sync", oTpk[64:128, hc, :], ohb[:], so)

    sc_att.close()
    p.wait("gpsimd", mm, mm.v)
    for c in range(NCH):
        wo_ld = p.dma("gpsimd", wbuf[:, c, :], wo_d[c], wl)
    ysb = p.sb("ysbC", [128, NCH, S], F32)
    p.wait("tensor", dv, dv.v)
    p.wait("tensor", so, so_val)

    def evac_y(m, ps):
        return p.act(ysb[:, m, :], ps, AF.Identity, bias=bocols[:, m:m + 1])
    ev_y = emit_proj(p, wbuf, wo_ld, wl, oTpk, evac_y, NCH, mm, ev)
    emit_postln_residual(p, ysb, ev_y, xdram, xodram, common, lx, mm, ev, dv, st)
    p.wait("sync", st[0], st[0].v)
    p.wait("sync", st[1], st[1].v)


def prep_l1_attn_consts(w_q, w_kv, w_out, b_out, scale):
    w_kv = np.asarray(w_kv, np.float32)
    E2 = np.zeros((NCH, 128, 16), np.float32)
    for c in range(NCH):
        E2[c, 0:64, 2 * c] = 1.0
        E2[c, 64:128, 2 * c + 1] = 1.0
    return {
        "wqcC": wchunks(w_q), "wkcC": wchunks(w_kv[:, :D]),
        "wvcC": wchunks(w_kv[:, D:]), "wocC": wchunks(w_out),
        "bocols": col_layout(b_out),
        "scalec": np.asarray(scale, np.float32).reshape(16, 1),
        "E2c": E2, "E2Tc": np.ascontiguousarray(E2.transpose(0, 2, 1)),
    }


def build_l1_ffn(nc):
    p = Prog(nc)
    xdram = nc.dram_tensor("x", [128, NCH, S], F32, kind="ExternalInput").ap()
    xodram = nc.dram_tensor("xo", [128, NCH, S], F32, kind="ExternalOutput").ap()
    emit_l1_ffn(p, xdram, xodram)
    p.run()


def emit_l1_ffn(p, xdram, xodram):
    nc = p.nc
    SH = 512
    NBL = S // SH
    wi_d = nc.dram_tensor("wiTc", [NCH, 128, FF], F32, kind="ExternalInput").ap()
    wo_d = nc.dram_tensor("woTc", [32, 128, D], F32, kind="ExternalInput").ap()
    A_d = nc.dram_tensor("Ac", [NE, NCH, 128, 16], F32, kind="ExternalInput").ap()
    B_d = nc.dram_tensor("Bc", [NE, 16, FF], F32, kind="ExternalInput").ap()
    gw_d = nc.dram_tensor("gwD", [NCH, 128, 8], F32, kind="ExternalInput").ap()
    gb_d = nc.dram_tensor("gbrow", [1, 8], F32, kind="ExternalInput").ap()
    bi_d = nc.dram_tensor("bicols", [128, 32], F32, kind="ExternalInput").ap()
    bo_d = nc.dram_tensor("borow", [1, D], F32, kind="ExternalInput").ap()
    sel_d = nc.dram_tensor("selD", [8, 8 * 128], F32, kind="ExternalInput").ap()
    eye_d = nc.dram_tensor("eyeD", [128, 128], F32, kind="ExternalInput").ap()

    lx = [p.sem("lx0"), p.sem("lx1")]
    wl = [p.sem("wl0"), p.sem("wl1")]
    ol = [p.sem("ol0"), p.sem("ol1")]
    mm = p.sem("mm")
    st = [p.sem("st0"), p.sem("st1")]
    g0 = p.sem("g0")
    sl = p.sem("sl")
    dv, ev = p.ec["vector"], p.ec["scalar"]
    gp = p.ec["gpsimd"]

    common = make_common(p)
    ysb = p.sb("ysbD", [128, NCH, S], F32)
    sc_ln = ExitStack()
    lnq = p.sb("lnqD", [128, NCH, S], BF16, scope=sc_ln, side="right")

    emit_ln_featmajor(p, xdram, lnq, common, lx, mm, ev, dv)
    v_lnq = dv.v

    sc_f = ExitStack()
    gwsb = p.sb("gwsbD", [128, NCH, 8], BF16, scope=sc_f)
    gbsb = p.sb("gbsb", [1, 8], BF16, scope=sc_f)
    bisb = p.sb("bisb", [128, 32], F32, scope=sc_f)
    borow = p.sb("borowD", [1, D], BF16, scope=sc_f)
    selsb = p.sb("selD_sb", [8, 8 * 128], BF16, scope=sc_f)
    eyef = p.sb("eyefD", [128, 128], F32, scope=sc_f)
    eyebf = p.sb("eyebfD", [128, 128], BF16, scope=sc_f)
    Asb = p.sb("Asb", [128, NE, NCH, 16], BF16, scope=sc_f)
    Bst = p.sb("BstD", [16, 2, NE, 128], BF16, scope=sc_f)
    esb = p.sb("esbD", [128, 8], F32, scope=sc_f)
    scn = p.sb("scnD", [128, 8], F32, scope=sc_f)
    zcol = p.sb("zcolD", [128, 1], F32, scope=sc_f)
    rcpc = p.sb("rcpcD", [128, 1], F32, scope=sc_f)
    m1c = p.sb("m1c", [128, 1], F32, scope=sc_f)
    m2c = p.sb("m2c", [128, 1], F32, scope=sc_f)
    nmc = p.sb("nmc", [128, 1], F32, scope=sc_f)
    vsc = p.sb("vsc", [128, 1], F32, scope=sc_f)
    ind1 = p.sb("ind1", [128, 8], F32, scope=sc_f)
    ind2 = p.sb("ind2", [128, 8], F32, scope=sc_f)
    dsb = p.sb("dsbD", [128, 8], F32, scope=sc_f)
    gtok = p.sb("gtok", [128, 8], F32, scope=sc_f)
    gatesDb = p.sb("gatesDb", [8, S], BF16, scope=sc_f)
    vs2row = p.sb("vs2row", [1, S], BF16, scope=sc_f)
    tsb = p.sb("tsbD", [16, NE, SH], BF16, scope=sc_f)
    GD = p.sb("GDsb", [128, NE, SH], BF16, scope=sc_f)
    Hb = p.sb("HbD", [128, 32, SH], BF16, scope=sc_f)
    Hacc = p.sb("HaccD", [128, SH], F32, scope=sc_f)
    tmpD = p.sb("tmpD", [128, SH], F32, scope=sc_f)
    gtile = p.sb("gtileD", [128, 2, SH], BF16, scope=sc_f)
    btile = p.sb("btileD", [128, 2, SH], BF16, scope=sc_f)
    wist = p.sb("wistD", [128, 2, NCH, 128], BF16, scope=sc_f)
    wost = p.sb("wostD", [128, 2, 512], BF16, scope=sc_f)

    cl = p.sem("cl")
    p.wait("gpsimd", dv, v_lnq)
    p.wait("sync", dv, v_lnq)
    for c in range(NCH):
        p.dma("gpsimd", gwsb[:, c, :], gw_d[c], cl)
    p.dma("gpsimd", borow[:], bo_d, cl)
    p.dma("gpsimd", selsb[:], sel_d, cl)
    p.dma("gpsimd", eyebf[:], eye_d, cl)
    p.dma("gpsimd", gbsb[:], gb_d, cl)
    for e in range(NE):
        for dc in range(NCH):
            ld_wl0 = p.dma("gpsimd", Asb[:, e, dc, :], A_d[e, dc], cl)
    p.dma("sync", bisb[:], bi_d, g0)
    ld_g0 = p.dma("sync", eyef[:], eye_d, g0)

    with nc.psum_tensor("LGpsD", [128, 16, 8], F32) as LGps, \
         nc.psum_tensor("TGpsA", [8, 128], F32) as TGpsA, \
         nc.psum_tensor("TGpsB", [8, 128], F32) as TGpsB, \
         nc.psum_tensor("TVpsA", [1, 128], F32) as TVpsA, \
         nc.psum_tensor("TVpsB", [1, 128], F32) as TVpsB:
        p.wait("tensor", dv, v_lnq)
        p.wait("tensor", cl, ld_wl0)
        p.wait("tensor", g0, ld_g0)
        p.wait("tensor", ev, ev.v)
        tg_guard = [0, 0]
        tv_guard = [0, 0]
        lg_read = 0
        for ts in range(16):
            for c in range(NCH):
                p.mm(LGps[:, ts, :], lnq[:, c, 128 * ts:128 * (ts + 1)],
                     gwsb[:, c, :], start=(c == 0), stop=False)
            mv = p.mm(LGps[:, ts, :], common["ones_rowbf"][:],
                      gbsb[:], start=False, stop=True, inc=mm)
            p.wait("scalar", mm, mv)
            if lg_read:
                p.wait("scalar", dv, lg_read)
            e_e = p.act(esb[:], LGps[:, ts, :], AF.Exp, accum_out=zcol[:])
            p.act(rcpc[:], zcol[:], AF.Ln)
            e_r = p.act(rcpc[:], rcpc[:], AF.Exp, scale=-1.0)
            e_s = p.act(scn[:], esb[:], AF.Identity, scale=rcpc[:])
            p.wait("vector", ev, e_s)
            p.vec("tensor_scalar", dsb[:], scn[:], 0.0, None, ALU.add,
                  ALU.max, accum_out=m1c[:])
            v_n = p.vec("tensor_scalar", nmc[:], m1c[:], -1.0, None, ALU.mult)
            p.wait("scalar", dv, v_n)
            e_d = p.act(dsb[:], scn[:], AF.Identity, bias=nmc[:])
            p.wait("vector", ev, e_d)
            p.vec("tensor_scalar", ind1[:], dsb[:], 0.0, None, ALU.is_ge)
            p.vec("scalar_tensor_tensor", dsb[:], ind1[:], -1e9, scn[:],
                  ALU.mult, ALU.add)
            p.vec("tensor_scalar", tmpD[:, 0:8], dsb[:], 0.0, None, ALU.add,
                  ALU.max, accum_out=m2c[:])
            v_n2 = p.vec("tensor_scalar", nmc[:], m2c[:], -1.0, None, ALU.mult)
            p.wait("scalar", dv, v_n2)
            e_d2 = p.act(dsb[:], dsb[:], AF.Identity, bias=nmc[:])
            p.wait("vector", ev, e_d2)
            p.vec("tensor_scalar", ind2[:], dsb[:], 0.0, None, ALU.is_ge)
            p.vec("tensor_add", ind1[:], ind1[:], ind2[:])
            v_vs = p.vec("tensor_add", vsc[:], m1c[:], m2c[:])
            p.wait("scalar", dv, v_vs)
            e_g = p.act(gtok[:], ind1[:], AF.Identity, scale=vsc[:])
            g = ts % 2
            tg = TGpsA if g == 0 else TGpsB
            tv = TVpsA if g == 0 else TVpsB
            p.wait("tensor", ev, e_g)
            if tg_guard[g]:
                p.wait("tensor", ev, tg_guard[g])
            m_t = p.t_transpose(tg[0:8, :], gtok[:], eyef[:], inc=mm)
            if tv_guard[g]:
                p.wait("tensor", ev, tv_guard[g])
            m_t2 = p.t_transpose(tv[0:1, :], vsc[:], eyef[:], inc=mm)
            p.wait("scalar", mm, m_t2)
            tg_guard[g] = p.act(gatesDb[:, 128 * ts:128 * (ts + 1)], tg[0:8, :],
                                AF.Copy)
            tv_guard[g] = p.act(vs2row[:, 128 * ts:128 * (ts + 1)], tv[0:1, :],
                                AF.Copy, scale=2.0)
            lg_read = p.ec["vector"].v
        e_gall = max(max(tg_guard), max(tv_guard))

    def load_wi(slot, bl, fc):
        for dc in range(NCH):
            p.dma("gpsimd", wist[:, slot, dc, :],
                  wi_d[dc, :, 128 * fc:128 * (fc + 1)], wl[slot])
        for e in range(NE):
            v = p.dma("gpsimd", Bst[0:16, slot, e, :],
                      B_d[e, :, 128 * fc:128 * (fc + 1)], wl[slot])
        return v

    def load_wo(slot, fc, dth):
        return p.dma("gpsimd", wost[:, slot, :],
                     wo_d[fc, :, 512 * dth:512 * (dth + 1)], ol[slot])

    wld = {}
    wld[(0, 0)] = load_wi(0, 0, 0)
    if True:
        p.wait("tensor", ev, e_gall)
        gt_ev = 0
        t_ev = 0
        b_guard = [0, 0]
        h_guard = [0, 0, 0, 0]
        y_read = 0
        acc_mm = [0, 0]
        hb_act = [0, 0]
        hb_ev = 0
        wo_mm = {}
        for bl in range(NBL):
            sb0 = SH * bl
            ph1 = ExitStack()
            Tps = ph1.enter_context(nc.psum_tensor(un("TpsD"), [16, SH], F32))
            GTps = ph1.enter_context(nc.psum_tensor(un("GTpsD"), [128, SH], F32))
            p.wait("tensor", ev, ev.v)
            p.wait("tensor", dv, dv.v)
            for e in range(NE):
                if t_ev:
                    p.wait("tensor", ev, t_ev)
                for dc in range(NCH):
                    mv = p.mm(Tps[:], Asb[:, e, dc, :],
                              lnq[:, dc, sb0:sb0 + SH],
                              start=(dc == 0), stop=(dc == NCH - 1),
                              inc=mm if dc == NCH - 1 else None)
                p.wait("scalar", mm, mv)
                t_ev = p.act(tsb[:, e, :], Tps[:], AF.Copy)
            for e in range(NE):
                if gt_ev:
                    p.wait("tensor", ev, gt_ev)
                mv = p.mm(GTps[:], selsb[0:8, 128 * e:128 * (e + 1)],
                          gatesDb[0:8, sb0:sb0 + SH], start=True, stop=True,
                          inc=mm)
                p.wait("scalar", mm, mv)
                gt_ev = p.act(GD[:, e, :], GTps[:], AF.Copy)
            ph1.close()
            ph2 = ExitStack()
            Bps = ph2.enter_context(nc.psum_tensor(un("BpsD"), [128, 2, SH], F32))
            Php = ph2.enter_context(nc.psum_tensor(un("PhD"), [128, 4, SH], F32))
            HAps = ph2.enter_context(nc.psum_tensor(un("HAD"), [128, 2, SH], F32))
            p.wait("tensor", ev, ev.v)
            p.wait("tensor", dv, dv.v)
            p.wait("scalar", mm, mm.v)
            for fc in range(32):
                nxt = (bl, fc + 1) if fc + 1 < 32 else ((bl + 1, 0) if bl + 1 < NBL else None)
                if nxt is not None:
                    slot_n = (fc + 1) % 2
                    p.wait("gpsimd", mm, mm.v)
                    wld[nxt] = load_wi(slot_n, *nxt)
                slot = fc % 2
                g = fc % 2
                p.wait("tensor", wl[slot], wld[(bl, fc)])
                if b_guard[g]:
                    p.wait("tensor", ev, b_guard[g])
                for dc in range(NCH):
                    mv = p.mm(Bps[:, g, :], wist[:, slot, dc, :],
                              lnq[:, dc, sb0:sb0 + SH],
                              start=(dc == 0), stop=(dc == NCH - 1),
                              inc=mm if dc == NCH - 1 else None)
                p.wait("scalar", mm, mv)
                b_guard[g] = p.act(btile[:, g, :], Bps[:, g, :], AF.Identity,
                                   bias=bisb[:, fc:fc + 1])
                stt_v = [0] * NE
                for e in range(NE):
                    gph = e % 4
                    if h_guard[gph]:
                        p.wait("tensor", dv, h_guard[gph])
                    p.mm(Php[:, gph, :], Bst[0:16, slot, e, :],
                         tsb[:, e, :], start=True, stop=False)
                    if e == 0:
                        p.wait("tensor", ev, b_guard[g])
                    mv = p.mm(Php[:, gph, :], eyebf[:], btile[:, g, :],
                              start=False, stop=True, inc=mm)
                    p.wait("vector", mm, mv)
                    par = e % 2
                    if acc_mm[par]:
                        p.wait("vector", mm, acc_mm[par])
                    h_guard[gph] = p.vec("scalar_tensor_tensor",
                                         gtile[:, par, :], Php[:, gph, :],
                                         0.0, GD[:, e, :], ALU.max, ALU.mult)
                    stt_v[e] = h_guard[gph]
                    if e >= 1:
                        p.wait("tensor", dv, stt_v[e - 1])
                        if e == 1:
                            p.wait("tensor", ev, hb_act[fc % 2])
                        acc_mm[(e - 1) % 2] = p.mm(
                            HAps[:, fc % 2, :], eyebf[:], gtile[:, (e - 1) % 2, :],
                            start=(e == 1), stop=False, inc=mm)
                p.wait("tensor", dv, stt_v[NE - 1])
                acc_mm[(NE - 1) % 2] = p.mm(
                    HAps[:, fc % 2, :], eyebf[:], gtile[:, (NE - 1) % 2, :],
                    start=False, stop=True, inc=mm)
                p.wait("scalar", mm, acc_mm[1])
                hb_act[fc % 2] = p.act(Hb[:, fc, :], HAps[:, fc % 2, :], AF.Copy)
                hb_ev = hb_act[fc % 2]
            ph2.close()
            ph3 = ExitStack()
            Yps = ph3.enter_context(nc.psum_tensor(un("YpsD"), [128, 4, SH], F32))
            p.wait("tensor", ev, ev.v)
            p.wait("tensor", dv, dv.v)
            p.wait("tensor", ev, hb_ev)
            for dth in range(2):
                wo_ld = {}
                p.wait("gpsimd", mm, mm.v)
                wo_ld[0] = load_wo(0, 0, dth)
                for fc in range(32):
                    slot = fc % 2
                    if fc + 1 < 32:
                        if fc >= 1:
                            p.wait("gpsimd", mm, wo_mm.get((fc - 1) % 2, 0))
                        wo_ld[fc + 1] = load_wo((fc + 1) % 2, fc + 1, dth)
                    p.wait("tensor", ol[slot], wo_ld[fc])
                    if fc == 0 and y_read:
                        p.wait("tensor", ev, y_read)
                    for dt in range(4):
                        mv = p.mm(Yps[:, dt, :], wost[:, slot, 128 * dt:128 * (dt + 1)],
                                  Hb[:, fc, :], start=(fc == 0), stop=False,
                                  inc=mm if dt == 3 else None)
                    wo_mm[slot] = mv
                for dt in range(4):
                    dtg = 4 * dth + dt
                    mv = p.mm(Yps[:, dt, :], borow[:, 128 * dtg:128 * (dtg + 1)],
                              vs2row[:, sb0:sb0 + SH], start=False, stop=True,
                              inc=mm)
                p.wait("scalar", mm, mv)
                for dt in range(4):
                    dtg = 4 * dth + dt
                    y_read = p.act(ysb[:, dtg, sb0:sb0 + SH], Yps[:, dt, :],
                                   AF.Copy)
            ph3.close()
    ev_y = y_read
    sc_f.close()
    sc_ln.close()

    emit_postln_residual(p, ysb, ev_y, xdram, xodram, common, lx, mm, ev, dv, st)
    p.wait("sync", st[0], st[0].v)
    p.wait("sync", st[1], st[1].v)


def prep_l1_ffn_consts(wi, bi, wo, bo, lora_A, lora_B, gate_w, gate_b):
    sel = np.zeros((8, 8 * 128), np.float32)
    for e in range(8):
        sel[e, 128 * e:128 * (e + 1)] = 1.0
    return {
        "wiTc": wchunks(np.asarray(wi, np.float32).T),
        "woTc": np.ascontiguousarray(
            np.asarray(wo, np.float32).T.reshape(32, 128, D)),
        "Ac": np.ascontiguousarray(
            np.asarray(lora_A, np.float32).transpose(0, 2, 1).reshape(NE, NCH, 128, 16)),
        "Bc": np.ascontiguousarray(np.asarray(lora_B, np.float32).transpose(0, 2, 1)),
        "gwD": wchunks(np.asarray(gate_w, np.float32).T),
        "gbrow": np.asarray(gate_b, np.float32).reshape(1, 8),
        "bicols": col_layout(bi),
        "borow": np.asarray(bo, np.float32).reshape(1, D),
        "selD": sel,
        "eyeD": np.eye(128, dtype=np.float32),
    }


class _NCProxy:
    def __init__(self, nc):
        object.__setattr__(self, "_nc", nc)

    def __getattr__(self, a):
        return getattr(self._nc, a)

    def psum_tensor(self, name, *a, **k):
        return self._nc.psum_tensor(un(name), *a, **k)

    def sbuf_tensor(self, name, *a, **k):
        return self._nc.sbuf_tensor(un(name), *a, **k)


def build_fused(nc):
    p = Prog(_NCProxy(nc))
    x0 = nc.dram_tensor("x", [128, NCH, S], F32, kind="ExternalInput").ap()
    x1 = nc.dram_tensor("x1d", [128, NCH, S], F32, kind="Internal").ap()
    x2 = nc.dram_tensor("x2d", [128, NCH, S], F32, kind="Internal").ap()
    x3 = nc.dram_tensor("x3d", [128, NCH, S], F32, kind="Internal").ap()
    xo = nc.dram_tensor("xo", [128, NCH, S], F32, kind="ExternalOutput").ap()
    phases = ((emit_l0_attn, x0, x1), (emit_l0_moe, x1, x2),
              (emit_l1_attn, x2, x3), (emit_l1_ffn, x3, xo))
    for emitter, xi, xod in phases:
        ph = ExitStack()
        p.default_scope = ph
        emitter(p, xi, xod)
        for eng in ("sync", "gpsimd"):
            p.wait(eng, p.sem("mm"), p.sem("mm").v)
            for ec in p.ec.values():
                p.wait(eng, ec, ec.v)
        ph.close()
    p.default_scope = None
    p.run()


def xcore_to_xT(x_core):
    return np.ascontiguousarray(np.asarray(x_core, np.float32).T.reshape(NCH, 128, S).transpose(1, 0, 2))


def xT_to_core(xo):
    return xo.transpose(1, 0, 2).reshape(D, S).T


def prep_l0_attn_inputs(x_core, wq, bq, wk, bk, wv, bv, wo, bo):
    Bfirst, Blast, B0T, B0sh = make_l0_consts()
    bcols = np.concatenate([col_layout(bq), col_layout(bk),
                            col_layout(bv), col_layout(bo)], axis=1)
    return {
        "x": xcore_to_xT(x_core),
        "wqc": wchunks(np.asarray(wq).T), "wkc": wchunks(np.asarray(wk).T),
        "wvc": wchunks(np.asarray(wv).T), "woc": wchunks(np.asarray(wo).T),
        "bcols": np.ascontiguousarray(bcols),
        "Bfirst": Bfirst, "Blast": Blast,
        "B0T": np.ascontiguousarray(B0T.reshape(128, 256)),
        "B0sh": np.ascontiguousarray(B0sh.reshape(128, 272)),
        "eye": np.eye(128, dtype=np.float32),
    }


def prep_l0_moe_consts(gate_weight, W1, b1, W2, b2, Ws1, bs1, Ws2, bs2):
    W1c = np.concatenate([np.asarray(W1, np.float32),
                          np.asarray(Ws1, np.float32)[None]], 0).reshape(9, 8, 128, FF)
    W2c = np.concatenate([np.asarray(W2, np.float32),
                          np.asarray(Ws2, np.float32)[None]], 0).reshape(9, 32, 128, D)
    b1c = np.stack([np.asarray(b, np.float32).reshape(32, 128).T
                    for b in list(b1) + [bs1]], 1)
    sel = np.zeros((8, 8 * 128), np.float32)
    for e in range(8):
        sel[e, 128 * e:128 * (e + 1)] = 1.0
    return {
        "gwc": wchunks(np.asarray(gate_weight, np.float32).T),
        "W1c": np.ascontiguousarray(W1c), "W2c": np.ascontiguousarray(W2c),
        "b1c": np.ascontiguousarray(b1c),
        "bs2c": col_layout(bs2),
        "b2r": np.ascontiguousarray(np.asarray(b2, np.float32)),
        "selc": sel,
        "eyeB": np.eye(128, dtype=np.float32),
    }


def kernel(**inputs):
    ii = {k: np.asarray(v) for k, v in inputs.items()}
    x = ii["L0_attn_x"].astype(np.float32)

    consts = prep_l0_moe_consts(
        ii["L0_ffn_gate_weight"], ii["L0_ffn_W1"], ii["L0_ffn_b1"],
        ii["L0_ffn_W2"], ii["L0_ffn_b2"], ii["L0_ffn_Ws1"], ii["L0_ffn_bs1"],
        ii["L0_ffn_Ws2"], ii["L0_ffn_bs2"])
    consts.update(prep_l1_attn_consts(
        ii["L1_attn_w_q"], ii["L1_attn_w_kv"], ii["L1_attn_w_out"],
        ii["L1_attn_b_out"], ii["L1_attn_scale"]))
    consts.update(prep_l1_ffn_consts(
        ii["L1_ffn_wi"], ii["L1_ffn_bi"], ii["L1_ffn_wo"], ii["L1_ffn_bo"],
        ii["L1_ffn_lora_A"], ii["L1_ffn_lora_B"], ii["L1_ffn_gate_w"],
        ii["L1_ffn_gate_b"]))

    nc = bass.Bass("TRN2", target_bir_lowering=False)
    build_fused(nc)
    in_maps = []
    for c in range(8):
        m = prep_l0_attn_inputs(
            x[c], ii["L0_attn_wq"], ii["L0_attn_bq"], ii["L0_attn_wk"],
            ii["L0_attn_bk"], ii["L0_attn_wv"], ii["L0_attn_bv"],
            ii["L0_attn_wo"], ii["L0_attn_bo"])
        m.update(consts)
        in_maps.append(m)
    res = run_bass_kernel_spmd(nc, in_maps, core_ids=list(range(8)))
    out = np.stack([xT_to_core(np.asarray(res.results[c]["xo"]))
                    for c in range(8)]).astype(np.float32)
    return out


# version 06
# speedup vs baseline: 1.2481x; 1.0157x over previous
import math
from contextlib import ExitStack

import numpy as np
import ml_dtypes
import concourse.bass as bass
import concourse.mybir as mybir
from concourse.bass_utils import run_bass_kernel_spmd

F32 = mybir.dt.float32
BF16 = mybir.dt.bfloat16
AF = mybir.ActivationFunctionType
ALU = mybir.AluOpType
AX = mybir.AxisListType

S = 2048
D = 1024
NCH = 8
H = 16
DH = 64
FF = 4096
NE = 8
BF = ml_dtypes.bfloat16
LN_EPS = 1e-5
GATEK = 1e20


def _l0_bias(delta):
    d = np.asarray(delta, dtype=np.float64)
    return (np.exp(-0.5 * (d / 2.0) ** 2) + np.cos(2.0 * np.pi * d / 401.0)).astype(np.float32)


def make_l0_consts():
    tabs = []
    for t in range(16):
        B = np.full((128, 256), -2e9, dtype=np.float32)
        q = 128 * t + np.arange(128)
        c = np.arange(128)
        for s_i, base in ((0, 128 * t - 2), (1, 128 * t + 126)):
            key = base + c
            dlt = key[:, None] - q[None, :]
            v = (np.abs(dlt) <= 2) & (key[:, None] >= 0) & (key[:, None] < S) & (key[:, None] != 0)
            blk = np.full((128, 128), -2e9, dtype=np.float32)
            blk[v] = _l0_bias(dlt)[v]
            B[:, 128 * s_i:128 * (s_i + 1)] = blk
        tabs.append(B)
    Bfirst = np.ascontiguousarray(np.concatenate(tabs[0:8], axis=1))
    Blast = np.ascontiguousarray(np.concatenate(tabs[8:16], axis=1))
    b0 = _l0_bias(np.arange(S))
    B0T = np.ascontiguousarray(
        np.tile(b0.reshape(16, 128).T[:, :, None], (1, 1, 16))).astype(np.float32)
    key = 128 * np.arange(17)[None, :] - 2 + np.arange(128)[:, None]
    b0sh = np.where((key >= 0) & (key < S), _l0_bias(np.abs(key)), np.float32(-2e9)).astype(np.float32)
    B0sh = np.ascontiguousarray(np.tile(b0sh[:, :, None], (1, 1, 16))).astype(np.float32)
    return Bfirst, Blast, B0T, B0sh


def col_layout(v):
    return np.ascontiguousarray(np.asarray(v, np.float32).reshape(-1, 128).T)


def wchunks(W):
    W = np.asarray(W, np.float32)
    K, M = W.shape
    return np.ascontiguousarray(W.reshape(K // 128, 128, M))



_UN = [0]


def un(base):
    _UN[0] += 1
    return f"{base}_{_UN[0]}"

ENGS = ("sync", "tensor", "vector", "scalar", "gpsimd")


class Ctr:
    def __init__(self, h):
        self.h = h
        self.v = 0

    def bump(self, n=1):
        self.v += n
        return self.v


class Prog:
    def __init__(self, nc):
        self.nc = nc
        self.ctx = ExitStack()
        self.q = {e: [] for e in ENGS}
        self._n = 0
        self._sems = {}
        self.default_scope = None
        self.ec = {"vector": self.sem("dvc"), "scalar": self.sem("evc"),
                   "gpsimd": self.sem("gpc")}

    def sem(self, name=None):
        if name is not None and name in self._sems:
            return self._sems[name]
        self._n += 1
        c = Ctr(self.ctx.enter_context(self.nc.semaphore(name or f"s{self._n}")))
        if name is not None:
            self._sems[name] = c
        return c

    LEDGER = []

    def sb(self, name, shape, dtype, scope=None, side=None):
        import math as _m
        free = 1
        for d in shape[1:]:
            free *= d
        bts = free * (2 if dtype == BF16 else 4)
        try:
            h = (scope or self.default_scope or self.ctx).enter_context(
                self.nc.sbuf_tensor(name, shape, dtype, side=side))
            Prog.LEDGER.append((name, bts, side or "left", self.nc.sbuf_base, self.nc.sbuf_top))
            return h
        except BaseException:
            for rec in Prog.LEDGER:
                print("  alloc", rec)
            raise

    def on(self, eng, fn):
        self.q[eng].append(fn)

    def wait(self, eng, ctr, val=None):
        v = ctr.v if val is None else val
        if v > 0:
            self.on(eng, lambda e, v=v: e.wait_ge(ctr.h, v))

    def dma(self, eng, out, in_, ctr):
        self.on(eng, lambda e: e.dma_start(out=out, in_=in_).then_inc(ctr.h, 16))
        return ctr.bump(16)

    def mm(self, out, lhsT, rhs, start, stop, inc=None):
        if inc is None:
            self.on("tensor", lambda e: e.matmul(out, lhsT, rhs, start=start, stop=stop))
            return None
        v = inc.bump()
        self.on("tensor", lambda e: e.matmul(out, lhsT, rhs, start=start, stop=stop).then_inc(inc.h, 1))
        return v

    def t_transpose(self, out, in_, ident, inc=None):
        if inc is None:
            self.on("tensor", lambda e: e.transpose(out, in_, ident))
            return None
        v = inc.bump()
        self.on("tensor", lambda e: e.transpose(out, in_, ident).then_inc(inc.h, 1))
        return v

    def _chained(self, eng, fn):
        c = self.ec[eng]
        prev = c.v
        v = c.bump()
        def run(e, prev=prev, fn=fn, c=c):
            if prev > 0:
                e.wait_ge(c.h, prev)
            fn(e).then_inc(c.h, 1)
        self.on(eng, run)
        return v

    def act(self, out, in_, func, bias=0.0, scale=1.0, accum_out=None, inc=None):
        return self._chained("scalar",
                             lambda e: e.activation(out, in_, func, bias=bias,
                                                    scale=scale, accum_out=accum_out))

    def vec(self, meth, *args, inc=None, **kw):
        return self._chained("vector", lambda e: getattr(e, meth)(*args, **kw))

    def gp(self, meth, *args, inc=None, **kw):
        return self._chained("gpsimd", lambda e: getattr(e, meth)(*args, **kw))

    def run(self):
        with self.nc.Block() as block:
            for eng in ENGS:
                fns = self.q[eng]
                if not fns:
                    continue
                getattr(block, eng)(lambda e, fns=fns: [f(e) for f in fns])
        self.ctx.close()


def make_common(p):
    c = {}
    c["ones_colbf"] = p.sb(un("ones_colbf"), [128, 1], BF16)
    c["ones_rowf"] = p.sb(un("ones_rowf"), [1, 128], F32)
    c["ones_rowbf"] = p.sb(un("ones_rowbf"), [1, 128], BF16)
    c["one1bf"] = p.sb(un("one1bf"), [1, 1], BF16)
    c["onesP"] = p.sb(un("onesP"), [65, 128], F32)
    p.vec("memset", c["onesP"][:], 1.0)
    p.vec("memset", c["ones_colbf"][:], 1.0)
    p.vec("memset", c["ones_rowf"][:], 1.0)
    p.vec("memset", c["ones_rowbf"][:], 1.0)
    c["init_dv"] = p.vec("memset", c["one1bf"][:], 1.0)
    return c


def emit_ln_featmajor(p, xdram, lnq, common, lx, mm, ev, dv, raw_gate=None,
                      raw_gate2=None, keep=None):
    nc = p.nc
    sc = ExitStack()
    xc = p.sb(un("lnxc"), [128, 2, S], F32, scope=sc)
    xbf = p.sb(un("lnxbf"), [128, S], BF16, scope=sc)
    xsq = p.sb(un("lnxsq"), [128, S], BF16, scope=sc)
    rows_scope = keep if keep is not None else sc
    rows_side = "right" if keep is not None else None
    murow = p.sb(un("murow"), [1, S], F32, scope=rows_scope, side=rows_side)
    s2row = p.sb(un("s2row"), [1, S], F32, scope=sc)
    Abcs = p.sb(un("Abcs"), [128, S], F32, scope=rows_scope, side=rows_side)
    Bbcs = p.sb(un("Bbcs"), [128, S], F32, scope=sc)
    tf = p.sb(un("lntf"), [128, S], F32, scope=sc)
    tf2 = p.sb(un("lntf2"), [128, S], F32, scope=sc)

    Vc, Ec, Mc = [0] * NCH, [0] * NCH, [0] * NCH
    with nc.psum_tensor(un("lnps1"), [1, S], F32) as ps1, \
         nc.psum_tensor(un("lnps2"), [1, S], F32) as ps2:
        for c in range(NCH):
            slot = c % 2
            if c >= 2:
                p.wait("sync", dv, Vc[c - 2])
                p.wait("sync", mm, Mc[c - 2])
            ldv = p.dma("sync", xc[:, slot, :], xdram[:, c, :], lx[slot])
            p.wait("vector", lx[slot], ldv)
            if c >= 1:
                p.wait("vector", mm, Mc[c - 1])
            Vc[c] = p.vec("tensor_copy", xbf[:], xc[:, slot, :], inc=dv)
            p.wait("scalar", dv, Vc[c])
            if c >= 1:
                p.wait("scalar", mm, Mc[c - 1])
            Ec[c] = p.act(xsq[:], xbf[:], AF.Square, inc=ev)
            p.wait("tensor", ev, Ec[c])
            for n in range(4):
                p.mm(ps1[:, 512 * n:512 * (n + 1)], common["ones_colbf"][:],
                     xbf[:, 512 * n:512 * (n + 1)], start=(c == 0), stop=(c == NCH - 1))
                p.mm(ps2[:, 512 * n:512 * (n + 1)], common["ones_colbf"][:],
                     xsq[:, 512 * n:512 * (n + 1)], start=(c == 0), stop=(c == NCH - 1),
                     inc=mm if (n == 3 and raw_gate is None) else None)
            if raw_gate is not None:
                raw_gate(c, xc[:, slot, :], ldv)
            Mc[c] = mm.v
        p.wait("scalar", mm, Mc[NCH - 1])
        p.act(murow[:], ps1[:], AF.Copy, scale=1.0 / D)
        e2 = p.act(s2row[:], ps2[:], AF.Copy, scale=1.0 / D, inc=ev)
    with nc.psum_tensor(un("lnpbA"), [128, S], F32) as pbA, \
         nc.psum_tensor(un("lnpbB"), [128, S], F32) as pbB:
        p.wait("tensor", ev, e2)
        mmv = None
        for n in range(4):
            p.mm(pbA[:, 512 * n:512 * (n + 1)], common["ones_rowf"][:],
                 murow[:, 512 * n:512 * (n + 1)], start=True, stop=True)
            mmv = p.mm(pbB[:, 512 * n:512 * (n + 1)], common["ones_rowf"][:],
                       s2row[:, 512 * n:512 * (n + 1)], start=True, stop=True,
                       inc=mm if n == 3 else None)
        p.wait("scalar", mm, mmv)
        p.act(Abcs[:], pbA[:], AF.Copy)
        e2b = p.act(Bbcs[:], pbB[:], AF.Copy, inc=ev)
        p.wait("vector", ev, e2b)
        p.vec("tensor_mul", tf[:], Abcs[:], Abcs[:])
        p.vec("scalar_tensor_tensor", tf[:], tf[:], -1.0, Bbcs[:], ALU.mult, ALU.add)
        d1 = p.vec("tensor_scalar", tf[:], tf[:], LN_EPS, None, ALU.add, inc=dv)
        p.wait("scalar", dv, d1)
        p.act(tf[:], tf[:], AF.Ln)
        e3 = p.act(tf[:], tf[:], AF.Exp, scale=-0.5, inc=ev)
        p.wait("vector", ev, e3)
        p.vec("scalar_tensor_tensor", Bbcs[:], Abcs[:], -1.0, tf[:],
              ALU.mult, ALU.mult)
        ev2 = p.vec("tensor_copy", Abcs[:], tf[:], inc=dv)
    applied = []
    M2c = [0] * NCH
    for c in range(NCH):
        slot = c % 2
        if c < 2:
            p.wait("sync", dv, Vc[NCH - 2 + c])
            p.wait("sync", mm, Mc[NCH - 1])
        else:
            p.wait("sync", dv, applied[c - 2])
            if raw_gate2 is not None:
                p.wait("sync", mm, M2c[c - 2])
        ldv = p.dma("sync", xc[:, slot, :], xdram[:, c, :], lx[slot])
        p.wait("vector", lx[slot], ldv)
        p.vec("tensor_mul", tf2[:], xc[:, slot, :], Abcs[:])
        applied.append(p.vec("tensor_add", lnq[:, c, :], tf2[:], Bbcs[:], inc=dv))
        if raw_gate2 is not None:
            raw_gate2(c, xc[:, slot, :], ldv)
            M2c[c] = mm.v
    sc.close()
    return murow, Abcs


def emit_proj(p, wsb, wld_val, wld_ctr, xin, evac, mcs, mm, ev):
    nc = p.nc
    evs = []
    with nc.psum_tensor(un("prps"), [128, 2, S], F32) as prps:
        p.wait("tensor", wld_ctr, wld_val)
        p.wait("tensor", ev, ev.v)
        for m in range(mcs):
            g = m % 2
            if m >= 2:
                p.wait("tensor", ev, evs[m - 2])
            for c in range(NCH):
                for n in range(4):
                    p.mm(prps[:, g, 512 * n:512 * (n + 1)],
                         wsb[:, c, 128 * m:128 * (m + 1)],
                         xin[:, c, 512 * n:512 * (n + 1)],
                         start=(c == 0), stop=(c == NCH - 1),
                         inc=mm if (c == NCH - 1 and n == 3) else None)
            p.wait("scalar", mm, mm.v)
            evs.append(evac(m, prps[:, g, :]))
    return evs[-1]


def emit_postln_residual(p, ysb, ev_y, xdram, xodram, common, lx, mm, ev, dv, st):
    nc = p.nc
    sc = ExitStack()
    ybf = p.sb(un("plybf"), [128, S], BF16, scope=sc)
    ysq = p.sb(un("plysq"), [128, S], BF16, scope=sc)
    murow = p.sb(un("plmu"), [1, S], F32, scope=sc)
    s2row = p.sb(un("pls2"), [1, S], F32, scope=sc)
    Abcs = p.sb(un("plA"), [128, S], F32, scope=sc)
    Bbcs = p.sb(un("plB"), [128, S], F32, scope=sc)
    xc = p.sb(un("plxc"), [128, 2, S], F32, scope=sc)
    xoc = p.sb(un("plxoc"), [128, 2, S], F32, scope=sc)
    tf = p.sb(un("pltf"), [128, S], F32, scope=sc)

    p.wait("sync", mm, mm.v)
    p.wait("sync", dv, dv.v)
    p.wait("sync", ev, ev.v)
    with nc.psum_tensor(un("plps1"), [1, S], F32) as ps1, \
         nc.psum_tensor(un("plps2"), [1, S], F32) as ps2:
        p.wait("vector", ev, ev_y)
        Mc = [0] * NCH
        for c in range(NCH):
            if c >= 1:
                p.wait("vector", mm, Mc[c - 1])
            vc = p.vec("tensor_copy", ybf[:], ysb[:, c, :], inc=dv)
            p.wait("scalar", dv, vc)
            if c >= 1:
                p.wait("scalar", mm, Mc[c - 1])
            ec = p.act(ysq[:], ybf[:], AF.Square, inc=ev)
            p.wait("tensor", ev, ec)
            for n in range(4):
                p.mm(ps1[:, 512 * n:512 * (n + 1)], common["ones_colbf"][:],
                     ybf[:, 512 * n:512 * (n + 1)], start=(c == 0), stop=(c == NCH - 1))
                p.mm(ps2[:, 512 * n:512 * (n + 1)], common["ones_colbf"][:],
                     ysq[:, 512 * n:512 * (n + 1)], start=(c == 0), stop=(c == NCH - 1),
                     inc=mm if n == 3 else None)
            Mc[c] = mm.v
        p.wait("scalar", mm, Mc[NCH - 1])
        p.act(murow[:], ps1[:], AF.Copy, scale=1.0 / D)
        e2 = p.act(s2row[:], ps2[:], AF.Copy, scale=1.0 / D, inc=ev)
    with nc.psum_tensor(un("plpbA"), [128, S], F32) as pbA, \
         nc.psum_tensor(un("plpbB"), [128, S], F32) as pbB:
        p.wait("tensor", ev, e2)
        for n in range(4):
            p.mm(pbA[:, 512 * n:512 * (n + 1)], common["ones_rowf"][:],
                 murow[:, 512 * n:512 * (n + 1)], start=True, stop=True)
            mv = p.mm(pbB[:, 512 * n:512 * (n + 1)], common["ones_rowf"][:],
                      s2row[:, 512 * n:512 * (n + 1)], start=True, stop=True,
                      inc=mm if n == 3 else None)
        p.wait("scalar", mm, mv)
        p.act(Abcs[:], pbA[:], AF.Copy)
        e2b = p.act(Bbcs[:], pbB[:], AF.Copy, inc=ev)
        p.wait("vector", ev, e2b)
        p.vec("tensor_mul", tf[:], Abcs[:], Abcs[:])
        p.vec("scalar_tensor_tensor", tf[:], tf[:], -1.0, Bbcs[:], ALU.mult, ALU.add)
        d1 = p.vec("tensor_scalar", tf[:], tf[:], LN_EPS, None, ALU.add, inc=dv)
        p.wait("scalar", dv, d1)
        p.act(tf[:], tf[:], AF.Ln)
        e3 = p.act(tf[:], tf[:], AF.Exp, scale=-0.5, inc=ev)
        p.wait("vector", ev, e3)
        p.vec("scalar_tensor_tensor", Bbcs[:], Abcs[:], -1.0, tf[:],
              ALU.mult, ALU.mult)
        ev2 = p.vec("tensor_copy", Abcs[:], tf[:], inc=dv)
    outs = [0, 0]
    lxv = [lx[0].v, lx[1].v]
    for c in range(NCH):
        slot = c % 2
        if c >= 2:
            p.wait("sync", st[slot], outs[slot])
        ldv = p.dma("sync", xc[:, slot, :], xdram[:, c, :], lx[slot])
        p.wait("vector", lx[slot], ldv)
        p.vec("tensor_mul", tf[:], ysb[:, c, :], Abcs[:])
        p.vec("tensor_add", tf[:], tf[:], Bbcs[:])
        dvv = p.vec("tensor_add", xoc[:, slot, :], tf[:], xc[:, slot, :], inc=dv)
        p.wait("sync", dv, dvv)
        outs[slot] = p.dma("sync", xodram[:, c, :], xoc[:, slot, :], st[slot])
    sc.close()
    return None



def _finish_dummy(p, xodram, st):
    zb = p.sb(un("zdum"), [128, S], F32)
    p.wait("vector", st[0], st[0].v)
    p.wait("vector", st[1], st[1].v)
    p.wait("sync", p.ec["vector"], p.vec("memset", zb[:], 0.0))
    for c in range(NCH):
        p.dma("sync", xodram[:, c, :], zb[:], st[c % 2])
    p.wait("sync", st[0], st[0].v)
    p.wait("sync", st[1], st[1].v)

def build_l0_attn(nc, upto=7):
    p = Prog(nc)
    xdram = nc.dram_tensor("x", [128, NCH, S], F32, kind="ExternalInput").ap()
    xodram = nc.dram_tensor("xo", [128, NCH, S], F32, kind="ExternalOutput").ap()
    emit_l0_attn(p, xdram, xodram, upto)
    p.run()


def emit_l0_attn(p, xdram, xodram, upto=7):
    dbg = upto if upto in (29, 30, 31, 32, 33) else None
    nc = p.nc
    wq_d = nc.dram_tensor("wqc", [NCH, 128, D], F32, kind="ExternalInput").ap()
    wk_d = nc.dram_tensor("wkc", [NCH, 128, D], F32, kind="ExternalInput").ap()
    wv_d = nc.dram_tensor("wvc", [NCH, 128, D], F32, kind="ExternalInput").ap()
    wo_d = nc.dram_tensor("woc", [NCH, 128, D], F32, kind="ExternalInput").ap()
    bcols_d = nc.dram_tensor("bcols", [128, 4 * NCH], F32, kind="ExternalInput").ap()
    Bfirst_d = nc.dram_tensor("Bfirst", [128, 2048], F32, kind="ExternalInput").ap()
    Blast_d = nc.dram_tensor("Blast", [128, 2048], F32, kind="ExternalInput").ap()
    B0T_d = nc.dram_tensor("B0T", [128, 256], F32, kind="ExternalInput").ap()
    B0sh_d = nc.dram_tensor("B0sh", [128, 272], F32, kind="ExternalInput").ap()
    eye_d = nc.dram_tensor("eye", [128, 128], F32, kind="ExternalInput").ap()
    e0_dram = nc.dram_tensor("e0bounce", [16, S], BF16,
                             kind="ExternalOutput" if upto == 5 else "Internal").ap()
    oT_dram = nc.dram_tensor("oTd", [128, NCH, S], BF16,
                             kind="ExternalOutput" if upto == 5 else "Internal").ap()
    dbgz = nc.dram_tensor("dbgz", [16, 2, 1024], F32,
                          kind="ExternalOutput" if upto == 5 else "Internal").ap() if upto == 5 else None
    dbgv = nc.dram_tensor("dbgv", [128, NCH * 17 * 128], BF16,
                          kind="ExternalOutput").ap() if upto == 5 else None
    dbgr = nc.dram_tensor("dbgr", [2, 1024], BF16,
                          kind="ExternalOutput").ap() if upto == 5 else None
    v0_dram = nc.dram_tensor("v0bounce", [NCH, 128], BF16,
                             kind="ExternalOutput" if upto == 5 else "Internal").ap()
    vm_dram = nc.dram_tensor("vmbounce", [NCH, 128], F32,
                             kind="ExternalOutput" if upto == 5 else "Internal").ap()

    ldc = p.sem("ldc")
    ldg = p.sem("ldg")
    lx = [p.sem("lxA"), p.sem("lxB")]
    wl = [p.sem("wlA"), p.sem("wlB")]
    e0s = p.sem("e0s")
    g0 = p.sem("g0")
    oTs = p.sem("oTs")
    mm = p.sem("mm")
    ev = p.ec["scalar"]
    dv = p.ec["vector"]
    st = [p.sem("stA"), p.sem("stB")]

    common = make_common(p)
    eye = p.sb(un("eyebf"), [128, 128], BF16)
    ld_eye = p.dma("gpsimd", eye[:], eye_d, ldg)
    eyef = p.sb(un("eyef"), [128, 128], F32)
    p.dma("sync", eyef[:], eye_d, ldc)
    bcols = p.sb(un("bcolsb"), [128, 4 * NCH], F32)
    ld_const = p.dma("sync", bcols[:], bcols_d, ldc)

    kT = p.sb(un("kT"), [128, NCH, 2176], BF16)
    wscope = ExitStack()
    wbuf = p.sb(un("wbuf"), [128, NCH, D], BF16, scope=wscope, side="right")

    def load_w(wd):
        for c in range(NCH):
            v = p.dma("gpsimd", wbuf[:, c, :], wd[c], wl[0])
        return v

    wk_ld = load_w(wk_d)

    lnscope = ExitStack()
    lnq = p.sb(un("lnq"), [128, NCH, S], BF16, scope=lnscope, side="right")

    emit_ln_featmajor(p, xdram, lnq, common, lx, mm, ev, dv)
    if upto < 2:
        lnscope.close(); wscope.close()
        _finish_dummy(p, xodram, st)
        return p

    p.wait("vector", ev, ev.v)
    pad_dv = p.vec("memset", kT[:, :, 0:128], 0.0)

    bq = lambda m: bcols[:, 0 * NCH + m:0 * NCH + m + 1]
    bk = lambda m: bcols[:, 1 * NCH + m:1 * NCH + m + 1]
    bv = lambda m: bcols[:, 2 * NCH + m:2 * NCH + m + 1]
    bo = lambda m: bcols[:, 3 * NCH + m:3 * NCH + m + 1]
    p.wait("tensor", dv, dv.v)
    p.wait("tensor", ldc, ld_const)

    def ev_k(m, ps):
        return p.act(kT[:, m, 128:2176], ps, AF.Identity, bias=bk(m))

    emit_proj(p, wbuf, wk_ld, wl[0], lnq, ev_k, NCH, mm, ev)
    p.wait("gpsimd", mm, mm.v)
    wv_ld = load_w(wv_d)
    if upto < 3:
        lnscope.close(); wscope.close()
        _finish_dummy(p, xodram, st)
        return p

    p3p = ExitStack()
    vsh = p.sb(un("vsh"), [128, NCH, 17, 128], BF16, scope=p3p)
    vmcol = p.sb(un("vmcol"), [128, NCH], F32, scope=p3p)
    v0c = p.sb(un("v0c"), [128, NCH], BF16, scope=p3p)
    v0p = p.sb(un("v0p"), [1, 1024], BF16, scope=p3p)
    vmr = p.sb(un("vmr"), [1, 1024], BF16, scope=p3p)
    ov = p.sb(un("ov"), [64, 16], BF16, scope=p3p)
    p3v = ExitStack()
    vtmp = p.sb(un("vtmp"), [128, 2176], BF16, scope=p3v)
    vstage = p.sb(un("vstage"), [NCH, 128], BF16, scope=p3v)
    vmstage = p.sb(un("vmstage"), [NCH, 128], F32, scope=p3v)
    vmrraw = p.sb(un("vmrraw"), [1, 1024], F32, scope=p3v)
    p.wait("vector", ev, ev.v)
    p.vec("memset", vtmp[:, 0:128], 0.0)
    p.vec("memset", vsh[:, :, 16, :], 0.0)

    with nc.psum_tensor(un("vpps"), [128, S], F32) as vpps, \
         nc.psum_tensor(un("vtpsA"), [128, 128], BF16) as vtpsA, \
         nc.psum_tensor(un("vtpsB"), [128, 128], BF16) as vtpsB:
        vtps = [vtpsA, vtpsB]
        p.wait("tensor", wl[0], wv_ld)
        p.wait("tensor", ldg, ld_eye)
        p.wait("tensor", ev, ev.v)
        p.wait("tensor", dv, dv.v)
        trs = [0, 0]
        e_vt = 0
        dv_v0 = 0
        tr_mm = 0
        for hp in range(NCH):
            p.wait("tensor", ev, e_vt)
            for c in range(NCH):
                for n in range(4):
                    p.mm(vpps[:, 512 * n:512 * (n + 1)],
                         wbuf[:, c, 128 * hp:128 * (hp + 1)],
                         lnq[:, c, 512 * n:512 * (n + 1)],
                         start=(c == 0), stop=(c == NCH - 1),
                         inc=mm if (c == NCH - 1 and n == 3) else None)
            p.wait("scalar", mm, mm.v)
            if hp >= 1:
                p.wait("scalar", dv, dv_v0)
                p.wait("scalar", mm, tr_mm)
            e_vt = p.act(vtmp[:, 128:2176], vpps[:], AF.Identity, bias=bv(hp))
            if dbg == 29:
                dv_v0 = dv.v
                tr_mm = mm.v
                continue
            p.wait("vector", ev, e_vt)
            p.vec("tensor_reduce", vmcol[:, hp:hp + 1].unsqueeze(2),
                  vtmp[:, 128:2176].unsqueeze(1), axis=AX.X, op=ALU.add)
            dv_v0 = p.vec("tensor_copy", v0c[:, hp:hp + 1], vtmp[:, 128:129])
            if dbg == 30:
                tr_mm = mm.v
                continue
            p.wait("tensor", ev, e_vt)
            for j in range(17):
                g = j % 2
                p.wait("tensor", ev, trs[g])
                if j < 16:
                    tr_mm = p.t_transpose(vtps[g][:, :],
                                          vtmp[:, 126 + 128 * j:254 + 128 * j],
                                          eye[:], inc=mm)
                else:
                    tr_mm = p.t_transpose(vtps[g][0:2, :], vtmp[:, 2174:2176],
                                          eye[:], inc=mm)
                p.wait("scalar", mm, tr_mm)
                rows = 128 if j < 16 else 2
                p.act(vsh[0:rows, hp, j, 0:64], vtps[g][0:rows, 0:64], AF.Copy)
                trs[g] = p.act(vsh[0:rows, hp, j, 64:128], vtps[g][0:rows, 64:128],
                               AF.Copy)
        p.wait("tensor", dv, dv_v0)
        if dbg is None or dbg >= 32:
            p.wait("tensor", ev, trs[0])
            p.wait("tensor", ev, trs[1])
            t1 = p.t_transpose(vtps[0][0:NCH, :], v0c[:], eye[:], inc=mm)
            p.wait("scalar", mm, t1)
            e1 = p.act(vstage[:], vtps[0][0:NCH, :], AF.Copy, inc=ev)
            p.wait("gpsimd", ev, e1)
            b1 = p.dma("gpsimd", v0_dram, vstage[:], g0)
    if dbg is None or dbg >= 33:
        with nc.psum_tensor(un("vmtps"), [128, 128], F32) as vmtps:
            p.wait("tensor", ev, ev.v)
            t2 = p.mm(vmtps[0:NCH, :], vmcol[:], eyef[:], start=True, stop=True, inc=mm)
            p.wait("scalar", mm, t2)
            e2 = p.act(vmstage[:], vmtps[0:NCH, :], AF.Copy, inc=ev)
            p.wait("gpsimd", ev, e2)
            b2 = p.dma("gpsimd", vm_dram, vmstage[:], g0)
            p.wait("gpsimd", g0, g0.v)
            b3 = p.dma("gpsimd", v0p[:], v0_dram.rearrange("c w -> (c w)").unsqueeze(0), g0)
            b4 = p.dma("gpsimd", vmrraw[:], vm_dram.rearrange("c w -> (c w)").unsqueeze(0), g0)
            p.wait("scalar", g0, g0.v)
            e3 = p.act(vmr[:], vmrraw[:], AF.Copy, scale=1e-30 / S, inc=ev)
    if upto in (29, 30, 31, 32, 33):
        p3v.close(); p3p.close(); lnscope.close(); wscope.close()
        _finish_dummy(p, xodram, st)
        return p
    p3v.close()
    qkscope = ExitStack()
    qTe = p.sb(un("qTe"), [128, NCH, S], BF16, scope=qkscope)
    qTo = p.sb(un("qTo"), [128, NCH, S], BF16, scope=qkscope)
    p.wait("gpsimd", mm, mm.v)
    wq_ld = load_w(wq_d)
    p.wait("vector", ev, ev.v)
    p.vec("memset", qTe[64:128, :, :], 0.0)
    qz_dv = p.vec("memset", qTo[0:64, :, :], 0.0)
    p.wait("tensor", dv, qz_dv)

    def ev_q(m, ps):
        p.act(qTe[0:64, m, :], ps[0:64, :], AF.Identity, bias=bq(m)[0:64, :])
        return p.act(qTo[64:128, m, :], ps[64:128, :], AF.Identity, bias=bq(m)[64:128, :])

    emit_proj(p, wbuf, wq_ld, wl[0], lnq, ev_q, NCH, mm, ev)
    lnscope.close()
    wscope.close()
    if upto < 4:
        qkscope.close(); p3p.close()
        _finish_dummy(p, xodram, st)
        return p

    p3e = ExitStack()
    B0T = p.sb(un("B0Tb"), [128, 256], F32, scope=p3e)
    B0sh = p.sb(un("B0shb"), [128, 272], F32, scope=p3e)
    k0m = p.sb(un("k0m"), [128, NCH, 16], BF16, scope=p3e)
    q0m = p.sb(un("q0m"), [128, NCH, 16], BF16, scope=p3e)
    e0rows = p.sb(un("e0rows"), [16, S], BF16, scope=p3e)
    e0T = p.sb(un("e0T"), [128, 256], BF16, scope=p3e)
    er0 = p.sb(un("er0"), [128, 17 * 16], BF16, scope=p3e)
    c0sb = p.sb(un("c0sb"), [1, 1024], F32, scope=p3e)
    z0sb = p.sb(un("z0sb"), [1, 16], F32, scope=p3e)
    rz0 = p.sb(un("rz0"), [1, 16], F32, scope=p3e)
    e0sg = p.sb(un("e0sg"), [128, 272], F32, scope=p3e)
    e0sg2 = p.sb(un("e0sg2"), [128, 272], F32, scope=p3e)

    p.wait("sync", mm, mm.v)
    p.wait("sync", ev, ev.v)
    p.dma("sync", B0T[:], B0T_d, ldc)
    ld_tabs = p.dma("sync", B0sh[:], B0sh_d, ldc)
    p.wait("vector", ev, ev.v)
    p.vec("memset", k0m[:], 0.0)
    p.vec("memset", q0m[:], 0.0)
    for c in range(NCH):
        p.vec("tensor_copy", k0m[0:64, c, 2 * c:2 * c + 1], kT[0:64, c, 128:129])
        p.vec("tensor_copy", k0m[64:128, c, 2 * c + 1:2 * c + 2], kT[64:128, c, 128:129])
        p.vec("tensor_copy", q0m[0:64, c, 2 * c:2 * c + 1], qTe[0:64, c, 0:1])
        p.vec("tensor_copy", q0m[64:128, c, 2 * c + 1:2 * c + 2], qTo[64:128, c, 0:1])
    dv_pre = p.vec("memset", er0[:], 0.0)

    with nc.psum_tensor(un("s0ps"), [128, 256], F32) as s0ps, \
         nc.psum_tensor(un("r0ps"), [128, 272], F32) as r0ps, \
         nc.psum_tensor(un("tpsA"), [16, 128], BF16) as tpsA, \
         nc.psum_tensor(un("tpsB"), [16, 128], BF16) as tpsB:
        tps = [tpsA, tpsB]
        p.wait("tensor", dv, dv_pre)
        p.wait("tensor", ev, ev.v)
        p.wait("tensor", ldc, ld_tabs)
        for qt in range(16):
            for c in range(NCH):
                p.mm(s0ps[:, 16 * qt:16 * (qt + 1)], qTe[:, c, 128 * qt:128 * (qt + 1)],
                     k0m[:, c, :], start=(c == 0), stop=False)
                p.mm(s0ps[:, 16 * qt:16 * (qt + 1)], qTo[:, c, 128 * qt:128 * (qt + 1)],
                     k0m[:, c, :], start=False, stop=(c == NCH - 1),
                     inc=mm if (qt == 15 and c == NCH - 1) else None)
        for jt in range(17):
            off = 126 + 128 * jt
            w = 128 if jt < 16 else 2
            if w < 128:
                p.mm(r0ps[:, 16 * jt:16 * (jt + 1)], kT[:, 0, 0:128],
                     q0m[:, 0, :], start=True, stop=False)
            for c in range(NCH):
                p.mm(r0ps[0:w, 16 * jt:16 * (jt + 1)], kT[:, c, off:off + w],
                     q0m[:, c, :], start=(c == 0 and w == 128), stop=(c == NCH - 1 and w == 128),
                     inc=mm if (jt == 16 and c == NCH - 1 and w == 128) else None)
            if w < 128:
                p.mm(r0ps[:, 16 * jt:16 * (jt + 1)], kT[:, 0, 0:128],
                     q0m[:, 0, :], start=False, stop=True,
                     inc=mm if jt == 16 else None)
        p.wait("vector", mm, mm.v)
        p.wait("vector", ev, ev.v)
        p.vec("scalar_tensor_tensor", e0sg[:, 0:256], s0ps[:], 0.125, B0T[:],
              ALU.mult, ALU.add)
        d1 = p.vec("scalar_tensor_tensor", e0sg2[:, 0:256], e0sg[:, 0:256], GATEK,
                   e0sg[:, 0:256], ALU.mult, ALU.min)
        p.wait("scalar", dv, d1)
        e1 = p.act(e0T[:], e0sg2[:, 0:256], AF.Exp)
        p.wait("vector", ev, e1)
        p.vec("scalar_tensor_tensor", e0sg[:], r0ps[:], 0.125, B0sh[:],
              ALU.mult, ALU.add)
        d2 = p.vec("scalar_tensor_tensor", e0sg2[:], e0sg[:], GATEK,
                   e0sg[:], ALU.mult, ALU.min)
        p.wait("scalar", dv, d2)
        e2r = p.act(er0[:], e0sg2[:], AF.Exp)
        tevs = [0, 0]
        for qt in range(16):
            g = qt % 2
            p.wait("tensor", ev, max(e1, tevs[g]))
            tq = p.t_transpose(tps[g][:, :], e0T[:, 16 * qt:16 * (qt + 1)], eye[:], inc=mm)
            p.wait("scalar", mm, tq)
            tevs[g] = p.act(e0rows[:, 128 * qt:128 * (qt + 1)], tps[g][:, :], AF.Copy)
        p.wait("gpsimd", ev, max(tevs))
        g_e0 = p.dma("gpsimd", e0_dram, e0rows[:], g0)

    with nc.psum_tensor(un("c0ps"), [1, 1024], F32) as c0ps, \
         nc.psum_tensor(un("z0ps"), [1, 16], F32) as z0ps:
        p.wait("tensor", ev, ev.v)
        p.wait("tensor", dv, dv.v)
        for h in range(H):
            hp = h // 2
            o64 = 64 * (h % 2)
            for jt in range(17):
                p.mm(c0ps[:, 64 * h:64 * h + 64], er0[:, 16 * jt + h:16 * jt + h + 1],
                     vsh[:, hp, jt, o64:o64 + 64], start=(jt == 0), stop=False)
                p.mm(z0ps[:, h:h + 1], er0[:, 16 * jt + h:16 * jt + h + 1],
                     common["ones_colbf"][:], start=(jt == 0), stop=(jt == 16))
            mz = p.mm(c0ps[:, 64 * h:64 * h + 64], common["one1bf"][:],
                      vmr[:, 64 * h:64 * h + 64], start=False, stop=True,
                      inc=mm if h == H - 1 else None)
        p.wait("scalar", mm, mz)
        p.act(c0sb[:], c0ps[:], AF.Copy)
        ez0 = p.act(z0sb[:], z0ps[:], AF.Copy)
    p.wait("vector", ev, ez0)
    d3 = p.vec("tensor_scalar", rz0[:], z0sb[:], 1e-30, None, ALU.add)
    p.wait("scalar", dv, d3)
    p.act(z0sb[:], rz0[:], AF.Ln)
    erz = p.act(rz0[:], z0sb[:], AF.Exp, scale=-1.0)
    with nc.psum_tensor(un("ovps"), [64, 16], F32) as ovps:
        p.wait("tensor", ev, erz)
        for h in range(H):
            mv = p.mm(ovps[:, h:h + 1], c0sb[:, 64 * h:64 * h + 64], rz0[:, h:h + 1],
                      start=True, stop=True, inc=mm if h == H - 1 else None)
        p.wait("scalar", mm, mv)
        e_ov = p.act(ov[:], ovps[:], AF.Copy)
    p3e.close()
    if upto < 5:
        qkscope.close(); p3p.close()
        _finish_dummy(p, xodram, st)
        return p

    p3b = ExitStack()
    Bfirst = p.sb(un("Bfirstb"), [128, 2048], F32, scope=p3b)
    Blast = p.sb(un("Blastb"), [128, 2048], F32, scope=p3b)
    e0hb = p.sb(un("e0hb"), [1, S], BF16, scope=p3b)
    p.wait("sync", mm, mm.v)
    p.wait("sync", ev, ev.v)
    p.wait("sync", dv, dv.v)
    p.wait("gpsimd", mm, mm.v)
    p.wait("gpsimd", ev, ev.v)
    p.wait("gpsimd", g0, g0.v)
    p.dma("sync", Bfirst[:], Bfirst_d, ldc)
    ld_tabs2a = p.dma("sync", Blast[:], Blast_d, ldc)

    p4 = ExitStack()
    zrow = p.sb(un("zrow"), [1, 1024], F32, scope=p4)
    Rsb = p.sb(un("Rsb"), [64, 1024], F32, scope=p4)
    oTtmp = p.sb(un("oTtmp"), [64, S], BF16, scope=p4)
    Pb = p.sb(un("Pb"), [128, 2, 1024], BF16, scope=p4)
    sgb = p.sb(un("sgb"), [128, 512], F32, scope=p4)
    agb = p.sb(un("agb"), [128, 512], F32, scope=p4)

    with nc.psum_tensor(un("Sps"), [128, 1024], F32) as Sps, \
         nc.psum_tensor(un("ctxps"), [64, 1024], F32) as ctxps, \
         nc.psum_tensor(un("zps"), [1, 1024], F32) as zps, \
         nc.psum_tensor(un("Rps"), [64, 1024], F32) as Rps:
        p.wait("tensor", ev, e_ov)
        p.wait("tensor", dv, dv.v)
        p.wait("vector", ldc, ld_tabs2a)
        p.wait("gpsimd", g0, g_e0)
        e0ld = p.dma("gpsimd", e0hb[:], e0_dram[0:1, :], e0s)
        last_S_dv = 0
        last_exp = [0, 0]
        pv_done = 0
        norm_dv = 0
        rbc_ev = 0
        oT_dma = []
        for h in range(H):
            hp, o64 = h // 2, 64 * (h % 2)
            qT_h = qTe if h % 2 == 0 else qTo
            p.wait("tensor", e0s, e0ld)
            for half in range(2):
                for g2 in range(2):
                    g = 2 * half + g2
                    Btab = Bfirst if g < 2 else Blast
                    p.wait("tensor", dv, last_S_dv)
                    for ti in range(4):
                        t = 4 * g + ti
                        off = 126 + 128 * t
                        p.mm(Sps[:, 256 * ti:256 * ti + 128],
                             kT[:, hp, off:off + 128],
                             qT_h[:, hp, 128 * t:128 * (t + 1)],
                             start=True, stop=True)
                        if t < 15:
                            p.mm(Sps[:, 256 * ti + 128:256 * (ti + 1)],
                                 kT[:, hp, off + 128:off + 256],
                                 qT_h[:, hp, 128 * t:128 * (t + 1)],
                                 start=True, stop=True,
                                 inc=mm if ti == 3 else None)
                        else:
                            p.mm(Sps[:, 256 * ti + 128:256 * (ti + 1)],
                                 kT[:, hp, 0:128],
                                 qT_h[:, hp, 128 * t:128 * (t + 1)],
                                 start=True, stop=False)
                            p.mm(Sps[0:2, 256 * ti + 128:256 * (ti + 1)],
                                 kT[:, hp, off + 128:off + 130],
                                 qT_h[:, hp, 128 * t:128 * (t + 1)],
                                 start=False, stop=False)
                            p.mm(Sps[:, 256 * ti + 128:256 * (ti + 1)],
                                 kT[:, hp, 0:128],
                                 qT_h[:, hp, 128 * t:128 * (t + 1)],
                                 start=False, stop=True,
                                 inc=mm if ti == 3 else None)
                    Mg = mm.v
                    p.wait("vector", mm, Mg)
                    p.wait("vector", ev, max(last_exp))
                    p.wait("scalar", mm, pv_done)
                    for n2 in range(2):
                        Bsl = Btab[:, 256 * ((4 * g) % 8) + 512 * n2:
                                   256 * ((4 * g) % 8) + 512 * (n2 + 1)]
                        p.wait("vector", ev, max(last_exp))
                        p.vec("scalar_tensor_tensor", sgb[:], Sps[:, 512 * n2:512 * (n2 + 1)],
                              0.125, Bsl, ALU.mult, ALU.add)
                        dgg = p.vec("scalar_tensor_tensor", agb[:], sgb[:], GATEK, sgb[:],
                                    ALU.mult, ALU.min)
                        p.wait("scalar", dv, dgg)
                        last_exp[g % 2] = p.act(Pb[:, g % 2, 512 * n2:512 * (n2 + 1)],
                                                agb[:], AF.Exp)
                    last_S_dv = dgg
                p.wait("tensor", ev, last_exp[0])
                p.wait("tensor", ev, last_exp[1])
                if pv_done:
                    p.wait("tensor", dv, norm_dv)
                for g2 in range(2):
                    g = 2 * half + g2
                    for ti in range(4):
                        t = 4 * g + ti
                        cc = 128 * (t % 8)
                        p.mm(ctxps[:, cc:cc + 128], vsh[:, hp, t, o64:o64 + 64],
                             Pb[:, g % 2, 256 * ti:256 * ti + 128], start=True, stop=False)
                        p.mm(ctxps[:, cc:cc + 128], vsh[:, hp, t + 1, o64:o64 + 64],
                             Pb[:, g % 2, 256 * ti + 128:256 * (ti + 1)],
                             start=False, stop=False)
                        p.mm(ctxps[:, cc:cc + 128], v0p[:, 64 * h:64 * h + 64],
                             e0hb[:, 128 * t:128 * (t + 1)], start=False, stop=False)
                        p.mm(ctxps[:, cc:cc + 128], vmr[:, 64 * h:64 * h + 64],
                             common["ones_rowbf"][:], start=False, stop=True)
                        p.mm(zps[:, cc:cc + 128], common["ones_colbf"][:],
                             Pb[:, g % 2, 256 * ti:256 * ti + 128], start=True, stop=False)
                        p.mm(zps[:, cc:cc + 128], common["ones_colbf"][:],
                             Pb[:, g % 2, 256 * ti + 128:256 * (ti + 1)],
                             start=False, stop=False)
                        p.mm(zps[:, cc:cc + 128], common["one1bf"][:],
                             e0hb[:, 128 * t:128 * (t + 1)], start=False, stop=True,
                             inc=mm if (g2 == 1 and ti == 3) else None)
                pv_done = mm.v
                p.wait("scalar", mm, pv_done)
                ez = p.act(zrow[:], zps[:], AF.Identity)
                p.wait("vector", ev, ez)
                dz = p.vec("tensor_scalar", zrow[:], zrow[:], 1e-30, None, ALU.add)
                p.wait("scalar", dv, dz)
                p.act(zrow[:], zrow[:], AF.Ln)
                ez2 = p.act(zrow[:], zrow[:], AF.Exp, scale=-1.0)
                if dbgz is not None:
                    p.wait("sync", ev, ez2)
                    p.dma("sync", dbgz[h:h + 1, half, :].rearrange("a b -> a b") if False else dbgz[h, half, :].unsqueeze(0), zrow[:], st[0])
                p.wait("tensor", ev, ez2)
                p.wait("tensor", ev, rbc_ev)
                for n in range(2):
                    mv = p.mm(Rps[:, 512 * n:512 * (n + 1)], common["ones_rowf"][0:1, 0:64],
                              zrow[:, 512 * n:512 * (n + 1)], start=True, stop=True,
                              inc=mm if n == 1 else None)
                p.wait("scalar", mm, mv)
                rbc_ev = p.act(Rsb[:], Rps[:], AF.Copy)
                if oT_dma and half == 0:
                    p.wait("vector", oTs, oT_dma[-1])
                p.wait("vector", ev, rbc_ev)
                norm_dv = p.vec("tensor_mul", oTtmp[:, 1024 * half:1024 * (half + 1)],
                                ctxps[:], Rsb[:])
            p.wait("vector", ev, e_ov)
            norm_dv = p.vec("tensor_copy", oTtmp[:, 0:1], ov[:, h:h + 1])
            p.wait("gpsimd", dv, norm_dv)
            o64h = 64 * (h % 2)
            oT_dma.append(p.dma("gpsimd", oT_dram[o64h:o64h + 64, hp, :], oTtmp[:], oTs))
            if h + 1 < H:
                p.wait("gpsimd", mm, pv_done)
                e0ld = p.dma("gpsimd", e0hb[:], e0_dram[h + 1:h + 2, :], e0s)
    if dbgv is not None:
        p.wait("sync", dv, dv.v)
        p.dma("sync", dbgv, vsh.ap().rearrange("p a b c -> p (a b c)"), st[0])
        p.dma("sync", dbgr[0:1, :], v0p[:], st[0])
        p.dma("sync", dbgr[1:2, :], vmr[:], st[0])
    p4.close()
    p3b.close()
    qkscope.close()
    p3p.close()
    if upto < 6:
        _finish_dummy(p, xodram, st)
        return p

    p5 = ExitStack()
    wbuf2 = p.sb(un("wbuf2"), [128, NCH, D], BF16, scope=p5, side="right")
    y = p.sb(un("yattn"), [128, NCH, S], F32)
    ostr = p.sb(un("ostr"), [128, 2, S], BF16, scope=p5)
    osm = [p.sem("osA"), p.sem("osB")]
    p.wait("gpsimd", mm, mm.v)
    p.wait("gpsimd", dv, dv.v)
    p.wait("gpsimd", ev, ev.v)
    for c in range(NCH):
        wo_ld = p.dma("gpsimd", wbuf2[:, c, :], wo_d[c], wl[1])
    p.wait("sync", oTs, oTs.v)
    ev_yv = None
    with nc.psum_tensor(un("prps5"), [128, 2, S], F32) as prps:
        p.wait("tensor", wl[1], wo_ld)
        p.wait("tensor", ev, ev.v)
        seq = [(m, c) for m in range(NCH) for c in range(NCH)]
        loads, cons = {}, {}
        evs = []
        for i in range(2):
            loads[i] = p.dma("sync", ostr[:, i % 2, :], oT_dram[:, seq[i][1], :], osm[i % 2])
        for i, (m, c) in enumerate(seq):
            g = m % 2
            slot = i % 2
            if c == 0 and m >= 2:
                p.wait("tensor", ev, evs[m - 2])
            p.wait("tensor", osm[slot], loads[i])
            for n in range(4):
                p.mm(prps[:, g, 512 * n:512 * (n + 1)],
                     wbuf2[:, c, 128 * m:128 * (m + 1)],
                     ostr[:, slot, 512 * n:512 * (n + 1)],
                     start=(c == 0), stop=(c == NCH - 1),
                     inc=mm if n == 3 else None)
            cons[i] = mm.v
            j = i + 2
            if j < len(seq):
                p.wait("sync", mm, cons[i])
                loads[j] = p.dma("sync", ostr[:, j % 2, :], oT_dram[:, seq[j][1], :], osm[j % 2])
            if c == NCH - 1:
                p.wait("scalar", mm, mm.v)
                evs.append(p.act(y[:, m, :], prps[:, g, :], AF.Identity, bias=bo(m)))
        ev_yv = evs[-1]
    p5.close()
    if upto < 7:
        _finish_dummy(p, xodram, st)
        return p

    emit_postln_residual(p, y, ev_yv, xdram, xodram, common, lx, mm, ev, dv, st)
    p.wait("sync", st[0], st[0].v)
    p.wait("sync", st[1], st[1].v)
    return p


def build_l0_moe(nc, upto=7):
    p = Prog(nc)
    xdram = nc.dram_tensor("x", [128, NCH, S], F32, kind="ExternalInput").ap()
    xodram = nc.dram_tensor("xo", [128, NCH, S], F32, kind="ExternalOutput").ap()
    emit_l0_moe(p, xdram, xodram, upto)
    p.run()


def emit_l0_moe(p, xdram, xodram, upto=7):
    nc = p.nc
    gw_d = nc.dram_tensor("gwc", [NCH, 128, 8], F32, kind="ExternalInput").ap()
    W1_d = nc.dram_tensor("W1c", [9, NCH, 128, FF], F32, kind="ExternalInput").ap()
    W2_d = nc.dram_tensor("W2c", [9, 32, 128, D], F32, kind="ExternalInput").ap()
    b1_d = nc.dram_tensor("b1c", [128, 9, 32], F32, kind="ExternalInput").ap()
    bs2_d = nc.dram_tensor("bs2c", [128, 8], F32, kind="ExternalInput").ap()
    b2_d = nc.dram_tensor("b2r", [8, D], F32, kind="ExternalInput").ap()
    sel_d = nc.dram_tensor("selc", [8, 8 * 128], F32, kind="ExternalInput").ap()
    eye_d = nc.dram_tensor("eyeB", [128, 128], F32, kind="ExternalInput").ap()
    scd_t = nc.dram_tensor("scd", [8, S], F32, kind="Internal")
    gath_t = nc.dram_tensor("gath", [128, 1024], F32, kind="Internal")
    gdbg = nc.dram_tensor("gdbg", [8, S], F32,
                          kind="ExternalOutput").ap() if upto in (0, 1) else None
    ldbg = nc.dram_tensor("ldbg", [128, 128], F32,
                          kind="ExternalOutput").ap() if upto == 0 else None
    tdbg = nc.dram_tensor("tdbg", [1, 1], F32,
                          kind="ExternalOutput").ap() if upto in (0, 1) else None

    lx = [p.sem("lx0"), p.sem("lx1")]
    wl2 = [p.sem("wl20"), p.sem("wl21")]
    mm = p.sem("mm")
    st = [p.sem("st0"), p.sem("st1")]
    g0 = p.sem("g0")
    cc = p.sem("cc")
    ag = p.sem("ag")
    dv, ev = p.ec["vector"], p.ec["scalar"]

    common = make_common(p)
    ones_colf = p.sb(un("ones_colf"), [128, 1], F32)
    ones8cf = p.sb(un("ones8cf"), [8, 1], F32)
    ones_row8f = p.sb(un("ones_row8f"), [1, 8], F32)
    ones_rowW = p.sb(un("ones_rowW"), [1, 512], F32)
    p.vec("memset", ones_colf[:], 1.0)
    p.vec("memset", ones8cf[:], 1.0)
    p.vec("memset", ones_row8f[:], 1.0)
    p.vec("memset", ones_rowW[:], 1.0)

    sc_e = ExitStack()
    lnq = p.sb("lnq", [128, NCH, S], BF16, scope=sc_e)
    gwsb = p.sb("gwsb", [128, NCH, 8], F32, scope=sc_e)
    eyef = p.sb("eyef", [128, 128], F32, scope=sc_e)
    b1sb = p.sb("b1sb", [128, 9, 32], F32, scope=sc_e)
    bs2sb = p.sb("bs2sb", [128, 8], F32, scope=sc_e)
    b2sb = p.sb("b2sb", [8, D], BF16, scope=sc_e)
    selsb = p.sb("selsb", [8, 8 * 128], BF16, scope=sc_e)
    gatesT = p.sb("gatesT", [8, S], BF16, scope=sc_e)
    bs_lo = p.sb("bs_lo", [1, 1], F32, scope=sc_e)
    logits = p.sb("logitsb", [128, 128], F32, scope=sc_e)

    FB = 512
    NFB = FF // FB
    NFT = FB // 128
    steps = [(e, fb) for e in range(9) for fb in range(NFB)]
    NST = len(steps)
    sc_w = ExitStack()
    W1sb = p.sb("W1sb", [128, 2, NCH, FB], BF16, scope=sc_w, side="right")
    W2sb = p.sb("W2sb", [128, 2, NFT, 1024], BF16, scope=sc_w, side="right")

    def load_blk(slot, e, fb):
        for dc in range(NCH):
            p.dma("gpsimd", W1sb[:, slot, dc, :],
                  W1_d[e, dc, :, FB * fb:FB * (fb + 1)], wl2[slot])
        for fc in range(NFT):
            v = p.dma("gpsimd", W2sb[:, slot, fc, :],
                      W2_d[e, NFT * fb + fc], wl2[slot])
        return v

    wld = [0] * NST
    wld[0] = load_blk(0, 0, 0)
    for c in range(NCH):
        ld_gw = p.dma("sync", gwsb[:, c, :], gw_d[c], g0)
    p.dma("sync", eyef[:], eye_d, g0)
    p.dma("sync", b1sb[:], b1_d, g0)
    ld_small = p.dma("sync", bs2sb[:], bs2_d, g0)
    p.dma("gpsimd", b2sb[:], b2_d, wl2[0])
    ld_sel = p.dma("gpsimd", selsb[:], sel_d, wl2[0])
    wld[0] = ld_sel

    ps_a = ExitStack()
    LGref = [None, None, None]
    lg_evac = [0, None]

    def rg2(c, xslot, ldv):
        if LGref[0] is None:
            LGref[0] = ps_a.enter_context(nc.psum_tensor("LGps", [128, 128], F32))
            LGref[1] = ps_a.enter_context(nc.psum_tensor("RSps", [128, 128], F32))
            LGref[2] = ps_a.enter_context(nc.psum_tensor("G1ps", [1, 8], F32))
        LGps = LGref[0]
        p.wait("tensor", lx[c % 2], ldv)
        if c == 0:
            p.wait("tensor", g0, ld_small)
            p.wait("tensor", ev, ev.v)
            p.wait("tensor", dv, dv.v)
        else:
            p.wait("tensor", lg_evac[1], lg_evac[0])
        for ts in range(16):
            p.mm(LGps[:, 8 * ts:8 * (ts + 1)], xslot[:, 128 * ts:128 * (ts + 1)],
                 gwsb[:, c, :], start=True, stop=True,
                 inc=mm if ts == 15 else None)
        if c == 0:
            p.wait("scalar", mm, mm.v)
            lg_evac[0] = p.act(logits[:], LGps[:], AF.Copy)
            lg_evac[1] = ev
        else:
            p.wait("vector", mm, mm.v)
            lg_evac[0] = p.vec("tensor_add", logits[:], logits[:], LGps[:])
            lg_evac[1] = dv

    kp = ExitStack()
    murow, Abcs = emit_ln_featmajor(p, xdram, lnq, common, lx, mm, ev, dv,
                                    raw_gate2=rg2, keep=kp)
    v_lnq = dv.v
    yacc = p.sb("yacc", [128, NCH, S], F32, scope=sc_e)
    sc_g = ExitStack()
    esb = p.sb("esb", [128, 128], F32, scope=sc_g)
    scT = p.sb("scT", [8, S], F32, scope=sc_g)
    zrow = p.sb("zrow", [1, S], F32, scope=sc_g)
    taurow = p.sb("taurow", [1, S], F32, scope=sc_g)
    allsc = p.sb("allsc", [128, 1024], F32, scope=sc_g)
    ind8 = p.sb("ind8", [8, S], F32, scope=sc_g)
    shiftb = p.sb("shiftb", [128, 1024], F32, scope=sc_g)
    negmidcol = p.sb("negmidcol", [128, 1], F32, scope=sc_g)
    cntcol = p.sb("cntcol", [128, 1], F32, scope=sc_g)
    bs_hi = p.sb("bs_hi", [1, 1], F32, scope=sc_g)
    bs_mid = p.sb("bs_mid", [1, 1], F32, scope=sc_g)
    bs_nm = p.sb("bs_nm", [1, 1], F32, scope=sc_g)
    bs_cond = p.sb("bs_cond", [1, 1], F32, scope=sc_g)
    bs_d1 = p.sb("bs_d1", [1, 1], F32, scope=sc_g)
    bs_d2 = p.sb("bs_d2", [1, 1], F32, scope=sc_g)

    LGps, RSps, G1ps = LGref
    for c in range(NCH):
        p.mm(G1ps[:], ones_colf[:], gwsb[:, c, :], start=(c == 0), stop=(c == NCH - 1),
             inc=mm if c == NCH - 1 else None)
    p.wait("scalar", mm, mm.v)
    g1neg = p.sb("g1neg", [1, 8], F32, scope=sc_g)
    e_g1 = p.act(g1neg[:], G1ps[:], AF.Copy, scale=-1.0)
    p.wait("tensor", ev, e_g1)
    p.wait("tensor", lg_evac[1], lg_evac[0])
    for ts in range(16):
        p.mm(LGps[:, 8 * ts:8 * (ts + 1)], murow[:, 128 * ts:128 * (ts + 1)],
             g1neg[:], start=True, stop=True,
             inc=mm if ts == 15 else None)
        p.mm(RSps[:, 8 * ts:8 * (ts + 1)], Abcs[0:1, 128 * ts:128 * (ts + 1)],
             ones_row8f[:], start=True, stop=True,
             inc=mm if ts == 15 else None)
    p.wait("scalar", mm, mm.v)
    rstile = p.sb("rstile", [128, 128], F32, scope=sc_g)
    e_rs = p.act(rstile[:], RSps[:], AF.Copy)
    p.wait("vector", ev, e_rs)
    p.vec("tensor_add", logits[:], logits[:], LGps[:])
    v_lg = p.vec("tensor_mul", logits[:], logits[:], rstile[:])
    kp.close()

    p.wait("scalar", dv, v_lg)
    e_es = p.act(esb[:], logits[:], AF.Exp)
    ps_a.close()
    with nc.psum_tensor("TEpsA", [8, 128], F32) as TEpsA, \
         nc.psum_tensor("TEpsB", [8, 128], F32) as TEpsB, \
         nc.psum_tensor("Zps", [1, S], F32) as Zps:
        p.wait("tensor", ev, e_es)
        p.wait("tensor", g0, ld_small)
        te = [0, 0]
        for ts in range(16):
            tp = TEpsA if ts % 2 == 0 else TEpsB
            if te[ts % 2]:
                p.wait("tensor", ev, te[ts % 2])
            mv = p.t_transpose(tp[0:8, :], esb[:, 8 * ts:8 * (ts + 1)], eyef[:], inc=mm)
            p.wait("scalar", mm, mv)
            te[ts % 2] = p.act(scT[:, 128 * ts:128 * (ts + 1)], tp[0:8, :], AF.Copy)
        p.wait("tensor", ev, max(te))
        for n in range(4):
            mv = p.mm(Zps[:, 512 * n:512 * (n + 1)], ones8cf[:],
                      scT[:, 512 * n:512 * (n + 1)], start=True, stop=True,
                      inc=mm if n == 3 else None)
        p.wait("scalar", mm, mv)
        p.act(zrow[:], Zps[:], AF.Ln)
        e_rz = p.act(zrow[:], zrow[:], AF.Exp, scale=-1.0)
    with nc.psum_tensor("RZps", [8, S], F32) as RZps:
        p.wait("tensor", ev, e_rz)
        for n in range(4):
            mv = p.mm(RZps[:, 512 * n:512 * (n + 1)], ones_row8f[:],
                      zrow[:, 512 * n:512 * (n + 1)], start=True, stop=True,
                      inc=mm if n == 3 else None)
        p.wait("vector", mm, mv)
        v_sc = p.vec("tensor_mul", scT[:], scT[:], RZps[:])
    if upto == 0:
        p.vec("memset", bs_lo[:], 0.0)
        p.wait("sync", dv, v_sc)
        fsb = p.sb(un("gdbgf"), [8, S], F32, scope=sc_g)
        vv = p.vec("tensor_copy", fsb[:], scT[:])
        p.wait("sync", dv, vv)
        p.dma("sync", gdbg, fsb[:], st[0])
        p.dma("sync", tdbg, bs_lo[:], st[0])
        p.dma("sync", ldbg, logits[:], st[0])
        p.wait("sync", st[0], st[0].v)
        sc_g.close()
        sc_e.close()
        _finish_dummy(p, xodram, st)
        return
    p.wait("sync", dv, v_sc)
    v_out = p.dma("sync", scd_t.ap(), scT[:], ag)
    p.wait("gpsimd", ag, v_out)
    p.on("gpsimd", lambda e: e.collective_compute(
        "AllGather", ALU.bypass, replica_groups=[list(range(8))],
        ins=[scd_t.ap().opt()], outs=[gath_t.ap().opt()]).then_inc(cc.h, 1))
    cc.bump(1)
    p.wait("sync", cc, 1)
    v_all = p.dma("sync", allsc[:], gath_t.ap(), ag)

    KTOP = 32768
    p.vec("memset", bs_lo[:], 0.0)
    p.vec("memset", bs_hi[:], 1.1)
    p.vec("memset", bs_mid[:], 0.55)
    with nc.psum_tensor("NMps", [128, 1], F32) as NMps, \
         nc.psum_tensor("CTps", [1, 1], F32) as CTps:
        p.wait("vector", ag, v_all)
        e_evac = 0
        ct_read = 0
        for it in range(34):
            v_nm = p.vec("tensor_scalar", bs_nm[:], bs_mid[:], -1.0, None, ALU.mult)
            p.wait("tensor", dv, v_nm)
            if e_evac:
                p.wait("tensor", ev, e_evac)
            m1 = p.mm(NMps[:], common["ones_rowf"][:], bs_nm[:], start=True,
                      stop=True, inc=mm)
            p.wait("scalar", mm, m1)
            e_evac = p.act(negmidcol[:], NMps[:], AF.Copy)
            e_sh = p.act(shiftb[:], allsc[:], AF.Identity, bias=negmidcol[:])
            p.wait("vector", ev, e_sh)
            v_ct = p.vec("tensor_scalar", shiftb[:], shiftb[:], 0.0, None,
                         ALU.is_ge, ALU.add, accum_out=cntcol[:])
            p.wait("tensor", dv, v_ct)
            if ct_read:
                p.wait("tensor", dv, ct_read)
            m2 = p.mm(CTps[:], cntcol[:], ones_colf[:], start=True, stop=True, inc=mm)
            p.wait("vector", mm, m2)
            p.vec("tensor_scalar", bs_cond[:], CTps[:], KTOP - 0.5, None, ALU.is_ge)
            p.vec("tensor_sub", bs_d1[:], bs_mid[:], bs_lo[:])
            p.vec("tensor_mul", bs_d1[:], bs_d1[:], bs_cond[:])
            p.vec("tensor_add", bs_lo[:], bs_lo[:], bs_d1[:])
            p.vec("tensor_sub", bs_d2[:], bs_hi[:], bs_mid[:])
            p.vec("tensor_mul", bs_d2[:], bs_d2[:], bs_cond[:])
            p.vec("tensor_add", bs_hi[:], bs_mid[:], bs_d2[:])
            p.vec("tensor_add", bs_mid[:], bs_lo[:], bs_hi[:])
            ct_read = p.vec("tensor_scalar", bs_mid[:], bs_mid[:], 0.5, None, ALU.mult)
        p.wait("tensor", dv, ct_read)
    with nc.psum_tensor("TRps", [1, S], F32) as TRps:
        for n in range(4):
            mv = p.mm(TRps[:, 512 * n:512 * (n + 1)], bs_lo[:], ones_rowW[:],
                      start=True, stop=True, inc=mm if n == 3 else None)
        p.wait("scalar", mm, mv)
        e_tr = p.act(taurow[:], TRps[:], AF.Copy)
    with nc.psum_tensor("TPps", [8, S], F32) as TPps:
        p.wait("tensor", ev, e_tr)
        for n in range(4):
            mv = p.mm(TPps[:, 512 * n:512 * (n + 1)], ones_row8f[:],
                      taurow[:, 512 * n:512 * (n + 1)], start=True, stop=True,
                      inc=mm if n == 3 else None)
        p.wait("vector", mm, mv)
        p.vec("tensor_tensor", ind8[:], scT[:], TPps[:], ALU.is_ge)
        v_gates = p.vec("tensor_mul", gatesT[:], scT[:], ind8[:])

    if upto != 1:
        sc_g.close()
        sc_h = ExitStack()
        hg = p.sb("hgsb", [128, NFT, S], BF16, scope=sc_h)
        htmp = p.sb("htmp", [128, 2, 512], BF16, scope=sc_h)
        Gsb = p.sb("Gsb", [128, S], BF16, scope=sc_h)

    if upto == 1:
        p.wait("sync", dv, v_gates)
        fsb = p.sb(un("gdbgf"), [8, S], F32, scope=sc_g)
        vv = p.vec("tensor_copy", fsb[:], gatesT[:])
        p.wait("sync", dv, vv)
        p.dma("sync", gdbg, fsb[:], st[0])
        p.dma("sync", tdbg, bs_lo[:], st[0])
        p.wait("sync", st[0], st[0].v)
        sc_g.close()
        sc_e.close()
        _finish_dummy(p, xodram, st)
        return

    gpc = p.ec["gpsimd"]
    with nc.psum_tensor("GTps", [128, 512], F32) as GTps, \
         nc.psum_tensor("Hps", [128, 2, 512], F32) as Hps, \
         nc.psum_tensor("Yps", [128, 4, 512], F32) as Yps:
        mmdone = [0] * NST
        gt_ev = 0
        vhg_last = 0
        ehg_last = 0
        y_mm_last = 0
        hev = [0, 0]
        yev = [None] * 4
        for i, (e, fb) in enumerate(steps):
            slot = i % 2
            if i + 1 < NST:
                en, fbn = steps[i + 1]
                if i >= 1:
                    p.wait("gpsimd", mm, mmdone[i - 1])
                wld[i + 1] = load_blk(slot ^ 1, en, fbn)
            p.wait("tensor", wl2[slot], wld[i])
            if fb == 0 and e < 8:
                if e == 0:
                    p.wait("tensor", dv, v_gates)
                if vhg_last:
                    p.wait("scalar", dv, vhg_last)
                for n in range(4):
                    if gt_ev:
                        p.wait("tensor", ev, gt_ev)
                    mv = p.mm(GTps[:], selsb[0:8, 128 * e:128 * (e + 1)],
                              gatesT[0:8, 512 * n:512 * (n + 1)], start=True,
                              stop=True, inc=mm)
                    p.wait("scalar", mm, mv)
                    gt_ev = p.act(Gsb[:, 512 * n:512 * (n + 1)], GTps[:], AF.Copy)
            if i == 0:
                p.wait("tensor", dv, v_lnq)
            first_hg = True
            for ft in range(NFT):
                for n in range(4):
                    g = (ft * 4 + n) % 2
                    if hev[g]:
                        p.wait("tensor", ev, hev[g])
                    for dc in range(NCH):
                        mv = p.mm(Hps[:, g, :],
                                  W1sb[:, slot, dc, 128 * ft:128 * (ft + 1)],
                                  lnq[:, dc, 512 * n:512 * (n + 1)],
                                  start=(dc == 0), stop=(dc == NCH - 1),
                                  inc=mm if dc == NCH - 1 else None)
                    p.wait("scalar", mm, mv)
                    if first_hg and y_mm_last:
                        if e == 8:
                            p.wait("scalar", mm, y_mm_last)
                        else:
                            p.wait("vector", mm, y_mm_last)
                    bcol = b1sb[:, e, NFT * fb + ft:NFT * fb + ft + 1]
                    if e == 8:
                        hev[g] = p.act(hg[:, ft, 512 * n:512 * (n + 1)],
                                       Hps[:, g, :], AF.Gelu_apprx_tanh, bias=bcol)
                        ehg_last = hev[g]
                    else:
                        if vhg_last:
                            p.wait("scalar", dv, vhg_last)
                        hev[g] = p.act(htmp[:, g, :], Hps[:, g, :],
                                       AF.Gelu_apprx_tanh, bias=bcol)
                        p.wait("vector", ev, hev[g])
                        if first_hg:
                            p.wait("vector", ev, gt_ev)
                        vhg_last = p.vec("tensor_mul",
                                         hg[:, ft, 512 * n:512 * (n + 1)],
                                         htmp[:, g, :], Gsb[:, 512 * n:512 * (n + 1)])
                    first_hg = False
            if e == 8:
                p.wait("tensor", ev, ehg_last)
            else:
                p.wait("tensor", dv, vhg_last)
            for dt in range(NCH):
                for n in range(4):
                    g = (dt * 4 + n) % 4
                    if yev[g] is not None:
                        p.wait("tensor", yev[g][0], yev[g][1])
                    extra = (e == 0 and fb == 0)
                    for fc in range(NFT):
                        mv = p.mm(Yps[:, g, :],
                                  W2sb[:, slot, fc, 128 * dt:128 * (dt + 1)],
                                  hg[:, fc, 512 * n:512 * (n + 1)],
                                  start=(fc == 0),
                                  stop=(fc == NFT - 1 and not extra),
                                  inc=mm if (fc == NFT - 1 and not extra) else None)
                    if extra:
                        mv = p.mm(Yps[:, g, :], b2sb[0:8, 128 * dt:128 * (dt + 1)],
                                  gatesT[0:8, 512 * n:512 * (n + 1)],
                                  start=False, stop=True, inc=mm)
                    if e == 0 and fb == 0:
                        p.wait("scalar", mm, mv)
                        yev[g] = (ev, p.act(yacc[:, dt, 512 * n:512 * (n + 1)],
                                            Yps[:, g, :], AF.Identity,
                                            bias=bs2sb[:, dt:dt + 1]))
                    else:
                        p.wait("vector", mm, mv)
                        yev[g] = (dv, p.vec("tensor_add",
                                            yacc[:, dt, 512 * n:512 * (n + 1)],
                                            yacc[:, dt, 512 * n:512 * (n + 1)],
                                            Yps[:, g, :]))
                    y_mm_last = mv
            mmdone[i] = mm.v
    sc_h.close()
    sc_w.close()

    emit_postln_residual(p, yacc, 0, xdram, xodram, common, lx, mm, ev, dv, st)
    p.wait("sync", st[0], st[0].v)
    p.wait("sync", st[1], st[1].v)
    sc_e.close()


def build_l1_attn(nc):
    p = Prog(nc)
    xdram = nc.dram_tensor("x", [128, NCH, S], F32, kind="ExternalInput").ap()
    xodram = nc.dram_tensor("xo", [128, NCH, S], F32, kind="ExternalOutput").ap()
    emit_l1_attn(p, xdram, xodram)
    p.run()


def emit_l1_attn(p, xdram, xodram):
    nc = p.nc
    wq_d = nc.dram_tensor("wqcC", [NCH, 128, D], F32, kind="ExternalInput").ap()
    wk_d = nc.dram_tensor("wkcC", [NCH, 128, D], F32, kind="ExternalInput").ap()
    wv_d = nc.dram_tensor("wvcC", [NCH, 128, D], F32, kind="ExternalInput").ap()
    wo_d = nc.dram_tensor("wocC", [NCH, 128, D], F32, kind="ExternalInput").ap()
    bo_d = nc.dram_tensor("bocols", [128, 8], F32, kind="ExternalInput").ap()
    sc_d = nc.dram_tensor("scalec", [16, 1], F32, kind="ExternalInput").ap()
    E2_d = nc.dram_tensor("E2c", [NCH, 128, 16], F32, kind="ExternalInput").ap()
    E2T_d = nc.dram_tensor("E2Tc", [NCH, 16, 128], F32, kind="ExternalInput").ap()

    lx = [p.sem("lx0"), p.sem("lx1")]
    wl = p.sem("wl")
    mm = p.sem("mm")
    st = [p.sem("st0"), p.sem("st1")]
    g0 = p.sem("g0")
    sq = p.sem("sq")
    so = p.sem("so")
    dv, ev = p.ec["vector"], p.ec["scalar"]

    common = make_common(p)
    wbuf = p.sb("wbuf", [128, NCH, D], BF16)
    scalecol = p.sb("scalecol", [16, 1], F32)
    bocols = p.sb("bocols_sb", [128, 8], F32)
    sc_ln = ExitStack()
    lnq = p.sb("lnqC", [128, NCH, S], BF16, scope=sc_ln, side="right")

    for c in range(NCH):
        wq_ld = p.dma("gpsimd", wbuf[:, c, :], wq_d[c], wl)
    p.dma("sync", scalecol[:], sc_d, g0)
    ld_g0 = p.dma("sync", bocols[:], bo_d, g0)

    emit_ln_featmajor(p, xdram, lnq, common, lx, mm, ev, dv)

    sc_att = ExitStack()
    qT = p.sb("qT", [128, NCH, S], BF16, scope=sc_att)
    kT = p.sb("kT", [128, NCH, S], BF16, scope=sc_att)
    vtok = p.sb("vtok", [128, 16, 1024], BF16, scope=sc_att)
    E2sb = p.sb("E2sb", [128, NCH, 16], BF16, scope=sc_att)
    E2Tsb = p.sb("E2Tsb", [16, NCH, 128], BF16, scope=sc_att)
    ssq = p.sb("ssq", [16, S], F32, scope=sc_att)
    qfac = p.sb("qfac", [16, S], BF16, scope=sc_att)
    zr = p.sb("zr", [1, 512], F32, scope=sc_att)
    resb = p.sb("resb", [64, 512], BF16, scope=sc_att)
    e2l = p.sem("e2l")
    for c in range(NCH):
        p.dma("gpsimd", E2sb[:, c, :], E2_d[c], e2l)
        e2_ld = p.dma("gpsimd", E2Tsb[:, c, :], E2T_d[c], e2l)

    def mk_evac(dst):
        def evac(m, ps):
            return p.act(dst[:, m, :], ps, AF.Copy, inc=None)
        return evac
    p.wait("tensor", dv, dv.v)
    emit_proj(p, wbuf, wq_ld, wl, lnq, mk_evac(qT), NCH, mm, ev)
    p.wait("gpsimd", mm, mm.v)
    for c in range(NCH):
        wk_ld = p.dma("gpsimd", wbuf[:, c, :], wk_d[c], wl)
    emit_proj(p, wbuf, wk_ld, wl, lnq, mk_evac(kT), NCH, mm, ev)
    p.wait("gpsimd", mm, mm.v)
    for c in range(NCH):
        wv_ld = p.dma("gpsimd", wbuf[:, c, :], wv_d[c], wl)
    with nc.psum_tensor("Vps", [128, 2, 512], F32) as Vps:
        p.wait("tensor", wl, wv_ld)
        p.wait("tensor", ev, ev.v)
        vps_guard = [0, 0]
        for ts in range(16):
            for mh in range(2):
                g = (ts * 2 + mh) % 2
                if vps_guard[g]:
                    p.wait("tensor", ev, vps_guard[g])
                for dc in range(NCH):
                    mv = p.mm(Vps[:, g, :], lnq[:, dc, 128 * ts:128 * (ts + 1)],
                              wbuf[:, dc, 512 * mh:512 * (mh + 1)],
                              start=(dc == 0), stop=(dc == NCH - 1),
                              inc=mm if dc == NCH - 1 else None)
                p.wait("scalar", mm, mv)
                vps_guard[g] = p.act(vtok[:, ts, 512 * mh:512 * (mh + 1)],
                                     Vps[:, g, :], AF.Copy)
        p.wait("tensor", ev, max(vps_guard))
    sc_ln.close()

    sqpp = p.sb("sqpp", [128, 2, S], BF16, scope=sc_att)
    v_qn = v_kn = 0
    for which, tgt, bias in (("q", qT, scalecol[:]), ("k", kT, 0.0)):
        with nc.psum_tensor(un("SSps"), [16, S], F32) as SSps:
            p.wait("tensor", ev, ev.v)
            p.wait("tensor", dv, dv.v)
            sq_guard = [0, 0]
            for c in range(NCH):
                g = c % 2
                if sq_guard[g]:
                    p.wait("scalar", mm, sq_guard[g])
                e_sq = p.act(sqpp[:, g, :], tgt[:, c, :], AF.Square)
                p.wait("tensor", ev, e_sq)
                if c == 0 and which == "q":
                    p.wait("tensor", e2l, e2_ld)
                    p.wait("tensor", g0, ld_g0)
                for n in range(4):
                    mv = p.mm(SSps[:, 512 * n:512 * (n + 1)], E2sb[:, c, :],
                              sqpp[:, g, 512 * n:512 * (n + 1)],
                              start=(c == 0), stop=(c == NCH - 1),
                              inc=mm if n == 3 else None)
                sq_guard[g] = mv
            p.wait("vector", mm, mm.v)
            p.vec("tensor_scalar", ssq[:], SSps[:], 1e-24, None, ALU.add)
        d_ln = p.ec["vector"].v
        p.wait("scalar", dv, d_ln)
        p.act(ssq[:], ssq[:], AF.Ln)
        e_f = p.act(qfac[:], ssq[:], AF.Exp, scale=-0.5, bias=bias)
        with nc.psum_tensor(un("EXps"), [128, 2, 512], F32) as EXps:
            p.wait("tensor", ev, e_f)
            ex_guard = [0, 0]
            for c in range(NCH):
                for n in range(4):
                    g = (c * 4 + n) % 2
                    if ex_guard[g]:
                        p.wait("tensor", dv, ex_guard[g])
                    mv = p.mm(EXps[:, g, :], E2Tsb[0:16, c, :],
                              qfac[0:16, 512 * n:512 * (n + 1)],
                              start=True, stop=True, inc=mm)
                    p.wait("vector", mm, mv)
                    ex_guard[g] = p.vec("tensor_mul",
                                        tgt[:, c, 512 * n:512 * (n + 1)],
                                        tgt[:, c, 512 * n:512 * (n + 1)],
                                        EXps[:, g, :])
            if which == "q":
                v_qn = max(ex_guard)
            else:
                v_kn = max(ex_guard)

    qhb = p.sb("qhb", [64, S], BF16, scope=sc_att)
    khb = p.sb("khb", [64, S], BF16, scope=sc_att)
    ohb = p.sb("ohb", [64, S], BF16, scope=sc_att)
    oTpk = p.sb("oTpk", [128, NCH, S], BF16, side="right")
    Esb = p.sb("EsbC", [128, 16, 512], BF16, scope=sc_att)
    with nc.psum_tensor("EpsA", [128, 512], F32) as EpsA, \
         nc.psum_tensor("EpsB", [128, 512], F32) as EpsB, \
         nc.psum_tensor("Zps", [1, 512], F32) as Zps, \
         nc.psum_tensor("REps", [64, 512], F32) as REps, \
         nc.psum_tensor("Cps", [64, 512], F32) as Cps:
        p.wait("sync", dv, max(v_qn, v_kn))
        p.wait("tensor", dv, max(v_qn, v_kn))
        qh_mm = kh_mm = 0
        pv_prev = [0] * 16
        eps_guard = [0, 0]
        z_ln_prev = 0
        re_copy_prev = 0
        v_o = 0
        so_val = 0
        for h in range(16):
            hc, half = h // 2, h % 2
            if half == 0:
                qh_ap, kh_ap = qT[0:64, hc, :], kT[0:64, hc, :]
            else:
                p.wait("sync", mm, max(qh_mm, kh_mm))
                rq = p.dma("sync", qhb[:], qT[64:128, hc, :], sq)
                rk = p.dma("sync", khb[:], kT[64:128, hc, :], sq)
                p.wait("tensor", sq, rk)
                qh_ap, kh_ap = qhb[:], khb[:]
            for qt in range(4):
                exp_v = [0] * 16
                for kc in range(17):
                    if kc < 16:
                        g = kc % 2
                        Eps = EpsA if g == 0 else EpsB
                        if eps_guard[g]:
                            p.wait("tensor", ev, eps_guard[g])
                        mv = p.mm(Eps[:], kh_ap[:, 128 * kc:128 * (kc + 1)],
                                  qh_ap[:, 512 * qt:512 * (qt + 1)],
                                  start=True, stop=True, inc=mm)
                        if half == 1:
                            qh_mm = kh_mm = mv
                        p.wait("scalar", mm, mv)
                        p.wait("scalar", mm, pv_prev[kc])
                        eps_guard[g] = p.act(Esb[:, kc, :], Eps[:], AF.Exp)
                        exp_v[kc] = eps_guard[g]
                    if kc >= 1:
                        j = kc - 1
                        p.wait("tensor", ev, exp_v[j])
                        if j == 0:
                            if z_ln_prev:
                                p.wait("tensor", ev, z_ln_prev)
                            if v_o:
                                p.wait("tensor", dv, v_o)
                        p.mm(Zps[:], common["ones_colbf"][:], Esb[:, j, :],
                             start=(j == 0), stop=(j == 15),
                             inc=mm if j == 15 else None)
                        pv_prev[j] = p.mm(Cps[:], vtok[:, j, 64 * h:64 * (h + 1)],
                                          Esb[:, j, :], start=(j == 0),
                                          stop=(j == 15),
                                          inc=mm if j == 15 else None)
                p.wait("scalar", mm, pv_prev[15])
                p.act(zr[:], Zps[:], AF.Ln)
                z_ln_prev = p.act(zr[:], zr[:], AF.Exp, scale=-1.0)
                if re_copy_prev:
                    p.wait("tensor", ev, re_copy_prev)
                p.wait("tensor", ev, z_ln_prev)
                mv = p.mm(REps[:], common["ones_rowf"][0:1, 0:64], zr[:],
                          start=True, stop=True, inc=mm)
                p.wait("scalar", mm, mv)
                if v_o:
                    p.wait("scalar", dv, v_o)
                re_copy_prev = p.act(resb[:], REps[:], AF.Copy)
                p.wait("vector", mm, pv_prev[15])
                p.wait("vector", ev, re_copy_prev)
                if half == 0:
                    dst = oTpk[0:64, hc, 512 * qt:512 * (qt + 1)]
                else:
                    if qt == 0 and so_val:
                        p.wait("vector", so, so_val)
                    dst = ohb[:, 512 * qt:512 * (qt + 1)]
                v_o = p.vec("tensor_mul", dst, Cps[:], resb[:])
            if half == 1:
                p.wait("sync", dv, v_o)
                so_val = p.dma("sync", oTpk[64:128, hc, :], ohb[:], so)

    sc_att.close()
    p.wait("gpsimd", mm, mm.v)
    for c in range(NCH):
        wo_ld = p.dma("gpsimd", wbuf[:, c, :], wo_d[c], wl)
    ysb = p.sb("ysbC", [128, NCH, S], F32)
    p.wait("tensor", dv, dv.v)
    p.wait("tensor", so, so_val)

    def evac_y(m, ps):
        return p.act(ysb[:, m, :], ps, AF.Identity, bias=bocols[:, m:m + 1])
    ev_y = emit_proj(p, wbuf, wo_ld, wl, oTpk, evac_y, NCH, mm, ev)
    emit_postln_residual(p, ysb, ev_y, xdram, xodram, common, lx, mm, ev, dv, st)
    p.wait("sync", st[0], st[0].v)
    p.wait("sync", st[1], st[1].v)


def prep_l1_attn_consts(w_q, w_kv, w_out, b_out, scale):
    w_kv = np.asarray(w_kv, np.float32)
    E2 = np.zeros((NCH, 128, 16), np.float32)
    for c in range(NCH):
        E2[c, 0:64, 2 * c] = 1.0
        E2[c, 64:128, 2 * c + 1] = 1.0
    return {
        "wqcC": wchunks(w_q), "wkcC": wchunks(w_kv[:, :D]),
        "wvcC": wchunks(w_kv[:, D:]), "wocC": wchunks(w_out),
        "bocols": col_layout(b_out),
        "scalec": np.asarray(scale, np.float32).reshape(16, 1),
        "E2c": E2, "E2Tc": np.ascontiguousarray(E2.transpose(0, 2, 1)),
    }


def build_l1_ffn(nc):
    p = Prog(nc)
    xdram = nc.dram_tensor("x", [128, NCH, S], F32, kind="ExternalInput").ap()
    xodram = nc.dram_tensor("xo", [128, NCH, S], F32, kind="ExternalOutput").ap()
    emit_l1_ffn(p, xdram, xodram)
    p.run()


def emit_l1_ffn(p, xdram, xodram):
    nc = p.nc
    SH = 512
    NBL = S // SH
    wi_d = nc.dram_tensor("wiTc", [NCH, 128, FF], F32, kind="ExternalInput").ap()
    wo_d = nc.dram_tensor("woTc", [32, 128, D], F32, kind="ExternalInput").ap()
    A_d = nc.dram_tensor("Ac", [NE, NCH, 128, 16], F32, kind="ExternalInput").ap()
    B_d = nc.dram_tensor("Bc", [NE, 16, FF], F32, kind="ExternalInput").ap()
    gw_d = nc.dram_tensor("gwD", [NCH, 128, 8], F32, kind="ExternalInput").ap()
    gb_d = nc.dram_tensor("gbrow", [1, 8], F32, kind="ExternalInput").ap()
    bi_d = nc.dram_tensor("bicols", [128, 32], F32, kind="ExternalInput").ap()
    bo_d = nc.dram_tensor("borow", [1, D], F32, kind="ExternalInput").ap()
    sel_d = nc.dram_tensor("selD", [8, 8 * 128], F32, kind="ExternalInput").ap()
    eye_d = nc.dram_tensor("eyeD", [128, 128], F32, kind="ExternalInput").ap()

    lx = [p.sem("lx0"), p.sem("lx1")]
    wl = [p.sem("wl0"), p.sem("wl1")]
    ol = [p.sem("ol0"), p.sem("ol1")]
    mm = p.sem("mm")
    st = [p.sem("st0"), p.sem("st1")]
    g0 = p.sem("g0")
    sl = p.sem("sl")
    dv, ev = p.ec["vector"], p.ec["scalar"]
    gp = p.ec["gpsimd"]

    common = make_common(p)
    ysb = p.sb("ysbD", [128, NCH, S], F32)
    sc_ln = ExitStack()
    lnq = p.sb("lnqD", [128, NCH, S], BF16, scope=sc_ln, side="right")

    emit_ln_featmajor(p, xdram, lnq, common, lx, mm, ev, dv)
    v_lnq = dv.v

    sc_f = ExitStack()
    gwsb = p.sb("gwsbD", [128, NCH, 8], BF16, scope=sc_f)
    gbsb = p.sb("gbsb", [1, 8], BF16, scope=sc_f)
    bisb = p.sb("bisb", [128, 32], F32, scope=sc_f)
    borow = p.sb("borowD", [1, D], BF16, scope=sc_f)
    selsb = p.sb("selD_sb", [8, 8 * 128], BF16, scope=sc_f)
    eyef = p.sb("eyefD", [128, 128], F32, scope=sc_f)
    eyebf = p.sb("eyebfD", [128, 128], BF16, scope=sc_f)
    Asb = p.sb("Asb", [128, NE, NCH, 16], BF16, scope=sc_f)
    Bst = p.sb("BstD", [16, 2, NE, 128], BF16, scope=sc_f)
    esb = p.sb("esbD", [128, 8], F32, scope=sc_f)
    scn = p.sb("scnD", [128, 8], F32, scope=sc_f)
    zcol = p.sb("zcolD", [128, 1], F32, scope=sc_f)
    rcpc = p.sb("rcpcD", [128, 1], F32, scope=sc_f)
    m1c = p.sb("m1c", [128, 1], F32, scope=sc_f)
    m2c = p.sb("m2c", [128, 1], F32, scope=sc_f)
    nmc = p.sb("nmc", [128, 1], F32, scope=sc_f)
    vsc = p.sb("vsc", [128, 1], F32, scope=sc_f)
    ind1 = p.sb("ind1", [128, 8], F32, scope=sc_f)
    ind2 = p.sb("ind2", [128, 8], F32, scope=sc_f)
    dsb = p.sb("dsbD", [128, 8], F32, scope=sc_f)
    gtok = p.sb("gtok", [128, 8], F32, scope=sc_f)
    gatesDb = p.sb("gatesDb", [8, S], BF16, scope=sc_f)
    vs2row = p.sb("vs2row", [1, S], BF16, scope=sc_f)
    tsb = p.sb("tsbD", [16, NE, SH], BF16, scope=sc_f)
    GD = p.sb("GDsb", [128, NE, SH], BF16, scope=sc_f)
    Hb = p.sb("HbD", [128, 32, SH], BF16, scope=sc_f)
    Hacc = p.sb("HaccD", [128, SH], F32, scope=sc_f)
    tmpD = p.sb("tmpD", [128, SH], F32, scope=sc_f)
    gtile = p.sb("gtileD", [128, 4, SH], BF16, scope=sc_f)
    btile = p.sb("btileD", [128, 2, SH], BF16, scope=sc_f)
    wist = p.sb("wistD", [128, 2, NCH, 128], BF16, scope=sc_f)
    wost = p.sb("wostD", [128, 2, 512], BF16, scope=sc_f)

    cl = p.sem("cl")
    p.wait("gpsimd", dv, v_lnq)
    p.wait("sync", dv, v_lnq)
    for c in range(NCH):
        p.dma("gpsimd", gwsb[:, c, :], gw_d[c], cl)
    p.dma("gpsimd", borow[:], bo_d, cl)
    p.dma("gpsimd", selsb[:], sel_d, cl)
    p.dma("gpsimd", eyebf[:], eye_d, cl)
    p.dma("gpsimd", gbsb[:], gb_d, cl)
    for e in range(NE):
        for dc in range(NCH):
            ld_wl0 = p.dma("gpsimd", Asb[:, e, dc, :], A_d[e, dc], cl)
    p.dma("sync", bisb[:], bi_d, g0)
    ld_g0 = p.dma("sync", eyef[:], eye_d, g0)

    with nc.psum_tensor("LGpsD", [128, 16, 8], F32) as LGps, \
         nc.psum_tensor("TGpsA", [8, 128], F32) as TGpsA, \
         nc.psum_tensor("TGpsB", [8, 128], F32) as TGpsB, \
         nc.psum_tensor("TVpsA", [1, 128], F32) as TVpsA, \
         nc.psum_tensor("TVpsB", [1, 128], F32) as TVpsB:
        p.wait("tensor", dv, v_lnq)
        p.wait("tensor", cl, ld_wl0)
        p.wait("tensor", g0, ld_g0)
        p.wait("tensor", ev, ev.v)
        tg_guard = [0, 0]
        tv_guard = [0, 0]
        lg_read = 0
        for ts in range(16):
            for c in range(NCH):
                p.mm(LGps[:, ts, :], lnq[:, c, 128 * ts:128 * (ts + 1)],
                     gwsb[:, c, :], start=(c == 0), stop=False)
            mv = p.mm(LGps[:, ts, :], common["ones_rowbf"][:],
                      gbsb[:], start=False, stop=True, inc=mm)
            p.wait("scalar", mm, mv)
            if lg_read:
                p.wait("scalar", dv, lg_read)
            e_e = p.act(esb[:], LGps[:, ts, :], AF.Exp, accum_out=zcol[:])
            p.act(rcpc[:], zcol[:], AF.Ln)
            e_r = p.act(rcpc[:], rcpc[:], AF.Exp, scale=-1.0)
            e_s = p.act(scn[:], esb[:], AF.Identity, scale=rcpc[:])
            p.wait("vector", ev, e_s)
            p.vec("tensor_scalar", dsb[:], scn[:], 0.0, None, ALU.add,
                  ALU.max, accum_out=m1c[:])
            v_n = p.vec("tensor_scalar", nmc[:], m1c[:], -1.0, None, ALU.mult)
            p.wait("scalar", dv, v_n)
            e_d = p.act(dsb[:], scn[:], AF.Identity, bias=nmc[:])
            p.wait("vector", ev, e_d)
            p.vec("tensor_scalar", ind1[:], dsb[:], 0.0, None, ALU.is_ge)
            p.vec("scalar_tensor_tensor", dsb[:], ind1[:], -1e9, scn[:],
                  ALU.mult, ALU.add)
            p.vec("tensor_scalar", tmpD[:, 0:8], dsb[:], 0.0, None, ALU.add,
                  ALU.max, accum_out=m2c[:])
            v_n2 = p.vec("tensor_scalar", nmc[:], m2c[:], -1.0, None, ALU.mult)
            p.wait("scalar", dv, v_n2)
            e_d2 = p.act(dsb[:], dsb[:], AF.Identity, bias=nmc[:])
            p.wait("vector", ev, e_d2)
            p.vec("tensor_scalar", ind2[:], dsb[:], 0.0, None, ALU.is_ge)
            p.vec("tensor_add", ind1[:], ind1[:], ind2[:])
            v_vs = p.vec("tensor_add", vsc[:], m1c[:], m2c[:])
            p.wait("scalar", dv, v_vs)
            e_g = p.act(gtok[:], ind1[:], AF.Identity, scale=vsc[:])
            g = ts % 2
            tg = TGpsA if g == 0 else TGpsB
            tv = TVpsA if g == 0 else TVpsB
            p.wait("tensor", ev, e_g)
            if tg_guard[g]:
                p.wait("tensor", ev, tg_guard[g])
            m_t = p.t_transpose(tg[0:8, :], gtok[:], eyef[:], inc=mm)
            if tv_guard[g]:
                p.wait("tensor", ev, tv_guard[g])
            m_t2 = p.t_transpose(tv[0:1, :], vsc[:], eyef[:], inc=mm)
            p.wait("scalar", mm, m_t2)
            tg_guard[g] = p.act(gatesDb[:, 128 * ts:128 * (ts + 1)], tg[0:8, :],
                                AF.Copy)
            tv_guard[g] = p.act(vs2row[:, 128 * ts:128 * (ts + 1)], tv[0:1, :],
                                AF.Copy, scale=2.0)
            lg_read = p.ec["vector"].v
        e_gall = max(max(tg_guard), max(tv_guard))

    def load_wi(slot, bl, fc):
        for dc in range(NCH):
            p.dma("gpsimd", wist[:, slot, dc, :],
                  wi_d[dc, :, 128 * fc:128 * (fc + 1)], wl[slot])
        for e in range(NE):
            v = p.dma("gpsimd", Bst[0:16, slot, e, :],
                      B_d[e, :, 128 * fc:128 * (fc + 1)], wl[slot])
        return v

    def load_wo(slot, fc, dth):
        return p.dma("gpsimd", wost[:, slot, :],
                     wo_d[fc, :, 512 * dth:512 * (dth + 1)], ol[slot])

    wld = {}
    wld[(0, 0)] = load_wi(0, 0, 0)
    if True:
        p.wait("tensor", ev, e_gall)
        gt_ev = 0
        t_ev = 0
        b_guard = [0, 0]
        h_guard = [0, 0, 0, 0]
        y_read = 0
        acc_mm = [0, 0, 0, 0]
        hb_act = [0, 0]
        hb_ev = 0
        wo_mm = {}
        for bl in range(NBL):
            sb0 = SH * bl
            ph1 = ExitStack()
            Tps = ph1.enter_context(nc.psum_tensor(un("TpsD"), [16, SH], F32))
            GTps = ph1.enter_context(nc.psum_tensor(un("GTpsD"), [128, SH], F32))
            p.wait("tensor", ev, ev.v)
            p.wait("tensor", dv, dv.v)
            for e in range(NE):
                if t_ev:
                    p.wait("tensor", ev, t_ev)
                for dc in range(NCH):
                    mv = p.mm(Tps[:], Asb[:, e, dc, :],
                              lnq[:, dc, sb0:sb0 + SH],
                              start=(dc == 0), stop=(dc == NCH - 1),
                              inc=mm if dc == NCH - 1 else None)
                p.wait("scalar", mm, mv)
                t_ev = p.act(tsb[:, e, :], Tps[:], AF.Copy)
            for e in range(NE):
                if gt_ev:
                    p.wait("tensor", ev, gt_ev)
                mv = p.mm(GTps[:], selsb[0:8, 128 * e:128 * (e + 1)],
                          gatesDb[0:8, sb0:sb0 + SH], start=True, stop=True,
                          inc=mm)
                p.wait("scalar", mm, mv)
                gt_ev = p.act(GD[:, e, :], GTps[:], AF.Copy)
            ph1.close()
            ph2 = ExitStack()
            Bps = ph2.enter_context(nc.psum_tensor(un("BpsD"), [128, 2, SH], F32))
            Php = ph2.enter_context(nc.psum_tensor(un("PhD"), [128, 4, SH], F32))
            HAps = ph2.enter_context(nc.psum_tensor(un("HAD"), [128, 2, SH], F32))
            p.wait("tensor", ev, ev.v)
            p.wait("tensor", dv, dv.v)
            p.wait("scalar", mm, mm.v)
            for fc in range(32):
                nxt = (bl, fc + 1) if fc + 1 < 32 else ((bl + 1, 0) if bl + 1 < NBL else None)
                if nxt is not None:
                    slot_n = (fc + 1) % 2
                    p.wait("gpsimd", mm, mm.v)
                    wld[nxt] = load_wi(slot_n, *nxt)
                slot = fc % 2
                g = fc % 2
                p.wait("tensor", wl[slot], wld[(bl, fc)])
                if b_guard[g]:
                    p.wait("tensor", ev, b_guard[g])
                for dc in range(NCH):
                    mv = p.mm(Bps[:, g, :], wist[:, slot, dc, :],
                              lnq[:, dc, sb0:sb0 + SH],
                              start=(dc == 0), stop=(dc == NCH - 1),
                              inc=mm if dc == NCH - 1 else None)
                p.wait("scalar", mm, mv)
                b_guard[g] = p.act(btile[:, g, :], Bps[:, g, :], AF.Identity,
                                   bias=bisb[:, fc:fc + 1])
                stt_v = [0] * NE
                for e in range(NE):
                    gph = e % 4
                    if h_guard[gph]:
                        p.wait("tensor", dv, h_guard[gph])
                    p.mm(Php[:, gph, :], Bst[0:16, slot, e, :],
                         tsb[:, e, :], start=True, stop=False)
                    if e == 0:
                        p.wait("tensor", ev, b_guard[g])
                    mv = p.mm(Php[:, gph, :], eyebf[:], btile[:, g, :],
                              start=False, stop=True, inc=mm)
                    p.wait("vector", mm, mv)
                    if acc_mm[e % 4]:
                        p.wait("vector", mm, acc_mm[e % 4])
                    h_guard[gph] = p.vec("scalar_tensor_tensor",
                                         gtile[:, e % 4, :], Php[:, gph, :],
                                         0.0, GD[:, e, :], ALU.max, ALU.mult)
                    stt_v[e] = h_guard[gph]
                    if e >= 2:
                        j = e - 2
                        p.wait("tensor", dv, stt_v[j])
                        if j == 0:
                            p.wait("tensor", ev, hb_act[fc % 2])
                        acc_mm[j % 4] = p.mm(
                            HAps[:, fc % 2, :], eyebf[:], gtile[:, j % 4, :],
                            start=(j == 0), stop=False, inc=mm)
                for j in (NE - 2, NE - 1):
                    p.wait("tensor", dv, stt_v[j])
                    acc_mm[j % 4] = p.mm(
                        HAps[:, fc % 2, :], eyebf[:], gtile[:, j % 4, :],
                        start=False, stop=(j == NE - 1), inc=mm)
                p.wait("scalar", mm, acc_mm[(NE - 1) % 4])
                hb_act[fc % 2] = p.act(Hb[:, fc, :], HAps[:, fc % 2, :], AF.Copy)
                hb_ev = hb_act[fc % 2]
            ph2.close()
            ph3 = ExitStack()
            Yps = ph3.enter_context(nc.psum_tensor(un("YpsD"), [128, 4, SH], F32))
            p.wait("tensor", ev, ev.v)
            p.wait("tensor", dv, dv.v)
            p.wait("tensor", ev, hb_ev)
            for dth in range(2):
                wo_ld = {}
                p.wait("gpsimd", mm, mm.v)
                wo_ld[0] = load_wo(0, 0, dth)
                for fc in range(32):
                    slot = fc % 2
                    if fc + 1 < 32:
                        if fc >= 1:
                            p.wait("gpsimd", mm, wo_mm.get((fc - 1) % 2, 0))
                        wo_ld[fc + 1] = load_wo((fc + 1) % 2, fc + 1, dth)
                    p.wait("tensor", ol[slot], wo_ld[fc])
                    if fc == 0 and y_read:
                        p.wait("tensor", ev, y_read)
                    for dt in range(4):
                        mv = p.mm(Yps[:, dt, :], wost[:, slot, 128 * dt:128 * (dt + 1)],
                                  Hb[:, fc, :], start=(fc == 0), stop=False,
                                  inc=mm if dt == 3 else None)
                    wo_mm[slot] = mv
                for dt in range(4):
                    dtg = 4 * dth + dt
                    mv = p.mm(Yps[:, dt, :], borow[:, 128 * dtg:128 * (dtg + 1)],
                              vs2row[:, sb0:sb0 + SH], start=False, stop=True,
                              inc=mm)
                p.wait("scalar", mm, mv)
                for dt in range(4):
                    dtg = 4 * dth + dt
                    y_read = p.act(ysb[:, dtg, sb0:sb0 + SH], Yps[:, dt, :],
                                   AF.Copy)
            ph3.close()
    ev_y = y_read
    sc_f.close()
    sc_ln.close()

    emit_postln_residual(p, ysb, ev_y, xdram, xodram, common, lx, mm, ev, dv, st)
    p.wait("sync", st[0], st[0].v)
    p.wait("sync", st[1], st[1].v)


def prep_l1_ffn_consts(wi, bi, wo, bo, lora_A, lora_B, gate_w, gate_b):
    sel = np.zeros((8, 8 * 128), np.float32)
    for e in range(8):
        sel[e, 128 * e:128 * (e + 1)] = 1.0
    return {
        "wiTc": wchunks(np.asarray(wi, np.float32).T),
        "woTc": np.ascontiguousarray(
            np.asarray(wo, np.float32).T.reshape(32, 128, D)),
        "Ac": np.ascontiguousarray(
            np.asarray(lora_A, np.float32).transpose(0, 2, 1).reshape(NE, NCH, 128, 16)),
        "Bc": np.ascontiguousarray(np.asarray(lora_B, np.float32).transpose(0, 2, 1)),
        "gwD": wchunks(np.asarray(gate_w, np.float32).T),
        "gbrow": np.asarray(gate_b, np.float32).reshape(1, 8),
        "bicols": col_layout(bi),
        "borow": np.asarray(bo, np.float32).reshape(1, D),
        "selD": sel,
        "eyeD": np.eye(128, dtype=np.float32),
    }


class _NCProxy:
    def __init__(self, nc):
        object.__setattr__(self, "_nc", nc)

    def __getattr__(self, a):
        return getattr(self._nc, a)

    def psum_tensor(self, name, *a, **k):
        return self._nc.psum_tensor(un(name), *a, **k)

    def sbuf_tensor(self, name, *a, **k):
        return self._nc.sbuf_tensor(un(name), *a, **k)


def build_fused(nc):
    p = Prog(_NCProxy(nc))
    x0 = nc.dram_tensor("x", [128, NCH, S], F32, kind="ExternalInput").ap()
    x1 = nc.dram_tensor("x1d", [128, NCH, S], F32, kind="Internal").ap()
    x2 = nc.dram_tensor("x2d", [128, NCH, S], F32, kind="Internal").ap()
    x3 = nc.dram_tensor("x3d", [128, NCH, S], F32, kind="Internal").ap()
    xo = nc.dram_tensor("xo", [128, NCH, S], F32, kind="ExternalOutput").ap()
    phases = ((emit_l0_attn, x0, x1), (emit_l0_moe, x1, x2),
              (emit_l1_attn, x2, x3), (emit_l1_ffn, x3, xo))
    for emitter, xi, xod in phases:
        ph = ExitStack()
        p.default_scope = ph
        emitter(p, xi, xod)
        for eng in ("sync", "gpsimd"):
            p.wait(eng, p.sem("mm"), p.sem("mm").v)
            for ec in p.ec.values():
                p.wait(eng, ec, ec.v)
        ph.close()
    p.default_scope = None
    p.run()


def xcore_to_xT(x_core):
    return np.ascontiguousarray(np.asarray(x_core, np.float32).T.reshape(NCH, 128, S).transpose(1, 0, 2))


def xT_to_core(xo):
    return xo.transpose(1, 0, 2).reshape(D, S).T


def prep_l0_attn_inputs(x_core, wq, bq, wk, bk, wv, bv, wo, bo):
    Bfirst, Blast, B0T, B0sh = make_l0_consts()
    bcols = np.concatenate([col_layout(bq), col_layout(bk),
                            col_layout(bv), col_layout(bo)], axis=1)
    return {
        "x": xcore_to_xT(x_core),
        "wqc": wchunks(np.asarray(wq).T), "wkc": wchunks(np.asarray(wk).T),
        "wvc": wchunks(np.asarray(wv).T), "woc": wchunks(np.asarray(wo).T),
        "bcols": np.ascontiguousarray(bcols),
        "Bfirst": Bfirst, "Blast": Blast,
        "B0T": np.ascontiguousarray(B0T.reshape(128, 256)),
        "B0sh": np.ascontiguousarray(B0sh.reshape(128, 272)),
        "eye": np.eye(128, dtype=np.float32),
    }


def prep_l0_moe_consts(gate_weight, W1, b1, W2, b2, Ws1, bs1, Ws2, bs2):
    W1c = np.concatenate([np.asarray(W1, np.float32),
                          np.asarray(Ws1, np.float32)[None]], 0).reshape(9, 8, 128, FF)
    W2c = np.concatenate([np.asarray(W2, np.float32),
                          np.asarray(Ws2, np.float32)[None]], 0).reshape(9, 32, 128, D)
    b1c = np.stack([np.asarray(b, np.float32).reshape(32, 128).T
                    for b in list(b1) + [bs1]], 1)
    sel = np.zeros((8, 8 * 128), np.float32)
    for e in range(8):
        sel[e, 128 * e:128 * (e + 1)] = 1.0
    return {
        "gwc": wchunks(np.asarray(gate_weight, np.float32).T),
        "W1c": np.ascontiguousarray(W1c), "W2c": np.ascontiguousarray(W2c),
        "b1c": np.ascontiguousarray(b1c),
        "bs2c": col_layout(bs2),
        "b2r": np.ascontiguousarray(np.asarray(b2, np.float32)),
        "selc": sel,
        "eyeB": np.eye(128, dtype=np.float32),
    }


def kernel(**inputs):
    ii = {k: np.asarray(v) for k, v in inputs.items()}
    x = ii["L0_attn_x"].astype(np.float32)

    consts = prep_l0_moe_consts(
        ii["L0_ffn_gate_weight"], ii["L0_ffn_W1"], ii["L0_ffn_b1"],
        ii["L0_ffn_W2"], ii["L0_ffn_b2"], ii["L0_ffn_Ws1"], ii["L0_ffn_bs1"],
        ii["L0_ffn_Ws2"], ii["L0_ffn_bs2"])
    consts.update(prep_l1_attn_consts(
        ii["L1_attn_w_q"], ii["L1_attn_w_kv"], ii["L1_attn_w_out"],
        ii["L1_attn_b_out"], ii["L1_attn_scale"]))
    consts.update(prep_l1_ffn_consts(
        ii["L1_ffn_wi"], ii["L1_ffn_bi"], ii["L1_ffn_wo"], ii["L1_ffn_bo"],
        ii["L1_ffn_lora_A"], ii["L1_ffn_lora_B"], ii["L1_ffn_gate_w"],
        ii["L1_ffn_gate_b"]))

    nc = bass.Bass("TRN2", target_bir_lowering=False)
    build_fused(nc)
    in_maps = []
    for c in range(8):
        m = prep_l0_attn_inputs(
            x[c], ii["L0_attn_wq"], ii["L0_attn_bq"], ii["L0_attn_wk"],
            ii["L0_attn_bk"], ii["L0_attn_wv"], ii["L0_attn_bv"],
            ii["L0_attn_wo"], ii["L0_attn_bo"])
        m.update(consts)
        in_maps.append(m)
    res = run_bass_kernel_spmd(nc, in_maps, core_ids=list(range(8)))
    out = np.stack([xT_to_core(np.asarray(res.results[c]["xo"]))
                    for c in range(8)]).astype(np.float32)
    return out


# version 07
# speedup vs baseline: 1.3417x; 1.0750x over previous
import math
from contextlib import ExitStack

import numpy as np
import ml_dtypes
import concourse.bass as bass
import concourse.mybir as mybir
from concourse.bass_utils import run_bass_kernel_spmd

F32 = mybir.dt.float32
BF16 = mybir.dt.bfloat16
AF = mybir.ActivationFunctionType
ALU = mybir.AluOpType
AX = mybir.AxisListType

S = 2048
D = 1024
NCH = 8
H = 16
DH = 64
FF = 4096
NE = 8
BF = ml_dtypes.bfloat16
LN_EPS = 1e-5
GATEK = 1e20


def _l0_bias(delta):
    d = np.asarray(delta, dtype=np.float64)
    return (np.exp(-0.5 * (d / 2.0) ** 2) + np.cos(2.0 * np.pi * d / 401.0)).astype(np.float32)


def make_l0_consts():
    tabs = []
    for t in range(16):
        B = np.full((128, 256), -2e9, dtype=np.float32)
        q = 128 * t + np.arange(128)
        c = np.arange(128)
        for s_i, base in ((0, 128 * t - 2), (1, 128 * t + 126)):
            key = base + c
            dlt = key[:, None] - q[None, :]
            v = (np.abs(dlt) <= 2) & (key[:, None] >= 0) & (key[:, None] < S) & (key[:, None] != 0)
            blk = np.full((128, 128), -2e9, dtype=np.float32)
            blk[v] = _l0_bias(dlt)[v]
            B[:, 128 * s_i:128 * (s_i + 1)] = blk
        tabs.append(B)
    Bfirst = np.ascontiguousarray(np.concatenate(tabs[0:8], axis=1))
    Blast = np.ascontiguousarray(np.concatenate(tabs[8:16], axis=1))
    b0 = _l0_bias(np.arange(S))
    B0T = np.ascontiguousarray(
        np.tile(b0.reshape(16, 128).T[:, :, None], (1, 1, 16))).astype(np.float32)
    key = 128 * np.arange(17)[None, :] - 2 + np.arange(128)[:, None]
    b0sh = np.where((key >= 0) & (key < S), _l0_bias(np.abs(key)), np.float32(-2e9)).astype(np.float32)
    B0sh = np.ascontiguousarray(np.tile(b0sh[:, :, None], (1, 1, 16))).astype(np.float32)
    return Bfirst, Blast, B0T, B0sh


def col_layout(v):
    return np.ascontiguousarray(np.asarray(v, np.float32).reshape(-1, 128).T)


def wchunks(W):
    W = np.asarray(W, np.float32)
    K, M = W.shape
    return np.ascontiguousarray(W.reshape(K // 128, 128, M))



_UN = [0]


def un(base):
    _UN[0] += 1
    return f"{base}_{_UN[0]}"

ENGS = ("sync", "tensor", "vector", "scalar", "gpsimd")


class Ctr:
    def __init__(self, h):
        self.h = h
        self.v = 0

    def bump(self, n=1):
        self.v += n
        return self.v


class Prog:
    def __init__(self, nc):
        self.nc = nc
        self.ctx = ExitStack()
        self.q = {e: [] for e in ENGS}
        self._n = 0
        self._sems = {}
        self.default_scope = None
        self.ec = {"vector": self.sem("dvc"), "scalar": self.sem("evc"),
                   "gpsimd": self.sem("gpc")}

    def sem(self, name=None):
        if name is not None and name in self._sems:
            return self._sems[name]
        self._n += 1
        c = Ctr(self.ctx.enter_context(self.nc.semaphore(name or f"s{self._n}")))
        if name is not None:
            self._sems[name] = c
        return c

    LEDGER = []

    def sb(self, name, shape, dtype, scope=None, side=None):
        import math as _m
        free = 1
        for d in shape[1:]:
            free *= d
        bts = free * (2 if dtype == BF16 else 4)
        try:
            h = (scope or self.default_scope or self.ctx).enter_context(
                self.nc.sbuf_tensor(name, shape, dtype, side=side))
            Prog.LEDGER.append((name, bts, side or "left", self.nc.sbuf_base, self.nc.sbuf_top))
            return h
        except BaseException:
            for rec in Prog.LEDGER:
                print("  alloc", rec)
            raise

    def on(self, eng, fn):
        self.q[eng].append(fn)

    def wait(self, eng, ctr, val=None):
        v = ctr.v if val is None else val
        if v > 0:
            self.on(eng, lambda e, v=v: e.wait_ge(ctr.h, v))

    def dma(self, eng, out, in_, ctr):
        self.on(eng, lambda e: e.dma_start(out=out, in_=in_).then_inc(ctr.h, 16))
        return ctr.bump(16)

    def mm(self, out, lhsT, rhs, start, stop, inc=None):
        if inc is None:
            self.on("tensor", lambda e: e.matmul(out, lhsT, rhs, start=start, stop=stop))
            return None
        v = inc.bump()
        self.on("tensor", lambda e: e.matmul(out, lhsT, rhs, start=start, stop=stop).then_inc(inc.h, 1))
        return v

    def t_transpose(self, out, in_, ident, inc=None):
        if inc is None:
            self.on("tensor", lambda e: e.transpose(out, in_, ident))
            return None
        v = inc.bump()
        self.on("tensor", lambda e: e.transpose(out, in_, ident).then_inc(inc.h, 1))
        return v

    def _chained(self, eng, fn):
        c = self.ec[eng]
        prev = c.v
        v = c.bump()
        def run(e, prev=prev, fn=fn, c=c):
            if prev > 0:
                e.wait_ge(c.h, prev)
            fn(e).then_inc(c.h, 1)
        self.on(eng, run)
        return v

    def act(self, out, in_, func, bias=0.0, scale=1.0, accum_out=None, inc=None):
        return self._chained("scalar",
                             lambda e: e.activation(out, in_, func, bias=bias,
                                                    scale=scale, accum_out=accum_out))

    def vec(self, meth, *args, inc=None, **kw):
        return self._chained("vector", lambda e: getattr(e, meth)(*args, **kw))

    def gp(self, meth, *args, inc=None, **kw):
        return self._chained("gpsimd", lambda e: getattr(e, meth)(*args, **kw))

    def run(self):
        with self.nc.Block() as block:
            for eng in ENGS:
                fns = self.q[eng]
                if not fns:
                    continue
                getattr(block, eng)(lambda e, fns=fns: [f(e) for f in fns])
        self.ctx.close()


def make_common(p):
    c = {}
    c["ones_colbf"] = p.sb(un("ones_colbf"), [128, 1], BF16)
    c["ones_rowf"] = p.sb(un("ones_rowf"), [1, 128], F32)
    c["ones_rowbf"] = p.sb(un("ones_rowbf"), [1, 128], BF16)
    c["one1bf"] = p.sb(un("one1bf"), [1, 1], BF16)
    c["onesP"] = p.sb(un("onesP"), [65, 128], F32)
    p.vec("memset", c["onesP"][:], 1.0)
    p.vec("memset", c["ones_colbf"][:], 1.0)
    p.vec("memset", c["ones_rowf"][:], 1.0)
    p.vec("memset", c["ones_rowbf"][:], 1.0)
    c["init_dv"] = p.vec("memset", c["one1bf"][:], 1.0)
    return c


def emit_ln_featmajor(p, xdram, lnq, common, lx, mm, ev, dv, raw_gate=None,
                      raw_gate2=None, keep=None):
    nc = p.nc
    sc = ExitStack()
    xc = p.sb(un("lnxc"), [128, 2, S], F32, scope=sc)
    xbf = p.sb(un("lnxbf"), [128, S], BF16, scope=sc)
    xsq = p.sb(un("lnxsq"), [128, S], BF16, scope=sc)
    rows_scope = keep if keep is not None else sc
    rows_side = "right" if keep is not None else None
    murow = p.sb(un("murow"), [1, S], F32, scope=rows_scope, side=rows_side)
    s2row = p.sb(un("s2row"), [1, S], F32, scope=sc)
    Abcs = p.sb(un("Abcs"), [128, S], F32, scope=rows_scope, side=rows_side)
    Bbcs = p.sb(un("Bbcs"), [128, S], F32, scope=sc)
    tf = p.sb(un("lntf"), [128, S], F32, scope=sc)
    tf2 = p.sb(un("lntf2"), [128, S], F32, scope=sc)

    Vc, Ec, Mc = [0] * NCH, [0] * NCH, [0] * NCH
    with nc.psum_tensor(un("lnps1"), [1, S], F32) as ps1, \
         nc.psum_tensor(un("lnps2"), [1, S], F32) as ps2:
        for c in range(NCH):
            slot = c % 2
            if c >= 2:
                p.wait("sync", dv, Vc[c - 2])
                p.wait("sync", mm, Mc[c - 2])
            ldv = p.dma("sync", xc[:, slot, :], xdram[:, c, :], lx[slot])
            p.wait("vector", lx[slot], ldv)
            if c >= 1:
                p.wait("vector", mm, Mc[c - 1])
            Vc[c] = p.vec("tensor_copy", xbf[:], xc[:, slot, :], inc=dv)
            p.wait("scalar", dv, Vc[c])
            if c >= 1:
                p.wait("scalar", mm, Mc[c - 1])
            Ec[c] = p.act(xsq[:], xbf[:], AF.Square, inc=ev)
            p.wait("tensor", ev, Ec[c])
            for n in range(4):
                p.mm(ps1[:, 512 * n:512 * (n + 1)], common["ones_colbf"][:],
                     xbf[:, 512 * n:512 * (n + 1)], start=(c == 0), stop=(c == NCH - 1))
                p.mm(ps2[:, 512 * n:512 * (n + 1)], common["ones_colbf"][:],
                     xsq[:, 512 * n:512 * (n + 1)], start=(c == 0), stop=(c == NCH - 1),
                     inc=mm if (n == 3 and raw_gate is None) else None)
            if raw_gate is not None:
                raw_gate(c, xc[:, slot, :], ldv)
            Mc[c] = mm.v
        p.wait("scalar", mm, Mc[NCH - 1])
        p.act(murow[:], ps1[:], AF.Copy, scale=1.0 / D)
        e2 = p.act(s2row[:], ps2[:], AF.Copy, scale=1.0 / D, inc=ev)
    with nc.psum_tensor(un("lnpbA"), [128, S], F32) as pbA, \
         nc.psum_tensor(un("lnpbB"), [128, S], F32) as pbB:
        p.wait("tensor", ev, e2)
        mmv = None
        for n in range(4):
            p.mm(pbA[:, 512 * n:512 * (n + 1)], common["ones_rowf"][:],
                 murow[:, 512 * n:512 * (n + 1)], start=True, stop=True)
            mmv = p.mm(pbB[:, 512 * n:512 * (n + 1)], common["ones_rowf"][:],
                       s2row[:, 512 * n:512 * (n + 1)], start=True, stop=True,
                       inc=mm if n == 3 else None)
        p.wait("scalar", mm, mmv)
        p.act(Abcs[:], pbA[:], AF.Copy)
        e2b = p.act(Bbcs[:], pbB[:], AF.Copy, inc=ev)
        p.wait("vector", ev, e2b)
        p.vec("tensor_mul", tf[:], Abcs[:], Abcs[:])
        p.vec("scalar_tensor_tensor", tf[:], tf[:], -1.0, Bbcs[:], ALU.mult, ALU.add)
        d1 = p.vec("tensor_scalar", tf[:], tf[:], LN_EPS, None, ALU.add, inc=dv)
        p.wait("scalar", dv, d1)
        p.act(tf[:], tf[:], AF.Ln)
        e3 = p.act(tf[:], tf[:], AF.Exp, scale=-0.5, inc=ev)
        p.wait("vector", ev, e3)
        p.vec("scalar_tensor_tensor", Bbcs[:], Abcs[:], -1.0, tf[:],
              ALU.mult, ALU.mult)
        ev2 = p.vec("tensor_copy", Abcs[:], tf[:], inc=dv)
    applied = []
    M2c = [0] * NCH
    for c in range(NCH):
        slot = c % 2
        if c < 2:
            p.wait("sync", dv, Vc[NCH - 2 + c])
            p.wait("sync", mm, Mc[NCH - 1])
        else:
            p.wait("sync", dv, applied[c - 2])
            if raw_gate2 is not None:
                p.wait("sync", mm, M2c[c - 2])
        ldv = p.dma("sync", xc[:, slot, :], xdram[:, c, :], lx[slot])
        p.wait("vector", lx[slot], ldv)
        p.vec("tensor_mul", tf2[:], xc[:, slot, :], Abcs[:])
        applied.append(p.vec("tensor_add", lnq[:, c, :], tf2[:], Bbcs[:], inc=dv))
        if raw_gate2 is not None:
            raw_gate2(c, xc[:, slot, :], ldv)
            M2c[c] = mm.v
    sc.close()
    return murow, Abcs


def emit_proj(p, wsb, wld_val, wld_ctr, xin, evac, mcs, mm, ev):
    nc = p.nc
    evs = []
    with nc.psum_tensor(un("prps"), [128, 2, S], F32) as prps:
        p.wait("tensor", wld_ctr, wld_val)
        p.wait("tensor", ev, ev.v)
        for m in range(mcs):
            g = m % 2
            if m >= 2:
                p.wait("tensor", ev, evs[m - 2])
            for c in range(NCH):
                for n in range(4):
                    p.mm(prps[:, g, 512 * n:512 * (n + 1)],
                         wsb[:, c, 128 * m:128 * (m + 1)],
                         xin[:, c, 512 * n:512 * (n + 1)],
                         start=(c == 0), stop=(c == NCH - 1),
                         inc=mm if (c == NCH - 1 and n == 3) else None)
            p.wait("scalar", mm, mm.v)
            evs.append(evac(m, prps[:, g, :]))
    return evs[-1]


def emit_postln_residual(p, ysb, ev_y, xdram, xodram, common, lx, mm, ev, dv, st):
    nc = p.nc
    sc = ExitStack()
    ybf = p.sb(un("plybf"), [128, S], BF16, scope=sc)
    ysq = p.sb(un("plysq"), [128, S], BF16, scope=sc)
    murow = p.sb(un("plmu"), [1, S], F32, scope=sc)
    s2row = p.sb(un("pls2"), [1, S], F32, scope=sc)
    Abcs = p.sb(un("plA"), [128, S], F32, scope=sc)
    Bbcs = p.sb(un("plB"), [128, S], F32, scope=sc)
    xc = p.sb(un("plxc"), [128, 2, S], F32, scope=sc)
    xoc = p.sb(un("plxoc"), [128, 2, S], F32, scope=sc)
    tf = p.sb(un("pltf"), [128, S], F32, scope=sc)

    p.wait("sync", mm, mm.v)
    p.wait("sync", dv, dv.v)
    p.wait("sync", ev, ev.v)
    with nc.psum_tensor(un("plps1"), [1, S], F32) as ps1, \
         nc.psum_tensor(un("plps2"), [1, S], F32) as ps2:
        p.wait("vector", ev, ev_y)
        Mc = [0] * NCH
        for c in range(NCH):
            if c >= 1:
                p.wait("vector", mm, Mc[c - 1])
            vc = p.vec("tensor_copy", ybf[:], ysb[:, c, :], inc=dv)
            p.wait("scalar", dv, vc)
            if c >= 1:
                p.wait("scalar", mm, Mc[c - 1])
            ec = p.act(ysq[:], ybf[:], AF.Square, inc=ev)
            p.wait("tensor", ev, ec)
            for n in range(4):
                p.mm(ps1[:, 512 * n:512 * (n + 1)], common["ones_colbf"][:],
                     ybf[:, 512 * n:512 * (n + 1)], start=(c == 0), stop=(c == NCH - 1))
                p.mm(ps2[:, 512 * n:512 * (n + 1)], common["ones_colbf"][:],
                     ysq[:, 512 * n:512 * (n + 1)], start=(c == 0), stop=(c == NCH - 1),
                     inc=mm if n == 3 else None)
            Mc[c] = mm.v
        p.wait("scalar", mm, Mc[NCH - 1])
        p.act(murow[:], ps1[:], AF.Copy, scale=1.0 / D)
        e2 = p.act(s2row[:], ps2[:], AF.Copy, scale=1.0 / D, inc=ev)
    with nc.psum_tensor(un("plpbA"), [128, S], F32) as pbA, \
         nc.psum_tensor(un("plpbB"), [128, S], F32) as pbB:
        p.wait("tensor", ev, e2)
        for n in range(4):
            p.mm(pbA[:, 512 * n:512 * (n + 1)], common["ones_rowf"][:],
                 murow[:, 512 * n:512 * (n + 1)], start=True, stop=True)
            mv = p.mm(pbB[:, 512 * n:512 * (n + 1)], common["ones_rowf"][:],
                      s2row[:, 512 * n:512 * (n + 1)], start=True, stop=True,
                      inc=mm if n == 3 else None)
        p.wait("scalar", mm, mv)
        p.act(Abcs[:], pbA[:], AF.Copy)
        e2b = p.act(Bbcs[:], pbB[:], AF.Copy, inc=ev)
        p.wait("vector", ev, e2b)
        p.vec("tensor_mul", tf[:], Abcs[:], Abcs[:])
        p.vec("scalar_tensor_tensor", tf[:], tf[:], -1.0, Bbcs[:], ALU.mult, ALU.add)
        d1 = p.vec("tensor_scalar", tf[:], tf[:], LN_EPS, None, ALU.add, inc=dv)
        p.wait("scalar", dv, d1)
        p.act(tf[:], tf[:], AF.Ln)
        e3 = p.act(tf[:], tf[:], AF.Exp, scale=-0.5, inc=ev)
        p.wait("vector", ev, e3)
        p.vec("scalar_tensor_tensor", Bbcs[:], Abcs[:], -1.0, tf[:],
              ALU.mult, ALU.mult)
        ev2 = p.vec("tensor_copy", Abcs[:], tf[:], inc=dv)
    outs = [0, 0]
    lxv = [lx[0].v, lx[1].v]
    for c in range(NCH):
        slot = c % 2
        if c >= 2:
            p.wait("sync", st[slot], outs[slot])
        ldv = p.dma("sync", xc[:, slot, :], xdram[:, c, :], lx[slot])
        p.wait("vector", lx[slot], ldv)
        p.vec("tensor_mul", tf[:], ysb[:, c, :], Abcs[:])
        p.vec("tensor_add", tf[:], tf[:], Bbcs[:])
        dvv = p.vec("tensor_add", xoc[:, slot, :], tf[:], xc[:, slot, :], inc=dv)
        p.wait("sync", dv, dvv)
        outs[slot] = p.dma("sync", xodram[:, c, :], xoc[:, slot, :], st[slot])
    sc.close()
    return None



def _finish_dummy(p, xodram, st):
    zb = p.sb(un("zdum"), [128, S], F32)
    p.wait("vector", st[0], st[0].v)
    p.wait("vector", st[1], st[1].v)
    p.wait("sync", p.ec["vector"], p.vec("memset", zb[:], 0.0))
    for c in range(NCH):
        p.dma("sync", xodram[:, c, :], zb[:], st[c % 2])
    p.wait("sync", st[0], st[0].v)
    p.wait("sync", st[1], st[1].v)

def build_l0_attn(nc, upto=7):
    p = Prog(nc)
    xdram = nc.dram_tensor("x", [128, NCH, S], F32, kind="ExternalInput").ap()
    xodram = nc.dram_tensor("xo", [128, NCH, S], F32, kind="ExternalOutput").ap()
    emit_l0_attn(p, xdram, xodram, upto)
    p.run()


def emit_l0_attn(p, xdram, xodram, upto=7):
    dbg = upto if upto in (29, 30, 31, 32, 33) else None
    nc = p.nc
    wq_d = nc.dram_tensor("wqc", [NCH, 128, D], F32, kind="ExternalInput").ap()
    wk_d = nc.dram_tensor("wkc", [NCH, 128, D], F32, kind="ExternalInput").ap()
    wv_d = nc.dram_tensor("wvc", [NCH, 128, D], F32, kind="ExternalInput").ap()
    wo_d = nc.dram_tensor("woc", [NCH, 128, D], F32, kind="ExternalInput").ap()
    bcols_d = nc.dram_tensor("bcols", [128, 4 * NCH], F32, kind="ExternalInput").ap()
    Bfirst_d = nc.dram_tensor("Bfirst", [128, 2048], F32, kind="ExternalInput").ap()
    Blast_d = nc.dram_tensor("Blast", [128, 2048], F32, kind="ExternalInput").ap()
    B0T_d = nc.dram_tensor("B0T", [128, 256], F32, kind="ExternalInput").ap()
    B0sh_d = nc.dram_tensor("B0sh", [128, 272], F32, kind="ExternalInput").ap()
    eye_d = nc.dram_tensor("eye", [128, 128], F32, kind="ExternalInput").ap()
    e0_dram = nc.dram_tensor("e0bounce", [16, S], BF16,
                             kind="ExternalOutput" if upto == 5 else "Internal").ap()
    oT_dram = nc.dram_tensor("oTd", [128, NCH, S], BF16,
                             kind="ExternalOutput" if upto == 5 else "Internal").ap()
    dbgz = nc.dram_tensor("dbgz", [16, 2, 1024], F32,
                          kind="ExternalOutput" if upto == 5 else "Internal").ap() if upto == 5 else None
    dbgv = nc.dram_tensor("dbgv", [128, NCH * 17 * 128], BF16,
                          kind="ExternalOutput").ap() if upto == 5 else None
    dbgr = nc.dram_tensor("dbgr", [2, 1024], BF16,
                          kind="ExternalOutput").ap() if upto == 5 else None
    v0_dram = nc.dram_tensor("v0bounce", [NCH, 128], BF16,
                             kind="ExternalOutput" if upto == 5 else "Internal").ap()
    vm_dram = nc.dram_tensor("vmbounce", [NCH, 128], F32,
                             kind="ExternalOutput" if upto == 5 else "Internal").ap()

    ldc = p.sem("ldc")
    ldg = p.sem("ldg")
    lx = [p.sem("lxA"), p.sem("lxB")]
    wl = [p.sem("wlA"), p.sem("wlB")]
    e0s = p.sem("e0s")
    g0 = p.sem("g0")
    oTs = p.sem("oTs")
    mm = p.sem("mm")
    ev = p.ec["scalar"]
    dv = p.ec["vector"]
    st = [p.sem("stA"), p.sem("stB")]

    common = make_common(p)
    eye = p.sb(un("eyebf"), [128, 128], BF16)
    ld_eye = p.dma("gpsimd", eye[:], eye_d, ldg)
    eyef = p.sb(un("eyef"), [128, 128], F32)
    p.dma("sync", eyef[:], eye_d, ldc)
    bcols = p.sb(un("bcolsb"), [128, 4 * NCH], F32)
    ld_const = p.dma("sync", bcols[:], bcols_d, ldc)

    kT = p.sb(un("kT"), [128, NCH, 2176], BF16)
    wscope = ExitStack()
    wbuf = p.sb(un("wbuf"), [128, NCH, D], BF16, scope=wscope, side="right")

    def load_w(wd):
        return p.dma("gpsimd", wbuf[:, :, :],
                     wd.rearrange("c p f -> p c f"), wl[0])

    wk_ld = load_w(wk_d)

    lnscope = ExitStack()
    lnq = p.sb(un("lnq"), [128, NCH, S], BF16, scope=lnscope, side="right")

    emit_ln_featmajor(p, xdram, lnq, common, lx, mm, ev, dv)
    if upto < 2:
        lnscope.close(); wscope.close()
        _finish_dummy(p, xodram, st)
        return p

    p.wait("vector", ev, ev.v)
    pad_dv = p.vec("memset", kT[:, :, 0:128], 0.0)

    bq = lambda m: bcols[:, 0 * NCH + m:0 * NCH + m + 1]
    bk = lambda m: bcols[:, 1 * NCH + m:1 * NCH + m + 1]
    bv = lambda m: bcols[:, 2 * NCH + m:2 * NCH + m + 1]
    bo = lambda m: bcols[:, 3 * NCH + m:3 * NCH + m + 1]
    p.wait("tensor", dv, dv.v)
    p.wait("tensor", ldc, ld_const)

    def ev_k(m, ps):
        return p.act(kT[:, m, 128:2176], ps, AF.Identity, bias=bk(m))

    emit_proj(p, wbuf, wk_ld, wl[0], lnq, ev_k, NCH, mm, ev)
    p.wait("gpsimd", mm, mm.v)
    wv_ld = load_w(wv_d)
    if upto < 3:
        lnscope.close(); wscope.close()
        _finish_dummy(p, xodram, st)
        return p

    p3p = ExitStack()
    vsh = p.sb(un("vsh"), [128, NCH, 17, 128], BF16, scope=p3p)
    vmcol = p.sb(un("vmcol"), [128, NCH], F32, scope=p3p)
    v0c = p.sb(un("v0c"), [128, NCH], BF16, scope=p3p)
    v0p = p.sb(un("v0p"), [1, 1024], BF16, scope=p3p)
    vmr = p.sb(un("vmr"), [1, 1024], BF16, scope=p3p)
    ov = p.sb(un("ov"), [64, 16], BF16, scope=p3p)
    p3v = ExitStack()
    vtmp = p.sb(un("vtmp"), [128, 2176], BF16, scope=p3v)
    vstage = p.sb(un("vstage"), [NCH, 128], BF16, scope=p3v)
    vmstage = p.sb(un("vmstage"), [NCH, 128], F32, scope=p3v)
    vmrraw = p.sb(un("vmrraw"), [1, 1024], F32, scope=p3v)
    p.wait("vector", ev, ev.v)
    p.vec("memset", vtmp[:, 0:128], 0.0)
    p.vec("memset", vsh[:, :, 16, :], 0.0)

    with nc.psum_tensor(un("vpps"), [128, S], F32) as vpps, \
         nc.psum_tensor(un("vtpsA"), [128, 128], BF16) as vtpsA, \
         nc.psum_tensor(un("vtpsB"), [128, 128], BF16) as vtpsB:
        vtps = [vtpsA, vtpsB]
        p.wait("tensor", wl[0], wv_ld)
        p.wait("tensor", ldg, ld_eye)
        p.wait("tensor", ev, ev.v)
        p.wait("tensor", dv, dv.v)
        trs = [0, 0]
        e_vt = 0
        dv_v0 = 0
        tr_mm = 0
        for hp in range(NCH):
            p.wait("tensor", ev, e_vt)
            for c in range(NCH):
                for n in range(4):
                    p.mm(vpps[:, 512 * n:512 * (n + 1)],
                         wbuf[:, c, 128 * hp:128 * (hp + 1)],
                         lnq[:, c, 512 * n:512 * (n + 1)],
                         start=(c == 0), stop=(c == NCH - 1),
                         inc=mm if (c == NCH - 1 and n == 3) else None)
            p.wait("scalar", mm, mm.v)
            if hp >= 1:
                p.wait("scalar", dv, dv_v0)
                p.wait("scalar", mm, tr_mm)
            e_vt = p.act(vtmp[:, 128:2176], vpps[:], AF.Identity, bias=bv(hp))
            if dbg == 29:
                dv_v0 = dv.v
                tr_mm = mm.v
                continue
            p.wait("vector", ev, e_vt)
            p.vec("tensor_reduce", vmcol[:, hp:hp + 1].unsqueeze(2),
                  vtmp[:, 128:2176].unsqueeze(1), axis=AX.X, op=ALU.add)
            dv_v0 = p.vec("tensor_copy", v0c[:, hp:hp + 1], vtmp[:, 128:129])
            if dbg == 30:
                tr_mm = mm.v
                continue
            p.wait("tensor", ev, e_vt)
            for j in range(17):
                g = j % 2
                p.wait("tensor", ev, trs[g])
                if j < 16:
                    tr_mm = p.t_transpose(vtps[g][:, :],
                                          vtmp[:, 126 + 128 * j:254 + 128 * j],
                                          eye[:], inc=mm)
                else:
                    tr_mm = p.t_transpose(vtps[g][0:2, :], vtmp[:, 2174:2176],
                                          eye[:], inc=mm)
                p.wait("scalar", mm, tr_mm)
                rows = 128 if j < 16 else 2
                p.act(vsh[0:rows, hp, j, 0:64], vtps[g][0:rows, 0:64], AF.Copy)
                trs[g] = p.act(vsh[0:rows, hp, j, 64:128], vtps[g][0:rows, 64:128],
                               AF.Copy)
        p.wait("tensor", dv, dv_v0)
        if dbg is None or dbg >= 32:
            p.wait("tensor", ev, trs[0])
            p.wait("tensor", ev, trs[1])
            t1 = p.t_transpose(vtps[0][0:NCH, :], v0c[:], eye[:], inc=mm)
            p.wait("scalar", mm, t1)
            e1 = p.act(vstage[:], vtps[0][0:NCH, :], AF.Copy, inc=ev)
            p.wait("gpsimd", ev, e1)
            b1 = p.dma("gpsimd", v0_dram, vstage[:], g0)
    if dbg is None or dbg >= 33:
        with nc.psum_tensor(un("vmtps"), [128, 128], F32) as vmtps:
            p.wait("tensor", ev, ev.v)
            t2 = p.mm(vmtps[0:NCH, :], vmcol[:], eyef[:], start=True, stop=True, inc=mm)
            p.wait("scalar", mm, t2)
            e2 = p.act(vmstage[:], vmtps[0:NCH, :], AF.Copy, inc=ev)
            p.wait("gpsimd", ev, e2)
            b2 = p.dma("gpsimd", vm_dram, vmstage[:], g0)
            p.wait("gpsimd", g0, g0.v)
            b3 = p.dma("gpsimd", v0p[:], v0_dram.rearrange("c w -> (c w)").unsqueeze(0), g0)
            b4 = p.dma("gpsimd", vmrraw[:], vm_dram.rearrange("c w -> (c w)").unsqueeze(0), g0)
            p.wait("scalar", g0, g0.v)
            e3 = p.act(vmr[:], vmrraw[:], AF.Copy, scale=1e-30 / S, inc=ev)
    if upto in (29, 30, 31, 32, 33):
        p3v.close(); p3p.close(); lnscope.close(); wscope.close()
        _finish_dummy(p, xodram, st)
        return p
    p3v.close()
    qkscope = ExitStack()
    qTe = p.sb(un("qTe"), [128, NCH, S], BF16, scope=qkscope)
    qTo = p.sb(un("qTo"), [128, NCH, S], BF16, scope=qkscope)
    p.wait("gpsimd", mm, mm.v)
    wq_ld = load_w(wq_d)
    p.wait("vector", ev, ev.v)
    p.vec("memset", qTe[64:128, :, :], 0.0)
    qz_dv = p.vec("memset", qTo[0:64, :, :], 0.0)
    p.wait("tensor", dv, qz_dv)

    def ev_q(m, ps):
        p.act(qTe[0:64, m, :], ps[0:64, :], AF.Identity, bias=bq(m)[0:64, :])
        return p.act(qTo[64:128, m, :], ps[64:128, :], AF.Identity, bias=bq(m)[64:128, :])

    emit_proj(p, wbuf, wq_ld, wl[0], lnq, ev_q, NCH, mm, ev)
    lnscope.close()
    wscope.close()
    if upto < 4:
        qkscope.close(); p3p.close()
        _finish_dummy(p, xodram, st)
        return p

    p3e = ExitStack()
    B0T = p.sb(un("B0Tb"), [128, 256], F32, scope=p3e)
    B0sh = p.sb(un("B0shb"), [128, 272], F32, scope=p3e)
    k0m = p.sb(un("k0m"), [128, NCH, 16], BF16, scope=p3e)
    q0m = p.sb(un("q0m"), [128, NCH, 16], BF16, scope=p3e)
    e0rows = p.sb(un("e0rows"), [16, S], BF16, scope=p3e)
    e0T = p.sb(un("e0T"), [128, 256], BF16, scope=p3e)
    er0 = p.sb(un("er0"), [128, 17 * 16], BF16, scope=p3e)
    c0sb = p.sb(un("c0sb"), [1, 1024], F32, scope=p3e)
    z0sb = p.sb(un("z0sb"), [1, 16], F32, scope=p3e)
    rz0 = p.sb(un("rz0"), [1, 16], F32, scope=p3e)
    e0sg = p.sb(un("e0sg"), [128, 272], F32, scope=p3e)
    e0sg2 = p.sb(un("e0sg2"), [128, 272], F32, scope=p3e)

    p.wait("sync", mm, mm.v)
    p.wait("sync", ev, ev.v)
    p.dma("sync", B0T[:], B0T_d, ldc)
    ld_tabs = p.dma("sync", B0sh[:], B0sh_d, ldc)
    p.wait("vector", ev, ev.v)
    p.vec("memset", k0m[:], 0.0)
    p.vec("memset", q0m[:], 0.0)
    for c in range(NCH):
        p.vec("tensor_copy", k0m[0:64, c, 2 * c:2 * c + 1], kT[0:64, c, 128:129])
        p.vec("tensor_copy", k0m[64:128, c, 2 * c + 1:2 * c + 2], kT[64:128, c, 128:129])
        p.vec("tensor_copy", q0m[0:64, c, 2 * c:2 * c + 1], qTe[0:64, c, 0:1])
        p.vec("tensor_copy", q0m[64:128, c, 2 * c + 1:2 * c + 2], qTo[64:128, c, 0:1])
    dv_pre = p.vec("memset", er0[:], 0.0)

    with nc.psum_tensor(un("s0ps"), [128, 256], F32) as s0ps, \
         nc.psum_tensor(un("r0ps"), [128, 272], F32) as r0ps, \
         nc.psum_tensor(un("tpsA"), [16, 128], BF16) as tpsA, \
         nc.psum_tensor(un("tpsB"), [16, 128], BF16) as tpsB:
        tps = [tpsA, tpsB]
        p.wait("tensor", dv, dv_pre)
        p.wait("tensor", ev, ev.v)
        p.wait("tensor", ldc, ld_tabs)
        for qt in range(16):
            for c in range(NCH):
                p.mm(s0ps[:, 16 * qt:16 * (qt + 1)], qTe[:, c, 128 * qt:128 * (qt + 1)],
                     k0m[:, c, :], start=(c == 0), stop=False)
                p.mm(s0ps[:, 16 * qt:16 * (qt + 1)], qTo[:, c, 128 * qt:128 * (qt + 1)],
                     k0m[:, c, :], start=False, stop=(c == NCH - 1),
                     inc=mm if (qt == 15 and c == NCH - 1) else None)
        for jt in range(17):
            off = 126 + 128 * jt
            w = 128 if jt < 16 else 2
            if w < 128:
                p.mm(r0ps[:, 16 * jt:16 * (jt + 1)], kT[:, 0, 0:128],
                     q0m[:, 0, :], start=True, stop=False)
            for c in range(NCH):
                p.mm(r0ps[0:w, 16 * jt:16 * (jt + 1)], kT[:, c, off:off + w],
                     q0m[:, c, :], start=(c == 0 and w == 128), stop=(c == NCH - 1 and w == 128),
                     inc=mm if (jt == 16 and c == NCH - 1 and w == 128) else None)
            if w < 128:
                p.mm(r0ps[:, 16 * jt:16 * (jt + 1)], kT[:, 0, 0:128],
                     q0m[:, 0, :], start=False, stop=True,
                     inc=mm if jt == 16 else None)
        p.wait("vector", mm, mm.v)
        p.wait("vector", ev, ev.v)
        p.vec("scalar_tensor_tensor", e0sg[:, 0:256], s0ps[:], 0.125, B0T[:],
              ALU.mult, ALU.add)
        d1 = p.vec("scalar_tensor_tensor", e0sg2[:, 0:256], e0sg[:, 0:256], GATEK,
                   e0sg[:, 0:256], ALU.mult, ALU.min)
        p.wait("scalar", dv, d1)
        e1 = p.act(e0T[:], e0sg2[:, 0:256], AF.Exp)
        p.wait("vector", ev, e1)
        p.vec("scalar_tensor_tensor", e0sg[:], r0ps[:], 0.125, B0sh[:],
              ALU.mult, ALU.add)
        d2 = p.vec("scalar_tensor_tensor", e0sg2[:], e0sg[:], GATEK,
                   e0sg[:], ALU.mult, ALU.min)
        p.wait("scalar", dv, d2)
        e2r = p.act(er0[:], e0sg2[:], AF.Exp)
        tevs = [0, 0]
        for qt in range(16):
            g = qt % 2
            p.wait("tensor", ev, max(e1, tevs[g]))
            tq = p.t_transpose(tps[g][:, :], e0T[:, 16 * qt:16 * (qt + 1)], eye[:], inc=mm)
            p.wait("scalar", mm, tq)
            tevs[g] = p.act(e0rows[:, 128 * qt:128 * (qt + 1)], tps[g][:, :], AF.Copy)
        p.wait("gpsimd", ev, max(tevs))
        g_e0 = p.dma("gpsimd", e0_dram, e0rows[:], g0)

    with nc.psum_tensor(un("c0ps"), [1, 1024], F32) as c0ps, \
         nc.psum_tensor(un("z0ps"), [1, 16], F32) as z0ps:
        p.wait("tensor", ev, ev.v)
        p.wait("tensor", dv, dv.v)
        for h in range(H):
            hp = h // 2
            o64 = 64 * (h % 2)
            for jt in range(17):
                p.mm(c0ps[:, 64 * h:64 * h + 64], er0[:, 16 * jt + h:16 * jt + h + 1],
                     vsh[:, hp, jt, o64:o64 + 64], start=(jt == 0), stop=False)
                p.mm(z0ps[:, h:h + 1], er0[:, 16 * jt + h:16 * jt + h + 1],
                     common["ones_colbf"][:], start=(jt == 0), stop=(jt == 16))
            mz = p.mm(c0ps[:, 64 * h:64 * h + 64], common["one1bf"][:],
                      vmr[:, 64 * h:64 * h + 64], start=False, stop=True,
                      inc=mm if h == H - 1 else None)
        p.wait("scalar", mm, mz)
        p.act(c0sb[:], c0ps[:], AF.Copy)
        ez0 = p.act(z0sb[:], z0ps[:], AF.Copy)
    p.wait("vector", ev, ez0)
    d3 = p.vec("tensor_scalar", rz0[:], z0sb[:], 1e-30, None, ALU.add)
    p.wait("scalar", dv, d3)
    p.act(z0sb[:], rz0[:], AF.Ln)
    erz = p.act(rz0[:], z0sb[:], AF.Exp, scale=-1.0)
    with nc.psum_tensor(un("ovps"), [64, 16], F32) as ovps:
        p.wait("tensor", ev, erz)
        for h in range(H):
            mv = p.mm(ovps[:, h:h + 1], c0sb[:, 64 * h:64 * h + 64], rz0[:, h:h + 1],
                      start=True, stop=True, inc=mm if h == H - 1 else None)
        p.wait("scalar", mm, mv)
        e_ov = p.act(ov[:], ovps[:], AF.Copy)
    p3e.close()
    if upto < 5:
        qkscope.close(); p3p.close()
        _finish_dummy(p, xodram, st)
        return p

    p3b = ExitStack()
    Bfirst = p.sb(un("Bfirstb"), [128, 2048], F32, scope=p3b)
    Blast = p.sb(un("Blastb"), [128, 2048], F32, scope=p3b)
    e0hb = p.sb(un("e0hb"), [1, S], BF16, scope=p3b)
    p.wait("sync", mm, mm.v)
    p.wait("sync", ev, ev.v)
    p.wait("sync", dv, dv.v)
    p.wait("gpsimd", mm, mm.v)
    p.wait("gpsimd", ev, ev.v)
    p.wait("gpsimd", g0, g0.v)
    p.dma("sync", Bfirst[:], Bfirst_d, ldc)
    ld_tabs2a = p.dma("sync", Blast[:], Blast_d, ldc)

    p4 = ExitStack()
    zrow = p.sb(un("zrow"), [1, 1024], F32, scope=p4)
    Rsb = p.sb(un("Rsb"), [64, 1024], F32, scope=p4)
    oTtmp = p.sb(un("oTtmp"), [64, S], BF16, scope=p4)
    Pb = p.sb(un("Pb"), [128, 2, 1024], BF16, scope=p4)
    sgb = p.sb(un("sgb"), [128, 512], F32, scope=p4)
    agb = p.sb(un("agb"), [128, 512], F32, scope=p4)

    with nc.psum_tensor(un("Sps"), [128, 1024], F32) as Sps, \
         nc.psum_tensor(un("ctxps"), [64, 1024], F32) as ctxps, \
         nc.psum_tensor(un("zps"), [1, 1024], F32) as zps, \
         nc.psum_tensor(un("Rps"), [64, 1024], F32) as Rps:
        p.wait("tensor", ev, e_ov)
        p.wait("tensor", dv, dv.v)
        p.wait("vector", ldc, ld_tabs2a)
        p.wait("gpsimd", g0, g_e0)
        e0ld = p.dma("gpsimd", e0hb[:], e0_dram[0:1, :], e0s)
        last_S_dv = 0
        last_exp = [0, 0]
        pv_done = 0
        norm_dv = 0
        rbc_ev = 0
        oT_dma = []
        for h in range(H):
            hp, o64 = h // 2, 64 * (h % 2)
            qT_h = qTe if h % 2 == 0 else qTo
            p.wait("tensor", e0s, e0ld)
            for half in range(2):
                for g2 in range(2):
                    g = 2 * half + g2
                    Btab = Bfirst if g < 2 else Blast
                    p.wait("tensor", dv, last_S_dv)
                    for ti in range(4):
                        t = 4 * g + ti
                        off = 126 + 128 * t
                        p.mm(Sps[:, 256 * ti:256 * ti + 128],
                             kT[:, hp, off:off + 128],
                             qT_h[:, hp, 128 * t:128 * (t + 1)],
                             start=True, stop=True)
                        if t < 15:
                            p.mm(Sps[:, 256 * ti + 128:256 * (ti + 1)],
                                 kT[:, hp, off + 128:off + 256],
                                 qT_h[:, hp, 128 * t:128 * (t + 1)],
                                 start=True, stop=True,
                                 inc=mm if ti == 3 else None)
                        else:
                            p.mm(Sps[:, 256 * ti + 128:256 * (ti + 1)],
                                 kT[:, hp, 0:128],
                                 qT_h[:, hp, 128 * t:128 * (t + 1)],
                                 start=True, stop=False)
                            p.mm(Sps[0:2, 256 * ti + 128:256 * (ti + 1)],
                                 kT[:, hp, off + 128:off + 130],
                                 qT_h[:, hp, 128 * t:128 * (t + 1)],
                                 start=False, stop=False)
                            p.mm(Sps[:, 256 * ti + 128:256 * (ti + 1)],
                                 kT[:, hp, 0:128],
                                 qT_h[:, hp, 128 * t:128 * (t + 1)],
                                 start=False, stop=True,
                                 inc=mm if ti == 3 else None)
                    Mg = mm.v
                    p.wait("vector", mm, Mg)
                    p.wait("vector", ev, max(last_exp))
                    p.wait("scalar", mm, pv_done)
                    for n2 in range(2):
                        Bsl = Btab[:, 256 * ((4 * g) % 8) + 512 * n2:
                                   256 * ((4 * g) % 8) + 512 * (n2 + 1)]
                        p.wait("vector", ev, max(last_exp))
                        p.vec("scalar_tensor_tensor", sgb[:], Sps[:, 512 * n2:512 * (n2 + 1)],
                              0.125, Bsl, ALU.mult, ALU.add)
                        dgg = p.vec("scalar_tensor_tensor", agb[:], sgb[:], GATEK, sgb[:],
                                    ALU.mult, ALU.min)
                        p.wait("scalar", dv, dgg)
                        last_exp[g % 2] = p.act(Pb[:, g % 2, 512 * n2:512 * (n2 + 1)],
                                                agb[:], AF.Exp)
                    last_S_dv = dgg
                p.wait("tensor", ev, last_exp[0])
                p.wait("tensor", ev, last_exp[1])
                if pv_done:
                    p.wait("tensor", dv, norm_dv)
                for g2 in range(2):
                    g = 2 * half + g2
                    for ti in range(4):
                        t = 4 * g + ti
                        cc = 128 * (t % 8)
                        p.mm(ctxps[:, cc:cc + 128], vsh[:, hp, t, o64:o64 + 64],
                             Pb[:, g % 2, 256 * ti:256 * ti + 128], start=True, stop=False)
                        p.mm(ctxps[:, cc:cc + 128], vsh[:, hp, t + 1, o64:o64 + 64],
                             Pb[:, g % 2, 256 * ti + 128:256 * (ti + 1)],
                             start=False, stop=False)
                        p.mm(ctxps[:, cc:cc + 128], v0p[:, 64 * h:64 * h + 64],
                             e0hb[:, 128 * t:128 * (t + 1)], start=False, stop=False)
                        p.mm(ctxps[:, cc:cc + 128], vmr[:, 64 * h:64 * h + 64],
                             common["ones_rowbf"][:], start=False, stop=True)
                        p.mm(zps[:, cc:cc + 128], common["ones_colbf"][:],
                             Pb[:, g % 2, 256 * ti:256 * ti + 128], start=True, stop=False)
                        p.mm(zps[:, cc:cc + 128], common["ones_colbf"][:],
                             Pb[:, g % 2, 256 * ti + 128:256 * (ti + 1)],
                             start=False, stop=False)
                        p.mm(zps[:, cc:cc + 128], common["one1bf"][:],
                             e0hb[:, 128 * t:128 * (t + 1)], start=False, stop=True,
                             inc=mm if (g2 == 1 and ti == 3) else None)
                pv_done = mm.v
                p.wait("scalar", mm, pv_done)
                ez = p.act(zrow[:], zps[:], AF.Identity)
                p.wait("vector", ev, ez)
                dz = p.vec("tensor_scalar", zrow[:], zrow[:], 1e-30, None, ALU.add)
                p.wait("scalar", dv, dz)
                p.act(zrow[:], zrow[:], AF.Ln)
                ez2 = p.act(zrow[:], zrow[:], AF.Exp, scale=-1.0)
                if dbgz is not None:
                    p.wait("sync", ev, ez2)
                    p.dma("sync", dbgz[h:h + 1, half, :].rearrange("a b -> a b") if False else dbgz[h, half, :].unsqueeze(0), zrow[:], st[0])
                p.wait("tensor", ev, ez2)
                p.wait("tensor", ev, rbc_ev)
                for n in range(2):
                    mv = p.mm(Rps[:, 512 * n:512 * (n + 1)], common["ones_rowf"][0:1, 0:64],
                              zrow[:, 512 * n:512 * (n + 1)], start=True, stop=True,
                              inc=mm if n == 1 else None)
                p.wait("scalar", mm, mv)
                rbc_ev = p.act(Rsb[:], Rps[:], AF.Copy)
                if oT_dma and half == 0:
                    p.wait("vector", oTs, oT_dma[-1])
                p.wait("vector", ev, rbc_ev)
                norm_dv = p.vec("tensor_mul", oTtmp[:, 1024 * half:1024 * (half + 1)],
                                ctxps[:], Rsb[:])
            p.wait("vector", ev, e_ov)
            norm_dv = p.vec("tensor_copy", oTtmp[:, 0:1], ov[:, h:h + 1])
            p.wait("gpsimd", dv, norm_dv)
            o64h = 64 * (h % 2)
            oT_dma.append(p.dma("gpsimd", oT_dram[o64h:o64h + 64, hp, :], oTtmp[:], oTs))
            if h + 1 < H:
                p.wait("gpsimd", mm, pv_done)
                e0ld = p.dma("gpsimd", e0hb[:], e0_dram[h + 1:h + 2, :], e0s)
    if dbgv is not None:
        p.wait("sync", dv, dv.v)
        p.dma("sync", dbgv, vsh.ap().rearrange("p a b c -> p (a b c)"), st[0])
        p.dma("sync", dbgr[0:1, :], v0p[:], st[0])
        p.dma("sync", dbgr[1:2, :], vmr[:], st[0])
    p4.close()
    p3b.close()
    qkscope.close()
    p3p.close()
    if upto < 6:
        _finish_dummy(p, xodram, st)
        return p

    p5 = ExitStack()
    wbuf2 = p.sb(un("wbuf2"), [128, NCH, D], BF16, scope=p5, side="right")
    y = p.sb(un("yattn"), [128, NCH, S], F32)
    ostr = p.sb(un("ostr"), [128, 2, S], BF16, scope=p5)
    osm = [p.sem("osA"), p.sem("osB")]
    p.wait("gpsimd", mm, mm.v)
    p.wait("gpsimd", dv, dv.v)
    p.wait("gpsimd", ev, ev.v)
    wo_ld = p.dma("gpsimd", wbuf2[:, :, :],
                  wo_d.rearrange("c p f -> p c f"), wl[1])
    p.wait("sync", oTs, oTs.v)
    ev_yv = None
    with nc.psum_tensor(un("prps5"), [128, 2, S], F32) as prps:
        p.wait("tensor", wl[1], wo_ld)
        p.wait("tensor", ev, ev.v)
        seq = [(m, c) for m in range(NCH) for c in range(NCH)]
        loads, cons = {}, {}
        evs = []
        for i in range(2):
            loads[i] = p.dma("sync", ostr[:, i % 2, :], oT_dram[:, seq[i][1], :], osm[i % 2])
        for i, (m, c) in enumerate(seq):
            g = m % 2
            slot = i % 2
            if c == 0 and m >= 2:
                p.wait("tensor", ev, evs[m - 2])
            p.wait("tensor", osm[slot], loads[i])
            for n in range(4):
                p.mm(prps[:, g, 512 * n:512 * (n + 1)],
                     wbuf2[:, c, 128 * m:128 * (m + 1)],
                     ostr[:, slot, 512 * n:512 * (n + 1)],
                     start=(c == 0), stop=(c == NCH - 1),
                     inc=mm if n == 3 else None)
            cons[i] = mm.v
            j = i + 2
            if j < len(seq):
                p.wait("sync", mm, cons[i])
                loads[j] = p.dma("sync", ostr[:, j % 2, :], oT_dram[:, seq[j][1], :], osm[j % 2])
            if c == NCH - 1:
                p.wait("scalar", mm, mm.v)
                evs.append(p.act(y[:, m, :], prps[:, g, :], AF.Identity, bias=bo(m)))
        ev_yv = evs[-1]
    p5.close()
    if upto < 7:
        _finish_dummy(p, xodram, st)
        return p

    emit_postln_residual(p, y, ev_yv, xdram, xodram, common, lx, mm, ev, dv, st)
    p.wait("sync", st[0], st[0].v)
    p.wait("sync", st[1], st[1].v)
    return p


def build_l0_moe(nc, upto=7):
    p = Prog(nc)
    xdram = nc.dram_tensor("x", [128, NCH, S], F32, kind="ExternalInput").ap()
    xodram = nc.dram_tensor("xo", [128, NCH, S], F32, kind="ExternalOutput").ap()
    emit_l0_moe(p, xdram, xodram, upto)
    p.run()


def emit_l0_moe(p, xdram, xodram, upto=7):
    nc = p.nc
    gw_d = nc.dram_tensor("gwc", [NCH, 128, 8], F32, kind="ExternalInput").ap()
    W1_d = nc.dram_tensor("W1c", [9, NCH, 128, FF], F32, kind="ExternalInput").ap()
    W2_d = nc.dram_tensor("W2c", [9, 32, 128, D], F32, kind="ExternalInput").ap()
    b1_d = nc.dram_tensor("b1c", [128, 9, 32], F32, kind="ExternalInput").ap()
    bs2_d = nc.dram_tensor("bs2c", [128, 8], F32, kind="ExternalInput").ap()
    b2_d = nc.dram_tensor("b2r", [8, D], F32, kind="ExternalInput").ap()
    sel_d = nc.dram_tensor("selc", [8, 8 * 128], F32, kind="ExternalInput").ap()
    eye_d = nc.dram_tensor("eyeB", [128, 128], F32, kind="ExternalInput").ap()
    scd_t = nc.dram_tensor("scd", [8, S], F32, kind="Internal")
    gath_t = nc.dram_tensor("gath", [128, 1024], F32, kind="Internal")
    gdbg = nc.dram_tensor("gdbg", [8, S], F32,
                          kind="ExternalOutput").ap() if upto in (0, 1) else None
    ldbg = nc.dram_tensor("ldbg", [128, 128], F32,
                          kind="ExternalOutput").ap() if upto == 0 else None
    tdbg = nc.dram_tensor("tdbg", [1, 1], F32,
                          kind="ExternalOutput").ap() if upto in (0, 1) else None

    lx = [p.sem("lx0"), p.sem("lx1")]
    wl2 = [p.sem("wl20"), p.sem("wl21")]
    mm = p.sem("mm")
    st = [p.sem("st0"), p.sem("st1")]
    g0 = p.sem("g0")
    cc = p.sem("cc")
    ag = p.sem("ag")
    dv, ev = p.ec["vector"], p.ec["scalar"]

    common = make_common(p)
    ones_colf = p.sb(un("ones_colf"), [128, 1], F32)
    ones8cf = p.sb(un("ones8cf"), [8, 1], F32)
    ones_row8f = p.sb(un("ones_row8f"), [1, 8], F32)
    ones_rowW = p.sb(un("ones_rowW"), [1, 512], F32)
    p.vec("memset", ones_colf[:], 1.0)
    p.vec("memset", ones8cf[:], 1.0)
    p.vec("memset", ones_row8f[:], 1.0)
    p.vec("memset", ones_rowW[:], 1.0)

    sc_e = ExitStack()
    lnq = p.sb("lnq", [128, NCH, S], BF16, scope=sc_e)
    gwsb = p.sb("gwsb", [128, NCH, 8], F32, scope=sc_e)
    eyef = p.sb("eyef", [128, 128], F32, scope=sc_e)
    b1sb = p.sb("b1sb", [128, 9, 32], F32, scope=sc_e)
    bs2sb = p.sb("bs2sb", [128, 8], F32, scope=sc_e)
    b2sb = p.sb("b2sb", [8, D], BF16, scope=sc_e)
    selsb = p.sb("selsb", [8, 8 * 128], BF16, scope=sc_e)
    gatesT = p.sb("gatesT", [8, S], BF16, scope=sc_e)
    bs_lo = p.sb("bs_lo", [1, 1], F32, scope=sc_e)
    logits = p.sb("logitsb", [128, 128], F32, scope=sc_e)

    FB = 512
    NFB = FF // FB
    NFT = FB // 128
    steps = [(e, fb) for e in range(9) for fb in range(NFB)]
    NST = len(steps)
    sc_w = ExitStack()
    W1sb = p.sb("W1sb", [128, 2, NCH, FB], BF16, scope=sc_w, side="right")
    W2sb = p.sb("W2sb", [128, 2, NFT, 1024], BF16, scope=sc_w, side="right")

    def load_blk(slot, e, fb):
        p.dma("gpsimd", W1sb[:, slot, :, :],
              W1_d[e, :, :, FB * fb:FB * (fb + 1)].rearrange("c p f -> p c f"),
              wl2[slot])
        v = p.dma("gpsimd", W2sb[:, slot, :, :],
                  W2_d[e, NFT * fb:NFT * (fb + 1)].rearrange("c p f -> p c f"),
                  wl2[slot])
        return v

    wld = [0] * NST
    wld[0] = load_blk(0, 0, 0)
    for c in range(NCH):
        ld_gw = p.dma("sync", gwsb[:, c, :], gw_d[c], g0)
    p.dma("sync", eyef[:], eye_d, g0)
    p.dma("sync", b1sb[:], b1_d, g0)
    ld_small = p.dma("sync", bs2sb[:], bs2_d, g0)
    p.dma("gpsimd", b2sb[:], b2_d, wl2[0])
    ld_sel = p.dma("gpsimd", selsb[:], sel_d, wl2[0])
    wld[0] = ld_sel

    ps_a = ExitStack()
    LGref = [None, None, None]
    lg_evac = [0, None]

    def rg2(c, xslot, ldv):
        if LGref[0] is None:
            LGref[0] = ps_a.enter_context(nc.psum_tensor("LGps", [128, 128], F32))
            LGref[1] = ps_a.enter_context(nc.psum_tensor("RSps", [128, 128], F32))
            LGref[2] = ps_a.enter_context(nc.psum_tensor("G1ps", [1, 8], F32))
        LGps = LGref[0]
        p.wait("tensor", lx[c % 2], ldv)
        if c == 0:
            p.wait("tensor", g0, ld_small)
            p.wait("tensor", ev, ev.v)
            p.wait("tensor", dv, dv.v)
        else:
            p.wait("tensor", lg_evac[1], lg_evac[0])
        for ts in range(16):
            p.mm(LGps[:, 8 * ts:8 * (ts + 1)], xslot[:, 128 * ts:128 * (ts + 1)],
                 gwsb[:, c, :], start=True, stop=True,
                 inc=mm if ts == 15 else None)
        if c == 0:
            p.wait("scalar", mm, mm.v)
            lg_evac[0] = p.act(logits[:], LGps[:], AF.Copy)
            lg_evac[1] = ev
        else:
            p.wait("vector", mm, mm.v)
            lg_evac[0] = p.vec("tensor_add", logits[:], logits[:], LGps[:])
            lg_evac[1] = dv

    kp = ExitStack()
    murow, Abcs = emit_ln_featmajor(p, xdram, lnq, common, lx, mm, ev, dv,
                                    raw_gate2=rg2, keep=kp)
    v_lnq = dv.v
    yacc = p.sb("yacc", [128, NCH, S], F32, scope=sc_e)
    sc_g = ExitStack()
    esb = p.sb("esb", [128, 128], F32, scope=sc_g)
    scT = p.sb("scT", [8, S], F32, scope=sc_g)
    zrow = p.sb("zrow", [1, S], F32, scope=sc_g)
    taurow = p.sb("taurow", [1, S], F32, scope=sc_g)
    allsc = p.sb("allsc", [128, 1024], F32, scope=sc_g)
    ind8 = p.sb("ind8", [8, S], F32, scope=sc_g)
    shiftb = p.sb("shiftb", [128, 1024], F32, scope=sc_g)
    negmidcol = p.sb("negmidcol", [128, 1], F32, scope=sc_g)
    cntcol = p.sb("cntcol", [128, 1], F32, scope=sc_g)
    bs_hi = p.sb("bs_hi", [1, 1], F32, scope=sc_g)
    bs_mid = p.sb("bs_mid", [1, 1], F32, scope=sc_g)
    bs_nm = p.sb("bs_nm", [1, 1], F32, scope=sc_g)
    bs_cond = p.sb("bs_cond", [1, 1], F32, scope=sc_g)
    bs_d1 = p.sb("bs_d1", [1, 1], F32, scope=sc_g)
    bs_d2 = p.sb("bs_d2", [1, 1], F32, scope=sc_g)

    LGps, RSps, G1ps = LGref
    for c in range(NCH):
        p.mm(G1ps[:], ones_colf[:], gwsb[:, c, :], start=(c == 0), stop=(c == NCH - 1),
             inc=mm if c == NCH - 1 else None)
    p.wait("scalar", mm, mm.v)
    g1neg = p.sb("g1neg", [1, 8], F32, scope=sc_g)
    e_g1 = p.act(g1neg[:], G1ps[:], AF.Copy, scale=-1.0)
    p.wait("tensor", ev, e_g1)
    p.wait("tensor", lg_evac[1], lg_evac[0])
    for ts in range(16):
        p.mm(LGps[:, 8 * ts:8 * (ts + 1)], murow[:, 128 * ts:128 * (ts + 1)],
             g1neg[:], start=True, stop=True,
             inc=mm if ts == 15 else None)
        p.mm(RSps[:, 8 * ts:8 * (ts + 1)], Abcs[0:1, 128 * ts:128 * (ts + 1)],
             ones_row8f[:], start=True, stop=True,
             inc=mm if ts == 15 else None)
    p.wait("scalar", mm, mm.v)
    rstile = p.sb("rstile", [128, 128], F32, scope=sc_g)
    e_rs = p.act(rstile[:], RSps[:], AF.Copy)
    p.wait("vector", ev, e_rs)
    p.vec("tensor_add", logits[:], logits[:], LGps[:])
    v_lg = p.vec("tensor_mul", logits[:], logits[:], rstile[:])
    kp.close()

    p.wait("scalar", dv, v_lg)
    e_es = p.act(esb[:], logits[:], AF.Exp)
    ps_a.close()
    with nc.psum_tensor("TEpsA", [8, 128], F32) as TEpsA, \
         nc.psum_tensor("TEpsB", [8, 128], F32) as TEpsB, \
         nc.psum_tensor("Zps", [1, S], F32) as Zps:
        p.wait("tensor", ev, e_es)
        p.wait("tensor", g0, ld_small)
        te = [0, 0]
        for ts in range(16):
            tp = TEpsA if ts % 2 == 0 else TEpsB
            if te[ts % 2]:
                p.wait("tensor", ev, te[ts % 2])
            mv = p.t_transpose(tp[0:8, :], esb[:, 8 * ts:8 * (ts + 1)], eyef[:], inc=mm)
            p.wait("scalar", mm, mv)
            te[ts % 2] = p.act(scT[:, 128 * ts:128 * (ts + 1)], tp[0:8, :], AF.Copy)
        p.wait("tensor", ev, max(te))
        for n in range(4):
            mv = p.mm(Zps[:, 512 * n:512 * (n + 1)], ones8cf[:],
                      scT[:, 512 * n:512 * (n + 1)], start=True, stop=True,
                      inc=mm if n == 3 else None)
        p.wait("scalar", mm, mv)
        p.act(zrow[:], Zps[:], AF.Ln)
        e_rz = p.act(zrow[:], zrow[:], AF.Exp, scale=-1.0)
    with nc.psum_tensor("RZps", [8, S], F32) as RZps:
        p.wait("tensor", ev, e_rz)
        for n in range(4):
            mv = p.mm(RZps[:, 512 * n:512 * (n + 1)], ones_row8f[:],
                      zrow[:, 512 * n:512 * (n + 1)], start=True, stop=True,
                      inc=mm if n == 3 else None)
        p.wait("vector", mm, mv)
        v_sc = p.vec("tensor_mul", scT[:], scT[:], RZps[:])
    if upto == 0:
        p.vec("memset", bs_lo[:], 0.0)
        p.wait("sync", dv, v_sc)
        fsb = p.sb(un("gdbgf"), [8, S], F32, scope=sc_g)
        vv = p.vec("tensor_copy", fsb[:], scT[:])
        p.wait("sync", dv, vv)
        p.dma("sync", gdbg, fsb[:], st[0])
        p.dma("sync", tdbg, bs_lo[:], st[0])
        p.dma("sync", ldbg, logits[:], st[0])
        p.wait("sync", st[0], st[0].v)
        sc_g.close()
        sc_e.close()
        _finish_dummy(p, xodram, st)
        return
    p.wait("sync", dv, v_sc)
    v_out = p.dma("sync", scd_t.ap(), scT[:], ag)
    p.wait("gpsimd", ag, v_out)
    p.on("gpsimd", lambda e: e.collective_compute(
        "AllGather", ALU.bypass, replica_groups=[list(range(8))],
        ins=[scd_t.ap().opt()], outs=[gath_t.ap().opt()]).then_inc(cc.h, 1))
    cc.bump(1)
    p.wait("sync", cc, 1)
    v_all = p.dma("sync", allsc[:], gath_t.ap(), ag)

    KTOP = 32768
    p.vec("memset", bs_lo[:], 0.0)
    p.vec("memset", bs_hi[:], 1.1)
    p.vec("memset", bs_mid[:], 0.55)
    with nc.psum_tensor("NMps", [128, 1], F32) as NMps, \
         nc.psum_tensor("CTps", [1, 1], F32) as CTps:
        p.wait("vector", ag, v_all)
        e_evac = 0
        ct_read = 0
        for it in range(34):
            v_nm = p.vec("tensor_scalar", bs_nm[:], bs_mid[:], -1.0, None, ALU.mult)
            p.wait("tensor", dv, v_nm)
            if e_evac:
                p.wait("tensor", ev, e_evac)
            m1 = p.mm(NMps[:], common["ones_rowf"][:], bs_nm[:], start=True,
                      stop=True, inc=mm)
            p.wait("scalar", mm, m1)
            e_evac = p.act(negmidcol[:], NMps[:], AF.Copy)
            e_sh = p.act(shiftb[:], allsc[:], AF.Identity, bias=negmidcol[:])
            p.wait("vector", ev, e_sh)
            v_ct = p.vec("tensor_scalar", shiftb[:], shiftb[:], 0.0, None,
                         ALU.is_ge, ALU.add, accum_out=cntcol[:])
            p.wait("tensor", dv, v_ct)
            if ct_read:
                p.wait("tensor", dv, ct_read)
            m2 = p.mm(CTps[:], cntcol[:], ones_colf[:], start=True, stop=True, inc=mm)
            p.wait("vector", mm, m2)
            p.vec("tensor_scalar", bs_cond[:], CTps[:], KTOP - 0.5, None, ALU.is_ge)
            p.vec("tensor_sub", bs_d1[:], bs_mid[:], bs_lo[:])
            p.vec("tensor_mul", bs_d1[:], bs_d1[:], bs_cond[:])
            p.vec("tensor_add", bs_lo[:], bs_lo[:], bs_d1[:])
            p.vec("tensor_sub", bs_d2[:], bs_hi[:], bs_mid[:])
            p.vec("tensor_mul", bs_d2[:], bs_d2[:], bs_cond[:])
            p.vec("tensor_add", bs_hi[:], bs_mid[:], bs_d2[:])
            p.vec("tensor_add", bs_mid[:], bs_lo[:], bs_hi[:])
            ct_read = p.vec("tensor_scalar", bs_mid[:], bs_mid[:], 0.5, None, ALU.mult)
        p.wait("tensor", dv, ct_read)
    with nc.psum_tensor("TRps", [1, S], F32) as TRps:
        for n in range(4):
            mv = p.mm(TRps[:, 512 * n:512 * (n + 1)], bs_lo[:], ones_rowW[:],
                      start=True, stop=True, inc=mm if n == 3 else None)
        p.wait("scalar", mm, mv)
        e_tr = p.act(taurow[:], TRps[:], AF.Copy)
    with nc.psum_tensor("TPps", [8, S], F32) as TPps:
        p.wait("tensor", ev, e_tr)
        for n in range(4):
            mv = p.mm(TPps[:, 512 * n:512 * (n + 1)], ones_row8f[:],
                      taurow[:, 512 * n:512 * (n + 1)], start=True, stop=True,
                      inc=mm if n == 3 else None)
        p.wait("vector", mm, mv)
        p.vec("tensor_tensor", ind8[:], scT[:], TPps[:], ALU.is_ge)
        v_gates = p.vec("tensor_mul", gatesT[:], scT[:], ind8[:])

    if upto != 1:
        sc_g.close()
        sc_h = ExitStack()
        hg = p.sb("hgsb", [128, NFT, S], BF16, scope=sc_h)
        htmp = p.sb("htmp", [128, 2, 512], BF16, scope=sc_h)
        Gsb = p.sb("Gsb", [128, S], BF16, scope=sc_h)

    if upto == 1:
        p.wait("sync", dv, v_gates)
        fsb = p.sb(un("gdbgf"), [8, S], F32, scope=sc_g)
        vv = p.vec("tensor_copy", fsb[:], gatesT[:])
        p.wait("sync", dv, vv)
        p.dma("sync", gdbg, fsb[:], st[0])
        p.dma("sync", tdbg, bs_lo[:], st[0])
        p.wait("sync", st[0], st[0].v)
        sc_g.close()
        sc_e.close()
        _finish_dummy(p, xodram, st)
        return

    gpc = p.ec["gpsimd"]
    with nc.psum_tensor("GTps", [128, 512], F32) as GTps, \
         nc.psum_tensor("Hps", [128, 2, 512], F32) as Hps, \
         nc.psum_tensor("Yps", [128, 4, 512], F32) as Yps:
        mmdone = [0] * NST
        gt_ev = 0
        vhg_last = 0
        ehg_last = 0
        y_mm_last = 0
        hev = [0, 0]
        yev = [None] * 4
        for i, (e, fb) in enumerate(steps):
            slot = i % 2
            if i + 1 < NST:
                en, fbn = steps[i + 1]
                if i >= 1:
                    p.wait("gpsimd", mm, mmdone[i - 1])
                wld[i + 1] = load_blk(slot ^ 1, en, fbn)
            p.wait("tensor", wl2[slot], wld[i])
            if fb == 0 and e < 8:
                if e == 0:
                    p.wait("tensor", dv, v_gates)
                if vhg_last:
                    p.wait("scalar", dv, vhg_last)
                for n in range(4):
                    if gt_ev:
                        p.wait("tensor", ev, gt_ev)
                    mv = p.mm(GTps[:], selsb[0:8, 128 * e:128 * (e + 1)],
                              gatesT[0:8, 512 * n:512 * (n + 1)], start=True,
                              stop=True, inc=mm)
                    p.wait("scalar", mm, mv)
                    gt_ev = p.act(Gsb[:, 512 * n:512 * (n + 1)], GTps[:], AF.Copy)
            if i == 0:
                p.wait("tensor", dv, v_lnq)
            first_hg = True
            for ft in range(NFT):
                for n in range(4):
                    g = (ft * 4 + n) % 2
                    if hev[g]:
                        p.wait("tensor", ev, hev[g])
                    for dc in range(NCH):
                        mv = p.mm(Hps[:, g, :],
                                  W1sb[:, slot, dc, 128 * ft:128 * (ft + 1)],
                                  lnq[:, dc, 512 * n:512 * (n + 1)],
                                  start=(dc == 0), stop=(dc == NCH - 1),
                                  inc=mm if dc == NCH - 1 else None)
                    p.wait("scalar", mm, mv)
                    if first_hg and y_mm_last:
                        if e == 8:
                            p.wait("scalar", mm, y_mm_last)
                        else:
                            p.wait("vector", mm, y_mm_last)
                    bcol = b1sb[:, e, NFT * fb + ft:NFT * fb + ft + 1]
                    if e == 8:
                        hev[g] = p.act(hg[:, ft, 512 * n:512 * (n + 1)],
                                       Hps[:, g, :], AF.Gelu_apprx_tanh, bias=bcol)
                        ehg_last = hev[g]
                    else:
                        if vhg_last:
                            p.wait("scalar", dv, vhg_last)
                        hev[g] = p.act(htmp[:, g, :], Hps[:, g, :],
                                       AF.Gelu_apprx_tanh, bias=bcol)
                        p.wait("vector", ev, hev[g])
                        if first_hg:
                            p.wait("vector", ev, gt_ev)
                        vhg_last = p.vec("tensor_mul",
                                         hg[:, ft, 512 * n:512 * (n + 1)],
                                         htmp[:, g, :], Gsb[:, 512 * n:512 * (n + 1)])
                    first_hg = False
            if e == 8:
                p.wait("tensor", ev, ehg_last)
            else:
                p.wait("tensor", dv, vhg_last)
            for dt in range(NCH):
                for n in range(4):
                    g = (dt * 4 + n) % 4
                    if yev[g] is not None:
                        p.wait("tensor", yev[g][0], yev[g][1])
                    extra = (e == 0 and fb == 0)
                    for fc in range(NFT):
                        mv = p.mm(Yps[:, g, :],
                                  W2sb[:, slot, fc, 128 * dt:128 * (dt + 1)],
                                  hg[:, fc, 512 * n:512 * (n + 1)],
                                  start=(fc == 0),
                                  stop=(fc == NFT - 1 and not extra),
                                  inc=mm if (fc == NFT - 1 and not extra) else None)
                    if extra:
                        mv = p.mm(Yps[:, g, :], b2sb[0:8, 128 * dt:128 * (dt + 1)],
                                  gatesT[0:8, 512 * n:512 * (n + 1)],
                                  start=False, stop=True, inc=mm)
                    if e == 0 and fb == 0:
                        p.wait("scalar", mm, mv)
                        yev[g] = (ev, p.act(yacc[:, dt, 512 * n:512 * (n + 1)],
                                            Yps[:, g, :], AF.Identity,
                                            bias=bs2sb[:, dt:dt + 1]))
                    else:
                        p.wait("vector", mm, mv)
                        yev[g] = (dv, p.vec("tensor_add",
                                            yacc[:, dt, 512 * n:512 * (n + 1)],
                                            yacc[:, dt, 512 * n:512 * (n + 1)],
                                            Yps[:, g, :]))
                    y_mm_last = mv
            mmdone[i] = mm.v
    sc_h.close()
    sc_w.close()

    emit_postln_residual(p, yacc, 0, xdram, xodram, common, lx, mm, ev, dv, st)
    p.wait("sync", st[0], st[0].v)
    p.wait("sync", st[1], st[1].v)
    sc_e.close()


def build_l1_attn(nc):
    p = Prog(nc)
    xdram = nc.dram_tensor("x", [128, NCH, S], F32, kind="ExternalInput").ap()
    xodram = nc.dram_tensor("xo", [128, NCH, S], F32, kind="ExternalOutput").ap()
    emit_l1_attn(p, xdram, xodram)
    p.run()


def emit_l1_attn(p, xdram, xodram):
    nc = p.nc
    wq_d = nc.dram_tensor("wqcC", [NCH, 128, D], F32, kind="ExternalInput").ap()
    wk_d = nc.dram_tensor("wkcC", [NCH, 128, D], F32, kind="ExternalInput").ap()
    wv_d = nc.dram_tensor("wvcC", [NCH, 128, D], F32, kind="ExternalInput").ap()
    wo_d = nc.dram_tensor("wocC", [NCH, 128, D], F32, kind="ExternalInput").ap()
    bo_d = nc.dram_tensor("bocols", [128, 8], F32, kind="ExternalInput").ap()
    sc_d = nc.dram_tensor("scalec", [16, 1], F32, kind="ExternalInput").ap()
    E2_d = nc.dram_tensor("E2c", [NCH, 128, 16], F32, kind="ExternalInput").ap()
    E2T_d = nc.dram_tensor("E2Tc", [NCH, 16, 128], F32, kind="ExternalInput").ap()

    lx = [p.sem("lx0"), p.sem("lx1")]
    wl = p.sem("wl")
    mm = p.sem("mm")
    st = [p.sem("st0"), p.sem("st1")]
    g0 = p.sem("g0")
    sq = p.sem("sq")
    so = p.sem("so")
    dv, ev = p.ec["vector"], p.ec["scalar"]

    common = make_common(p)
    wbuf = p.sb("wbuf", [128, NCH, D], BF16)
    scalecol = p.sb("scalecol", [16, 1], F32)
    bocols = p.sb("bocols_sb", [128, 8], F32)
    sc_ln = ExitStack()
    lnq = p.sb("lnqC", [128, NCH, S], BF16, scope=sc_ln, side="right")

    wq_ld = p.dma("gpsimd", wbuf[:, :, :],
                  wq_d.rearrange("c p f -> p c f"), wl)
    p.dma("sync", scalecol[:], sc_d, g0)
    ld_g0 = p.dma("sync", bocols[:], bo_d, g0)

    emit_ln_featmajor(p, xdram, lnq, common, lx, mm, ev, dv)

    sc_att = ExitStack()
    qT = p.sb("qT", [128, NCH, S], BF16, scope=sc_att)
    kT = p.sb("kT", [128, NCH, S], BF16, scope=sc_att)
    vtok65 = p.sb("vtok65", [128, 16, 16, 65], BF16, scope=sc_att)
    E2sb = p.sb("E2sb", [128, NCH, 16], BF16, scope=sc_att)
    E2Tsb = p.sb("E2Tsb", [16, NCH, 128], BF16, scope=sc_att)
    ssq = p.sb("ssq", [16, S], F32, scope=sc_att)
    qfac = p.sb("qfac", [16, S], BF16, scope=sc_att)
    zr = p.sb("zr", [1, 512], F32, scope=sc_att)
    resb = p.sb("resb", [64, 512], BF16, scope=sc_att)
    e2l = p.sem("e2l")
    for c in range(NCH):
        p.dma("gpsimd", E2sb[:, c, :], E2_d[c], e2l)
        e2_ld = p.dma("gpsimd", E2Tsb[:, c, :], E2T_d[c], e2l)

    def mk_evac(dst):
        def evac(m, ps):
            return p.act(dst[:, m, :], ps, AF.Copy, inc=None)
        return evac
    p.wait("tensor", dv, dv.v)
    emit_proj(p, wbuf, wq_ld, wl, lnq, mk_evac(qT), NCH, mm, ev)
    p.wait("gpsimd", mm, mm.v)
    wk_ld = p.dma("gpsimd", wbuf[:, :, :],
                  wk_d.rearrange("c p f -> p c f"), wl)
    emit_proj(p, wbuf, wk_ld, wl, lnq, mk_evac(kT), NCH, mm, ev)
    p.wait("gpsimd", mm, mm.v)
    wv_ld = p.dma("gpsimd", wbuf[:, :, :],
                  wv_d.rearrange("c p f -> p c f"), wl)
    v_ones = p.vec("memset", vtok65[:, :, :, :], 1.0)
    with nc.psum_tensor("Vps", [128, 2, 512], F32) as Vps:
        p.wait("tensor", wl, wv_ld)
        p.wait("tensor", ev, ev.v)
        p.wait("scalar", dv, v_ones)
        vps_guard = [0, 0]
        for ts in range(16):
            for mh in range(2):
                g = (ts * 2 + mh) % 2
                if vps_guard[g]:
                    p.wait("tensor", ev, vps_guard[g])
                for dc in range(NCH):
                    mv = p.mm(Vps[:, g, :], lnq[:, dc, 128 * ts:128 * (ts + 1)],
                              wbuf[:, dc, 512 * mh:512 * (mh + 1)],
                              start=(dc == 0), stop=(dc == NCH - 1),
                              inc=mm if dc == NCH - 1 else None)
                p.wait("scalar", mm, mv)
                vps_guard[g] = p.act(vtok65[:, ts, 8 * mh:8 * (mh + 1), 0:64],
                                     Vps[:, g, :], AF.Copy)
        p.wait("tensor", ev, max(vps_guard))
    sc_ln.close()

    sqpp = p.sb("sqpp", [128, 2, S], BF16, scope=sc_att)
    v_qn = v_kn = 0
    for which, tgt, bias in (("q", qT, scalecol[:]), ("k", kT, 0.0)):
        with nc.psum_tensor(un("SSps"), [16, S], F32) as SSps:
            p.wait("tensor", ev, ev.v)
            p.wait("tensor", dv, dv.v)
            sq_guard = [0, 0]
            for c in range(NCH):
                g = c % 2
                if sq_guard[g]:
                    p.wait("scalar", mm, sq_guard[g])
                e_sq = p.act(sqpp[:, g, :], tgt[:, c, :], AF.Square)
                p.wait("tensor", ev, e_sq)
                if c == 0 and which == "q":
                    p.wait("tensor", e2l, e2_ld)
                    p.wait("tensor", g0, ld_g0)
                for n in range(4):
                    mv = p.mm(SSps[:, 512 * n:512 * (n + 1)], E2sb[:, c, :],
                              sqpp[:, g, 512 * n:512 * (n + 1)],
                              start=(c == 0), stop=(c == NCH - 1),
                              inc=mm if n == 3 else None)
                sq_guard[g] = mv
            p.wait("vector", mm, mm.v)
            p.vec("tensor_scalar", ssq[:], SSps[:], 1e-24, None, ALU.add)
        d_ln = p.ec["vector"].v
        p.wait("scalar", dv, d_ln)
        p.act(ssq[:], ssq[:], AF.Ln)
        e_f = p.act(qfac[:], ssq[:], AF.Exp, scale=-0.5, bias=bias)
        with nc.psum_tensor(un("EXps"), [128, 2, 512], F32) as EXps:
            p.wait("tensor", ev, e_f)
            ex_guard = [0, 0]
            for c in range(NCH):
                for n in range(4):
                    g = (c * 4 + n) % 2
                    if ex_guard[g]:
                        p.wait("tensor", dv, ex_guard[g])
                    mv = p.mm(EXps[:, g, :], E2Tsb[0:16, c, :],
                              qfac[0:16, 512 * n:512 * (n + 1)],
                              start=True, stop=True, inc=mm)
                    p.wait("vector", mm, mv)
                    ex_guard[g] = p.vec("tensor_mul",
                                        tgt[:, c, 512 * n:512 * (n + 1)],
                                        tgt[:, c, 512 * n:512 * (n + 1)],
                                        EXps[:, g, :])
            if which == "q":
                v_qn = max(ex_guard)
            else:
                v_kn = max(ex_guard)

    qhb = p.sb("qhb", [64, S], BF16, scope=sc_att)
    khb = p.sb("khb", [64, S], BF16, scope=sc_att)
    ohb = p.sb("ohb", [64, S], BF16, scope=sc_att)
    oTpk = p.sb("oTpk", [128, NCH, S], BF16, side="right")
    Esb = p.sb("EsbC", [128, 16, 512], BF16, scope=sc_att)
    with nc.psum_tensor("Eps4", [128, 4, 512], F32) as Eps4, \
         nc.psum_tensor("REps", [64, 512], F32) as REps, \
         nc.psum_tensor("Cps", [65, 512], F32) as Cps:
        p.wait("sync", dv, max(v_qn, v_kn))
        p.wait("tensor", dv, max(v_qn, v_kn))
        qh_mm = kh_mm = 0
        pv_prev = [0] * 16
        eps_guard = [0, 0]
        z_ln_prev = 0
        re_copy_prev = 0
        v_o = 0
        so_val = 0
        for h in range(16):
            hc, half = h // 2, h % 2
            if half == 0:
                qh_ap, kh_ap = qT[0:64, hc, :], kT[0:64, hc, :]
            else:
                p.wait("sync", mm, max(qh_mm, kh_mm))
                rq = p.dma("sync", qhb[:], qT[64:128, hc, :], sq)
                rk = p.dma("sync", khb[:], kT[64:128, hc, :], sq)
                p.wait("tensor", sq, rk)
                qh_ap, kh_ap = qhb[:], khb[:]
            for qt in range(4):
                exp_pair = [0] * 8
                for j in range(9):
                    if j < 8:
                        bp = j % 2
                        if eps_guard[bp]:
                            p.wait("tensor", ev, eps_guard[bp])
                        for kc in (2 * j, 2 * j + 1):
                            mv = p.mm(Eps4[:, kc % 4, :],
                                      kh_ap[:, 128 * kc:128 * (kc + 1)],
                                      qh_ap[:, 512 * qt:512 * (qt + 1)],
                                      start=True, stop=True,
                                      inc=mm if kc % 2 == 1 else None)
                        if half == 1:
                            qh_mm = kh_mm = mv
                        p.wait("scalar", mm, mv)
                        p.wait("scalar", mm, max(pv_prev[2 * j], pv_prev[2 * j + 1]))
                        eps_guard[bp] = p.act(Esb[:, 2 * j:2 * j + 2, :],
                                              Eps4[:, 2 * bp:2 * bp + 2, :], AF.Exp)
                        exp_pair[j] = eps_guard[bp]
                    if j >= 1:
                        jj = j - 1
                        p.wait("tensor", ev, exp_pair[jj])
                        for kc in (2 * jj, 2 * jj + 1):
                            if kc == 0:
                                if z_ln_prev:
                                    p.wait("tensor", ev, z_ln_prev)
                                if v_o:
                                    p.wait("tensor", dv, v_o)
                            pv_prev[kc] = p.mm(Cps[:, :], vtok65[:, kc, h, :],
                                               Esb[:, kc, :], start=(kc == 0),
                                               stop=(kc == 15), inc=mm)
                p.wait("scalar", mm, pv_prev[15])
                p.act(zr[:], Cps[64:65, :], AF.Ln)
                z_ln_prev = p.act(zr[:], zr[:], AF.Exp, scale=-1.0)
                if re_copy_prev:
                    p.wait("tensor", ev, re_copy_prev)
                p.wait("tensor", ev, z_ln_prev)
                mv = p.mm(REps[:], common["ones_rowf"][0:1, 0:64], zr[:],
                          start=True, stop=True, inc=mm)
                p.wait("scalar", mm, mv)
                if v_o:
                    p.wait("scalar", dv, v_o)
                re_copy_prev = p.act(resb[:], REps[:], AF.Copy)
                p.wait("vector", mm, pv_prev[15])
                p.wait("vector", ev, re_copy_prev)
                if half == 0:
                    dst = oTpk[0:64, hc, 512 * qt:512 * (qt + 1)]
                else:
                    if qt == 0 and so_val:
                        p.wait("vector", so, so_val)
                    dst = ohb[:, 512 * qt:512 * (qt + 1)]
                v_o = p.vec("tensor_mul", dst, Cps[0:64, :], resb[:])
            if half == 1:
                p.wait("sync", dv, v_o)
                so_val = p.dma("sync", oTpk[64:128, hc, :], ohb[:], so)

    sc_att.close()
    p.wait("gpsimd", mm, mm.v)
    wo_ld = p.dma("gpsimd", wbuf[:, :, :],
                  wo_d.rearrange("c p f -> p c f"), wl)
    ysb = p.sb("ysbC", [128, NCH, S], F32)
    p.wait("tensor", dv, dv.v)
    p.wait("tensor", so, so_val)

    def evac_y(m, ps):
        return p.act(ysb[:, m, :], ps, AF.Identity, bias=bocols[:, m:m + 1])
    ev_y = emit_proj(p, wbuf, wo_ld, wl, oTpk, evac_y, NCH, mm, ev)
    emit_postln_residual(p, ysb, ev_y, xdram, xodram, common, lx, mm, ev, dv, st)
    p.wait("sync", st[0], st[0].v)
    p.wait("sync", st[1], st[1].v)


def prep_l1_attn_consts(w_q, w_kv, w_out, b_out, scale):
    w_kv = np.asarray(w_kv, np.float32)
    E2 = np.zeros((NCH, 128, 16), np.float32)
    for c in range(NCH):
        E2[c, 0:64, 2 * c] = 1.0
        E2[c, 64:128, 2 * c + 1] = 1.0
    return {
        "wqcC": wchunks(w_q), "wkcC": wchunks(w_kv[:, :D]),
        "wvcC": wchunks(w_kv[:, D:]), "wocC": wchunks(w_out),
        "bocols": col_layout(b_out),
        "scalec": np.asarray(scale, np.float32).reshape(16, 1),
        "E2c": E2, "E2Tc": np.ascontiguousarray(E2.transpose(0, 2, 1)),
    }


def build_l1_ffn(nc):
    p = Prog(nc)
    xdram = nc.dram_tensor("x", [128, NCH, S], F32, kind="ExternalInput").ap()
    xodram = nc.dram_tensor("xo", [128, NCH, S], F32, kind="ExternalOutput").ap()
    emit_l1_ffn(p, xdram, xodram)
    p.run()


def emit_l1_ffn(p, xdram, xodram):
    nc = p.nc
    SH = 512
    NBL = S // SH
    wi_d = nc.dram_tensor("wiTc", [NCH, 128, FF], F32, kind="ExternalInput").ap()
    wo_d = nc.dram_tensor("woTc", [32, 128, D], F32, kind="ExternalInput").ap()
    A_d = nc.dram_tensor("Ac", [NE, NCH, 128, 16], F32, kind="ExternalInput").ap()
    B_d = nc.dram_tensor("Bc", [NE, 16, FF], F32, kind="ExternalInput").ap()
    gw_d = nc.dram_tensor("gwD", [NCH, 128, 8], F32, kind="ExternalInput").ap()
    gb_d = nc.dram_tensor("gbrow", [1, 8], F32, kind="ExternalInput").ap()
    bi_d = nc.dram_tensor("bicols", [128, 32], F32, kind="ExternalInput").ap()
    bo_d = nc.dram_tensor("borow", [1, D], F32, kind="ExternalInput").ap()
    sel_d = nc.dram_tensor("selD", [8, 8 * 128], F32, kind="ExternalInput").ap()
    eye_d = nc.dram_tensor("eyeD", [128, 128], F32, kind="ExternalInput").ap()

    lx = [p.sem("lx0"), p.sem("lx1")]
    wl = [p.sem("wl0"), p.sem("wl1")]
    ol = [p.sem("ol0"), p.sem("ol1")]
    mm = p.sem("mm")
    st = [p.sem("st0"), p.sem("st1")]
    g0 = p.sem("g0")
    sl = p.sem("sl")
    dv, ev = p.ec["vector"], p.ec["scalar"]
    gp = p.ec["gpsimd"]

    common = make_common(p)
    ysb = p.sb("ysbD", [128, NCH, S], F32)
    sc_ln = ExitStack()
    lnq = p.sb("lnqD", [128, NCH, S], BF16, scope=sc_ln, side="right")

    emit_ln_featmajor(p, xdram, lnq, common, lx, mm, ev, dv)
    v_lnq = dv.v

    sc_f = ExitStack()
    gwsb = p.sb("gwsbD", [128, NCH, 8], BF16, scope=sc_f)
    gbsb = p.sb("gbsb", [1, 8], BF16, scope=sc_f)
    bisb = p.sb("bisb", [128, 32], F32, scope=sc_f)
    borow = p.sb("borowD", [1, D], BF16, scope=sc_f)
    selsb = p.sb("selD_sb", [8, 8 * 128], BF16, scope=sc_f)
    eyef = p.sb("eyefD", [128, 128], F32, scope=sc_f)
    eyebf = p.sb("eyebfD", [128, 128], BF16, scope=sc_f)
    Asb = p.sb("Asb", [128, NE, NCH, 16], BF16, scope=sc_f)
    Bst = p.sb("BstD", [16, 2, NE, 128], BF16, scope=sc_f)
    esb = p.sb("esbD", [128, 8], F32, scope=sc_f)
    scn = p.sb("scnD", [128, 8], F32, scope=sc_f)
    zcol = p.sb("zcolD", [128, 1], F32, scope=sc_f)
    rcpc = p.sb("rcpcD", [128, 1], F32, scope=sc_f)
    m1c = p.sb("m1c", [128, 1], F32, scope=sc_f)
    m2c = p.sb("m2c", [128, 1], F32, scope=sc_f)
    nmc = p.sb("nmc", [128, 1], F32, scope=sc_f)
    vsc = p.sb("vsc", [128, 1], F32, scope=sc_f)
    ind1 = p.sb("ind1", [128, 8], F32, scope=sc_f)
    ind2 = p.sb("ind2", [128, 8], F32, scope=sc_f)
    dsb = p.sb("dsbD", [128, 8], F32, scope=sc_f)
    gtok = p.sb("gtok", [128, 8], F32, scope=sc_f)
    gatesDb = p.sb("gatesDb", [8, S], BF16, scope=sc_f)
    vs2row = p.sb("vs2row", [1, S], BF16, scope=sc_f)
    tsb = p.sb("tsbD", [16, NE, SH], BF16, scope=sc_f)
    GD = p.sb("GDsb", [128, NE, SH], BF16, scope=sc_f)
    Hb = p.sb("HbD", [128, 32, SH], BF16, scope=sc_f)
    Hacc = p.sb("HaccD", [128, SH], F32, scope=sc_f)
    tmpD = p.sb("tmpD", [128, SH], F32, scope=sc_f)
    gtile = p.sb("gtileD", [128, 4, SH], BF16, scope=sc_f)
    btile = p.sb("btileD", [128, 2, SH], BF16, scope=sc_f)
    wist = p.sb("wistD", [128, 2, NCH, 128], BF16, scope=sc_f)
    wost = p.sb("wostD", [128, 2, 512], BF16, scope=sc_f)

    cl = p.sem("cl")
    p.wait("gpsimd", dv, v_lnq)
    p.wait("sync", dv, v_lnq)
    for c in range(NCH):
        p.dma("gpsimd", gwsb[:, c, :], gw_d[c], cl)
    p.dma("gpsimd", borow[:], bo_d, cl)
    p.dma("gpsimd", selsb[:], sel_d, cl)
    p.dma("gpsimd", eyebf[:], eye_d, cl)
    p.dma("gpsimd", gbsb[:], gb_d, cl)
    for e in range(NE):
        for dc in range(NCH):
            ld_wl0 = p.dma("gpsimd", Asb[:, e, dc, :], A_d[e, dc], cl)
    p.dma("sync", bisb[:], bi_d, g0)
    ld_g0 = p.dma("sync", eyef[:], eye_d, g0)

    with nc.psum_tensor("LGpsD", [128, 16, 8], F32) as LGps, \
         nc.psum_tensor("TGpsA", [8, 128], F32) as TGpsA, \
         nc.psum_tensor("TGpsB", [8, 128], F32) as TGpsB, \
         nc.psum_tensor("TVpsA", [1, 128], F32) as TVpsA, \
         nc.psum_tensor("TVpsB", [1, 128], F32) as TVpsB:
        p.wait("tensor", dv, v_lnq)
        p.wait("tensor", cl, ld_wl0)
        p.wait("tensor", g0, ld_g0)
        p.wait("tensor", ev, ev.v)
        tg_guard = [0, 0]
        tv_guard = [0, 0]
        lg_read = 0
        for ts in range(16):
            for c in range(NCH):
                p.mm(LGps[:, ts, :], lnq[:, c, 128 * ts:128 * (ts + 1)],
                     gwsb[:, c, :], start=(c == 0), stop=False)
            mv = p.mm(LGps[:, ts, :], common["ones_rowbf"][:],
                      gbsb[:], start=False, stop=True, inc=mm)
            p.wait("scalar", mm, mv)
            if lg_read:
                p.wait("scalar", dv, lg_read)
            e_e = p.act(esb[:], LGps[:, ts, :], AF.Exp, accum_out=zcol[:])
            p.act(rcpc[:], zcol[:], AF.Ln)
            e_r = p.act(rcpc[:], rcpc[:], AF.Exp, scale=-1.0)
            e_s = p.act(scn[:], esb[:], AF.Identity, scale=rcpc[:])
            p.wait("vector", ev, e_s)
            p.vec("tensor_scalar", dsb[:], scn[:], 0.0, None, ALU.add,
                  ALU.max, accum_out=m1c[:])
            v_n = p.vec("tensor_scalar", nmc[:], m1c[:], -1.0, None, ALU.mult)
            p.wait("scalar", dv, v_n)
            e_d = p.act(dsb[:], scn[:], AF.Identity, bias=nmc[:])
            p.wait("vector", ev, e_d)
            p.vec("tensor_scalar", ind1[:], dsb[:], 0.0, None, ALU.is_ge)
            p.vec("scalar_tensor_tensor", dsb[:], ind1[:], -1e9, scn[:],
                  ALU.mult, ALU.add)
            p.vec("tensor_scalar", tmpD[:, 0:8], dsb[:], 0.0, None, ALU.add,
                  ALU.max, accum_out=m2c[:])
            v_n2 = p.vec("tensor_scalar", nmc[:], m2c[:], -1.0, None, ALU.mult)
            p.wait("scalar", dv, v_n2)
            e_d2 = p.act(dsb[:], dsb[:], AF.Identity, bias=nmc[:])
            p.wait("vector", ev, e_d2)
            p.vec("tensor_scalar", ind2[:], dsb[:], 0.0, None, ALU.is_ge)
            p.vec("tensor_add", ind1[:], ind1[:], ind2[:])
            v_vs = p.vec("tensor_add", vsc[:], m1c[:], m2c[:])
            p.wait("scalar", dv, v_vs)
            e_g = p.act(gtok[:], ind1[:], AF.Identity, scale=vsc[:])
            g = ts % 2
            tg = TGpsA if g == 0 else TGpsB
            tv = TVpsA if g == 0 else TVpsB
            p.wait("tensor", ev, e_g)
            if tg_guard[g]:
                p.wait("tensor", ev, tg_guard[g])
            m_t = p.t_transpose(tg[0:8, :], gtok[:], eyef[:], inc=mm)
            if tv_guard[g]:
                p.wait("tensor", ev, tv_guard[g])
            m_t2 = p.t_transpose(tv[0:1, :], vsc[:], eyef[:], inc=mm)
            p.wait("scalar", mm, m_t2)
            tg_guard[g] = p.act(gatesDb[:, 128 * ts:128 * (ts + 1)], tg[0:8, :],
                                AF.Copy)
            tv_guard[g] = p.act(vs2row[:, 128 * ts:128 * (ts + 1)], tv[0:1, :],
                                AF.Copy, scale=2.0)
            lg_read = p.ec["vector"].v
        e_gall = max(max(tg_guard), max(tv_guard))

    def load_wi(slot, bl, fc):
        p.dma("gpsimd", wist[:, slot, :, :],
              wi_d[:, :, 128 * fc:128 * (fc + 1)].rearrange("c p f -> p c f"),
              wl[slot])
        v = p.dma("gpsimd", Bst[0:16, slot, :, :],
                  B_d[:, :, 128 * fc:128 * (fc + 1)].rearrange("e r f -> r e f"),
                  wl[slot])
        return v

    def load_wo(slot, fc, dth):
        return p.dma("gpsimd", wost[:, slot, :],
                     wo_d[fc, :, 512 * dth:512 * (dth + 1)], ol[slot])

    wld = {}
    wld[(0, 0)] = load_wi(0, 0, 0)
    if True:
        p.wait("tensor", ev, e_gall)
        gt_ev = 0
        t_ev = 0
        b_guard = [0, 0]
        h_guard = [0, 0, 0, 0]
        y_read = 0
        acc_mm = [0, 0, 0, 0]
        hb_act = [0, 0]
        hb_ev = 0
        wo_mm = {}
        for bl in range(NBL):
            sb0 = SH * bl
            ph1 = ExitStack()
            Tps = ph1.enter_context(nc.psum_tensor(un("TpsD"), [16, SH], F32))
            GTps = ph1.enter_context(nc.psum_tensor(un("GTpsD"), [128, SH], F32))
            p.wait("tensor", ev, ev.v)
            p.wait("tensor", dv, dv.v)
            for e in range(NE):
                if t_ev:
                    p.wait("tensor", ev, t_ev)
                for dc in range(NCH):
                    mv = p.mm(Tps[:], Asb[:, e, dc, :],
                              lnq[:, dc, sb0:sb0 + SH],
                              start=(dc == 0), stop=(dc == NCH - 1),
                              inc=mm if dc == NCH - 1 else None)
                p.wait("scalar", mm, mv)
                t_ev = p.act(tsb[:, e, :], Tps[:], AF.Copy)
            for e in range(NE):
                if gt_ev:
                    p.wait("tensor", ev, gt_ev)
                mv = p.mm(GTps[:], selsb[0:8, 128 * e:128 * (e + 1)],
                          gatesDb[0:8, sb0:sb0 + SH], start=True, stop=True,
                          inc=mm)
                p.wait("scalar", mm, mv)
                gt_ev = p.act(GD[:, e, :], GTps[:], AF.Copy)
            ph1.close()
            ph2 = ExitStack()
            Bps = ph2.enter_context(nc.psum_tensor(un("BpsD"), [128, 2, SH], F32))
            Php = ph2.enter_context(nc.psum_tensor(un("PhD"), [128, 4, SH], F32))
            HAps = ph2.enter_context(nc.psum_tensor(un("HAD"), [128, 2, SH], F32))
            p.wait("tensor", ev, ev.v)
            p.wait("tensor", dv, dv.v)
            p.wait("scalar", mm, mm.v)
            for fc in range(32):
                nxt = (bl, fc + 1) if fc + 1 < 32 else ((bl + 1, 0) if bl + 1 < NBL else None)
                if nxt is not None:
                    slot_n = (fc + 1) % 2
                    p.wait("gpsimd", mm, mm.v)
                    wld[nxt] = load_wi(slot_n, *nxt)
                slot = fc % 2
                g = fc % 2
                p.wait("tensor", wl[slot], wld[(bl, fc)])
                if b_guard[g]:
                    p.wait("tensor", ev, b_guard[g])
                for dc in range(NCH):
                    mv = p.mm(Bps[:, g, :], wist[:, slot, dc, :],
                              lnq[:, dc, sb0:sb0 + SH],
                              start=(dc == 0), stop=(dc == NCH - 1),
                              inc=mm if dc == NCH - 1 else None)
                p.wait("scalar", mm, mv)
                b_guard[g] = p.act(btile[:, g, :], Bps[:, g, :], AF.Identity,
                                   bias=bisb[:, fc:fc + 1])
                stt_v = [0] * NE
                for e in range(NE):
                    gph = e % 4
                    if h_guard[gph]:
                        p.wait("tensor", dv, h_guard[gph])
                    p.mm(Php[:, gph, :], Bst[0:16, slot, e, :],
                         tsb[:, e, :], start=True, stop=False)
                    if e == 0:
                        p.wait("tensor", ev, b_guard[g])
                    mv = p.mm(Php[:, gph, :], eyebf[:], btile[:, g, :],
                              start=False, stop=True, inc=mm)
                    p.wait("vector", mm, mv)
                    if acc_mm[e % 4]:
                        p.wait("vector", mm, acc_mm[e % 4])
                    h_guard[gph] = p.vec("scalar_tensor_tensor",
                                         gtile[:, e % 4, :], Php[:, gph, :],
                                         0.0, GD[:, e, :], ALU.max, ALU.mult)
                    stt_v[e] = h_guard[gph]
                    if e >= 2:
                        j = e - 2
                        p.wait("tensor", dv, stt_v[j])
                        if j == 0:
                            p.wait("tensor", ev, hb_act[fc % 2])
                        acc_mm[j % 4] = p.mm(
                            HAps[:, fc % 2, :], eyebf[:], gtile[:, j % 4, :],
                            start=(j == 0), stop=False, inc=mm)
                for j in (NE - 2, NE - 1):
                    p.wait("tensor", dv, stt_v[j])
                    acc_mm[j % 4] = p.mm(
                        HAps[:, fc % 2, :], eyebf[:], gtile[:, j % 4, :],
                        start=False, stop=(j == NE - 1), inc=mm)
                p.wait("scalar", mm, acc_mm[(NE - 1) % 4])
                hb_act[fc % 2] = p.act(Hb[:, fc, :], HAps[:, fc % 2, :], AF.Copy)
                hb_ev = hb_act[fc % 2]
            ph2.close()
            ph3 = ExitStack()
            Yps = ph3.enter_context(nc.psum_tensor(un("YpsD"), [128, 4, SH], F32))
            p.wait("tensor", ev, ev.v)
            p.wait("tensor", dv, dv.v)
            p.wait("tensor", ev, hb_ev)
            for dth in range(2):
                wo_ld = {}
                p.wait("gpsimd", mm, mm.v)
                wo_ld[0] = load_wo(0, 0, dth)
                for fc in range(32):
                    slot = fc % 2
                    if fc + 1 < 32:
                        if fc >= 1:
                            p.wait("gpsimd", mm, wo_mm.get((fc - 1) % 2, 0))
                        wo_ld[fc + 1] = load_wo((fc + 1) % 2, fc + 1, dth)
                    p.wait("tensor", ol[slot], wo_ld[fc])
                    if fc == 0 and y_read:
                        p.wait("tensor", ev, y_read)
                    for dt in range(4):
                        mv = p.mm(Yps[:, dt, :], wost[:, slot, 128 * dt:128 * (dt + 1)],
                                  Hb[:, fc, :], start=(fc == 0), stop=False,
                                  inc=mm if dt == 3 else None)
                    wo_mm[slot] = mv
                for dt in range(4):
                    dtg = 4 * dth + dt
                    mv = p.mm(Yps[:, dt, :], borow[:, 128 * dtg:128 * (dtg + 1)],
                              vs2row[:, sb0:sb0 + SH], start=False, stop=True,
                              inc=mm)
                p.wait("scalar", mm, mv)
                for dt in range(4):
                    dtg = 4 * dth + dt
                    y_read = p.act(ysb[:, dtg, sb0:sb0 + SH], Yps[:, dt, :],
                                   AF.Copy)
            ph3.close()
    ev_y = y_read
    sc_f.close()
    sc_ln.close()

    emit_postln_residual(p, ysb, ev_y, xdram, xodram, common, lx, mm, ev, dv, st)
    p.wait("sync", st[0], st[0].v)
    p.wait("sync", st[1], st[1].v)


def prep_l1_ffn_consts(wi, bi, wo, bo, lora_A, lora_B, gate_w, gate_b):
    sel = np.zeros((8, 8 * 128), np.float32)
    for e in range(8):
        sel[e, 128 * e:128 * (e + 1)] = 1.0
    return {
        "wiTc": wchunks(np.asarray(wi, np.float32).T),
        "woTc": np.ascontiguousarray(
            np.asarray(wo, np.float32).T.reshape(32, 128, D)),
        "Ac": np.ascontiguousarray(
            np.asarray(lora_A, np.float32).transpose(0, 2, 1).reshape(NE, NCH, 128, 16)),
        "Bc": np.ascontiguousarray(np.asarray(lora_B, np.float32).transpose(0, 2, 1)),
        "gwD": wchunks(np.asarray(gate_w, np.float32).T),
        "gbrow": np.asarray(gate_b, np.float32).reshape(1, 8),
        "bicols": col_layout(bi),
        "borow": np.asarray(bo, np.float32).reshape(1, D),
        "selD": sel,
        "eyeD": np.eye(128, dtype=np.float32),
    }


class _NCProxy:
    def __init__(self, nc):
        object.__setattr__(self, "_nc", nc)

    def __getattr__(self, a):
        return getattr(self._nc, a)

    def psum_tensor(self, name, *a, **k):
        return self._nc.psum_tensor(un(name), *a, **k)

    def sbuf_tensor(self, name, *a, **k):
        return self._nc.sbuf_tensor(un(name), *a, **k)


def build_fused(nc):
    p = Prog(_NCProxy(nc))
    x0 = nc.dram_tensor("x", [128, NCH, S], F32, kind="ExternalInput").ap()
    x1 = nc.dram_tensor("x1d", [128, NCH, S], F32, kind="Internal").ap()
    x2 = nc.dram_tensor("x2d", [128, NCH, S], F32, kind="Internal").ap()
    x3 = nc.dram_tensor("x3d", [128, NCH, S], F32, kind="Internal").ap()
    xo = nc.dram_tensor("xo", [128, NCH, S], F32, kind="ExternalOutput").ap()
    phases = ((emit_l0_attn, x0, x1), (emit_l0_moe, x1, x2),
              (emit_l1_attn, x2, x3), (emit_l1_ffn, x3, xo))
    for emitter, xi, xod in phases:
        ph = ExitStack()
        p.default_scope = ph
        emitter(p, xi, xod)
        for eng in ("sync", "gpsimd"):
            p.wait(eng, p.sem("mm"), p.sem("mm").v)
            for ec in p.ec.values():
                p.wait(eng, ec, ec.v)
        ph.close()
    p.default_scope = None
    p.run()


def xcore_to_xT(x_core):
    return np.ascontiguousarray(np.asarray(x_core, np.float32).T.reshape(NCH, 128, S).transpose(1, 0, 2))


def xT_to_core(xo):
    return xo.transpose(1, 0, 2).reshape(D, S).T


def prep_l0_attn_inputs(x_core, wq, bq, wk, bk, wv, bv, wo, bo):
    Bfirst, Blast, B0T, B0sh = make_l0_consts()
    bcols = np.concatenate([col_layout(bq), col_layout(bk),
                            col_layout(bv), col_layout(bo)], axis=1)
    return {
        "x": xcore_to_xT(x_core),
        "wqc": wchunks(np.asarray(wq).T), "wkc": wchunks(np.asarray(wk).T),
        "wvc": wchunks(np.asarray(wv).T), "woc": wchunks(np.asarray(wo).T),
        "bcols": np.ascontiguousarray(bcols),
        "Bfirst": Bfirst, "Blast": Blast,
        "B0T": np.ascontiguousarray(B0T.reshape(128, 256)),
        "B0sh": np.ascontiguousarray(B0sh.reshape(128, 272)),
        "eye": np.eye(128, dtype=np.float32),
    }


def prep_l0_moe_consts(gate_weight, W1, b1, W2, b2, Ws1, bs1, Ws2, bs2):
    W1c = np.concatenate([np.asarray(W1, np.float32),
                          np.asarray(Ws1, np.float32)[None]], 0).reshape(9, 8, 128, FF)
    W2c = np.concatenate([np.asarray(W2, np.float32),
                          np.asarray(Ws2, np.float32)[None]], 0).reshape(9, 32, 128, D)
    b1c = np.stack([np.asarray(b, np.float32).reshape(32, 128).T
                    for b in list(b1) + [bs1]], 1)
    sel = np.zeros((8, 8 * 128), np.float32)
    for e in range(8):
        sel[e, 128 * e:128 * (e + 1)] = 1.0
    return {
        "gwc": wchunks(np.asarray(gate_weight, np.float32).T),
        "W1c": np.ascontiguousarray(W1c), "W2c": np.ascontiguousarray(W2c),
        "b1c": np.ascontiguousarray(b1c),
        "bs2c": col_layout(bs2),
        "b2r": np.ascontiguousarray(np.asarray(b2, np.float32)),
        "selc": sel,
        "eyeB": np.eye(128, dtype=np.float32),
    }


def kernel(**inputs):
    ii = {k: np.asarray(v) for k, v in inputs.items()}
    x = ii["L0_attn_x"].astype(np.float32)

    consts = prep_l0_moe_consts(
        ii["L0_ffn_gate_weight"], ii["L0_ffn_W1"], ii["L0_ffn_b1"],
        ii["L0_ffn_W2"], ii["L0_ffn_b2"], ii["L0_ffn_Ws1"], ii["L0_ffn_bs1"],
        ii["L0_ffn_Ws2"], ii["L0_ffn_bs2"])
    consts.update(prep_l1_attn_consts(
        ii["L1_attn_w_q"], ii["L1_attn_w_kv"], ii["L1_attn_w_out"],
        ii["L1_attn_b_out"], ii["L1_attn_scale"]))
    consts.update(prep_l1_ffn_consts(
        ii["L1_ffn_wi"], ii["L1_ffn_bi"], ii["L1_ffn_wo"], ii["L1_ffn_bo"],
        ii["L1_ffn_lora_A"], ii["L1_ffn_lora_B"], ii["L1_ffn_gate_w"],
        ii["L1_ffn_gate_b"]))

    nc = bass.Bass("TRN2", target_bir_lowering=False)
    build_fused(nc)
    in_maps = []
    for c in range(8):
        m = prep_l0_attn_inputs(
            x[c], ii["L0_attn_wq"], ii["L0_attn_bq"], ii["L0_attn_wk"],
            ii["L0_attn_bk"], ii["L0_attn_wv"], ii["L0_attn_bv"],
            ii["L0_attn_wo"], ii["L0_attn_bo"])
        m.update(consts)
        in_maps.append(m)
    res = run_bass_kernel_spmd(nc, in_maps, core_ids=list(range(8)))
    out = np.stack([xT_to_core(np.asarray(res.results[c]["xo"]))
                    for c in range(8)]).astype(np.float32)
    return out
